# version 07
# baseline (speedup 1.0000x reference)
import numpy as np
import ml_dtypes
import concourse.bass as bass
import concourse.mybir as mybir
from concourse.bass_utils import run_bass_kernel_spmd
from contextlib import ExitStack

F32 = mybir.dt.float32
BF16 = mybir.dt.bfloat16
AF = mybir.ActivationFunctionType
ALU = mybir.AluOpType
AX = mybir.AxisListType

S = 2048
D = 1024
TT = S // 128
CC = D // 128
GW = S + 1

NCORES = 8


ENGINES = ("tensor", "vector", "scalar", "gpsimd", "sync")
ENG_SEM = {"tensor": "T", "vector": "V", "scalar": "Sc", "gpsimd": "G", "sync": "Sy"}


class Sched:
    def __init__(self):
        self.ops = {e: [] for e in ENGINES}
        self.count = {}
        self.last_waited = {e: {} for e in ENGINES}
        self.sem_names = set(ENG_SEM.values())
        self._consts = {}
        self._const_pool = None

    def _tok(self, sem):
        return (sem, self.count.get(sem, 0))

    def op(self, eng, fn, waits=(), inc=True, dma_sem=None):
        sem = dma_sem if dma_sem else ENG_SEM[eng]
        amount = 16 if dma_sem else 1
        tgt = None
        if inc:
            self.count[sem] = self.count.get(sem, 0) + amount
            tgt = self.count[sem]
        lw = self.last_waited[eng]
        eff_waits = []
        for (ws, wc) in waits:
            if wc <= 0:
                continue
            if lw.get(ws, 0) >= wc:
                continue
            eff_waits.append((ws, wc))
            lw[ws] = wc
        self.ops[eng].append((tuple(eff_waits), fn, sem if inc else None, amount))
        return (sem, tgt if tgt is not None else self.count.get(sem, 0))

    def barrier(self):
        toks = [self._tok(s) for s in self.sem_names]
        for e in ENGINES:
            def mk(fn_eng=e):
                def fn(eng, sems):
                    return None
                return fn
            self.op(e, mk(), waits=toks, inc=False)
        return toks

    def const_col(self, name_val):
        val = name_val
        key = float(val)
        if key not in self._consts:
            idx = len(self._consts)
            assert idx < self._const_pool.shape[1] if hasattr(self._const_pool, "shape") else True
            tok = self.op("vector", lambda e, s, idx=idx, val=val: e.memset(
                self._const_pool[:, idx:idx + 1], val))
            self._consts[key] = (idx, tok)
        idx, tok = self._consts[key]
        return self._const_pool[:, idx:idx + 1], tok

    def run(self, nc, sems):
        with nc.Block() as block:
            for eng_name in ENGINES:
                deco = getattr(block, eng_name)
                ops = self.ops[eng_name]

                def make_body(ops=ops):
                    def body(eng):
                        for eff_waits, fn, sem, amount in ops:
                            for (ws, wc) in eff_waits:
                                eng.wait_ge(sems[ws], wc)
                            ins = fn(eng, sems)
                            if sem is not None:
                                assert ins is not None, "op with inc must return instruction"
                                ins.then_inc(sems[sem], amount)
                    return body

                deco(make_body())


class Stream:

    def __init__(self, sched, eng, name, depth=2):
        self.S = sched
        self.eng = eng
        self.depth = depth
        self.semn = [f"{name}{i}" for i in range(depth)]
        for s in self.semn:
            sched.sem_names.add(s)
        self.j = 0

    def dma(self, fn, waits=()):
        i = self.j % self.depth
        sem = self.semn[i]
        w = list(waits)
        if self.j >= self.depth:
            w.append((sem, self.S.count.get(sem, 0)))
        tok = self.S.op(self.eng, fn, waits=w, dma_sem=sem)
        self.j += 1
        return tok

    def total_toks(self):
        return [(s, self.S.count.get(s, 0)) for s in self.semn]


def mk_streams(Sch, H):
    H["LdIn"] = Stream(Sch, "sync", "LdIn", 4)
    H["LdA"] = Stream(Sch, "sync", "LdA", 2)
    H["LdB"] = Stream(Sch, "sync", "LdB", 2)
    H["St"] = Stream(Sch, "sync", "St", 2)
    H["Wt"] = Stream(Sch, "gpsimd", "Wt", 4)
    H["_streams"] = [H["LdIn"], H["LdA"], H["LdB"], H["St"], H["Wt"]]


def finish_program(nc, Sch, H):
    waits = []
    for st in H["_streams"]:
        waits += st.total_toks()
    Sch.op("sync", lambda e2, s: None, waits=waits, inc=False)
    semstack = ExitStack()
    sems = {}
    for name in sorted(Sch.sem_names):
        sems[name] = semstack.enter_context(nc.semaphore(name))
    Sch.run(nc, sems)
    semstack.close()



def _rotary_tables_l0ffn():
    DIM_HEAD = 64
    WINDOW = 256
    half = np.arange(0, DIM_HEAD, 2, dtype=np.float64)
    inv_freq = 1.0 / (10000.0 ** (half / DIM_HEAD))
    t = np.arange(S, dtype=np.float64)
    freqs = t[:, None] * inv_freq[None, :]
    freqs = np.concatenate([freqs, freqs], -1)
    base_scale = (half + 0.4 * DIM_HEAD) / (1.4 * DIM_HEAD)
    power = (t - S // 2) / (WINDOW // 2)
    scale = base_scale[None, :] ** power[:, None]
    scale = np.concatenate([scale, scale], -1)
    cos, sin = np.cos(freqs), np.sin(freqs)
    inv = 1.0 / scale
    cs_q = (cos * scale).T
    sn_q_t = (sin * scale).T
    cs_k = (cos * inv).T
    sn_k_t = (sin * inv).T
    def signed(sn):
        s = sn.copy()
        s[:32, :] *= -1.0
        return s
    sn_q = signed(sn_q_t)
    sn_k = signed(sn_k_t)
    rot = np.array([(p + 32) % 64 if (p % 64) < 32 else p - 32 for p in range(64)])
    sn_q = sn_q[rot]
    sn_k = sn_k[rot]
    def dup(a):
        return np.concatenate([a, a], axis=0).astype(np.float32)
    return dup(cs_q), dup(sn_q), dup(cs_k), dup(sn_k)


def _rope_tables_l1():
    HEAD_DIM = 64
    pos = np.arange(S, dtype=np.float64)
    inv_freq = 10000.0 ** (-np.arange(0, HEAD_DIM, 2, dtype=np.float64) / HEAD_DIM)
    ang = pos[:, None] * inv_freq[None, :]
    C = np.cos(ang).astype(np.float32)
    Sn = np.sin(ang).astype(np.float32)
    Ct = np.concatenate([C[t * 128:(t + 1) * 128] for t in range(TT)], axis=1)
    St = np.concatenate([Sn[t * 128:(t + 1) * 128] for t in range(TT)], axis=1)
    return Ct.astype(np.float32), St.astype(np.float32)


def _l0ffn_window_masks():
    masks = np.zeros((4, 128, 256), dtype=np.float32)
    i = np.arange(128)[:, None]
    j = np.arange(256)[None, :]
    for kt in range(4):
        k = kt * 128 + i
        masks[kt] = np.where((j >= k - 256) & (j <= k), 0.0, -1.0e30)
    return masks.astype(ml_dtypes.bfloat16)


def _l0attn_diag_masks():
    m = np.zeros((4, 128, 512), dtype=np.float32)
    i = np.arange(128)[:, None]
    j = np.arange(512)[None, :]
    for r in range(4):
        m[r] = np.where(j > r * 128 + i, -1.0e30, 0.0)
    return m


def _l1_tri_mask():
    i = np.arange(128)[:, None]
    j = np.arange(128)[None, :]
    return (i <= j).astype(ml_dtypes.bfloat16)


def _reorder_w1(W1, b1, W2):
    DI = W2.shape[-2]
    nb = (DI + 127) // 128
    colblocks = []
    sizes = []
    for t in range(nb):
        lo, hi = t * 128, min((t + 1) * 128, DI)
        colblocks.append((np.arange(lo, hi), np.arange(lo, hi) + DI))
        sizes.append(hi - lo)
    order_a = []
    order = []
    for (ai, gi) in colblocks:
        order.append(("a", ai))
        order.append(("g", gi))
    col_idx = np.concatenate([idx for _, idx in order])
    W1r = W1[..., col_idx]
    b1r = b1[..., col_idx]
    W2r = W2
    return np.ascontiguousarray(W1r), np.ascontiguousarray(b1r), W2r, sizes




def host_prep_p2(inputs):
    f32 = np.float32
    W1 = np.asarray(inputs["L0_ffn_W1"], f32)
    b1 = np.asarray(inputs["L0_ffn_b1"], f32)
    W2 = np.asarray(inputs["L0_ffn_W2"], f32)
    W1r, b1r, W2r, _ = _reorder_w1(W1, b1, W2)
    qs = np.asarray(inputs["L0_ffn_q_scale"], f32)
    ks = np.asarray(inputs["L0_ffn_k_scale"], f32)
    lng = np.asarray(inputs["L0_ffn_ln_g"], f32).reshape(6, 8, 128)
    lnb = np.asarray(inputs["L0_ffn_ln_b"], f32).reshape(6, 8, 128)
    cols = np.zeros((6, 128, 62), f32)
    cols[:, :, 0] = np.concatenate([qs, qs], -1)
    cols[:, :, 1] = np.concatenate([ks, ks], -1)
    cols[:, :, 2:10] = lng.transpose(0, 2, 1)
    cols[:, :, 10:18] = lnb.transpose(0, 2, 1)
    off = 0
    ci = 18
    for t in range(21):
        for _ in range(2):
            cols[:, :, ci] = b1r[:, off:off + 128]
            off += 128
            ci += 1
    a42 = np.zeros((6, 128), f32); a42[:, :42] = b1r[:, off:off + 42]; off += 42
    g42 = np.zeros((6, 128), f32); g42[:, :42] = b1r[:, off:off + 42]; off += 42
    cols[:, :, 60] = a42
    cols[:, :, 61] = g42
    cs_q, sn_q, cs_k, sn_k = _rotary_tables_l0ffn()
    tbl = np.stack([cs_q, sn_q, cs_k, sn_k], axis=1).astype(ml_dtypes.bfloat16)
    ind = np.zeros((128, 8, 16), f32)
    for qc in range(8):
        ind[:64, qc, 2 * qc] = 1.0
        ind[64:, qc, 2 * qc + 1] = 1.0
    sel = np.zeros((16, 8, 128), f32)
    for e_ in range(8):
        sel[2 * e_, e_, 0:64] = 1.0
        sel[2 * e_ + 1, e_, 64:128] = 1.0
    return {
        "Wqkv": np.ascontiguousarray(np.asarray(inputs["L0_ffn_Wqkv"], f32)),
        "Wo6": np.ascontiguousarray(np.asarray(inputs["L0_ffn_Wo"], f32)),
        "W1r": np.ascontiguousarray(W1r),
        "W2": np.ascontiguousarray(W2r),
        "colsP2": np.ascontiguousarray(cols),
        "b2rows": np.asarray(inputs["L0_ffn_b2"], f32).astype(ml_dtypes.bfloat16),
        "tblP2": np.ascontiguousarray(tbl),
        "masksP2": np.ascontiguousarray(_l0ffn_window_masks()),
        "ind16": ind.reshape(128, 128).astype(ml_dtypes.bfloat16),
        "sel16": np.ascontiguousarray(sel.reshape(16, 8 * 128).astype(ml_dtypes.bfloat16)),
    }


def host_prep(inputs):
    f32 = np.float32
    d = {}
    x = np.asarray(inputs["L0_attn_x"], f32)
    d["x"] = x
    for k in ("L0_attn_Wq", "L0_attn_Wk", "L0_attn_Wv", "L0_attn_Wo"):
        d[k] = np.ascontiguousarray(np.asarray(inputs[k], f32))
    d["Wqkv"] = np.ascontiguousarray(np.asarray(inputs["L0_ffn_Wqkv"], f32))
    d["Wo6"] = np.ascontiguousarray(np.asarray(inputs["L0_ffn_Wo"], f32))
    W1 = np.asarray(inputs["L0_ffn_W1"], f32)
    b1 = np.asarray(inputs["L0_ffn_b1"], f32)
    W2 = np.asarray(inputs["L0_ffn_W2"], f32)
    W1r, b1r, W2r, blk = _reorder_w1(W1, b1, W2)
    d["W1r"] = W1r
    d["W2"] = np.ascontiguousarray(W2r)
    d["b2"] = np.ascontiguousarray(np.asarray(inputs["L0_ffn_b2"], f32))
    nb2 = b1r.shape[-1] // 0 if False else None
    cols = []
    off = 0
    for t in range(21):
        for part in range(2):
            cols.append(b1r[:, off:off + 128])
            off += 128
    a42 = np.zeros((6, 128), f32); a42[:, :42] = b1r[:, off:off + 42]; off += 42
    g42 = np.zeros((6, 128), f32); g42[:, :42] = b1r[:, off:off + 42]; off += 42
    cols.append(a42); cols.append(g42)
    assert off == b1r.shape[1]
    d["b1cols"] = np.ascontiguousarray(np.stack(cols, axis=1))
    qs = np.asarray(inputs["L0_ffn_q_scale"], f32)
    ks = np.asarray(inputs["L0_ffn_k_scale"], f32)
    d["qs_cols"] = np.ascontiguousarray(np.concatenate([qs, qs], 1))
    d["ks_cols"] = np.ascontiguousarray(np.concatenate([ks, ks], 1))
    g = np.asarray(inputs["L0_ffn_ln_g"], f32).reshape(6, 8, 128)
    b = np.asarray(inputs["L0_ffn_ln_b"], f32).reshape(6, 8, 128)
    d["lng"] = np.ascontiguousarray(g)
    d["lnb"] = np.ascontiguousarray(b)
    d["L1_WqT"] = np.ascontiguousarray(np.asarray(inputs["L1_attn_Wq"], f32).T)
    d["L1_WkT"] = np.ascontiguousarray(np.asarray(inputs["L1_attn_Wk"], f32).T)
    d["L1_WvT"] = np.ascontiguousarray(np.asarray(inputs["L1_attn_Wv"], f32).T)
    d["L1_WoT"] = np.ascontiguousarray(np.asarray(inputs["L1_attn_Wo"], f32).T)
    d["WrT"] = np.ascontiguousarray(np.asarray(inputs["L1_ffn_W_r"], f32).T)
    d["WkT2"] = np.ascontiguousarray(np.asarray(inputs["L1_ffn_W_k"], f32).T)
    d["WvT2"] = np.ascontiguousarray(np.asarray(inputs["L1_ffn_W_v"], f32).T)
    mu_r = np.asarray(inputs["L1_ffn_mu_r"], f32).reshape(CC, 128).T
    mu_k = np.asarray(inputs["L1_ffn_mu_k"], f32).reshape(CC, 128).T
    d["mu_r_cols"] = np.ascontiguousarray(mu_r)
    d["mu_k_cols"] = np.ascontiguousarray(mu_k)
    d["ident"] = np.eye(128, dtype=ml_dtypes.bfloat16)
    cs_q, sn_q, cs_k, sn_k = _rotary_tables_l0ffn()
    d["cs_q"], d["sn_q"], d["cs_k"], d["sn_k"] = cs_q, sn_q, cs_k, sn_k
    C1, S1 = _rope_tables_l1()
    d["ropeC"], d["ropeS"] = C1, S1
    d["wmask01"] = _l0ffn_window_masks()
    d["diagmask"] = _l0attn_diag_masks()
    d["tri01"] = _l1_tri_mask()
    ind = np.zeros((128, 8, 16), np.float32)
    for qc in range(8):
        ind[:64, qc, 2 * qc] = 1.0
        ind[64:, qc, 2 * qc + 1] = 1.0
    d["ind16"] = ind.reshape(128, 128).astype(ml_dtypes.bfloat16)
    ones_col = np.ones((128, 1), np.float32)
    d["ones_col_bf"] = ones_col.astype(ml_dtypes.bfloat16)
    d["b2_rows"] = np.asarray(inputs["L0_ffn_b2"], f32).astype(ml_dtypes.bfloat16)
    return d



def transpose_tiles(Sch, H, src_fn, dst_fn, ntiles, psb, ps_tok, waits=()):
    ev = None
    for i in range(ntiles):
        b = i % 2
        t_tok = Sch.op("tensor", lambda e, s, i=i, b=b: e.transpose(
            psb[b], src_fn(i), H["ident_sb"][:, :]),
            waits=tuple(waits) + (ps_tok[b],))
        ev = Sch.op("scalar", lambda e, s, i=i, b=b: e.copy(dst_fn(i), psb[b]),
                    waits=(t_tok,))
        ps_tok[b] = ev
    return ev, ps_tok



def emit_P1(nc, Sch, H):
    x_sb = H["x_sb"]
    u_sb = H["u_sb"]
    xnbf = H["xnbf_sb"]
    xnT = H["xnT_sb"]
    mv = H["mv_sb"]
    st6 = H["st6_sb"]
    rstd = H["rstd_sb"]
    psb = [H["psb0"], H["psb1"]]

    for cc in range(CC):
        z_tok = Sch.op("vector", lambda e, s, cc=cc: e.memset(
            xnT[:, cc * GW: cc * GW + 1], 0.0))
    ld = H["LdIn"].dma(lambda e, s: e.dma_start(
        x_sb[:, :].rearrange("p (t d) -> p t d", d=D),
        H["x_d"].rearrange("(t p) d -> p t d", p=128)))

    ps_tok = [("T", 0), ("T", 0)]
    last_ev = None
    for t in range(TT):
        xt = x_sb[:, t * D:(t + 1) * D]
        ut = u_sb[:, t * D:(t + 1) * D]
        s1a = Sch.op("vector", lambda e, s, xt=xt: e.bn_stats(st6[:, 0:6], xt[:, 0:512]), waits=(ld,))
        s1b = Sch.op("vector", lambda e, s, xt=xt: e.bn_stats(st6[:, 6:12], xt[:, 512:1024]), waits=(ld,))
        s2 = Sch.op("vector", lambda e, s: e.bn_aggr(mv[:, :], st6[:, :]), waits=(s1a, s1b))
        eps_ap, eps_tok = Sch.const_col(1e-5)
        s3 = Sch.op("scalar", lambda e, s, eps_ap=eps_ap: e.activation(
            rstd[:, 0:1], mv[:, 1:2], AF.Sqrt, bias=eps_ap), waits=(s2, eps_tok))
        s4 = Sch.op("vector", lambda e, s: e.reciprocal(rstd[:, 1:2], rstd[:, 0:1]), waits=(s3,))
        s5 = Sch.op("vector", lambda e, s, xt=xt, ut=ut: e.tensor_scalar(
            ut, xt, mv[:, 0:1], rstd[:, 1:2], ALU.subtract, ALU.mult), waits=(s4,))
        s6 = Sch.op("vector", lambda e, s, ut=ut: e.tensor_copy(xnbf[:, :], ut),
                    waits=(s5, last_ev if last_ev else ("T", 0)))
        last_ev, ps_tok = transpose_tiles(
            Sch, H,
            lambda i: xnbf[:, i * 128:(i + 1) * 128],
            lambda i, t=t: xnT[:, i * GW + 1 + t * 128: i * GW + 1 + (t + 1) * 128],
            CC, psb, ps_tok, waits=(s6, z_tok))
    st1 = H["St"].dma(lambda e, s: e.dma_start(
        H["u0_d"].rearrange("(t p) d -> p t d", p=128),
        u_sb[:, :].rearrange("p (t d) -> p t d", d=D)),
        waits=(("V", Sch.count.get("V", 0)),))
    st2 = H["St"].dma(lambda e, s: e.dma_start(H["xnT_d"], xnT[:, :]),
                 waits=(("Sc", Sch.count.get("Sc", 0)), z_tok))
    return [st1, st2]



W0 = 256
NW = S // W0
NHD = 16
DI2 = 2730
NPAIR = 22


def emit_P2_sublayer(nc, Sch, H, li, first, last):
    RA = H["RA_sb"]; RB = H["RB_sb"]; RC = H["RC_sb"]; RD = H["RD_sb"]
    RE = H["RE_sb"]; RF = H["RF_sb"]; RG = H["RG_sb"]
    ps = H["ps_f"]; psb = [H["psb0"], H["psb1"]]
    ident = H["ident_sb"]
    masks = H["masks_sb"]; expb = H["expb_sb"]; expm = H["expm_sb"]
    sqb = H["sqb_sb"]
    nfsc = H["nfsc_sb"]; nffsc = H["nffsc_sb"]; nrmsc = H["nrmsc_sb"]
    gridB = H["gridB_sb"]
    negrow = H["negrow_sb"]; negbc = H["negbc_sb"]
    oun = H["oun_sb"]; recb = H["recb_sb"]
    rec64 = H["rec64_sb"]
    statrow = RG.bitcast(F32)[0:1, 0:2048]
    cols = H["cols_sb"]; b2row = H["b2row_sb"]
    onec = H["ones_col_sb"]; onerf = H["ones_rowf_sb"]; onerb = H["ones_row_sb"]
    ind16 = H["ind16_sb"]; sel16 = H["sel16_sb"]
    ut = [H["uta_sb"], H["utb_sb"]]
    ub = [H["uba_sb"], H["ubb_sb"]]
    wstr = [H["wsa_sb"], H["wsb_sb"]]

    def ubf_r(ccj):
        return (expb if ccj < 4 else expm)[:, (ccj % 4) * 128:(ccj % 4) * 128 + 128]
    qv = H["uba_sb"][:, 0:512]
    t1b = H["uta_sb"][:, 0:512]
    t2b = H["uta_sb"][:, 512:1024]
    u_in = H[f"u_in_{li}"]; u_mid = H[f"u_mid_{li}"]; u_out = H[f"u_out_{li}"]
    qT_d = H["qT_scr_d"]; kT_d = H["kT_scr_d"]; v_d = H["v_scr_d"]
    bank_war = H.setdefault("_p2_bank_war", {i: ("Sc", 0) for i in range(6)})
    psb_war = H.setdefault("_p2_psb_war", [("Sc", 0), ("Sc", 0)])
    uTg_rdy = H["_p2_uTg_rdy"]

    def uTg_r(cc, rng):
        off = 0 if cc >= 4 else 1
        return RA[:, cc * GW + off + rng.start: cc * GW + off + rng.stop]

    qwin = [RC[:, 0:2048], RC[:, 10240:12288]]
    kwin = [RC[:, 2048:6144], RC[:, 12288:16384]]
    vwin = [RC[:, 6144:10240], RD[:, 12288:16384]]
    stage = RD[:, 0:4096]
    vstage = [RD[:, 4096:4608], RD[:, 4608:5120]]

    ld_wo = H["Wt"].dma(lambda e, s: e.dma_start(
        RF[:, 0:CC * 1024].rearrange("p (c n) -> p c n", n=1024),
        H["Wo6_d"][li].rearrange("(c p) n -> p c n", p=128)),
        waits=(H.get("_p2_rf_war", ("T", 0)),))
    ld_cols = H["LdIn"].dma(lambda e, s: e.dma_start(
        cols[:, 0:62], H["cols_d"][li]), waits=(H.get("_p2_cols_war", ("V", 0)),))
    ld_b2 = H["LdIn"].dma(lambda e, s: e.dma_start(
        b2row[0:1, :], H["b2_d"][li]), waits=(H.get("_p2_b2_war", ("T", 0)),))
    qs_col = cols[:, 0:1]; ks_col = cols[:, 1:2]
    lng = cols[:, 2:10]; lnb = cols[:, 10:18]; b1c = cols[:, 18:62]
    CSQ = wstr[0][:, 1024:1536]; SNQ = wstr[0][:, 1536:2048]
    CSK = wstr[1][:, 1024:1536]; SNK = wstr[1][:, 1536:2048]

    zgrid = Sch.op("vector", lambda e, s: e.memset(gridB[0:16, :], 0.0),
                   waits=(H.get("_p2_grid_war", ("V", 0)),))
    H["_p2_grid_war"] = zgrid

    sq_war = H.get("_p2_sq_war", ("T", 0))
    qv_war = H.get("_p2_qv_war", ("V", 0))
    v_st = {}
    q_st = {}
    k_st = {}
    for j in range(4):
        tbl_toks = []
        for ti, dst in enumerate((CSQ, SNQ, CSK, SNK)):
            tbl_toks.append(H["LdIn"].dma(lambda e, s, ti=ti, dst=dst, j=j: e.dma_start(
                dst, H["tbl_d"][:, ti, j * 512:(j + 1) * 512]),
                waits=(H.get(f"_p2_tbl_war_{ti}", ("V", 0)),)))
        for h2 in range(2):
            ldv = H["Wt"].dma(lambda e, s, h2=h2: e.dma_start(
                RG[:, 0:4096].rearrange("p (c n) -> p c n", n=512),
                H["Wqkv_d"][li].rearrange("(c p) n -> p c n", p=128)[:, :, 2048 + h2 * 512: 2048 + (h2 + 1) * 512]),
                waits=(("T", Sch.count.get("T", 0)),))
            for tt in range(4):
                t_abs = 4 * j + tt
                b = tt % 2
                vb = (t_abs * 2 + h2) % 2
                mmx = None
                for ccx in range(CC):
                    mmx = Sch.op("tensor", lambda e, s, t_abs=t_abs, ccx=ccx, b=b: e.matmul(
                        ps[b][:, :], uTg_r(ccx, slice(t_abs * 128, (t_abs + 1) * 128)),
                        RG[:, ccx * 512:(ccx + 1) * 512],
                        start=(ccx == 0), stop=(ccx == CC - 1)),
                        waits=(ldv, uTg_rdy, bank_war[b]), inc=(ccx == CC - 1))
                ev = Sch.op("scalar", lambda e, s, vb=vb, b=b: e.copy(
                    vstage[vb][:, :], ps[b][:, :]),
                    waits=(mmx, v_st.get((t_abs - 1, h2), ("Sc", 0)),
                           H.get("_p2_vstage_war_%d" % vb, ("V", 0))))
                bank_war[b] = ev
                stv = H["St"].dma(lambda e, s, t_abs=t_abs, h2=h2, vb=vb: e.dma_start(
                    v_d[:, t_abs * 1024 + h2 * 512: t_abs * 1024 + (h2 + 1) * 512],
                    vstage[vb][:, :]), waits=(ev,))
                H["_p2_vstage_war_%d" % vb] = stv
                v_st[(t_abs, h2)] = stv
        for which, wq, scr_d, st_map in (("q", 0, qT_d, q_st), ("k", 1024, kT_d, k_st)):
            raw_mm = None
            for e_ in range(CC):
                b = e_ % 2
                ldw = H["Wt"].dma(lambda e, s, e_=e_, wq=wq: e.dma_start(
                    wstr[e_ % 2][:, 0:1024].rearrange("p (c m) -> p c m", m=128),
                    H["Wqkv_d"][li].rearrange("(c p) n -> p c n", p=128)[:, :, wq + e_ * 128: wq + (e_ + 1) * 128]),
                    waits=(H.get(f"_p2_wstr_war_{e_ % 2}", ("T", 0)),))
                mmx = None
                for ccx in range(CC):
                    mmx = Sch.op("tensor", lambda e, s, e_=e_, ccx=ccx, b=b, j=j: e.matmul(
                        ps[b][:, :], wstr[e_ % 2][:, ccx * 128:(ccx + 1) * 128],
                        uTg_r(ccx, slice(j * 512, (j + 1) * 512)),
                        start=(ccx == 0), stop=(ccx == CC - 1)),
                        waits=(ldw, uTg_rdy, bank_war[b]), inc=(ccx == CC - 1))
                H[f"_p2_wstr_war_{e_ % 2}"] = mmx
                ev = Sch.op("scalar", lambda e, s, e_=e_, b=b: e.copy(
                    stage[:, e_ * 512:(e_ + 1) * 512], ps[b][:, :]),
                    waits=(mmx, H.get(f"_p2_stgw_{e_}", ("V", 0))))
                sq1 = Sch.op("scalar", lambda e, s, b=b: e.activation(
                    sqb[:, :], ps[b][:, :], AF.Square), waits=(mmx, sq_war))
                raw_mm = Sch.op("tensor", lambda e, s, e_=e_: e.matmul(
                    ps[4][0:16, 0:512], ind16[:, e_ * 16:(e_ + 1) * 16], sqb[:, :],
                    start=(e_ == 0), stop=(e_ == CC - 1), skip_group_check=True),
                    waits=(sq1,) + ((bank_war[4],) if e_ == 0 else ()))
                sq_war = raw_mm
                bank_war[b] = sq1
            raw_done = ("Sc", Sch.count.get("Sc", 0))
            ev1 = Sch.op("scalar", lambda e, s: e.copy(nfsc[0:16, 0:512], ps[4][0:16, 0:512]),
                         waits=(raw_mm, H.get("_p2_nfsc_war", ("V", 0))))
            bank_war[4] = ev1
            sr = Sch.op("scalar", lambda e, s: e.activation(
                nffsc[0:16, 0:512], nfsc[0:16, 0:512], AF.Sqrt),
                waits=(ev1, H.get("_p2_nffsc_war", ("V", 0))))
            mx = Sch.op("vector", lambda e, s: e.tensor_scalar_max(
                nfsc[0:16, 0:512], nffsc[0:16, 0:512], 1e-12), waits=(sr,))
            H["_p2_nffsc_war"] = mx

            def rec_fn(e, s):
                with nc.allow_low_precision(reason="l2norm scale bf16 operand"):
                    return e.reciprocal(nrmsc[0:16, 0:512], nfsc[0:16, 0:512])
            rc_ = Sch.op("vector", rec_fn, waits=(mx, H.get("_p2_nrmsc_war", ("Sc", 0))))
            H["_p2_nfsc_war"] = rc_
            fin_mm = None
            for e_ in range(CC):
                tile_ap = stage[:, e_ * 512:(e_ + 1) * 512]
                col_sc = qs_col if which == "q" else ks_col
                m1 = Sch.op("vector", lambda e, s, tile_ap=tile_ap, col_sc=col_sc: e.tensor_scalar_mul(
                    qv, tile_ap, col_sc),
                    waits=(ld_cols, qv_war, raw_done, H.get("_p2_qv_sc_war", ("Sc", 0))))
                r1 = Sch.op("vector", lambda e, s, which=which: e.tensor_mul(
                    t1b, qv, CSQ if which == "q" else CSK), waits=(m1,) + tuple(tbl_toks))
                s2m = Sch.op("vector", lambda e, s, which=which: e.tensor_mul(
                    H["ubb_sb"][:, 0:512], qv,
                    wstr[0 if which == "q" else 1][:, 1536:2048]),
                    waits=(m1,) + tuple(tbl_toks) + (H.get("_p2_ubb_war", ("V", 0)),))
                rts = []
                for qq in range(4):
                    pb_ = qq * 32
                    sp_ = pb_ + 32 if (pb_ % 64) < 32 else pb_ - 32
                    rts.append(Sch.op("vector", lambda e, s, pb_=pb_, sp_=sp_: e.tensor_copy(
                        H["uta_sb"][pb_:pb_ + 32, 512:1024],
                        H["ubb_sb"][sp_:sp_ + 32, 0:512]), waits=(s2m,)))
                H["_p2_ubb_war"] = rts[-1]
                ad = Sch.op("vector", lambda e, s: e.tensor_add(t2b, t1b, t2b),
                            waits=tuple(rts) + (r1,))
                bc = Sch.op("tensor", lambda e, s, e_=e_: e.matmul(
                    ps[5][:, 0:512], sel16[:, e_ * 128:(e_ + 1) * 128], nrmsc[0:16, 0:512],
                    start=True, stop=True), waits=(rc_, bank_war[5]))
                fq = Sch.op("vector", lambda e, s, tile_ap=tile_ap: e.tensor_mul(
                    tile_ap, t2b, ps[5][:, 0:512]), waits=(ad, bc))
                qv_war = fq
                bank_war[5] = fq
                stq = H["St"].dma(lambda e, s, e_=e_, j=j, scr_d=scr_d: e.dma_start(
                    scr_d[:, e_ * S + j * 512: e_ * S + (j + 1) * 512],
                    stage[:, e_ * 512:(e_ + 1) * 512]), waits=(fq,))
                st_map[(j, e_)] = stq
                H[f"_p2_stgw_{e_}"] = stq
                sq2 = Sch.op("scalar", lambda e, s: e.activation(
                    sqb[:, :], qv, AF.Square), waits=(fq, sq_war))
                H["_p2_qv_sc_war"] = sq2
                fin_mm = Sch.op("tensor", lambda e, s, e_=e_: e.matmul(
                    ps[4][32:48, 0:512], ind16[:, e_ * 16:(e_ + 1) * 16], sqb[:, :],
                    start=(e_ == 0), stop=(e_ == CC - 1), skip_group_check=True),
                    waits=(sq2,) + ((bank_war[4],) if e_ == 0 else ()))
                sq_war = fin_mm
            ev2 = Sch.op("scalar", lambda e, s: e.copy(nffsc[0:16, 0:512], ps[4][32:48, 0:512]),
                         waits=(fin_mm, H["_p2_nffsc_war"]))
            bank_war[4] = ev2
            m2a = Sch.op("vector", lambda e, s: e.tensor_mul(
                nffsc[0:16, 0:512], nffsc[0:16, 0:512], nrmsc[0:16, 0:512]),
                waits=(ev2, rc_))
            m2b = Sch.op("vector", lambda e, s: e.tensor_mul(
                nffsc[0:16, 0:512], nffsc[0:16, 0:512], nrmsc[0:16, 0:512]),
                waits=(m2a,))
            gcol = (0 if which == "q" else 8) + 2 * j
            gm = Sch.op("vector", lambda e, s, gcol=gcol: e.tensor_reduce(
                gridB[0:16, gcol:gcol + 2],
                nffsc[0:16, 0:512].rearrange("p (w c) -> p w c", c=W0),
                AX.X, ALU.max), waits=(m2b, H["_p2_grid_war"]))
            H["_p2_nffsc_war"] = gm
            H["_p2_grid_war"] = gm
        for ti in range(4):
            H[f"_p2_tbl_war_{ti}"] = ("V", Sch.count.get("V", 0))
    H["_p2_nrmsc_war"] = ("T", Sch.count.get("T", 0))

    gw_toks = (H["_p2_grid_war"],)
    km = []
    for w in range(NW):
        if w == 0:
            km.append(Sch.op("vector", lambda e, s: e.tensor_copy(
                gridB[0:16, 16:17], gridB[0:16, 8:9]), waits=gw_toks))
        else:
            km.append(Sch.op("vector", lambda e, s, w=w: e.tensor_max(
                gridB[0:16, 16 + w:17 + w], gridB[0:16, 7 + w:8 + w],
                gridB[0:16, 8 + w:9 + w]), waits=gw_toks + (km[-1],)))
    b2_ = Sch.op("vector", lambda e, s: e.tensor_mul(
        gridB[0:16, 24:32], gridB[0:16, 0:8], gridB[0:16, 16:24]), waits=(km[-1],))
    b3_ = Sch.op("scalar", lambda e, s: e.activation(
        gridB[0:16, 32:40], gridB[0:16, 24:32], AF.Sqrt), waits=(b2_,))
    b4_ = Sch.op("vector", lambda e, s: e.tensor_scalar_mul(
        gridB[0:16, 40:48], gridB[0:16, 32:40], -8.0), waits=(b3_,))
    H["_p2_grid_war"] = b4_
    st_g = H["St"].dma(lambda e, s: e.dma_start(H["bgrid_d"], gridB[0:16, 40:48]),
                       waits=(b4_,))
    ld_g = H["Wt"].dma(lambda e, s: e.dma_start(negrow[0:1, 0:128], H["bgrid_flat_d"]),
                       waits=(st_g, H.get("_p2_negrow_war", ("T", 0))))
    bcg = Sch.op("tensor", lambda e, s: e.matmul(
        ps[5][:, 0:128], onerb[0:1, :], negrow[0:1, 0:128], start=True, stop=True),
        waits=(ld_g, H["_onerb_tok"], bank_war[5]))
    H["_p2_negrow_war"] = bcg
    nbc = Sch.op("scalar", lambda e, s: e.copy(negbc[:, :], ps[5][:, 0:128]),
                 waits=(bcg, H.get("_p2_negbc_war", ("Sc", 0))))
    bank_war[5] = nbc

    eb_war = [("G", 0)] * 2
    ebm_war = [("T", 0)] * 2
    ob_war = {2: bank_war[2], 3: bank_war[3]}
    den_war = ("V", 0)
    win_ld = {}
    win_war = {0: ("T", 0), 1: ("T", 0)}

    def load_window(w):
        i = w % 2
        kt0 = max(0, w * 2 - 2)
        nkt = (w + 1) * 2 - kt0
        wq = H["LdA"].dma(lambda e, s, w=w, i=i: e.dma_start(
            qwin[i][:, :].rearrange("p (c m) -> p c m", m=W0),
            qT_d.rearrange("p (c m) -> p c m", m=S)[:, :, w * W0:(w + 1) * W0]),
            waits=(win_war[i],) + tuple(q_st[(w // 2, e_)] for e_ in range(CC)))
        wk = H["LdB"].dma(lambda e, s, w=w, i=i, kt0=kt0, nkt=nkt: e.dma_start(
            kwin[i][:, 0:CC * nkt * 128].rearrange("p (c m) -> p c m", m=nkt * 128),
            kT_d.rearrange("p (c m) -> p c m", m=S)[:, :, kt0 * 128: kt0 * 128 + nkt * 128]),
            waits=(win_war[i],) + tuple(k_st[(jj, e_)] for jj in range(max(0, w // 2 - 1), w // 2 + 1) for e_ in range(CC)))
        wv = H["LdIn"].dma(lambda e, s, w=w, i=i, kt0=kt0, nkt=nkt: e.dma_start(
            vwin[i][:, 0:nkt * 1024],
            v_d[:, kt0 * 1024: kt0 * 1024 + nkt * 1024]),
            waits=(win_war[i],) + tuple(v_st[(tabs, h2)] for tabs in range(kt0, kt0 + nkt) for h2 in range(2)))
        win_ld[w] = (wq, wk, wv, kt0, nkt)

    m01 = Sch.op("vector", lambda e, s: e.tensor_scalar(
        stage[:, 0:4 * W0], masks[:, 0:4 * W0], 0.0, None, ALU.is_equal),
        waits=(H["_masks_tok"], k_st[(3, 0)], k_st[(3, 1)]))
    load_window(0)
    load_window(1)
    outT_rdy = None
    for w in range(NW):
        i = w % 2
        wq, wk, wv, kt0, nkt = win_ld[w]
        for h in range(NHD):
            cc_t = h // 2
            prow = (h % 2) * 64
            ob = 2 + w % 2
            av_done = None
            den_last = None
            first_kt = True
            for kn in range(nkt):
                ktile = kt0 + kn
                sb_ = kn % 2
                eb = kn % 2
                kt_rel = ktile - (w * 2 - 2)
                smm = Sch.op("tensor", lambda e, s, kn=kn, sb_=sb_, cc_t=cc_t, prow=prow, i=i, nkt=nkt: e.matmul(
                    ps[sb_][:, 0:W0],
                    kwin[i][prow:prow + 64, cc_t * nkt * 128 + kn * 128: cc_t * nkt * 128 + (kn + 1) * 128],
                    qwin[i][prow:prow + 64, cc_t * W0:(cc_t + 1) * W0],
                    start=True, stop=True),
                    waits=(wq, wk, bank_war[sb_]))
                ex = Sch.op("scalar", lambda e, s, sb_=sb_, h=h, w=w, eb=eb: e.activation(
                    expb[:, eb * W0:(eb + 1) * W0], ps[sb_][:, 0:W0], AF.Exp,
                    bias=negbc[:, h * 8 + w: h * 8 + w + 1], scale=8.0),
                    waits=(smm, nbc, eb_war[eb]))
                bank_war[sb_] = ex
                gm = Sch.op("gpsimd", lambda e, s, eb=eb, kt_rel=kt_rel: e.tensor_mul(
                    expm[:, eb * W0:(eb + 1) * W0], expb[:, eb * W0:(eb + 1) * W0],
                    stage[:, kt_rel * W0:(kt_rel + 1) * W0]),
                    waits=(ex, m01, ebm_war[eb]))
                eb_war[eb] = gm
                gm_last = gm
                av_done = Sch.op("tensor", lambda e, s, h=h, kn=kn, ob=ob, first_kt=first_kt, nkt=nkt, eb=eb, i=i: e.matmul(
                    ps[ob][0:64, 0:W0],
                    vwin[i][:, kn * 1024 + h * 64: kn * 1024 + (h + 1) * 64],
                    expm[:, eb * W0:(eb + 1) * W0],
                    start=first_kt, stop=(kn == nkt - 1)),
                    waits=(gm, wv) + ((ob_war[ob],) if first_kt else ()))
                den_last = Sch.op("tensor", lambda e, s, kn=kn, first_kt=first_kt, nkt=nkt, eb=eb: e.matmul(
                    ps[4][0:1, 0:W0], onec[:, 0:1],
                    expm[:, eb * W0:(eb + 1) * W0],
                    start=first_kt, stop=(kn == nkt - 1), skip_group_check=True),
                    waits=(gm, H["_onec_tok"]) + ((den_war, bank_war[4]) if first_kt else ()),
                    inc=(kn == nkt - 1))
                ebm_war[eb] = den_last
                first_kt = False
            lnd = Sch.op("scalar", lambda e, s: e.activation(
                rec64[0:1, 0:W0], ps[4][0:1, 0:W0], AF.Ln),
                waits=(den_last, H.get("_p2_lnb_war", ("T", 0))))
            den_war = lnd
            bank_war[4] = lnd
            rex = Sch.op("scalar", lambda e, s: e.activation(
                recb[0:1, 0:W0], rec64[0:1, 0:W0], AF.Exp, scale=-1.0),
                waits=(lnd, H.get("_p2_recb_war", ("T", 0))))
            H["_p2_lnb_war"] = rex
            bcd = Sch.op("tensor", lambda e, s: e.matmul(
                ps[5][0:64, 0:W0], onerb[0:1, 0:64], recb[0:1, 0:W0],
                start=True, stop=True),
                waits=(rex, H["_onerb_tok"], bank_war[5]))
            H["_p2_recb_war"] = bcd
            evo = Sch.op("scalar", lambda e, s, ob=ob: e.copy(oun[:, 0:W0], ps[ob][0:64, 0:W0]),
                         waits=(av_done, H.get("_p2_oun_war", ("V", 0))))
            mlo = Sch.op("vector", lambda e, s, w=w, cc_t=cc_t, prow=prow: e.tensor_mul(
                RA[prow:prow + 64, cc_t * GW + 1 + w * W0: cc_t * GW + 1 + (w + 1) * W0],
                oun[:, 0:W0], ps[5][0:64, 0:W0]),
                waits=(bcd, evo, H.get("_p2_outT_war", ("T", 0))))
            bank_war[5] = mlo
            H["_p2_oun_war"] = mlo
            ob_war[ob] = mlo
            outT_rdy = mlo
        win_war[i] = ("T", Sch.count.get("T", 0))
        if w + 2 < NW:
            load_window(w + 2)
    H["_p2_stgw_0"] = gm_last
    H["_p2_stgw_1"] = gm_last

    zg = []
    for cc in range(CC):
        zg.append(Sch.op("vector", lambda e, s, cc=cc: e.memset(
            RB[:, cc * GW: cc * GW + 1], 0.0), waits=(H.get("_p2_rb_war", ("T", 0)),)))
    ld_u = {}
    fin_v = {}
    fin_st = {}

    def u_tile_load(t):
        a = t % 2
        wts = (fin_v[t - 2], fin_st[t - 2]) if t >= 2 else (qv_war, ("V", Sch.count.get("V", 0)))
        ld_u[t] = H["LdA"].dma(lambda e, s, t=t, a=a: e.dma_start(
            ut[a][:, :], u_in[t * 128:(t + 1) * 128, :]), waits=wts)

    u_tile_load(0)
    u_tile_load(1)
    wo_guard = outT_rdy
    tr_ps = [psb_war[0], psb_war[1]]
    for t in range(TT):
        a = t % 2
        for n2 in range(2):
            b = (t * 2 + n2) % 2
            mmx = None
            for vd in range(CC):
                mmx = Sch.op("tensor", lambda e, s, t=t, n2=n2, vd=vd, b=b: e.matmul(
                    ps[b][:, :], RA[:, vd * GW + 1 + t * 128: vd * GW + 1 + (t + 1) * 128],
                    RF[:, vd * 1024 + n2 * 512: vd * 1024 + (n2 + 1) * 512],
                    start=(vd == 0), stop=(vd == CC - 1)),
                    waits=(ld_wo, wo_guard, bank_war[b]), inc=(vd == CC - 1))
            c2 = Sch.op("vector", lambda e, s, n2=n2, a=a, b=b: e.tensor_add(
                ub[a][:, n2 * 512:(n2 + 1) * 512], ut[a][:, n2 * 512:(n2 + 1) * 512], ps[b][:, :]),
                waits=(mmx, ld_u[t], fin_v.get(t - 2, ("V", 0))))
            bank_war[b] = c2
            fin_v[t] = c2
        st = H["St"].dma(lambda e, s, t=t, a=a: e.dma_start(
            u_mid[t * 128:(t + 1) * 128, :], ub[a][:, :]), waits=(fin_v[t],))
        fin_st[t] = st
        cb1 = Sch.op("vector", lambda e, s, a=a: e.tensor_copy(expb[:, :], ub[a][:, 0:512]),
                     waits=(fin_v[t], ("Sc", Sch.count.get("Sc", 0)) if t == 0 else ("V", 0),
                            ("G", Sch.count.get("G", 0)) if t == 0 else ("V", 0)))
        cb = Sch.op("vector", lambda e, s, a=a: e.tensor_copy(expm[:, :], ub[a][:, 512:1024]),
                    waits=(cb1,))
        fin_v[t] = cb
        for ccj in range(CC):
            ebx = ccj % 2
            trk = Sch.op("tensor", lambda e, s, ccj=ccj, ebx=ebx: e.transpose(
                psb[ebx], ubf_r(ccj), ident[:, :]),
                waits=(cb, tr_ps[ebx], H["_ident_tok"]))
            ev = Sch.op("scalar", lambda e, s, t=t, ccj=ccj, ebx=ebx: e.copy(
                RB[:, ccj * GW + 1 + t * 128: ccj * GW + 1 + (t + 1) * 128], psb[ebx]),
                waits=(trk,) + tuple(zg))
            tr_ps[ebx] = ev
        if t + 2 < TT:
            u_tile_load(t + 2)
    psb_war[0], psb_war[1] = tr_ps
    xs2T_rdy = ("Sc", Sch.count.get("Sc", 0))

    def xs2_r(cc, rng):
        off = 0 if cc >= 4 else 1
        return RB[:, cc * GW + off + rng.start: cc * GW + off + rng.stop]

    eps_ap, eps_tok = Sch.const_col(1e-5)
    xn2_done = {}
    qv_war2 = ("V", Sch.count.get("V", 0))
    rd_war = ("T", Sch.count.get("T", 0))
    sq_war2 = sq_war
    for n in range(4):
        mm_mu = None
        mm_ss = None
        for cc in range(CC):
            mm_mu = Sch.op("tensor", lambda e, s, cc=cc, n=n: e.matmul(
                ps[4][0:1, 0:512], onec[:, 0:1],
                xs2_r(cc, slice(n * 512, (n + 1) * 512)),
                start=(cc == 0), stop=(cc == CC - 1)),
                waits=(xs2T_rdy, H["_onec_tok"]) + ((bank_war[4],) if cc == 0 else ()),
                inc=(cc == CC - 1))
            sq1 = Sch.op("scalar", lambda e, s, cc=cc, n=n: e.activation(
                sqb[:, :], xs2_r(cc, slice(n * 512, (n + 1) * 512)), AF.Square),
                waits=(xs2T_rdy, sq_war2))
            mm_ss = Sch.op("tensor", lambda e, s, cc=cc, n=n: e.matmul(
                ps[5][0:1, 0:512], onec[:, 0:1], sqb[:, :],
                start=(cc == 0), stop=(cc == CC - 1)),
                waits=(sq1,) + ((bank_war[5],) if cc == 0 else ()))
            sq_war2 = mm_ss
        e1 = Sch.op("scalar", lambda e, s: e.copy(statrow[0:1, 0:512], ps[4][0:1, 0:512]),
                    waits=(mm_mu, H.get("_p2_statrow_war", ("V", 0))))
        e2 = Sch.op("scalar", lambda e, s: e.copy(statrow[0:1, 512:1024], ps[5][0:1, 0:512]),
                    waits=(mm_ss,))
        bank_war[4] = e1
        bank_war[5] = e2
        t1_ = Sch.op("vector", lambda e, s: e.tensor_scalar_mul(
            statrow[0:1, 1024:1536], statrow[0:1, 0:512], 1.0 / D), waits=(e1, e2))
        t2_ = Sch.op("vector", lambda e, s: e.tensor_mul(
            statrow[0:1, 0:512], statrow[0:1, 1024:1536], statrow[0:1, 1024:1536]),
            waits=(t1_,))
        t3a = Sch.op("vector", lambda e, s: e.tensor_scalar_mul(
            statrow[0:1, 1536:2048], statrow[0:1, 512:1024], 1.0 / D), waits=(t2_,))
        t3_ = Sch.op("vector", lambda e, s: e.tensor_sub(
            statrow[0:1, 512:1024], statrow[0:1, 1536:2048], statrow[0:1, 0:512]),
            waits=(t3a,))
        t4_ = Sch.op("scalar", lambda e, s: e.activation(
            statrow[0:1, 1536:2048], statrow[0:1, 512:1024], AF.Sqrt,
            bias=eps_ap[0:1, 0:1]), waits=(t3_, eps_tok))
        t5_ = Sch.op("vector", lambda e, s: e.reciprocal(
            statrow[0:1, 0:512], statrow[0:1, 1536:2048]), waits=(t4_,))
        bmu = Sch.op("tensor", lambda e, s: e.matmul(
            ps[4][:, 0:512], onerf[0:1, :], statrow[0:1, 1024:1536],
            start=True, stop=True), waits=(t5_, H["_onerf_tok"], bank_war[4]))
        brs = Sch.op("tensor", lambda e, s: e.matmul(
            ps[5][:, 0:512], onerf[0:1, :], statrow[0:1, 0:512],
            start=True, stop=True), waits=(t5_, bank_war[5]))
        H["_p2_statrow_war"] = brs
        for cc in range(CC):
            v1x = Sch.op("vector", lambda e, s, cc=cc, n=n: e.tensor_sub(
                t1b, xs2_r(cc, slice(n * 512, (n + 1) * 512)), ps[4][:, 0:512]),
                waits=(bmu, qv_war2))
            v2x = Sch.op("vector", lambda e, s: e.tensor_mul(
                t2b, t1b, ps[5][:, 0:512]), waits=(brs, v1x))
            v3x = Sch.op("vector", lambda e, s, cc=cc, n=n: e.tensor_scalar(
                RD[:, cc * 2048 + n * 512: cc * 2048 + (n + 1) * 512],
                t2b, lng[:, cc:cc + 1], lnb[:, cc:cc + 1], ALU.mult, ALU.add),
                waits=(v2x, ld_cols, rd_war if (n == 0 and cc == 0) else ("V", 0)))
            qv_war2 = v3x
            xn2_done[(n, cc)] = v3x
        bank_war[4] = xn2_done[(n, CC - 1)]
        bank_war[5] = xn2_done[(n, CC - 1)]
    xn2_rdy = ("V", Sch.count.get("V", 0))

    def agt(j2):
        if j2 < 8:
            return RC[:, j2 * 2048: (j2 + 1) * 2048]
        if j2 < 16:
            return RB[:, (j2 - 8) * 2048: (j2 - 7) * 2048]
        if j2 < 20:
            return RF[:, (j2 - 16) * 2048: (j2 - 15) * 2048]
        return H["AG2_sb"][:, (j2 - 20) * 2048: (j2 - 19) * 2048]

    agt_done = {}
    wstr_war2 = [("T", Sch.count.get("T", 0)), ("T", Sch.count.get("T", 0))]
    gb_war = ("V", Sch.count.get("V", 0))
    agt_guard = ("T", Sch.count.get("T", 0))
    agt_sc_guard = ("Sc", Sch.count.get("Sc", 0))
    col_off = 0
    for j2 in range(NPAIR):
        szj = 128 if j2 < 21 else 42
        wbuf = j2 % 2
        ldw1 = H["Wt"].dma(lambda e, s, wbuf=wbuf, col_off=col_off, szj=szj: e.dma_start(
            wstr[wbuf][:, 0:2 * szj * CC].rearrange("p (c m) -> p c m", m=2 * szj),
            H["W1r_d"][li].rearrange("(c p) m -> p c m", p=128)[:, :, col_off: col_off + 2 * szj]),
            waits=(wstr_war2[wbuf],))
        for n in range(4):
            ba = 4 if n % 2 == 0 else 0
            bg = 5 if n % 2 == 0 else 1
            mma = None
            mmg = None
            for cc in range(CC):
                mma = Sch.op("tensor", lambda e, s, n=n, cc=cc, ba=ba, szj=szj, wbuf=wbuf: e.matmul(
                    ps[ba][0:szj, 0:512], wstr[wbuf][:, cc * 2 * szj: cc * 2 * szj + szj],
                    RD[:, cc * 2048 + n * 512: cc * 2048 + (n + 1) * 512],
                    start=(cc == 0), stop=(cc == CC - 1)),
                    waits=(ldw1, xn2_rdy) + ((bank_war[ba],) if cc == 0 else ()),
                    inc=(cc == CC - 1))
                mmg = Sch.op("tensor", lambda e, s, n=n, cc=cc, bg=bg, szj=szj, wbuf=wbuf: e.matmul(
                    ps[bg][0:szj, 0:512], wstr[wbuf][:, cc * 2 * szj + szj: cc * 2 * szj + 2 * szj],
                    RD[:, cc * 2048 + n * 512: cc * 2048 + (n + 1) * 512],
                    start=(cc == 0), stop=(cc == CC - 1)),
                    waits=((bank_war[bg],) if cc == 0 else ()), inc=(cc == CC - 1))
            gl = Sch.op("scalar", lambda e, s, j2=j2, bg=bg, szj=szj: e.activation(
                sqb[0:szj, 0:512], ps[bg][0:szj, 0:512], AF.Gelu,
                bias=b1c[0:szj, 2 * j2 + 1: 2 * j2 + 2]),
                waits=(mmg, gb_war, ld_cols, agt_sc_guard))
            agv = Sch.op("vector", lambda e, s, j2=j2, n=n, ba=ba, szj=szj: e.scalar_tensor_tensor(
                agt(j2)[0:szj, n * 512:(n + 1) * 512], ps[ba][0:szj, 0:512],
                b1c[0:szj, 2 * j2: 2 * j2 + 1], sqb[0:szj, 0:512], ALU.add, ALU.mult),
                waits=(gl, mma, agt_guard))
            gb_war = agv
            bank_war[ba] = agv
            bank_war[bg] = agv
            agt_done[(j2, n)] = agv
        wstr_war2[wbuf] = ("T", Sch.count.get("T", 0))
        col_off += 2 * szj

    def w2t(j2):
        if j2 < 16:
            return RD[:, j2 * 1024: (j2 + 1) * 1024]
        if j2 < 20:
            return RG[:, (j2 - 16) * 1024: (j2 - 15) * 1024]
        return RE[:, (j2 - 20) * 1024: (j2 - 19) * 1024]

    w2_guard = ("T", Sch.count.get("T", 0))
    ld_w2a = H["Wt"].dma(lambda e, s: e.dma_start(
        RD[:, 0:16 * 1024].rearrange("p (j m) -> p j m", m=1024),
        H["W2_d"][li][0:2048, :].rearrange("(j p) m -> p j m", p=128)),
        waits=(w2_guard,))
    ld_w2b = H["Wt"].dma(lambda e, s: e.dma_start(
        RG[:, 0:4096].rearrange("p (j m) -> p j m", m=1024),
        H["W2_d"][li][2048:2560, :].rearrange("(j p) m -> p j m", p=128)),
        waits=(w2_guard,))
    ld_w2c = H["Wt"].dma(lambda e, s: e.dma_start(
        RE[:, 0:1024], H["W2_d"][li][2560:2688, :]), waits=(w2_guard,))
    ld_w2d = H["Wt"].dma(lambda e, s: e.dma_start(
        RE[0:42, 1024:2048], H["W2_d"][li][2688:2730, :]), waits=(w2_guard,))
    w2_lds = (ld_w2a, ld_w2b, ld_w2c, ld_w2d)
    ld_u2 = {}
    fin2_v = {}
    fin2_st = {}

    def u2_tile_load(t):
        a = t % 2
        wts = (fin2_v[t - 2], fin2_st[t - 2]) if t >= 2 else (
            fin_st[TT - 2 + t], ("V", Sch.count.get("V", 0)))
        ld_u2[t] = H["LdB"].dma(lambda e, s, t=t, a=a: e.dma_start(
            ut[a][:, :], u_mid[t * 128:(t + 1) * 128, :]), waits=wts)

    u2_tile_load(0)
    u2_tile_load(1)
    zg1 = []
    for cc in range(CC):
        zg1.append(Sch.op("vector", lambda e, s, cc=cc: e.memset(
            RA[:, cc * GW: cc * GW + 1], 0.0), waits=(("T", Sch.count.get("T", 0)),)))
    tr2_ps = [psb_war[0], psb_war[1]]
    for t in range(TT):
        a = t % 2
        for n2 in range(2):
            b = (t * 2 + n2) % 2
            mmx = None
            for j2 in range(NPAIR):
                szj = 128 if j2 < 21 else 42
                mmx = Sch.op("tensor", lambda e, s, t=t, n2=n2, j2=j2, b=b, szj=szj: e.matmul(
                    ps[b][:, :], agt(j2)[0:szj, t * 128:(t + 1) * 128],
                    w2t(j2)[0:szj, n2 * 512:(n2 + 1) * 512],
                    start=(j2 == 0), stop=False, skip_group_check=True),
                    waits=w2_lds + (agt_done[(j2, t // 4)],) + ((bank_war[b],) if j2 == 0 else ()),
                    inc=False)
            mmb = Sch.op("tensor", lambda e, s, n2=n2, b=b: e.matmul(
                ps[b][:, :], onerb[0:1, 0:128], b2row[0:1, n2 * 512:(n2 + 1) * 512],
                start=False, stop=True, skip_group_check=True),
                waits=(ld_b2, H["_onerb_tok"]), inc=True)
            c2 = Sch.op("vector", lambda e, s, n2=n2, a=a, b=b: e.tensor_add(
                ub[a][:, n2 * 512:(n2 + 1) * 512], ut[a][:, n2 * 512:(n2 + 1) * 512],
                ps[b][:, :]), waits=(mmb, ld_u2[t], fin2_v.get(t - 2, ("V", 0))))
            bank_war[b] = c2
            fin2_v[t] = c2
        if last:
            sl = Sch.op("scalar", lambda e, s, a=a: e.activation(
                ut[a][:, 0:512], ub[a][:, 0:512], AF.Silu), waits=(fin2_v[t],))
            sl2 = Sch.op("scalar", lambda e, s, a=a: e.activation(
                ut[a][:, 512:1024], ub[a][:, 512:1024], AF.Silu), waits=(sl,))
            st = H["St"].dma(lambda e, s, t=t, a=a: e.dma_start(
                u_out[t * 128:(t + 1) * 128, :], ut[a][:, :]), waits=(sl2,))
            fin2_v[t] = sl2
        else:
            st = H["St"].dma(lambda e, s, t=t, a=a: e.dma_start(
                u_out[t * 128:(t + 1) * 128, :], ub[a][:, :]), waits=(fin2_v[t],))
        fin2_st[t] = st
        if not last:
            cb1 = Sch.op("vector", lambda e, s, a=a: e.tensor_copy(expb[:, :], ub[a][:, 0:512]),
                         waits=(fin2_v[t], ("Sc", Sch.count.get("Sc", 0)) if t == 0 else ("V", 0)))
            cb = Sch.op("vector", lambda e, s, a=a: e.tensor_copy(expm[:, :], ub[a][:, 512:1024]),
                        waits=(cb1,))
            fin2_v[t] = cb
            for ccj in range(CC):
                ebx = ccj % 2
                trk = Sch.op("tensor", lambda e, s, ccj=ccj, ebx=ebx: e.transpose(
                    psb[ebx], ubf_r(ccj), ident[:, :]),
                    waits=(cb, tr2_ps[ebx], H["_ident_tok"]))
                ev = Sch.op("scalar", lambda e, s, t=t, ccj=ccj, ebx=ebx: e.copy(
                    RA[:, ccj * GW + 1 + t * 128: ccj * GW + 1 + (t + 1) * 128], psb[ebx]),
                    waits=(trk,) + tuple(zg1))
                tr2_ps[ebx] = ev
        if t + 2 < TT:
            u2_tile_load(t + 2)
    psb_war[0], psb_war[1] = tr2_ps
    H["_p2_rf_war"] = ("T", Sch.count.get("T", 0))
    H["_p2_rb_war"] = ("T", Sch.count.get("T", 0))
    H["_p2_cols_war"] = ("V", Sch.count.get("V", 0))
    H["_p2_b2_war"] = ("T", Sch.count.get("T", 0))
    H["_p2_uTg_rdy"] = ("Sc", Sch.count.get("Sc", 0))
    H["_p2_negbc_war"] = ("Sc", Sch.count.get("Sc", 0))
    H["_p2_outT_war"] = ("T", Sch.count.get("T", 0))
    H["_p2_nrmsc_war"] = ("Sc", Sch.count.get("Sc", 0))
    H["_p2_sq_war"] = sq_war2
    H["_p2_qv_war"] = ("V", Sch.count.get("V", 0))
    H["_p2_stage_war"] = ("T", Sch.count.get("T", 0))
    H["_p2_grid_war"] = ("Sc", Sch.count.get("Sc", 0))


def alloc_p2(nc, e, H):
    H["RA_sb"] = e(nc.sbuf_tensor("RA_sb", [128, CC * GW], BF16))
    H["RB_sb"] = e(nc.sbuf_tensor("RB_sb", [128, CC * GW], BF16))
    H["RC_sb"] = e(nc.sbuf_tensor("RC_sb", [128, CC * 2048], BF16))
    H["RD_sb"] = e(nc.sbuf_tensor("RD_sb", [128, CC * 2048], BF16))
    H["RE_sb"] = e(nc.sbuf_tensor("RE_sb", [128, 2048], BF16))
    H["RF_sb"] = e(nc.sbuf_tensor("RF_sb", [128, 8192], BF16))
    H["RG_sb"] = e(nc.sbuf_tensor("RG_sb", [128, 4096], BF16))
    H["AG2_sb"] = e(nc.sbuf_tensor("AG2_sb", [128, 4096], BF16))
    H["masks_sb"] = e(nc.sbuf_tensor("masks_sb", [128, 4 * W0], BF16))
    H["expb_sb"] = e(nc.sbuf_tensor("expb_sb", [128, 2 * W0], BF16))
    H["expm_sb"] = e(nc.sbuf_tensor("expm_sb", [128, 2 * W0], BF16))
    H["sqb_sb"] = e(nc.sbuf_tensor("sqb_sb", [128, 512], BF16))
    H["nfsc_sb"] = e(nc.sbuf_tensor("nfsc_sb", [16, 512], F32))
    H["nffsc_sb"] = e(nc.sbuf_tensor("nffsc_sb", [16, 512], F32))
    H["nrmsc_sb"] = e(nc.sbuf_tensor("nrmsc_sb", [16, 512], BF16))
    H["gridB_sb"] = e(nc.sbuf_tensor("gridB_sb", [16, 256], F32))
    H["negrow_sb"] = e(nc.sbuf_tensor("negrow_sb", [1, 128], BF16))
    H["negbc_sb"] = e(nc.sbuf_tensor("negbc_sb", [128, 128], F32))
    H["oun_sb"] = e(nc.sbuf_tensor("oun_sb", [64, W0], BF16))
    H["recb_sb"] = e(nc.sbuf_tensor("recb_sb", [1, W0], BF16))
    H["rec64_sb"] = e(nc.sbuf_tensor("rec64_sb", [1, W0], F32))
    H["cols_sb"] = e(nc.sbuf_tensor("cols_sb", [128, 62], F32))
    H["b2row_sb"] = e(nc.sbuf_tensor("b2row_sb", [1, 1024], BF16))
    H["ones_col_sb"] = e(nc.sbuf_tensor("ones_col_sb", [128, 1], BF16))
    H["ones_rowf_sb"] = e(nc.sbuf_tensor("ones_rowf_sb", [1, 128], F32))
    H["ones_row_sb"] = e(nc.sbuf_tensor("ones_row_sb", [1, 128], BF16))
    H["ind16_sb"] = e(nc.sbuf_tensor("ind16_sb", [128, 128], BF16))
    H["sel16_sb"] = e(nc.sbuf_tensor("sel16_sb", [16, 8 * 128], BF16))
    H["uta_sb"] = e(nc.sbuf_tensor("uta_sb", [128, D], F32))
    H["utb_sb"] = e(nc.sbuf_tensor("utb_sb", [128, D], F32))
    H["uba_sb"] = e(nc.sbuf_tensor("uba_sb", [128, D], F32))
    H["ubb_sb"] = e(nc.sbuf_tensor("ubb_sb", [128, D], F32))
    H["wsa_sb"] = e(nc.sbuf_tensor("wsa_sb", [128, 2048], BF16))
    H["wsb_sb"] = e(nc.sbuf_tensor("wsb_sb", [128, 2048], BF16))
    H["ident_sb"] = e(nc.sbuf_tensor("ident_sb", [128, 128], BF16))


def p2_common_setup(nc, Sch, H):
    H["_onec_tok"] = Sch.op("vector", lambda e, s: e.memset(H["ones_col_sb"][:, :], 1.0))
    H["_onerb_tok"] = Sch.op("vector", lambda e, s: e.memset(H["ones_row_sb"][:, :], 1.0))
    H["_onerf_tok"] = Sch.op("vector", lambda e, s: e.memset(H["ones_rowf_sb"][:, :], 1.0))
    H["_ident_tok"] = H["LdIn"].dma(lambda e, s: e.dma_start(H["ident_sb"][:, :], H["ident_d"]))
    H["_masks_tok"] = H["LdIn"].dma(lambda e, s: e.dma_start(
        H["masks_sb"][:, :].rearrange("p (i n) -> p i n", n=W0),
        H["masks_d"].rearrange("i p n -> p i n")))
    ld_i16 = H["LdIn"].dma(lambda e, s: e.dma_start(H["ind16_sb"][:, :], H["ind16_d"]))
    ld_s16 = H["LdIn"].dma(lambda e, s: e.dma_start(H["sel16_sb"][0:16, :], H["sel16_d"]))
    Sch.op("tensor", lambda e, s: None, waits=(ld_i16, ld_s16, H["_ident_tok"]), inc=False)


def build_p2_program(li_list, debug_umid=False):
    nc = bass.Bass("TRN2", target_bir_lowering=False)
    H = {}
    first_li = li_list[0]
    last_li = li_list[-1]
    NL = len(li_list)
    H["Wqkv_full_d"] = nc.dram_tensor("Wqkv", [NL, D, 3072], F32, kind="ExternalInput").ap()
    H["Wo6_full_d"] = nc.dram_tensor("Wo6", [NL, D, D], F32, kind="ExternalInput").ap()
    H["W1r_full_d"] = nc.dram_tensor("W1r", [NL, D, 5460], F32, kind="ExternalInput").ap()
    H["W2_full_d"] = nc.dram_tensor("W2", [NL, DI2, D], F32, kind="ExternalInput").ap()
    H["cols_full_d"] = nc.dram_tensor("colsP2", [NL, 128, 62], F32, kind="ExternalInput").ap()
    H["b2_full_d"] = nc.dram_tensor("b2rows", [NL, 1024], BF16, kind="ExternalInput").ap()
    H["tbl_d"] = nc.dram_tensor("tblP2", [128, 4, S], BF16, kind="ExternalInput").ap()
    H["masks_d"] = nc.dram_tensor("masksP2", [4, 128, W0], BF16, kind="ExternalInput").ap()
    H["ident_d"] = nc.dram_tensor("ident", [128, 128], BF16, kind="ExternalInput").ap()
    H["ind16_d"] = nc.dram_tensor("ind16", [128, 128], BF16, kind="ExternalInput").ap()
    H["sel16_d"] = nc.dram_tensor("sel16", [16, 8 * 128], BF16, kind="ExternalInput").ap()
    H["uTg_in_d"] = nc.dram_tensor("uTg_in", [128, CC * GW], BF16, kind="ExternalInput").ap()
    H["bgrid_d"] = nc.dram_tensor("bgrid", [16, 8], F32).ap()
    H["qT_scr_d"] = nc.dram_tensor("qT_scr", [128, CC * S], BF16).ap()
    H["kT_scr_d"] = nc.dram_tensor("kT_scr", [128, CC * S], BF16).ap()
    H["v_scr_d"] = nc.dram_tensor("v_scr", [128, TT * 1024], BF16).ap()
    H["bgrid_flat_d"] = H["bgrid_d"].rearrange("a b -> (a b)").rearrange("(x y) -> x y", x=1)
    def sl3(apfull, li):
        r = li - li_list[0]
        return apfull[r:r + 1, :, :].rearrange("l a b -> (l a) b")
    H["Wqkv_d"] = {li: sl3(H["Wqkv_full_d"], li) for li in li_list}
    H["Wo6_d"] = {li: sl3(H["Wo6_full_d"], li) for li in li_list}
    H["W1r_d"] = {li: sl3(H["W1r_full_d"], li) for li in li_list}
    H["W2_d"] = {li: sl3(H["W2_full_d"], li) for li in li_list}
    H["cols_d"] = {li: sl3(H["cols_full_d"], li) for li in li_list}
    H["b2_d"] = {li: H["b2_full_d"][li - li_list[0]: li - li_list[0] + 1, :] for li in li_list}
    n_sub = len(li_list)
    H[f"u_in_{first_li}"] = nc.dram_tensor("u_in", [S, D], F32, kind="ExternalInput").ap()
    for i, li in enumerate(li_list):
        H[f"u_mid_{li}"] = nc.dram_tensor(
            f"u_mid_{li}", [S, D], F32,
            kind="ExternalOutput" if debug_umid else "Internal").ap()
        if li == last_li:
            H[f"u_out_{li}"] = nc.dram_tensor("u_out", [S, D], F32, kind="ExternalOutput").ap()
        else:
            H[f"u_out_{li}"] = nc.dram_tensor(f"u_pp_{li}", [S, D], F32).ap()
            H[f"u_in_{li + 1}"] = H[f"u_out_{li}"]
    H["uTg_out_d"] = nc.dram_tensor("uTg_out", [128, CC * GW], BF16, kind="ExternalOutput").ap()
    with ExitStack() as ctx:
        e = ctx.enter_context
        alloc_p2(nc, e, H)
        H["cpool"] = e(nc.sbuf_tensor("cpool", [128, 16], F32))
        H["ps_f"] = [e(nc.psum_tensor(f"psf{i}", [128, 512], F32))[:, :] for i in range(6)]
        H["psb0"] = e(nc.psum_tensor("psb0", [128, 128], BF16))[:, :]
        H["psb1"] = e(nc.psum_tensor("psb1", [128, 128], BF16))[:, :]
        Sch = Sched()
        Sch._const_pool = H["cpool"]
        mk_streams(Sch, H)
        p2_common_setup(nc, Sch, H)
        ld_uTg = H["LdIn"].dma(lambda e2, s: e2.dma_start(H["RA_sb"][:, :], H["uTg_in_d"]))
        H["_p2_uTg_rdy"] = ld_uTg
        for li in li_list:
            emit_P2_sublayer(nc, Sch, H, li, first=(li == first_li), last=(li == 5))
        H["St"].dma(lambda e2, s: e2.dma_start(H["uTg_out_d"], H["RA_sb"][:, :]),
                    waits=(H["_p2_uTg_rdy"],))
        finish_program(nc, Sch, H)
    return nc



def emit_P4a(nc, Sch, H):
    xn1T = H["xn1T_sb"]
    xrk = H["xrk_sb"]
    Wr = H["l1f_w_sb"]
    r_tm = H["r_sb"]
    kact = H["kact_sb"]
    mu = H["mu_sb"]
    om = H["om_sb"]
    tmp = [H["tmpa_sb"], H["tmpb_sb"]]
    gt = [H["gta_sb"], H["gtb_sb"]]
    ps = H["ps_f"]

    def wslice(which, cc, lo, hi):
        base = which * CC * 1024
        return Wr[:, base + cc * 1024 + lo: base + cc * 1024 + hi]

    ld_x = H["LdIn"].dma(lambda e, s: e.dma_start(xn1T[:, :], H["xn1T_d"]))
    ld_mu = H["LdIn"].dma(lambda e, s: e.dma_start(mu[:, 0:8], H["mu_r_d"]))
    ld_mk = H["LdIn"].dma(lambda e, s: e.dma_start(mu[:, 8:16], H["mu_k_d"]))
    lw = []
    for wi, wd in enumerate(("WrT_d", "WkT2_d", "WvT2_d")):
        lw.append(H["Wt"].dma(lambda e, s, wi=wi, wd=wd: e.dma_start(
            Wr[:, wi * CC * 1024:(wi + 1) * CC * 1024].rearrange("p (c n) -> p c n", n=1024),
            H[wd].rearrange("(c p) n -> p c n", p=128))))
    omt = Sch.op("vector", lambda e, s: e.tensor_scalar(
        om[:, :], mu[:, :], -1.0, 1.0, ALU.mult, ALU.add), waits=(ld_mu, ld_mk))

    def guard(cc, shifted):
        off = 0 if shifted else 1
        return xn1T[:, cc * GW + off: cc * GW + off + S]

    def build_xlerp(mu_off, out_tok_extra=()):
        toks = []
        for cc in range(CC):
            a = cc % 2
            t1 = Sch.op("vector", lambda e, s, cc=cc, a=a: e.tensor_scalar_mul(
                tmp[a][:, :], guard(cc, False), mu[:, mu_off + cc: mu_off + cc + 1]),
                waits=(ld_x, omt) + tuple(out_tok_extra))
            t2 = Sch.op("vector", lambda e, s, cc=cc, a=a: e.scalar_tensor_tensor(
                xrk[:, cc * S:(cc + 1) * S], guard(cc, True),
                om[:, mu_off + cc: mu_off + cc + 1], tmp[a][:, :],
                ALU.mult, ALU.add), waits=(t1,))
            toks.append(t2)
        return toks

    xr_toks = build_xlerp(0)
    r_toks = []
    for t in range(TT):
        for n in range(2):
            b = (t * 2 + n) % 2
            war = r_toks[-2] if len(r_toks) >= 2 else ("Sc", 0)
            mm = None
            for cc in range(CC):
                mm = Sch.op("tensor", lambda e, s, t=t, n=n, cc=cc, b=b: e.matmul(
                    ps[b][:, :], xrk[:, cc * S + t * 128: cc * S + (t + 1) * 128],
                    wslice(0, cc, n * 512, (n + 1) * 512),
                    start=(cc == 0), stop=(cc == CC - 1)),
                    waits=(xr_toks[cc], lw[0], war), inc=(cc == CC - 1))
            ev = Sch.op("scalar", lambda e, s, t=t, n=n, b=b: e.activation(
                r_tm[:, t * D + n * 512: t * D + (n + 1) * 512], ps[b][:, :], AF.Sigmoid),
                waits=(mm,))
            r_toks.append(ev)
    xk_toks = build_xlerp(8, out_tok_extra=(("T", Sch.count.get("T", 0)),))
    ka_toks = []
    relu_t = [H["tmpa_sb"], H["tmpb_sb"]]
    for kc in range(CC):
        for n in range(4):
            b = (kc * 4 + n) % 2 + 2
            war = ka_toks[-2] if len(ka_toks) >= 2 else ("Sc", 0)
            mm = None
            for cc in range(CC):
                mm = Sch.op("tensor", lambda e, s, kc=kc, n=n, cc=cc, b=b: e.matmul(
                    ps[b][:, :], wslice(1, cc, kc * 128, (kc + 1) * 128),
                    xrk[:, cc * S + n * 512: cc * S + (n + 1) * 512],
                    start=(cc == 0), stop=(cc == CC - 1)),
                    waits=(xk_toks[cc], lw[1], war), inc=(cc == CC - 1))
            a = (kc * 4 + n) % 2
            rl = Sch.op("vector", lambda e, s, a=a, b=b: e.tensor_scalar_max(
                relu_t[a][:, 0:512], ps[b][:, :], 0.0), waits=(mm,))
            sq = Sch.op("scalar", lambda e, s, kc=kc, n=n, a=a: e.activation(
                kact[:, kc * S + n * 512: kc * S + (n + 1) * 512],
                relu_t[a][:, 0:512], AF.Square), waits=(rl,))
            ka_toks.append(sq)
    out_toks = []
    st_toks = []
    for t in range(TT):
        a = t % 2
        for n in range(2):
            b = (t * 2 + n) % 2 + 4
            war = out_toks[-2] if len(out_toks) >= 2 else ("Sc", 0)
            mm = None
            for kc in range(CC):
                mm = Sch.op("tensor", lambda e, s, t=t, n=n, kc=kc, b=b: e.matmul(
                    ps[b][:, :], kact[:, kc * S + t * 128: kc * S + (t + 1) * 128],
                    wslice(2, kc, n * 512, (n + 1) * 512),
                    start=(kc == 0), stop=(kc == CC - 1)),
                    waits=(ka_toks[kc * 4 + t // 4], lw[2], war),
                    inc=(kc == CC - 1))
            wr_dma = st_toks[-2] if len(st_toks) >= 2 else ("V", 0)
            mu1 = Sch.op("vector", lambda e, s, t=t, n=n, a=a, b=b: e.tensor_mul(
                tmp[a][:, n * 512:(n + 1) * 512],
                r_tm[:, t * D + n * 512: t * D + (n + 1) * 512], ps[b][:, :]),
                waits=(mm, r_toks[t * 2 + n], ka_toks[-1],
                       out_toks[-3] if len(out_toks) >= 3 else ("Sc", 0)))
            si = Sch.op("scalar", lambda e, s, t=t, n=n, a=a: e.activation(
                gt[a][:, n * 512:(n + 1) * 512], tmp[a][:, n * 512:(n + 1) * 512], AF.Silu),
                waits=(mu1, wr_dma))
            out_toks.append(si)
        st = H["St"].dma(lambda e, s, t=t, a=a: e.dma_start(
            H["g1_d"][t * 128:(t + 1) * 128, :], gt[a][:, :]),
            waits=(out_toks[-1], out_toks[-2]))
        st_toks.append(st)
    return st_toks


def alloc_p4a(nc, e, H):
    H["xn1T_sb"] = e(nc.sbuf_tensor("xn1T_sb", [128, CC * GW], BF16))
    H["xrk_sb"] = e(nc.sbuf_tensor("xrk_sb", [128, CC * S], BF16))
    H["l1f_w_sb"] = e(nc.sbuf_tensor("l1f_w", [128, 3 * CC * 1024], BF16))
    H["r_sb"] = e(nc.sbuf_tensor("r_sb", [128, TT * D], BF16))
    H["kact_sb"] = e(nc.sbuf_tensor("kact_sb", [128, CC * S], BF16))
    H["mu_sb"] = e(nc.sbuf_tensor("mu_sb", [128, 16], F32))
    H["om_sb"] = e(nc.sbuf_tensor("om_sb", [128, 16], F32))
    H["tmpa_sb"] = e(nc.sbuf_tensor("tmpa", [128, S], F32))
    H["tmpb_sb"] = e(nc.sbuf_tensor("tmpb", [128, S], F32))
    H["gta_sb"] = e(nc.sbuf_tensor("gta", [128, D], F32))
    H["gtb_sb"] = e(nc.sbuf_tensor("gtb", [128, D], F32))


def build_p4a_program():
    nc = bass.Bass("TRN2", target_bir_lowering=False)
    H = {}
    H["xn1T_d"] = nc.dram_tensor("xn1T", [128, CC * GW], BF16, kind="ExternalInput").ap()
    H["WrT_d"] = nc.dram_tensor("WrT", [D, D], F32, kind="ExternalInput").ap()
    H["WkT2_d"] = nc.dram_tensor("WkT2", [D, D], F32, kind="ExternalInput").ap()
    H["WvT2_d"] = nc.dram_tensor("WvT2", [D, D], F32, kind="ExternalInput").ap()
    H["mu_r_d"] = nc.dram_tensor("mu_r_cols", [128, 8], F32, kind="ExternalInput").ap()
    H["mu_k_d"] = nc.dram_tensor("mu_k_cols", [128, 8], F32, kind="ExternalInput").ap()
    H["g1_d"] = nc.dram_tensor("g1", [S, D], F32, kind="ExternalOutput").ap()
    with ExitStack() as ctx:
        e = ctx.enter_context
        alloc_p4a(nc, e, H)
        H["cpool"] = e(nc.sbuf_tensor("cpool", [128, 16], F32))
        H["ps_f"] = [e(nc.psum_tensor(f"psf{i}", [128, 512], F32))[:, :] for i in range(6)]
        Sch = Sched()
        Sch._const_pool = H["cpool"]
        mk_streams(Sch, H)
        emit_P4a(nc, Sch, H)
        finish_program(nc, Sch, H)
    return nc



def emit_P3(nc, Sch, H):
    xnT = H["xnT_sb"]
    B = H["B_sb"]
    qt_ot = H["C_sb"]
    kT = H["D_sb"]
    v_tm = H["E_sb"]
    F = H["F_sb"]
    Wo = H["D_sb"]
    dmask = H["dmask_sb"]
    scr = H["scr_sb"]
    mx4 = H["mx4_sb"]
    nmx = H["nmx_sb"]
    den4 = H["den4_sb"]
    rec = H["rec_sb"]
    x0t = [H["x0a_sb"], H["x0b_sb"]]
    g0t = [H["g0a_sb"], H["g0b_sb"]]
    x1t = g0t
    xnbf = H["xnbf_sb"]
    mv = H["mv_sb"]; st6 = H["st6_sb"]; rstd = H["rstd_sb"]
    ps = H["ps_f"]
    psb = [H["psb0"], H["psb1"]]

    ld_x = H["LdIn"].dma(lambda e, s: e.dma_start(xnT[:, :], H["xnT_d"]))
    ld_dm = H["LdIn"].dma(lambda e, s: e.dma_start(
        dmask[:, :].rearrange("p (r n) -> p r n", n=512),
        H["dmask_d"].rearrange("r p n -> p r n")))
    ld_id = H["LdIn"].dma(lambda e, s: e.dma_start(H["ident_sb"][:, :], H["ident_d"]))
    def wload(dst, hbm, waits=()):
        return H["Wt"].dma(lambda e, s: e.dma_start(
            dst.rearrange("p (c n) -> p c n", n=1024),
            hbm.rearrange("(c p) n -> p c n", p=128)), waits=waits)
    ld_wq = wload(B[:, 0:8192], H["Wq_d"])
    ld_wk = wload(B[:, 8192:16384], H["Wk_d"])
    ld_wv = wload(F[:, :], H["Wv_d"])

    bank_war = {i: ("Sc", 0) for i in range(6)}
    proj_toks = {"q": [], "k": []}
    for which, wbase, ldw, dst in (("q", 0, ld_wq, qt_ot), ("k", 8192, ld_wk, kT)):
        for qc in range(CC):
            for n in range(4):
                b = (qc * 4 + n) % 4
                mm = None
                for dc in range(CC):
                    mm = Sch.op("tensor", lambda e, s, qc=qc, n=n, dc=dc, b=b, wbase=wbase: e.matmul(
                        ps[b][:, :], B[:, wbase + dc * 1024 + qc * 128: wbase + dc * 1024 + (qc + 1) * 128],
                        xnT[:, dc * GW + 1 + n * 512: dc * GW + 1 + (n + 1) * 512],
                        start=(dc == 0), stop=(dc == CC - 1)),
                        waits=(ld_x, ldw, bank_war[b]), inc=(dc == CC - 1))
                ev = Sch.op("scalar", lambda e, s, qc=qc, n=n, b=b, dst=dst: e.copy(
                    dst[:, qc * S + n * 512: qc * S + (n + 1) * 512], ps[b][:, :]), waits=(mm,))
                bank_war[b] = ev
                proj_toks[which].append(ev)
    v_toks = []
    for t in range(TT):
        for n in range(2):
            b = (t * 2 + n) % 4
            mm = None
            for dc in range(CC):
                mm = Sch.op("tensor", lambda e, s, t=t, n=n, dc=dc, b=b: e.matmul(
                    ps[b][:, :], xnT[:, dc * GW + 1 + t * 128: dc * GW + 1 + (t + 1) * 128],
                    F[:, dc * 1024 + n * 512: dc * 1024 + (n + 1) * 512],
                    start=(dc == 0), stop=(dc == CC - 1)),
                    waits=(ld_wv, bank_war[b]), inc=(dc == CC - 1))
            ev = Sch.op("scalar", lambda e, s, t=t, n=n, b=b: e.copy(
                v_tm[:, t * D + n * 512: t * D + (n + 1) * 512], ps[b][:, :]), waits=(mm,))
            bank_war[b] = ev
            v_toks.append(ev)
    v_done = v_toks[-1]

    SCL = 1.0 / 32.0
    raw_off = 0
    norm_off = [2048, 4096]
    et_off = [6144, 6272]
    exp_war = [v_done, v_done]
    et_war = [("T", 0), ("T", 0)]
    psb_war = [("Sc", 0), ("Sc", 0)]
    oT_evac_toks = []
    prev_nm = ("V", 0)
    for qt in range(TT):
        nch = qt // 4 + 1
        eo = norm_off[qt % 2]
        sc_mm = {}
        for dc in range(CC):
            for ch in range(nch):
                sc_mm[ch] = Sch.op("tensor", lambda e, s, qt=qt, dc=dc, ch=ch: e.matmul(
                    ps[ch][:, :], qt_ot[:, dc * S + qt * 128: dc * S + (qt + 1) * 128],
                    kT[:, dc * S + ch * 512: dc * S + (ch + 1) * 512],
                    start=(dc == 0), stop=(dc == CC - 1)),
                    waits=(proj_toks["q"][-1], proj_toks["k"][-1], prev_nm,
                           bank_war[ch],
                           oT_evac_toks[-1] if oT_evac_toks else ("Sc", 0)),
                    inc=(dc == CC - 1))
        dm = Sch.op("vector", lambda e, s, qt=qt, nch=nch: e.tensor_add(
            scr[:, :], ps[nch - 1][:, :], dmask[:, (qt % 4) * 512:(qt % 4 + 1) * 512]),
            waits=(sc_mm[nch - 1], ld_dm))
        mx_toks = []
        for ch in range(nch):
            src = scr[:, :] if ch == nch - 1 else ps[ch][:, :]
            mx_toks.append(Sch.op("vector", lambda e, s, ch=ch, src=src: e.reduce_max(
                mx4[:, ch:ch + 1], src, axis=AX.X), waits=(sc_mm[ch], dm)))
        nm = Sch.op("vector", lambda e, s, nch=nch: e.reduce_max(
            nmx[:, :], mx4[:, 0:nch], axis=AX.X, negate=True), waits=tuple(mx_toks))
        nm2 = Sch.op("vector", lambda e, s: e.tensor_scalar_mul(nmx[:, :], nmx[:, :], SCL),
                     waits=(nm,))
        exp_toks = []
        for ch in range(nch):
            src = scr[:, :] if ch == nch - 1 else ps[ch][:, :]
            exp_toks.append(Sch.op("scalar", lambda e, s, ch=ch, src=src: e.activation(
                F[:, raw_off + ch * 512: raw_off + (ch + 1) * 512], src, AF.Exp,
                bias=nmx[:, 0:1], scale=SCL, accum_out=den4[:, ch:ch + 1]),
                waits=(nm2, prev_nm)))
        dn = Sch.op("vector", lambda e, s, nch=nch: e.reduce_sum(
            rec[:, 0:1], den4[:, 0:nch], axis=AX.X), waits=tuple(exp_toks))
        dn2 = Sch.op("vector", lambda e, s: e.reciprocal(rec[:, 1:2], rec[:, 0:1]), waits=(dn,))
        nrm = Sch.op("vector", lambda e, s, eo=eo, nch=nch: e.tensor_scalar_mul(
            F[:, eo: eo + nch * 512], F[:, raw_off: raw_off + nch * 512], rec[:, 1:2]),
            waits=(dn2, exp_war[qt % 2]) + tuple(exp_toks))
        for ch in range(nch):
            bank_war[ch] = exp_toks[ch]
        last_av = {}
        for kt in range(qt + 1):
            eb = kt % 2
            tr = Sch.op("tensor", lambda e, s, eo=eo, kt=kt, eb=eb: e.transpose(
                psb[eb], F[:, eo + kt * 128: eo + (kt + 1) * 128], H["ident_sb"][:, :]),
                waits=(nrm, ld_id, psb_war[eb]))
            te = Sch.op("scalar", lambda e, s, kt=kt, eb=eb: e.copy(
                F[:, et_off[eb]: et_off[eb] + 128], psb[eb]), waits=(tr, et_war[eb]))
            psb_war[eb] = te
            for vd in range(CC):
                bank = 4 + vd // 4
                col = (vd % 4) * 128
                last_av[vd] = Sch.op("tensor", lambda e, s, kt=kt, vd=vd, eb=eb, bank=bank, col=col, qt=qt: e.matmul(
                    ps[bank][:, col:col + 128],
                    v_tm[:, kt * D + vd * 128: kt * D + (vd + 1) * 128],
                    F[:, et_off[eb]: et_off[eb] + 128],
                    start=(kt == 0 and vd % 4 == 0), stop=(kt == qt and vd % 4 == 3)),
                    waits=(te, v_toks[kt * 2 + vd // 4],
                           bank_war[bank] if kt == 0 else ("Sc", 0)),
                    inc=(kt == qt and vd % 4 == 3))
            et_war[eb] = last_av[7]
        exp_war[qt % 2] = ("T", Sch.count.get("T", 0))
        prev_nm = nrm
        gend = {4: last_av[3], 5: last_av[7]}
        for vd in range(CC):
            bank = 4 + vd // 4
            col = (vd % 4) * 128
            tok_ev = Sch.op("scalar", lambda e, s, vd=vd, qt=qt, bank=bank, col=col: e.copy(
                qt_ot[:, vd * S + qt * 128: vd * S + (qt + 1) * 128],
                ps[bank][:, col:col + 128]), waits=(gend[bank],))
            oT_evac_toks.append(tok_ev)
            bank_war[bank] = tok_ev

    attn_T_done = ("T", Sch.count.get("T", 0))
    ld_wo = H["Wt"].dma(lambda e, s: e.dma_start(
        Wo[:, 0:8192].rearrange("p (c n) -> p c n", n=1024),
        H["Wo_d"].rearrange("(c p) n -> p c n", p=128)), waits=(attn_T_done,))
    z_toks = []
    for cc in range(CC):
        z_toks.append(Sch.op("vector", lambda e, s, cc=cc: e.memset(
            B[:, cc * GW: cc * GW + 1], 0.0),
            waits=(proj_toks["q"][-1], proj_toks["k"][-1])))
    comb_toks = []
    ld_toks = {}
    v_guard = ("V", Sch.count.get("V", 0))
    sc_guard = ("Sc", Sch.count.get("Sc", 0))

    def emit_tile_loads(t):
        a = t % 2
        if t >= 2:
            waits = (ln_v_tok[t - 2], ln_st_tok[t - 2])
        else:
            waits = ()
        ld_toks[("x", t)] = H["LdA"].dma(lambda e, s, t=t, a=a: e.dma_start(
            x0t[a][:, :], H["x_d"][t * 128:(t + 1) * 128, :]), waits=waits)
        ld_toks[("g", t)] = H["LdB"].dma(lambda e, s, t=t, a=a: e.dma_start(
            g0t[a][:, :], H["g0_d"][t * 128:(t + 1) * 128, :]), waits=waits)

    st_x1 = []
    ln_toks = []
    ln_v_tok = {}
    ln_st_tok = {}
    ln_ps_tok = []
    emit_tile_loads(0)
    emit_tile_loads(1)
    for t in range(TT):
        a = t % 2
        for n in range(2):
            b = (t * 2 + n) % 2
            war = comb_toks[-2] if len(comb_toks) >= 2 else ("Sc", 0)
            mm = None
            for vd in range(CC):
                mm = Sch.op("tensor", lambda e, s, t=t, n=n, vd=vd, b=b: e.matmul(
                    ps[b][:, :], qt_ot[:, vd * S + t * 128: vd * S + (t + 1) * 128],
                    Wo[:, vd * 1024 + n * 512: vd * 1024 + (n + 1) * 512],
                    start=(vd == 0), stop=(vd == CC - 1)),
                    waits=(ld_wo, oT_evac_toks[-1], war, v_guard, bank_war[b]),
                    inc=(vd == CC - 1))
            c1 = Sch.op("vector", lambda e, s, t=t, n=n, a=a, b=b: e.tensor_mul(
                scr[:, :], g0t[a][:, n * 512:(n + 1) * 512], ps[b][:, :]),
                waits=(mm, ld_toks[("g", t)], ld_toks[("x", t)],
                       comb_toks[-1] if comb_toks else ("V", 0)))
            c2 = Sch.op("vector", lambda e, s, t=t, n=n, a=a: e.tensor_add(
                x1t[a][:, n * 512:(n + 1) * 512], x0t[a][:, n * 512:(n + 1) * 512], scr[:, :]),
                waits=(c1,))
            bank_war[b] = c1
            comb_toks.append(c2)
        st = H["St"].dma(lambda e, s, t=t, a=a: e.dma_start(
            H["x1_d"][t * 128:(t + 1) * 128, :], x1t[a][:, :]),
            waits=(comb_toks[-1], comb_toks[-2]))
        st_x1.append(st)
        s1a = Sch.op("vector", lambda e, s, a=a: e.bn_stats(st6[:, 0:6], x1t[a][:, 0:512]),
                     waits=(comb_toks[-1], comb_toks[-2]))
        s1b = Sch.op("vector", lambda e, s, a=a: e.bn_stats(st6[:, 6:12], x1t[a][:, 512:1024]),
                     waits=(s1a,))
        s2 = Sch.op("vector", lambda e, s: e.bn_aggr(mv[:, :], st6[:, :]), waits=(s1a, s1b))
        eps_ap, eps_tok = Sch.const_col(1e-5)
        s3 = Sch.op("scalar", lambda e, s, eps_ap=eps_ap: e.activation(
            rstd[:, 0:1], mv[:, 1:2], AF.Sqrt, bias=eps_ap), waits=(s2, eps_tok))
        s4 = Sch.op("vector", lambda e, s: e.reciprocal(rstd[:, 1:2], rstd[:, 0:1]), waits=(s3,))
        prev_tr = ln_toks[-1] if ln_toks else ("T", 0)
        s5 = Sch.op("vector", lambda e, s, a=a: e.tensor_scalar(
            xnbf[:, :], x1t[a][:, :], mv[:, 0:1], rstd[:, 1:2], ALU.subtract, ALU.mult),
            waits=(s4, prev_tr))
        pst = ln_ps_tok[0] if ln_ps_tok else [sc_guard, sc_guard]
        lastev, pst = transpose_tiles(
            Sch, H,
            lambda i: xnbf[:, i * 128:(i + 1) * 128],
            lambda i, t=t: B[:, i * GW + 1 + t * 128: i * GW + 1 + (t + 1) * 128],
            CC, psb, pst, waits=(s5,) + tuple(z_toks))
        ln_ps_tok[:] = [pst]
        ln_toks.append(lastev)
        ln_v_tok[t] = s5
        ln_st_tok[t] = st
        if t + 2 < TT:
            emit_tile_loads(t + 2)
    st_xn1 = H["St"].dma(lambda e, s: e.dma_start(H["xn1T_d"], B[:, 0:CC * GW]),
                    waits=(ln_toks[-1], ("Sc", Sch.count.get("Sc", 0))))
    return st_x1 + [st_xn1]


def alloc_p3(nc, e, H):
    H["scr_sb"] = e(nc.sbuf_tensor("scr", [128, 512], F32))
    H["xnT_sb"] = e(nc.sbuf_tensor("xnT_sb", [128, CC * GW], BF16))
    H["B_sb"] = e(nc.sbuf_tensor("B_sb", [128, CC * GW], BF16))
    H["C_sb"] = e(nc.sbuf_tensor("C_sb", [128, CC * S], BF16))
    H["D_sb"] = e(nc.sbuf_tensor("D_sb", [128, CC * S], BF16))
    H["E_sb"] = e(nc.sbuf_tensor("E_sb", [128, TT * D], BF16))
    H["F_sb"] = e(nc.sbuf_tensor("F_sb", [128, 8192], BF16))
    H["dmask_sb"] = e(nc.sbuf_tensor("dmask_sb", [128, 4 * 512], BF16))
    H["mx4_sb"] = e(nc.sbuf_tensor("mx4", [128, 4], F32))
    H["nmx_sb"] = e(nc.sbuf_tensor("nmx", [128, 1], F32))
    H["den4_sb"] = e(nc.sbuf_tensor("den4", [128, 4], F32))
    H["rec_sb"] = e(nc.sbuf_tensor("rec", [128, 2], F32))
    H["x0a_sb"] = e(nc.sbuf_tensor("x0a", [128, D], F32))
    H["x0b_sb"] = e(nc.sbuf_tensor("x0b", [128, D], F32))
    H["g0a_sb"] = e(nc.sbuf_tensor("g0a", [128, D], F32))
    H["g0b_sb"] = e(nc.sbuf_tensor("g0b", [128, D], F32))
    H["xnbf_sb"] = e(nc.sbuf_tensor("xnbf", [128, D], BF16))
    H["mv_sb"] = e(nc.sbuf_tensor("mv", [128, 2], F32))
    H["st6_sb"] = e(nc.sbuf_tensor("st6", [128, 12], F32))
    H["rstd_sb"] = e(nc.sbuf_tensor("rstd", [128, 2], F32))
    H["ident_sb"] = e(nc.sbuf_tensor("ident_sb", [128, 128], BF16))


def build_p3_program():
    nc = bass.Bass("TRN2", target_bir_lowering=False)
    H = {}
    H["xnT_d"] = nc.dram_tensor("xnT", [128, CC * GW], BF16, kind="ExternalInput").ap()
    H["x_d"] = nc.dram_tensor("x", [S, D], F32, kind="ExternalInput").ap()
    H["g0_d"] = nc.dram_tensor("g0", [S, D], F32, kind="ExternalInput").ap()
    H["Wq_d"] = nc.dram_tensor("Wq", [D, D], F32, kind="ExternalInput").ap()
    H["Wk_d"] = nc.dram_tensor("Wk", [D, D], F32, kind="ExternalInput").ap()
    H["Wv_d"] = nc.dram_tensor("Wv", [D, D], F32, kind="ExternalInput").ap()
    H["Wo_d"] = nc.dram_tensor("Wo", [D, D], F32, kind="ExternalInput").ap()
    H["dmask_d"] = nc.dram_tensor("dmask", [4, 128, 512], BF16, kind="ExternalInput").ap()
    H["ident_d"] = nc.dram_tensor("ident", [128, 128], BF16, kind="ExternalInput").ap()
    H["x1_d"] = nc.dram_tensor("x1", [S, D], F32, kind="ExternalOutput").ap()
    H["xn1T_d"] = nc.dram_tensor("xn1T", [128, CC * GW], BF16, kind="ExternalOutput").ap()
    with ExitStack() as ctx:
        e = ctx.enter_context
        alloc_p3(nc, e, H)
        H["cpool"] = e(nc.sbuf_tensor("cpool", [128, 16], F32))
        H["ps_f"] = [e(nc.psum_tensor(f"psf{i}", [128, 512], F32))[:, :] for i in range(6)]
        ps0 = e(nc.psum_tensor("psb0", [128, 128], BF16))
        ps1 = e(nc.psum_tensor("psb1", [128, 128], BF16))
        H["psb0"], H["psb1"] = ps0[:, :], ps1[:, :]
        Sch = Sched()
        Sch._const_pool = H["cpool"]
        mk_streams(Sch, H)
        emit_P3(nc, Sch, H)
        finish_program(nc, Sch, H)
    return nc



def emit_P4b(nc, Sch, H):
    xn1T = H["xn1T_sb"]
    WA = H["wa_sb"]
    WB = H["wb_sb"]
    WC = WB
    tm = H["tm_sb"]
    qT = H["qT_sb"]
    kT = H["kT_sb"]
    expb = H["expb_sb"]
    stage = H["stage_sb"]
    ropeC = H["ropeC_sb"]
    ropeS = H["ropeS_sb"]
    tri = H["tri_sb"]
    ssq = H["ssq_sb"]
    nB = H["nB_sb"]
    nBbf = H["nBbf_sb"]
    nrow = H["nrow_sb"]
    negbc = H["negbc_sb"]
    ones_row = H["ones_row_sb"]
    rec1 = H["rec1_sb"]
    oun = H["oun_sb"]
    rec64 = H["rec64_sb"]
    xf = xn1T.bitcast(F32)
    x1t = [xf[:, 0:D], xf[:, D:2 * D]]
    g1t = [xf[:, 2 * D:3 * D], xf[:, 3 * D:4 * D]]
    scr = xf[:, 4 * D:4 * D + 512]
    sqs = H["str4_sb"][:, 0:512]
    str4 = H["str4_sb"]
    ps = H["ps_f"]
    psb = [H["psb0"], H["psb1"]]
    bank_war = {i: ("Sc", 0) for i in range(5)}

    ld_x = H["LdIn"].dma(lambda e, s: e.dma_start(xn1T[:, :], H["xn1T_d"]))
    ld_tc = H["LdIn"].dma(lambda e, s: e.dma_start(ropeC[:, :], H["ropeC_d"]))
    ld_ts = H["LdIn"].dma(lambda e, s: e.dma_start(ropeS[:, :], H["ropeS_d"]))
    ld_tri = H["LdIn"].dma(lambda e, s: e.dma_start(tri[:, :], H["tri_d"]))
    ld_id = H["LdIn"].dma(lambda e, s: e.dma_start(H["ident_sb"][:, :], H["ident_d"]))

    def wload(dst, hbm, waits=()):
        return H["Wt"].dma(lambda e, s, dst=dst, hbm=hbm: e.dma_start(
            dst.rearrange("p (c n) -> p c n", n=1024),
            H[hbm].rearrange("(c p) n -> p c n", p=128)), waits=waits)
    ld_wq = wload(WA[:, 0:CC * 1024], "WqT_d")
    ld_wk = wload(WA[:, CC * 1024:2 * CC * 1024], "WkT_d")
    ld_wv = wload(WB[:, :], "WvT_d")

    one_tok = Sch.op("vector", lambda e, s: e.memset(ones_row[:, :], 1.0))
    ones_col = H["ones_col_sb"]
    onec_tok = Sch.op("vector", lambda e, s: e.memset(ones_col[:, :], 1.0))
    ones_rowf = H["ones_rowf_sb"]
    onef_tok = Sch.op("vector", lambda e, s: e.memset(ones_rowf[:, :], 1.0))
    zstage = Sch.op("vector", lambda e, s: e.memset(nBbf[:, :], 0.0))

    def proj_rope(which, wbase, ldw, dstT, ssq_off):
        toks = [("V", Sch.count.get("V", 0))]
        tr_last = None
        for t in range(TT):
            mm = {}
            for n in range(2):
                b = (t * 2 + n) % 2
                mmx = None
                for cc in range(CC):
                    mmx = Sch.op("tensor", lambda e, s, t=t, n=n, cc=cc, b=b, wbase=wbase: e.matmul(
                        ps[b][:, :], xn1T[:, cc * GW + 1 + t * 128: cc * GW + 1 + (t + 1) * 128],
                        WA[:, wbase + cc * 1024 + n * 512: wbase + cc * 1024 + (n + 1) * 512],
                        start=(cc == 0), stop=(cc == CC - 1)),
                        waits=(ld_x, ldw, bank_war[b]), inc=(cc == CC - 1))
                mm[n] = mmx
            rope_toks = []
            for n in range(2):
                b = (t * 2 + n) % 2
                pe = ps[b]
                Ct = ropeC[:, t * 32:(t + 1) * 32]
                St = ropeS[:, t * 32:(t + 1) * 32]
                def ap3(base_ap, off, st0, n0, st1, n1):
                    t_ = base_ap.tensor
                    return bass.AP(t_, base_ap.offset + off, [[base_ap.ap[0][0], 128], [st0, n0], [st1, n1]])
                teap = ap3(pe, 0, 64, 8, 2, 32)
                toap = ap3(pe, 1, 64, 8, 2, 32)
                Cb = ap3(Ct, 0, 0, 8, 1, 32)
                Sb = ap3(St, 0, 0, 8, 1, 32)
                sA = ap3(str4[:, 0:256], 0, 32, 8, 1, 32)
                sB = ap3(str4[:, 256:512], 0, 32, 8, 1, 32)
                sC = ap3(str4[:, 512:768], 0, 32, 8, 1, 32)
                sD = ap3(str4[:, 768:1024], 0, 32, 8, 1, 32)
                qe = ap3(tm[:, t * D + n * 512: t * D + (n + 1) * 512], 0, 64, 8, 2, 32)
                qo = ap3(tm[:, t * D + n * 512: t * D + (n + 1) * 512], 1, 64, 8, 2, 32)
                war_s = rope_toks[-1] if rope_toks else toks[-1]
                r1 = Sch.op("vector", lambda e, s, teap=teap, Cb=Cb, sA=sA: e.tensor_mul(sA, teap, Cb),
                            waits=(mm[n], ld_tc, war_s))
                r2 = Sch.op("vector", lambda e, s, toap=toap, Sb=Sb, sB=sB: e.tensor_mul(sB, toap, Sb),
                            waits=(mm[n], ld_ts, war_s))
                r3 = Sch.op("vector", lambda e, s, teap=teap, Sb=Sb, sC=sC: e.tensor_mul(sC, teap, Sb),
                            waits=(mm[n], war_s))
                r4 = Sch.op("vector", lambda e, s, toap=toap, Cb=Cb, sD=sD: e.tensor_mul(sD, toap, Cb),
                            waits=(mm[n], war_s))
                r5 = Sch.op("vector", lambda e, s, sA=sA, sB=sB, qe=qe: e.scalar_tensor_tensor(
                    qe, sB, -1.0, sA, ALU.mult, ALU.add), waits=(r1, r2))
                r6 = Sch.op("vector", lambda e, s, sC=sC, sD=sD, qo=qo: e.tensor_add(qo, sC, sD),
                            waits=(r3, r4))
                rope_toks += [r5, r6]
                bank_war[b] = r4
            s2 = None
            for half in range(2):
                s1 = Sch.op("vector", lambda e, s, t=t, half=half: e.tensor_mul(
                    sqs, tm[:, t * D + half * 512: t * D + half * 512 + 512],
                    tm[:, t * D + half * 512: t * D + half * 512 + 512]),
                    waits=tuple(rope_toks) + ((s2,) if s2 else ()))
                def red_fn(e, s, t=t, half=half, ssq_off=ssq_off):
                    with nc.allow_low_precision(reason="softmax bound only"):
                        return e.reduce_sum(
                            ssq[:, ssq_off + t * 16 + half * 8: ssq_off + t * 16 + half * 8 + 8],
                            sqs.rearrange("p (h d) -> p h d", d=64), axis=AX.X)
                s2 = Sch.op("vector", red_fn, waits=(s1,))
            toks.append(s2)
            for ccj in range(CC):
                eb = ccj % 2
                trw = psb_war[eb]
                trk = Sch.op("tensor", lambda e, s, t=t, ccj=ccj, eb=eb: e.transpose(
                    psb[eb], tm[:, t * D + ccj * 128: t * D + (ccj + 1) * 128], H["ident_sb"][:, :]),
                    waits=(rope_toks[0], rope_toks[1], rope_toks[2], rope_toks[3], ld_id, trw))
                ev = Sch.op("scalar", lambda e, s, t=t, ccj=ccj, eb=eb, dstT=dstT: e.copy(
                    dstT[:, ccj * S + t * 128: ccj * S + (t + 1) * 128], psb[eb]), waits=(trk,))
                psb_war[eb] = ev
                tr_last = ev
        return toks, tr_last

    psb_war = [("Sc", 0), ("Sc", 0)]
    q_toks, q_tr = proj_rope("q", 0, ld_wq, qT, 0)
    k_toks, k_tr = proj_rope("k", CC * 1024, ld_wk, kT, TT * 16)
    qk_done = ("Sc", Sch.count.get("Sc", 0))

    v_toks = []
    for t in range(TT):
        for n in range(2):
            b = (t * 2 + n) % 2
            mmx = None
            for cc in range(CC):
                mmx = Sch.op("tensor", lambda e, s, t=t, n=n, cc=cc, b=b: e.matmul(
                    ps[b][:, :], xn1T[:, cc * GW + 1 + t * 128: cc * GW + 1 + (t + 1) * 128],
                    WB[:, cc * 1024 + n * 512: cc * 1024 + (n + 1) * 512],
                    start=(cc == 0), stop=(cc == CC - 1)),
                    waits=(ld_wv, bank_war[b],
                           ld_wk if (t == 0 and n == 0) else ("V", 0)), inc=(cc == CC - 1))
            ev = Sch.op("scalar", lambda e, s, t=t, n=n, b=b: e.copy(
                WA[:, t * D + n * 512: t * D + (n + 1) * 512], ps[b][:, :]), waits=(mmx, qk_done))
            bank_war[b] = ev
            v_toks.append(ev)
    v_tm = WA
    ld_wo = wload(WC[:, :], "WoT_d", waits=(("T", Sch.count.get("T", 0)),))

    z0 = Sch.op("vector", lambda e, s: e.memset(nB[0:16, 0:1], 0.0))
    z1 = Sch.op("vector", lambda e, s: e.memset(nB[0:16, 2:3], 0.0))
    tr_war = ("V", 0)
    mq = mk = None
    for which, off, toks, base in (("q", 0, q_toks, 0), ("k", TT * 16, k_toks, 2)):
        run = z0 if which == "q" else z1
        for t in range(TT):
            trk = Sch.op("tensor", lambda e, s, t=t, off=off: e.transpose(
                psb[1][0:16, 0:128], ssq[:, off + t * 16: off + (t + 1) * 16],
                H["ident_sb"][:, :]), waits=(toks[t + 1], tr_war, psb_war[1]))
            tmx = Sch.op("vector", lambda e, s: e.reduce_max(
                nB[0:16, 4:5], psb[1][0:16, 0:128], axis=AX.X), waits=(trk,))
            psb_war[1] = tmx
            run = Sch.op("vector", lambda e, s, t=t, base=base: e.tensor_max(
                nB[0:16, base + (t + 1) % 2: base + (t + 1) % 2 + 1],
                nB[0:16, base + t % 2: base + t % 2 + 1], nB[0:16, 4:5]),
                waits=(tmx, run))
            tr_war = run
        if which == "q":
            mq = run
        else:
            mk = run
    qcol = TT % 2
    pr = Sch.op("vector", lambda e, s, qcol=qcol: e.tensor_mul(
        nB[0:16, 5:6], nB[0:16, qcol:qcol + 1], nB[0:16, 2 + qcol:3 + qcol]),
        waits=(mq, mk))
    sqr = Sch.op("scalar", lambda e, s: e.activation(nB[0:16, 6:7], nB[0:16, 5:6], AF.Sqrt), waits=(pr,))
    ngb = Sch.op("vector", lambda e, s: e.tensor_scalar_mul(nBbf[0:16, 0:1], nB[0:16, 6:7], -0.125),
                 waits=(sqr, zstage))
    trb = Sch.op("tensor", lambda e, s: e.transpose(psb[0][:, 0:16], nBbf[0:16, :],
                 H["ident_sb"][0:16, 0:16]), waits=(ngb, psb_war[0]))
    evb = Sch.op("scalar", lambda e, s: e.copy(nrow[:, :], psb[0][0:1, 0:16]), waits=(trb,))
    bcb = Sch.op("tensor", lambda e, s: e.matmul(
        H["ps_n"][:, 0:16], ones_row[0:1, :], nrow[0:1, :], start=True, stop=True),
        waits=(evb, one_tok, ("Sc", Sch.count.get("Sc", 0))))
    nbc = Sch.op("scalar", lambda e, s: e.copy(negbc[:, :], H["ps_n"][:, 0:16]), waits=(bcb,))
    psb_war[0] = evb

    SCL = 0.125
    outT = tm
    out_evacs = []
    eb_war = [("G", 0)] * 4
    stage_war = ("G", 0)
    den_war = ("V", 0)
    ps_n_war = nbc
    for h in range(16):
        cc_t = h // 2
        prow = (h % 2) * 64
        for qc in range(4):
            ob = 3 + qc % 2
            nkt = 4 * qc + 4
            av_last = None
            den_last = None
            for kt in range(nkt):
                off = max(0, kt * 128 - qc * 512)
                N = 512 - off
                sb_ = kt % 2
                diag = kt >= 4 * qc
                smm = Sch.op("tensor", lambda e, s, h=h, qc=qc, kt=kt, sb_=sb_, off=off, N=N, cc_t=cc_t, prow=prow: e.matmul(
                    ps[sb_][:, 0:N],
                    kT[prow:prow + 64, cc_t * S + kt * 128: cc_t * S + (kt + 1) * 128],
                    qT[prow:prow + 64, cc_t * S + qc * 512 + off: cc_t * S + (qc + 1) * 512],
                    start=True, stop=True),
                    waits=(q_tr, k_tr, bank_war[sb_]))
                eb = kt % 4
                if diag:
                    ex1 = Sch.op("scalar", lambda e, s, sb_=sb_, h=h: e.activation(
                        stage[:, 0:128], ps[sb_][:, 0:128], AF.Exp,
                        bias=negbc[:, h:h + 1], scale=SCL),
                        waits=(smm, nbc, stage_war))
                    toks_e = [ex1]
                    if N > 128:
                        ex2 = Sch.op("scalar", lambda e, s, sb_=sb_, eb=eb, N=N, h=h: e.activation(
                            expb[:, eb * 512 + 128: eb * 512 + N], ps[sb_][:, 128:N], AF.Exp,
                            bias=negbc[:, h:h + 1], scale=SCL),
                            waits=(smm, eb_war[eb]))
                        toks_e.append(ex2)
                    msk = Sch.op("gpsimd", lambda e, s, eb=eb: e.tensor_mul(
                        expb[:, eb * 512: eb * 512 + 128], stage[:, 0:128], tri[:, :]),
                        waits=(ex1, ld_tri, eb_war[eb]))
                    stage_war = msk
                    e_tok = (msk,) + tuple(toks_e)
                else:
                    ex = Sch.op("scalar", lambda e, s, sb_=sb_, eb=eb, N=N, h=h: e.activation(
                        expb[:, eb * 512: eb * 512 + N], ps[sb_][:, 0:N], AF.Exp,
                        bias=negbc[:, h:h + 1], scale=SCL),
                        waits=(smm, eb_war[eb], nbc))
                    e_tok = (ex,)
                bank_war[sb_] = e_tok[0]
                av_last = Sch.op("tensor", lambda e, s, h=h, kt=kt, eb=eb, ob=ob, off=off, N=N: e.matmul(
                    ps[ob][0:64, off:512], v_tm[:, kt * D + h * 64: kt * D + (h + 1) * 64],
                    expb[:, eb * 512: eb * 512 + N],
                    start=(kt == 0), stop=(kt == nkt - 1), skip_group_check=True),
                    waits=e_tok + ((v_toks[-1], bank_war[ob]) if kt == 0 else ()))
                den_last = Sch.op("tensor", lambda e, s, kt=kt, eb=eb, off=off, N=N: e.matmul(
                    ps[2][0:1, off:512], ones_col[:, 0:1],
                    expb[:, eb * 512: eb * 512 + N],
                    start=(kt == 0), stop=(kt == nkt - 1), skip_group_check=True),
                    waits=e_tok + (onec_tok,) + ((den_war,) if kt == 0 else ()),
                    inc=(kt == nkt - 1))
                eb_war[eb] = den_last
            lnd = Sch.op("scalar", lambda e, s: e.activation(
                rec1[0:1, 0:512], ps[2][0:1, 0:512], AF.Ln),
                waits=(den_last, H.get("_rec1_war", ("T", 0))))
            den_war = lnd
            rex = Sch.op("scalar", lambda e, s: e.activation(
                rec64[0:1, 0:512], rec1[0:1, 0:512], AF.Exp, scale=-1.0),
                waits=(lnd, H.get("_rec64_war", ("T", 0))))
            H["_rec1_war"] = rex
            bcd = Sch.op("tensor", lambda e, s: e.matmul(
                H["ps_n"][0:64, 0:512], ones_rowf[0:1, 0:64], rec64[0:1, 0:512],
                start=True, stop=True), waits=(rex, ps_n_war, onef_tok))
            H["_rec64_war"] = bcd
            evo = Sch.op("scalar", lambda e, s, ob=ob: e.copy(oun[:, :], ps[ob][0:64, 0:512]),
                         waits=(den_last, H.get("_oun_war", ("V", 0))))
            mlo = Sch.op("vector", lambda e, s, h=h, qc=qc, cc_t=cc_t, prow=prow: e.tensor_mul(
                outT[prow:prow + 64, cc_t * S + qc * 512: cc_t * S + (qc + 1) * 512],
                oun[:, :], H["ps_n"][0:64, 0:512]),
                waits=(bcd, evo, (v_toks[-1] if (h == 0 and qc == 0) else ("V", 0))))
            ps_n_war = mlo
            H["_oun_war"] = mlo
            bank_war[ob] = mlo
            out_evacs.append(mlo)

    comb_toks = []
    ld_toks = {}
    fin_v = {}
    fin_st = {}

    proj_T_done = ("T", Sch.count.get("T", 0))

    def tile_loads(t):
        a = t % 2
        if t >= 2:
            waits = (fin_v[t - 2], fin_st[t - 2])
        else:
            waits = (proj_T_done,)
        ld_toks[("x", t)] = H["LdA"].dma(lambda e, s, t=t, a=a: e.dma_start(
            x1t[a][:, :], H["x1_d"][t * 128:(t + 1) * 128, :]), waits=waits)
        ld_toks[("g", t)] = H["LdB"].dma(lambda e, s, t=t, a=a: e.dma_start(
            g1t[a][:, :], H["g1_d"][t * 128:(t + 1) * 128, :]), waits=waits)

    tile_loads(0)
    tile_loads(1)
    v_guard = ("V", Sch.count.get("V", 0))
    for t in range(TT):
        a = t % 2
        for n in range(2):
            b = (t * 2 + n) % 2
            mmx = None
            for vd in range(CC):
                mmx = Sch.op("tensor", lambda e, s, t=t, n=n, vd=vd, b=b: e.matmul(
                    ps[b][:, :], outT[:, vd * S + t * 128: vd * S + (t + 1) * 128],
                    WC[:, vd * 1024 + n * 512: vd * 1024 + (n + 1) * 512],
                    start=(vd == 0), stop=(vd == CC - 1)),
                    waits=(ld_wo, out_evacs[-1], bank_war[b], v_guard),
                    inc=(vd == CC - 1))
            c1 = Sch.op("vector", lambda e, s, t=t, n=n, a=a, b=b: e.tensor_mul(
                scr[:, :], g1t[a][:, n * 512:(n + 1) * 512], ps[b][:, :]),
                waits=(mmx, ld_toks[("g", t)], ld_toks[("x", t)],
                       comb_toks[-1] if comb_toks else ("V", 0)))
            c2 = Sch.op("vector", lambda e, s, t=t, n=n, a=a: e.tensor_add(
                g1t[a][:, n * 512:(n + 1) * 512], x1t[a][:, n * 512:(n + 1) * 512], scr[:, :]),
                waits=(c1,))
            bank_war[b] = c1
            comb_toks.append(c2)
        st = H["St"].dma(lambda e, s, t=t, a=a: e.dma_start(
            H["out_d"][t * 128:(t + 1) * 128, :], g1t[a][:, :]),
            waits=(comb_toks[-1], comb_toks[-2]))
        fin_v[t] = comb_toks[-1]
        fin_st[t] = st
        if t + 2 < TT:
            tile_loads(t + 2)


def alloc_p4b(nc, e, H):
    H["xn1T_sb"] = e(nc.sbuf_tensor("xn1T_sb", [128, CC * GW], BF16))
    H["wa_sb"] = e(nc.sbuf_tensor("wa_sb", [128, 2 * CC * 1024], BF16))
    H["wb_sb"] = e(nc.sbuf_tensor("wb_sb", [128, CC * 1024], BF16))
    H["tm_sb"] = e(nc.sbuf_tensor("tm_sb", [128, TT * D], BF16))
    H["qT_sb"] = e(nc.sbuf_tensor("qT_sb", [128, CC * S], BF16))
    H["kT_sb"] = e(nc.sbuf_tensor("kT_sb", [128, CC * S], BF16))
    H["expb_sb"] = e(nc.sbuf_tensor("expb_sb", [128, 4 * 512], BF16))
    H["stage_sb"] = e(nc.sbuf_tensor("stage_sb", [128, 128], BF16))
    H["ropeC_sb"] = e(nc.sbuf_tensor("ropeC_sb", [128, TT * 32], F32))
    H["ropeS_sb"] = e(nc.sbuf_tensor("ropeS_sb", [128, TT * 32], F32))
    H["tri_sb"] = e(nc.sbuf_tensor("tri_sb", [128, 128], BF16))
    H["ssq_sb"] = e(nc.sbuf_tensor("ssq_sb", [128, 2 * TT * 16], BF16))
    H["nB_sb"] = e(nc.sbuf_tensor("nB_sb", [16, 12], F32))
    H["nBbf_sb"] = e(nc.sbuf_tensor("nBbf_sb", [16, 128], BF16))
    H["nrow_sb"] = e(nc.sbuf_tensor("nrow_sb", [1, 16], BF16))
    H["negbc_sb"] = e(nc.sbuf_tensor("negbc_sb", [128, 16], F32))
    H["ones_row_sb"] = e(nc.sbuf_tensor("ones_row_sb", [1, 128], BF16))
    H["ones_col_sb"] = e(nc.sbuf_tensor("ones_col_sb", [128, 1], BF16))
    H["ones_rowf_sb"] = e(nc.sbuf_tensor("ones_rowf_sb", [1, 128], F32))
    H["rec1_sb"] = e(nc.sbuf_tensor("rec1_sb", [1, 512], F32))
    H["oun_sb"] = e(nc.sbuf_tensor("oun_sb", [64, 512], BF16))
    H["rec64_sb"] = e(nc.sbuf_tensor("rec64_sb", [1, 512], F32))
    H["str4_sb"] = e(nc.sbuf_tensor("str4", [128, 4 * 256], F32))
    H["ident_sb"] = e(nc.sbuf_tensor("ident_sb", [128, 128], BF16))


def build_p4b_program():
    nc = bass.Bass("TRN2", target_bir_lowering=False)
    H = {}
    H["xn1T_d"] = nc.dram_tensor("xn1T", [128, CC * GW], BF16, kind="ExternalInput").ap()
    H["WqT_d"] = nc.dram_tensor("WqT", [D, D], F32, kind="ExternalInput").ap()
    H["WkT_d"] = nc.dram_tensor("WkT", [D, D], F32, kind="ExternalInput").ap()
    H["WvT_d"] = nc.dram_tensor("WvT", [D, D], F32, kind="ExternalInput").ap()
    H["WoT_d"] = nc.dram_tensor("WoT", [D, D], F32, kind="ExternalInput").ap()
    H["ropeC_d"] = nc.dram_tensor("ropeC", [128, TT * 32], F32, kind="ExternalInput").ap()
    H["ropeS_d"] = nc.dram_tensor("ropeS", [128, TT * 32], F32, kind="ExternalInput").ap()
    H["tri_d"] = nc.dram_tensor("tri01", [128, 128], BF16, kind="ExternalInput").ap()
    H["ident_d"] = nc.dram_tensor("ident", [128, 128], BF16, kind="ExternalInput").ap()
    H["x1_d"] = nc.dram_tensor("x1", [S, D], F32, kind="ExternalInput").ap()
    H["g1_d"] = nc.dram_tensor("g1", [S, D], F32, kind="ExternalInput").ap()
    H["out_d"] = nc.dram_tensor("out", [S, D], F32, kind="ExternalOutput").ap()
    with ExitStack() as ctx:
        e = ctx.enter_context
        alloc_p4b(nc, e, H)
        H["cpool"] = e(nc.sbuf_tensor("cpool", [128, 16], F32))
        H["ps_f"] = [e(nc.psum_tensor(f"psf{i}", [128, 512], F32))[:, :] for i in range(5)]
        H["ps_n"] = e(nc.psum_tensor("ps_n", [128, 512], F32))[:, :]
        ps0 = e(nc.psum_tensor("psb0", [128, 128], BF16))
        H["psb0"] = ps0[:, :]
        H["psb1"] = e(nc.psum_tensor("psb1", [128, 128], BF16))[:, :]
        Sch = Sched()
        Sch._const_pool = H["cpool"]
        mk_streams(Sch, H)
        emit_P4b(nc, Sch, H)
        finish_program(nc, Sch, H)
    return nc



def build_p1_program():
    nc = bass.Bass("TRN2", target_bir_lowering=False)
    H = {}
    H["x_d"] = nc.dram_tensor("x", [S, D], F32, kind="ExternalInput").ap()
    H["ident_d"] = nc.dram_tensor("ident", [128, 128], BF16, kind="ExternalInput").ap()
    H["u0_d"] = nc.dram_tensor("u0", [S, D], F32, kind="ExternalOutput").ap()
    H["xnT_d"] = nc.dram_tensor("xnT", [128, CC * GW], BF16, kind="ExternalOutput").ap()
    with ExitStack() as ctx:
        e = ctx.enter_context
        H["x_sb"] = e(nc.sbuf_tensor("x_sb", [128, TT * D], F32))
        H["u_sb"] = e(nc.sbuf_tensor("u_sb", [128, TT * D], F32))
        H["xnbf_sb"] = e(nc.sbuf_tensor("xnbf", [128, D], BF16))
        H["xnT_sb"] = e(nc.sbuf_tensor("xnT_sb", [128, CC * GW], BF16))
        H["mv_sb"] = e(nc.sbuf_tensor("mv", [128, 2], F32))
        H["st6_sb"] = e(nc.sbuf_tensor("st6", [128, 12], F32))
        H["rstd_sb"] = e(nc.sbuf_tensor("rstd", [128, 2], F32))
        H["ident_sb"] = e(nc.sbuf_tensor("ident_sb", [128, 128], BF16))
        ps0 = e(nc.psum_tensor("psb0", [128, 128], BF16))
        ps1 = e(nc.psum_tensor("psb1", [128, 128], BF16))
        H["psb0"], H["psb1"] = ps0[:, :], ps1[:, :]
        Sch = Sched()
        H["cpool"] = e(nc.sbuf_tensor("cpool", [128, 16], F32))
        Sch._const_pool = H["cpool"]
        mk_streams(Sch, H)
        ld_id = H["Wt"].dma(lambda e2, s: e2.dma_start(H["ident_sb"][:, :], H["ident_d"]))
        Sch.op("tensor", lambda e2, s: None, waits=(ld_id,), inc=False)
        emit_P1(nc, Sch, H)
        finish_program(nc, Sch, H)
    return nc





class _NCProxy:

    def __init__(self, nc, prefix):
        self._nc = nc
        self._prefix = prefix

    def sbuf_tensor(self, name, *a, **k):
        return self._nc.sbuf_tensor(self._prefix + name, *a, **k)

    def __getattr__(self, k):
        return getattr(self._nc, k)


def build_fused_program(debug=False, phases=("P1", "P2", "P3", "P4a", "P4b"),
                        p2_lis=tuple(range(6))):
    nc = bass.Bass("TRN2", target_bir_lowering=False)
    H = {}
    EI = "ExternalInput"
    IK = "ExternalOutput" if debug else "Internal"
    H["x_d"] = nc.dram_tensor("x", [S, D], F32, kind=EI).ap()
    H["ident_d"] = nc.dram_tensor("ident", [128, 128], BF16, kind=EI).ap()
    H["Wq_d"] = nc.dram_tensor("Wq", [D, D], F32, kind=EI).ap()
    H["Wk_d"] = nc.dram_tensor("Wk", [D, D], F32, kind=EI).ap()
    H["Wv_d"] = nc.dram_tensor("Wv", [D, D], F32, kind=EI).ap()
    H["Wo_d"] = nc.dram_tensor("Wo", [D, D], F32, kind=EI).ap()
    H["dmask_d"] = nc.dram_tensor("dmask", [4, 128, 512], BF16, kind=EI).ap()
    H["Wqkv_full_d"] = nc.dram_tensor("Wqkv", [6, D, 3072], F32, kind=EI).ap()
    H["Wo6_full_d"] = nc.dram_tensor("Wo6", [6, D, D], F32, kind=EI).ap()
    H["W1r_full_d"] = nc.dram_tensor("W1r", [6, D, 5460], F32, kind=EI).ap()
    H["W2_full_d"] = nc.dram_tensor("W2", [6, DI2, D], F32, kind=EI).ap()
    H["cols_full_d"] = nc.dram_tensor("colsP2", [6, 128, 62], F32, kind=EI).ap()
    H["b2_full_d"] = nc.dram_tensor("b2rows", [6, 1024], BF16, kind=EI).ap()
    H["tbl_d"] = nc.dram_tensor("tblP2", [128, 4, S], BF16, kind=EI).ap()
    H["masks_d"] = nc.dram_tensor("masksP2", [4, 128, W0], BF16, kind=EI).ap()
    H["ind16_d"] = nc.dram_tensor("ind16", [128, 128], BF16, kind=EI).ap()
    H["sel16_d"] = nc.dram_tensor("sel16", [16, 8 * 128], BF16, kind=EI).ap()
    H["WrT_d"] = nc.dram_tensor("WrT", [D, D], F32, kind=EI).ap()
    H["WkT2_d"] = nc.dram_tensor("WkT2", [D, D], F32, kind=EI).ap()
    H["WvT2_d"] = nc.dram_tensor("WvT2", [D, D], F32, kind=EI).ap()
    H["mu_r_d"] = nc.dram_tensor("mu_r_cols", [128, 8], F32, kind=EI).ap()
    H["mu_k_d"] = nc.dram_tensor("mu_k_cols", [128, 8], F32, kind=EI).ap()
    H["WqT_d"] = nc.dram_tensor("WqT", [D, D], F32, kind=EI).ap()
    H["WkT_d"] = nc.dram_tensor("WkT", [D, D], F32, kind=EI).ap()
    H["WvT_d"] = nc.dram_tensor("WvT", [D, D], F32, kind=EI).ap()
    H["WoT_d"] = nc.dram_tensor("WoT", [D, D], F32, kind=EI).ap()
    H["ropeC_d"] = nc.dram_tensor("ropeC", [128, TT * 32], F32, kind=EI).ap()
    H["ropeS_d"] = nc.dram_tensor("ropeS", [128, TT * 32], F32, kind=EI).ap()
    H["tri_d"] = nc.dram_tensor("tri01", [128, 128], BF16, kind=EI).ap()
    H["u0_d"] = nc.dram_tensor("u0", [S, D], F32, kind=IK).ap()
    H["xnT_d"] = nc.dram_tensor("xnT", [128, CC * GW], BF16, kind=IK).ap()
    H["bgrid_d"] = nc.dram_tensor("bgrid", [16, 8], F32).ap()
    H["bgrid_flat_d"] = H["bgrid_d"].rearrange("a b -> (a b)").rearrange("(x y) -> x y", x=1)
    H["qT_scr_d"] = nc.dram_tensor("qT_scr", [128, CC * S], BF16).ap()
    H["kT_scr_d"] = nc.dram_tensor("kT_scr", [128, CC * S], BF16).ap()
    H["v_scr_d"] = nc.dram_tensor("v_scr", [128, TT * 1024], BF16).ap()
    H["u_in_0"] = H["u0_d"]
    for li in range(6):
        H[f"u_mid_{li}"] = nc.dram_tensor(f"u_mid_{li}", [S, D], F32, kind=IK).ap()
        H[f"u_out_{li}"] = nc.dram_tensor(f"u_pp_{li}", [S, D], F32, kind=IK).ap()
        if li < 5:
            H[f"u_in_{li + 1}"] = H[f"u_out_{li}"]
    H["g0_d"] = H["u_out_5"]
    H["x1_d"] = nc.dram_tensor("x1", [S, D], F32, kind=IK).ap()
    H["xn1T_d"] = nc.dram_tensor("xn1T", [128, CC * GW], BF16, kind=IK).ap()
    H["g1_d"] = nc.dram_tensor("g1", [S, D], F32, kind=IK).ap()
    H["out_d"] = nc.dram_tensor("out", [S, D], F32, kind="ExternalOutput").ap()
    def sl3(apfull, li):
        return apfull[li:li + 1, :, :].rearrange("l a b -> (l a) b")
    H["Wqkv_d"] = {li: sl3(H["Wqkv_full_d"], li) for li in range(6)}
    H["Wo6_d"] = {li: sl3(H["Wo6_full_d"], li) for li in range(6)}
    H["W1r_d"] = {li: sl3(H["W1r_full_d"], li) for li in range(6)}
    H["W2_d"] = {li: sl3(H["W2_full_d"], li) for li in range(6)}
    H["cols_d"] = {li: sl3(H["cols_full_d"], li) for li in range(6)}
    H["b2_d"] = {li: H["b2_full_d"][li:li + 1, :] for li in range(6)}

    with ExitStack() as ctx:
        e = ctx.enter_context
        H["cpool"] = e(nc.sbuf_tensor("cpool", [128, 16], F32))
        ps_f = [e(nc.psum_tensor(f"psf{i}", [128, 512], F32))[:, :] for i in range(6)]
        H["ps_f"] = ps_f
        H["psb0"] = e(nc.psum_tensor("psb0", [128, 128], BF16))[:, :]
        H["psb1"] = e(nc.psum_tensor("psb1", [128, 128], BF16))[:, :]
        Sch = Sched()
        Sch._const_pool = H["cpool"]
        mk_streams(Sch, H)

        if "P1" in phases:
            with ExitStack() as c1:
                e1 = c1.enter_context
                p1 = _NCProxy(nc, "p1_")
                H["x_sb"] = e1(p1.sbuf_tensor("x_sb", [128, TT * D], F32))
                H["u_sb"] = e1(p1.sbuf_tensor("u_sb", [128, TT * D], F32))
                H["xnbf_sb"] = e1(p1.sbuf_tensor("xnbf", [128, D], BF16))
                H["xnT_sb"] = e1(p1.sbuf_tensor("xnT_sb", [128, CC * GW], BF16))
                H["mv_sb"] = e1(p1.sbuf_tensor("mv", [128, 2], F32))
                H["st6_sb"] = e1(p1.sbuf_tensor("st6", [128, 12], F32))
                H["rstd_sb"] = e1(p1.sbuf_tensor("rstd", [128, 2], F32))
                H["ident_sb"] = e1(p1.sbuf_tensor("ident_sb", [128, 128], BF16))
                Hp1 = dict(H)
                ld_id = Hp1["Wt"].dma(
                    lambda e2, s: e2.dma_start(Hp1["ident_sb"][:, :], Hp1["ident_d"]))
                Sch.op("tensor", lambda e2, s: None, waits=(ld_id,), inc=False)
                emit_P1(nc, Sch, Hp1)
                Sch.barrier()

        if "P2" in phases:
            with ExitStack() as c2:
                e2_ = c2.enter_context
                p2 = _NCProxy(nc, "p2_")
                alloc_p2(p2, e2_, H)
                Hp2 = dict(H)
                p2_common_setup(nc, Sch, Hp2)
                ld_uTg = Hp2["LdIn"].dma(
                    lambda e2, s: e2.dma_start(Hp2["RA_sb"][:, :], Hp2["xnT_d"]))
                Hp2["_p2_uTg_rdy"] = ld_uTg
                for li in p2_lis:
                    emit_P2_sublayer(nc, Sch, Hp2, li, first=(li == p2_lis[0]), last=(li == 5))
                Sch.barrier()

        if any(p.startswith("P3stub") for p in phases):
            mode = [p for p in phases if p.startswith("P3stub")][0]
            with ExitStack() as c3:
                e3 = c3.enter_context
                p3 = _NCProxy(nc, "p3_")
                alloc_p3(p3, e3, H)
                B = H["B_sb"]
                F = H["F_sb"]
                bar_toks = tuple(Sch._tok(s) for s in sorted(Sch.sem_names))
                if mode == "P3stubX":
                    Sch.last_waited["sync"] = {}
                if mode == "P3stubH":
                    scrh = nc.dram_tensor("stub_scr", [128, CC * GW], BF16).ap()
                    H["LdIn"].dma(lambda e2, s: e2.dma_start(scrh, H["xnT_d"]))
                    H["LdIn"].dma(lambda e2, s: e2.dma_start(
                        scrh[:, 0:128], H["ident_d"]))
                    H["LdIn"].dma(lambda e2, s: e2.dma_start(
                        scrh[:, 128:256], H["ident_d"]))
                if mode in ("P3stub", "P3stubL", "P3stubX"):
                    w0 = bar_toks if mode == "P3stubX" else ()
                    H["LdIn"].dma(lambda e2, s: e2.dma_start(H["xnT_sb"][:, :], H["xnT_d"]),
                                  waits=w0)
                    H["LdIn"].dma(lambda e2, s: e2.dma_start(
                        H["dmask_sb"][:, :].rearrange("p (r n) -> p r n", n=512),
                        H["dmask_d"].rearrange("r p n -> p r n")), waits=w0)
                    H["LdIn"].dma(lambda e2, s: e2.dma_start(H["ident_sb"][:, :], H["ident_d"]),
                                  waits=w0)
                if mode in ("P3stub", "P3stubW"):
                    for dst, hbm in ((B[:, 0:8192], H["Wq_d"]), (B[:, 8192:16384], H["Wk_d"]),
                                     (F[:, :], H["Wv_d"])):
                        H["Wt"].dma(lambda e2, s, dst=dst, hbm=hbm: e2.dma_start(
                            dst.rearrange("p (c n) -> p c n", n=1024),
                            hbm.rearrange("(c p) n -> p c n", p=128)))
                Sch.barrier()
        if "P3" in phases:
            with ExitStack() as c3:
                e3 = c3.enter_context
                p3 = _NCProxy(nc, "p3_")
                alloc_p3(p3, e3, H)
                Hp3 = dict(H)
                emit_P3(nc, Sch, Hp3)
                Sch.barrier()

        if "P4a" in phases:
            with ExitStack() as c4:
                e4 = c4.enter_context
                p4 = _NCProxy(nc, "p4a_")
                alloc_p4a(p4, e4, H)
                Hp4 = dict(H)
                emit_P4a(nc, Sch, Hp4)
                Sch.barrier()

        if "P4b" in phases:
            with ExitStack() as c5:
                e5 = c5.enter_context
                p5 = _NCProxy(nc, "p4b_")
                alloc_p4b(p5, e5, H)
                Hp5 = dict(H)
                Hp5["ps_f"] = ps_f[0:5]
                Hp5["ps_n"] = ps_f[5]
                emit_P4b(nc, Sch, Hp5)
        finish_program(nc, Sch, H)
    return nc


FUSED = True


def kernel(**inputs):
    d = host_prep(inputs)
    hp2 = host_prep_p2(inputs)
    x = d["x"]
    n = NCORES

    if FUSED:
        nc = build_fused_program()
        shared = {
            "ident": d["ident"],
            "Wq": d["L0_attn_Wq"], "Wk": d["L0_attn_Wk"],
            "Wv": d["L0_attn_Wv"], "Wo": d["L0_attn_Wo"],
            "dmask": d["diagmask"].astype(ml_dtypes.bfloat16),
            "Wqkv": hp2["Wqkv"], "Wo6": hp2["Wo6"], "W1r": hp2["W1r"],
            "W2": hp2["W2"], "colsP2": hp2["colsP2"], "b2rows": hp2["b2rows"],
            "tblP2": hp2["tblP2"], "masksP2": hp2["masksP2"],
            "ind16": hp2["ind16"], "sel16": hp2["sel16"],
            "WrT": d["WrT"], "WkT2": d["WkT2"], "WvT2": d["WvT2"],
            "mu_r_cols": d["mu_r_cols"], "mu_k_cols": d["mu_k_cols"],
            "WqT": d["L1_WqT"], "WkT": d["L1_WkT"], "WvT": d["L1_WvT"],
            "WoT": d["L1_WoT"], "ropeC": d["ropeC"], "ropeS": d["ropeS"],
            "tri01": d["tri01"],
        }
        in_maps = []
        for b in range(n):
            m = dict(shared)
            m["x"] = np.ascontiguousarray(x[b])
            in_maps.append(m)
        res = run_bass_kernel_spmd(nc, in_maps, core_ids=list(range(n))).results
        return np.stack([np.asarray(r["out"], np.float32) for r in res], axis=0)

    def run(nc, per_core, shared):
        in_maps = []
        for b in range(n):
            m = dict(shared)
            for k, v in per_core.items():
                m[k] = v[b]
            in_maps.append(m)
        return run_bass_kernel_spmd(nc, in_maps, core_ids=list(range(n))).results

    ident = d["ident"]
    nc1 = build_p1_program()
    r1 = run(nc1, {"x": [np.ascontiguousarray(x[b]) for b in range(n)]},
             {"ident": ident})
    u0 = [np.asarray(r["u0"]) for r in r1]
    xnT = [np.asarray(r["xnT"]) for r in r1]
    u_cur = u0
    uTg_cur = xnT
    for li in range(6):
        nc2 = build_p2_program([li])
        shared = {
            "Wqkv": np.ascontiguousarray(hp2["Wqkv"][li:li + 1]),
            "Wo6": np.ascontiguousarray(hp2["Wo6"][li:li + 1]),
            "W1r": np.ascontiguousarray(hp2["W1r"][li:li + 1]),
            "W2": np.ascontiguousarray(hp2["W2"][li:li + 1]),
            "colsP2": np.ascontiguousarray(hp2["colsP2"][li:li + 1]),
            "b2rows": np.ascontiguousarray(hp2["b2rows"][li:li + 1]),
            "tblP2": hp2["tblP2"], "masksP2": hp2["masksP2"],
            "ident": ident, "ind16": hp2["ind16"], "sel16": hp2["sel16"],
        }
        r2 = run(nc2, {"u_in": u_cur, "uTg_in": uTg_cur}, shared)
        u_cur = [np.asarray(r["u_out"]) for r in r2]
        uTg_cur = [np.asarray(r["uTg_out"]) for r in r2]
    g0 = u_cur
    nc3 = build_p3_program()
    r3 = run(nc3, {"xnT": xnT,
                   "x": [np.ascontiguousarray(x[b]) for b in range(n)],
                   "g0": g0},
             {"Wq": d["L0_attn_Wq"], "Wk": d["L0_attn_Wk"],
              "Wv": d["L0_attn_Wv"], "Wo": d["L0_attn_Wo"],
              "dmask": d["diagmask"].astype(ml_dtypes.bfloat16), "ident": ident})
    x1 = [np.asarray(r["x1"]) for r in r3]
    xn1T = [np.asarray(r["xn1T"]) for r in r3]
    nc4a = build_p4a_program()
    r4a = run(nc4a, {"xn1T": xn1T},
              {"WrT": d["WrT"], "WkT2": d["WkT2"], "WvT2": d["WvT2"],
               "mu_r_cols": d["mu_r_cols"], "mu_k_cols": d["mu_k_cols"]})
    g1 = [np.asarray(r["g1"]) for r in r4a]
    nc4b = build_p4b_program()
    r4b = run(nc4b, {"xn1T": xn1T, "x1": x1, "g1": g1},
              {"WqT": d["L1_WqT"], "WkT": d["L1_WkT"], "WvT": d["L1_WvT"],
               "WoT": d["L1_WoT"], "ropeC": d["ropeC"], "ropeS": d["ropeS"],
               "tri01": d["tri01"], "ident": ident})
    out = np.stack([np.asarray(r["out"], np.float32) for r in r4b], axis=0)
    return out


# version 08
# speedup vs baseline: 1.0500x; 1.0500x over previous
import numpy as np
import ml_dtypes
import concourse.bass as bass
import concourse.mybir as mybir
from concourse.bass_utils import run_bass_kernel_spmd
from contextlib import ExitStack

F32 = mybir.dt.float32
BF16 = mybir.dt.bfloat16
AF = mybir.ActivationFunctionType
ALU = mybir.AluOpType
AX = mybir.AxisListType

S = 2048
D = 1024
TT = S // 128
CC = D // 128
GW = S + 1

NCORES = 8


ENGINES = ("tensor", "vector", "scalar", "gpsimd", "sync")
ENG_SEM = {"tensor": "T", "vector": "V", "scalar": "Sc", "gpsimd": "G", "sync": "Sy"}


class Sched:
    def __init__(self):
        self.ops = {e: [] for e in ENGINES}
        self.count = {}
        self.last_waited = {e: {} for e in ENGINES}
        self.sem_names = set(ENG_SEM.values())
        self._consts = {}
        self._const_pool = None

    def _tok(self, sem):
        return (sem, self.count.get(sem, 0))

    def op(self, eng, fn, waits=(), inc=True, dma_sem=None):
        sem = dma_sem if dma_sem else ENG_SEM[eng]
        amount = 16 if dma_sem else 1
        tgt = None
        if inc:
            self.count[sem] = self.count.get(sem, 0) + amount
            tgt = self.count[sem]
        lw = self.last_waited[eng]
        eff_waits = []
        for (ws, wc) in waits:
            if wc <= 0:
                continue
            if lw.get(ws, 0) >= wc:
                continue
            eff_waits.append((ws, wc))
            lw[ws] = wc
        self.ops[eng].append((tuple(eff_waits), fn, sem if inc else None, amount))
        return (sem, tgt if tgt is not None else self.count.get(sem, 0))

    def barrier(self):
        toks = [self._tok(s) for s in self.sem_names]
        for e in ENGINES:
            def mk(fn_eng=e):
                def fn(eng, sems):
                    return None
                return fn
            self.op(e, mk(), waits=toks, inc=False)
        return toks

    def const_col(self, name_val):
        val = name_val
        key = float(val)
        if key not in self._consts:
            idx = len(self._consts)
            assert idx < self._const_pool.shape[1] if hasattr(self._const_pool, "shape") else True
            tok = self.op("vector", lambda e, s, idx=idx, val=val: e.memset(
                self._const_pool[:, idx:idx + 1], val))
            self._consts[key] = (idx, tok)
        idx, tok = self._consts[key]
        return self._const_pool[:, idx:idx + 1], tok

    def run(self, nc, sems):
        with nc.Block() as block:
            for eng_name in ENGINES:
                deco = getattr(block, eng_name)
                ops = self.ops[eng_name]

                def make_body(ops=ops):
                    def body(eng):
                        for eff_waits, fn, sem, amount in ops:
                            for (ws, wc) in eff_waits:
                                eng.wait_ge(sems[ws], wc)
                            ins = fn(eng, sems)
                            if sem is not None:
                                assert ins is not None, "op with inc must return instruction"
                                ins.then_inc(sems[sem], amount)
                    return body

                deco(make_body())


class Stream:

    def __init__(self, sched, eng, name, depth=2):
        self.S = sched
        self.eng = eng
        self.depth = depth
        self.semn = [f"{name}{i}" for i in range(depth)]
        for s in self.semn:
            sched.sem_names.add(s)
        self.j = 0

    def dma(self, fn, waits=()):
        i = self.j % self.depth
        sem = self.semn[i]
        w = list(waits)
        if self.j >= self.depth:
            w.append((sem, self.S.count.get(sem, 0)))
        tok = self.S.op(self.eng, fn, waits=w, dma_sem=sem)
        self.j += 1
        return tok

    def total_toks(self):
        return [(s, self.S.count.get(s, 0)) for s in self.semn]


def mk_streams(Sch, H):
    H["LdIn"] = Stream(Sch, "sync", "LdIn", 4)
    H["LdA"] = Stream(Sch, "sync", "LdA", 2)
    H["LdB"] = Stream(Sch, "sync", "LdB", 2)
    H["St"] = Stream(Sch, "sync", "St", 2)
    H["Wt"] = Stream(Sch, "gpsimd", "Wt", 4)
    H["_streams"] = [H["LdIn"], H["LdA"], H["LdB"], H["St"], H["Wt"]]


def finish_program(nc, Sch, H):
    waits = []
    for st in H["_streams"]:
        waits += st.total_toks()
    Sch.op("sync", lambda e2, s: None, waits=waits, inc=False)
    semstack = ExitStack()
    sems = {}
    for name in sorted(Sch.sem_names):
        sems[name] = semstack.enter_context(nc.semaphore(name))
    Sch.run(nc, sems)
    semstack.close()



def _rotary_tables_l0ffn():
    DIM_HEAD = 64
    WINDOW = 256
    half = np.arange(0, DIM_HEAD, 2, dtype=np.float64)
    inv_freq = 1.0 / (10000.0 ** (half / DIM_HEAD))
    t = np.arange(S, dtype=np.float64)
    freqs = t[:, None] * inv_freq[None, :]
    freqs = np.concatenate([freqs, freqs], -1)
    base_scale = (half + 0.4 * DIM_HEAD) / (1.4 * DIM_HEAD)
    power = (t - S // 2) / (WINDOW // 2)
    scale = base_scale[None, :] ** power[:, None]
    scale = np.concatenate([scale, scale], -1)
    cos, sin = np.cos(freqs), np.sin(freqs)
    inv = 1.0 / scale
    cs_q = (cos * scale).T
    sn_q_t = (sin * scale).T
    cs_k = (cos * inv).T
    sn_k_t = (sin * inv).T
    def signed(sn):
        s = sn.copy()
        s[:32, :] *= -1.0
        return s
    sn_q = signed(sn_q_t)
    sn_k = signed(sn_k_t)
    rot = np.array([(p + 32) % 64 if (p % 64) < 32 else p - 32 for p in range(64)])
    sn_q = sn_q[rot]
    sn_k = sn_k[rot]
    def dup(a):
        return np.concatenate([a, a], axis=0).astype(np.float32)
    return dup(cs_q), dup(sn_q), dup(cs_k), dup(sn_k)


def _rope_tables_l1():
    HEAD_DIM = 64
    pos = np.arange(S, dtype=np.float64)
    inv_freq = 10000.0 ** (-np.arange(0, HEAD_DIM, 2, dtype=np.float64) / HEAD_DIM)
    ang = pos[:, None] * inv_freq[None, :]
    C = np.cos(ang).astype(np.float32)
    Sn = np.sin(ang).astype(np.float32)
    Ct = np.concatenate([C[t * 128:(t + 1) * 128] for t in range(TT)], axis=1)
    St = np.concatenate([Sn[t * 128:(t + 1) * 128] for t in range(TT)], axis=1)
    return Ct.astype(np.float32), St.astype(np.float32)


def _l0ffn_window_masks():
    masks = np.zeros((4, 128, 256), dtype=np.float32)
    i = np.arange(128)[:, None]
    j = np.arange(256)[None, :]
    for kt in range(4):
        k = kt * 128 + i
        masks[kt] = np.where((j >= k - 256) & (j <= k), 0.0, -1.0e30)
    return masks.astype(ml_dtypes.bfloat16)


def _l0attn_diag_masks():
    m = np.zeros((4, 128, 512), dtype=np.float32)
    i = np.arange(128)[:, None]
    j = np.arange(512)[None, :]
    for r in range(4):
        m[r] = np.where(j > r * 128 + i, -1.0e30, 0.0)
    return m


def _l1_tri_mask():
    i = np.arange(128)[:, None]
    j = np.arange(128)[None, :]
    return (i <= j).astype(ml_dtypes.bfloat16)


def _reorder_w1(W1, b1, W2):
    DI = W2.shape[-2]
    nb = (DI + 127) // 128
    colblocks = []
    sizes = []
    for t in range(nb):
        lo, hi = t * 128, min((t + 1) * 128, DI)
        colblocks.append((np.arange(lo, hi), np.arange(lo, hi) + DI))
        sizes.append(hi - lo)
    order_a = []
    order = []
    for (ai, gi) in colblocks:
        order.append(("a", ai))
        order.append(("g", gi))
    col_idx = np.concatenate([idx for _, idx in order])
    W1r = W1[..., col_idx]
    b1r = b1[..., col_idx]
    W2r = W2
    return np.ascontiguousarray(W1r), np.ascontiguousarray(b1r), W2r, sizes




def host_prep_p2(inputs):
    f32 = np.float32
    W1 = np.asarray(inputs["L0_ffn_W1"], f32)
    b1 = np.asarray(inputs["L0_ffn_b1"], f32)
    W2 = np.asarray(inputs["L0_ffn_W2"], f32)
    W1r, b1r, W2r, _ = _reorder_w1(W1, b1, W2)
    qs = np.asarray(inputs["L0_ffn_q_scale"], f32)
    ks = np.asarray(inputs["L0_ffn_k_scale"], f32)
    lng = np.asarray(inputs["L0_ffn_ln_g"], f32).reshape(6, 8, 128)
    lnb = np.asarray(inputs["L0_ffn_ln_b"], f32).reshape(6, 8, 128)
    cols = np.zeros((6, 128, 62), f32)
    cols[:, :, 0] = np.concatenate([qs, qs], -1)
    cols[:, :, 1] = np.concatenate([ks, ks], -1)
    cols[:, :, 2:10] = lng.transpose(0, 2, 1)
    cols[:, :, 10:18] = lnb.transpose(0, 2, 1)
    off = 0
    ci = 18
    for t in range(21):
        for _ in range(2):
            cols[:, :, ci] = b1r[:, off:off + 128]
            off += 128
            ci += 1
    a42 = np.zeros((6, 128), f32); a42[:, :42] = b1r[:, off:off + 42]; off += 42
    g42 = np.zeros((6, 128), f32); g42[:, :42] = b1r[:, off:off + 42]; off += 42
    cols[:, :, 60] = a42
    cols[:, :, 61] = g42
    cs_q, sn_q, cs_k, sn_k = _rotary_tables_l0ffn()
    tbl = np.stack([cs_q, sn_q, cs_k, sn_k], axis=1).astype(ml_dtypes.bfloat16)
    ind = np.zeros((128, 8, 16), f32)
    for qc in range(8):
        ind[:64, qc, 2 * qc] = 1.0
        ind[64:, qc, 2 * qc + 1] = 1.0
    sel = np.zeros((16, 8, 128), f32)
    for e_ in range(8):
        sel[2 * e_, e_, 0:64] = 1.0
        sel[2 * e_ + 1, e_, 64:128] = 1.0
    return {
        "Wqkv": np.ascontiguousarray(np.asarray(inputs["L0_ffn_Wqkv"], f32)),
        "Wo6": np.ascontiguousarray(np.asarray(inputs["L0_ffn_Wo"], f32)),
        "W1r": np.ascontiguousarray(W1r),
        "W2": np.ascontiguousarray(W2r),
        "colsP2": np.ascontiguousarray(cols),
        "b2rows": np.asarray(inputs["L0_ffn_b2"], f32).astype(ml_dtypes.bfloat16),
        "tblP2": np.ascontiguousarray(tbl),
        "masksP2": np.ascontiguousarray(_l0ffn_window_masks()),
        "ind16": ind.reshape(128, 128).astype(ml_dtypes.bfloat16),
        "sel16": np.ascontiguousarray(sel.reshape(16, 8 * 128).astype(ml_dtypes.bfloat16)),
    }


def host_prep(inputs):
    f32 = np.float32
    d = {}
    x = np.asarray(inputs["L0_attn_x"], f32)
    d["x"] = x
    for k in ("L0_attn_Wq", "L0_attn_Wk", "L0_attn_Wv", "L0_attn_Wo"):
        d[k] = np.ascontiguousarray(np.asarray(inputs[k], f32))
    d["Wqkv"] = np.ascontiguousarray(np.asarray(inputs["L0_ffn_Wqkv"], f32))
    d["Wo6"] = np.ascontiguousarray(np.asarray(inputs["L0_ffn_Wo"], f32))
    W1 = np.asarray(inputs["L0_ffn_W1"], f32)
    b1 = np.asarray(inputs["L0_ffn_b1"], f32)
    W2 = np.asarray(inputs["L0_ffn_W2"], f32)
    W1r, b1r, W2r, blk = _reorder_w1(W1, b1, W2)
    d["W1r"] = W1r
    d["W2"] = np.ascontiguousarray(W2r)
    d["b2"] = np.ascontiguousarray(np.asarray(inputs["L0_ffn_b2"], f32))
    nb2 = b1r.shape[-1] // 0 if False else None
    cols = []
    off = 0
    for t in range(21):
        for part in range(2):
            cols.append(b1r[:, off:off + 128])
            off += 128
    a42 = np.zeros((6, 128), f32); a42[:, :42] = b1r[:, off:off + 42]; off += 42
    g42 = np.zeros((6, 128), f32); g42[:, :42] = b1r[:, off:off + 42]; off += 42
    cols.append(a42); cols.append(g42)
    assert off == b1r.shape[1]
    d["b1cols"] = np.ascontiguousarray(np.stack(cols, axis=1))
    qs = np.asarray(inputs["L0_ffn_q_scale"], f32)
    ks = np.asarray(inputs["L0_ffn_k_scale"], f32)
    d["qs_cols"] = np.ascontiguousarray(np.concatenate([qs, qs], 1))
    d["ks_cols"] = np.ascontiguousarray(np.concatenate([ks, ks], 1))
    g = np.asarray(inputs["L0_ffn_ln_g"], f32).reshape(6, 8, 128)
    b = np.asarray(inputs["L0_ffn_ln_b"], f32).reshape(6, 8, 128)
    d["lng"] = np.ascontiguousarray(g)
    d["lnb"] = np.ascontiguousarray(b)
    d["L1_WqT"] = np.ascontiguousarray(np.asarray(inputs["L1_attn_Wq"], f32).T)
    d["L1_WkT"] = np.ascontiguousarray(np.asarray(inputs["L1_attn_Wk"], f32).T)
    d["L1_WvT"] = np.ascontiguousarray(np.asarray(inputs["L1_attn_Wv"], f32).T)
    d["L1_WoT"] = np.ascontiguousarray(np.asarray(inputs["L1_attn_Wo"], f32).T)
    d["WrT"] = np.ascontiguousarray(np.asarray(inputs["L1_ffn_W_r"], f32).T)
    d["WkT2"] = np.ascontiguousarray(np.asarray(inputs["L1_ffn_W_k"], f32).T)
    d["WvT2"] = np.ascontiguousarray(np.asarray(inputs["L1_ffn_W_v"], f32).T)
    mu_r = np.asarray(inputs["L1_ffn_mu_r"], f32).reshape(CC, 128).T
    mu_k = np.asarray(inputs["L1_ffn_mu_k"], f32).reshape(CC, 128).T
    d["mu_r_cols"] = np.ascontiguousarray(mu_r)
    d["mu_k_cols"] = np.ascontiguousarray(mu_k)
    d["ident"] = np.eye(128, dtype=ml_dtypes.bfloat16)
    cs_q, sn_q, cs_k, sn_k = _rotary_tables_l0ffn()
    d["cs_q"], d["sn_q"], d["cs_k"], d["sn_k"] = cs_q, sn_q, cs_k, sn_k
    C1, S1 = _rope_tables_l1()
    d["ropeC"], d["ropeS"] = C1, S1
    d["wmask01"] = _l0ffn_window_masks()
    d["diagmask"] = _l0attn_diag_masks()
    d["tri01"] = _l1_tri_mask()
    ind = np.zeros((128, 8, 16), np.float32)
    for qc in range(8):
        ind[:64, qc, 2 * qc] = 1.0
        ind[64:, qc, 2 * qc + 1] = 1.0
    d["ind16"] = ind.reshape(128, 128).astype(ml_dtypes.bfloat16)
    ones_col = np.ones((128, 1), np.float32)
    d["ones_col_bf"] = ones_col.astype(ml_dtypes.bfloat16)
    d["b2_rows"] = np.asarray(inputs["L0_ffn_b2"], f32).astype(ml_dtypes.bfloat16)
    return d



def transpose_tiles(Sch, H, src_fn, dst_fn, ntiles, psb, ps_tok, waits=()):
    ev = None
    for i in range(ntiles):
        b = i % 2
        t_tok = Sch.op("tensor", lambda e, s, i=i, b=b: e.transpose(
            psb[b], src_fn(i), H["ident_sb"][:, :]),
            waits=tuple(waits) + (ps_tok[b],))
        ev = Sch.op("scalar", lambda e, s, i=i, b=b: e.copy(dst_fn(i), psb[b]),
                    waits=(t_tok,))
        ps_tok[b] = ev
    return ev, ps_tok



def emit_P1(nc, Sch, H):
    x_sb = H["x_sb"]
    u_sb = H["u_sb"]
    xnbf = H["xnbf_sb"]
    xnT = H["xnT_sb"]
    mv = H["mv_sb"]
    st6 = H["st6_sb"]
    rstd = H["rstd_sb"]
    psb = [H["psb0"], H["psb1"]]

    for cc in range(CC):
        z_tok = Sch.op("vector", lambda e, s, cc=cc: e.memset(
            xnT[:, cc * GW: cc * GW + 1], 0.0))
    ld = H["LdIn"].dma(lambda e, s: e.dma_start(
        x_sb[:, :].rearrange("p (t d) -> p t d", d=D),
        H["x_d"].rearrange("(t p) d -> p t d", p=128)))

    ps_tok = [("T", 0), ("T", 0)]
    last_ev = None
    for t in range(TT):
        xt = x_sb[:, t * D:(t + 1) * D]
        ut = u_sb[:, t * D:(t + 1) * D]
        s1a = Sch.op("vector", lambda e, s, xt=xt: e.bn_stats(st6[:, 0:6], xt[:, 0:512]), waits=(ld,))
        s1b = Sch.op("vector", lambda e, s, xt=xt: e.bn_stats(st6[:, 6:12], xt[:, 512:1024]), waits=(ld,))
        s2 = Sch.op("vector", lambda e, s: e.bn_aggr(mv[:, :], st6[:, :]), waits=(s1a, s1b))
        eps_ap, eps_tok = Sch.const_col(1e-5)
        s3 = Sch.op("scalar", lambda e, s, eps_ap=eps_ap: e.activation(
            rstd[:, 0:1], mv[:, 1:2], AF.Sqrt, bias=eps_ap), waits=(s2, eps_tok))
        s4 = Sch.op("vector", lambda e, s: e.reciprocal(rstd[:, 1:2], rstd[:, 0:1]), waits=(s3,))
        s5 = Sch.op("vector", lambda e, s, xt=xt, ut=ut: e.tensor_scalar(
            ut, xt, mv[:, 0:1], rstd[:, 1:2], ALU.subtract, ALU.mult), waits=(s4,))
        s6 = Sch.op("vector", lambda e, s, ut=ut: e.tensor_copy(xnbf[:, :], ut),
                    waits=(s5, last_ev if last_ev else ("T", 0)))
        last_ev, ps_tok = transpose_tiles(
            Sch, H,
            lambda i: xnbf[:, i * 128:(i + 1) * 128],
            lambda i, t=t: xnT[:, i * GW + 1 + t * 128: i * GW + 1 + (t + 1) * 128],
            CC, psb, ps_tok, waits=(s6, z_tok))
    st1 = H["St"].dma(lambda e, s: e.dma_start(
        H["u0_d"].rearrange("(t p) d -> p t d", p=128),
        u_sb[:, :].rearrange("p (t d) -> p t d", d=D)),
        waits=(("V", Sch.count.get("V", 0)),))
    st2 = H["St"].dma(lambda e, s: e.dma_start(H["xnT_d"], xnT[:, :]),
                 waits=(("Sc", Sch.count.get("Sc", 0)), z_tok))
    return [st1, st2]



W0 = 256
NW = S // W0
NHD = 16
DI2 = 2730
NPAIR = 22


def emit_P2_sublayer(nc, Sch, H, li, first, last):
    RA = H["RA_sb"]; RB = H["RB_sb"]; RC = H["RC_sb"]; RD = H["RD_sb"]
    RE = H["RE_sb"]; RF = H["RF_sb"]; RG = H["RG_sb"]
    ps = H["ps_f"]; psb = [H["psb0"], H["psb1"]]
    ident = H["ident_sb"]
    masks = H["masks_sb"]; expb = H["expb_sb"]; expm = H["expm_sb"]
    sqb = H["sqb_sb"]
    nfsc = H["nfsc_sb"]; nffsc = H["nffsc_sb"]; nrmsc = H["nrmsc_sb"]
    gridB = H["gridB_sb"]
    negrow = H["negrow_sb"]; negbc = H["negbc_sb"]
    oun = H["oun_sb"]; recb = H["recb_sb"]
    rec64 = H["rec64_sb"]
    statrow = RG.bitcast(F32)[0:1, 0:2048]
    cols = H["cols_sb"]; b2row = H["b2row_sb"]
    onec = H["ones_col_sb"]; onerf = H["ones_rowf_sb"]; onerb = H["ones_row_sb"]
    ind16 = H["ind16_sb"]; sel16 = H["sel16_sb"]
    ut = [H["uta_sb"], H["utb_sb"]]
    ub = [H["uba_sb"], H["ubb_sb"]]
    wstr = [H["wsa_sb"], H["wsb_sb"]]

    def ubf_r(ccj):
        return (expb if ccj < 4 else expm)[:, (ccj % 4) * 128:(ccj % 4) * 128 + 128]
    qv = H["uba_sb"][:, 0:512]
    t1b = H["uta_sb"][:, 0:512]
    t2b = H["uta_sb"][:, 512:1024]
    u_in = H[f"u_in_{li}"]; u_mid = H[f"u_mid_{li}"]; u_out = H[f"u_out_{li}"]
    qT_d = H["qT_scr_d"]; kT_d = H["kT_scr_d"]; v_d = H["v_scr_d"]
    bank_war = H.setdefault("_p2_bank_war", {i: ("Sc", 0) for i in range(6)})
    psb_war = H.setdefault("_p2_psb_war", [("Sc", 0), ("Sc", 0)])
    uTg_rdy = H["_p2_uTg_rdy"]

    def uTg_r(cc, rng):
        off = 0 if cc >= 4 else 1
        return RA[:, cc * GW + off + rng.start: cc * GW + off + rng.stop]

    qwin = [RC[:, 0:2048], RC[:, 10240:12288]]
    kwin = [RC[:, 2048:6144], RC[:, 12288:16384]]
    vwin = [RC[:, 6144:10240], RD[:, 12288:16384]]
    stage = RD[:, 0:4096]
    vstage = [RD[:, 4096:4608], RD[:, 4608:5120]]

    ld_wo = H["Wt"].dma(lambda e, s: e.dma_start(
        RF[:, 0:CC * 1024].rearrange("p (c n) -> p c n", n=1024),
        H["Wo6_d"][li].rearrange("(c p) n -> p c n", p=128)),
        waits=(H.get("_p2_rf_war", ("T", 0)),))
    ld_cols = H["LdIn"].dma(lambda e, s: e.dma_start(
        cols[:, 0:62], H["cols_d"][li]), waits=(H.get("_p2_cols_war", ("V", 0)),))
    ld_b2 = H["LdIn"].dma(lambda e, s: e.dma_start(
        b2row[0:1, :], H["b2_d"][li]), waits=(H.get("_p2_b2_war", ("T", 0)),))
    qs_col = cols[:, 0:1]; ks_col = cols[:, 1:2]
    lng = cols[:, 2:10]; lnb = cols[:, 10:18]; b1c = cols[:, 18:62]
    CSQ = wstr[0][:, 1024:1536]; SNQ = wstr[0][:, 1536:2048]
    CSK = wstr[1][:, 1024:1536]; SNK = wstr[1][:, 1536:2048]

    zgrid = Sch.op("vector", lambda e, s: e.memset(gridB[0:16, :], 0.0),
                   waits=(H.get("_p2_grid_war", ("V", 0)),))
    H["_p2_grid_war"] = zgrid

    sq_war = H.get("_p2_sq_war", ("T", 0))
    qv_war = H.get("_p2_qv_war", ("V", 0))
    v_st = {}
    q_st = {}
    k_st = {}
    for j in range(4):
        tbl_toks = []
        for ti, dst in enumerate((CSQ, SNQ, CSK, SNK)):
            tbl_toks.append(H["LdIn"].dma(lambda e, s, ti=ti, dst=dst, j=j: e.dma_start(
                dst, H["tbl_d"][:, ti, j * 512:(j + 1) * 512]),
                waits=(H.get(f"_p2_tbl_war_{ti}", ("V", 0)),)))
        for h2 in range(2):
            ldv = H["Wt"].dma(lambda e, s, h2=h2: e.dma_start(
                RG[:, 0:4096].rearrange("p (c n) -> p c n", n=512),
                H["Wqkv_d"][li].rearrange("(c p) n -> p c n", p=128)[:, :, 2048 + h2 * 512: 2048 + (h2 + 1) * 512]),
                waits=(("T", Sch.count.get("T", 0)),))
            for tt in range(4):
                t_abs = 4 * j + tt
                b = tt % 2
                vb = (t_abs * 2 + h2) % 2
                mmx = None
                for ccx in range(CC):
                    mmx = Sch.op("tensor", lambda e, s, t_abs=t_abs, ccx=ccx, b=b: e.matmul(
                        ps[b][:, :], uTg_r(ccx, slice(t_abs * 128, (t_abs + 1) * 128)),
                        RG[:, ccx * 512:(ccx + 1) * 512],
                        start=(ccx == 0), stop=(ccx == CC - 1)),
                        waits=(ldv, uTg_rdy, bank_war[b]), inc=(ccx == CC - 1))
                ev = Sch.op("scalar", lambda e, s, vb=vb, b=b: e.copy(
                    vstage[vb][:, :], ps[b][:, :]),
                    waits=(mmx, v_st.get((t_abs - 1, h2), ("Sc", 0)),
                           H.get("_p2_vstage_war_%d" % vb, ("V", 0))))
                bank_war[b] = ev
                stv = H["St"].dma(lambda e, s, t_abs=t_abs, h2=h2, vb=vb: e.dma_start(
                    v_d[:, t_abs * 1024 + h2 * 512: t_abs * 1024 + (h2 + 1) * 512],
                    vstage[vb][:, :]), waits=(ev,))
                H["_p2_vstage_war_%d" % vb] = stv
                v_st[(t_abs, h2)] = stv
        for which, wq, scr_d, st_map in (("q", 0, qT_d, q_st), ("k", 1024, kT_d, k_st)):
            raw_mm = None
            for e_ in range(CC):
                b = e_ % 2
                ldw = H["Wt"].dma(lambda e, s, e_=e_, wq=wq: e.dma_start(
                    wstr[e_ % 2][:, 0:1024].rearrange("p (c m) -> p c m", m=128),
                    H["Wqkv_d"][li].rearrange("(c p) n -> p c n", p=128)[:, :, wq + e_ * 128: wq + (e_ + 1) * 128]),
                    waits=(H.get(f"_p2_wstr_war_{e_ % 2}", ("T", 0)),))
                mmx = None
                for ccx in range(CC):
                    mmx = Sch.op("tensor", lambda e, s, e_=e_, ccx=ccx, b=b, j=j: e.matmul(
                        ps[b][:, :], wstr[e_ % 2][:, ccx * 128:(ccx + 1) * 128],
                        uTg_r(ccx, slice(j * 512, (j + 1) * 512)),
                        start=(ccx == 0), stop=(ccx == CC - 1)),
                        waits=(ldw, uTg_rdy, bank_war[b]), inc=(ccx == CC - 1))
                H[f"_p2_wstr_war_{e_ % 2}"] = mmx
                ev = Sch.op("scalar", lambda e, s, e_=e_, b=b: e.copy(
                    stage[:, e_ * 512:(e_ + 1) * 512], ps[b][:, :]),
                    waits=(mmx, H.get(f"_p2_stgw_{e_}", ("V", 0))))
                sq1 = Sch.op("scalar", lambda e, s, b=b: e.activation(
                    sqb[:, :], ps[b][:, :], AF.Square), waits=(mmx, sq_war))
                raw_mm = Sch.op("tensor", lambda e, s, e_=e_: e.matmul(
                    ps[4][0:16, 0:512], ind16[:, e_ * 16:(e_ + 1) * 16], sqb[:, :],
                    start=(e_ == 0), stop=(e_ == CC - 1), skip_group_check=True),
                    waits=(sq1,) + ((bank_war[4],) if e_ == 0 else ()))
                sq_war = raw_mm
                bank_war[b] = sq1
            raw_done = ("Sc", Sch.count.get("Sc", 0))
            ev1 = Sch.op("scalar", lambda e, s: e.copy(nfsc[0:16, 0:512], ps[4][0:16, 0:512]),
                         waits=(raw_mm, H.get("_p2_nfsc_war", ("V", 0))))
            bank_war[4] = ev1
            sr = Sch.op("scalar", lambda e, s: e.activation(
                nffsc[0:16, 0:512], nfsc[0:16, 0:512], AF.Sqrt),
                waits=(ev1, H.get("_p2_nffsc_war", ("V", 0))))
            mx = Sch.op("vector", lambda e, s: e.tensor_scalar_max(
                nfsc[0:16, 0:512], nffsc[0:16, 0:512], 1e-12), waits=(sr,))
            H["_p2_nffsc_war"] = mx

            def rec_fn(e, s):
                with nc.allow_low_precision(reason="l2norm scale bf16 operand"):
                    return e.reciprocal(nrmsc[0:16, 0:512], nfsc[0:16, 0:512])
            rc_ = Sch.op("vector", rec_fn, waits=(mx, H.get("_p2_nrmsc_war", ("Sc", 0))))
            H["_p2_nfsc_war"] = rc_
            fin_mm = None
            for e_ in range(CC):
                tile_ap = stage[:, e_ * 512:(e_ + 1) * 512]
                col_sc = qs_col if which == "q" else ks_col
                m1 = Sch.op("vector", lambda e, s, tile_ap=tile_ap, col_sc=col_sc: e.tensor_scalar_mul(
                    qv, tile_ap, col_sc),
                    waits=(ld_cols, qv_war, raw_done, H.get("_p2_qv_sc_war", ("Sc", 0))))
                r1 = Sch.op("vector", lambda e, s, which=which: e.tensor_mul(
                    t1b, qv, CSQ if which == "q" else CSK), waits=(m1,) + tuple(tbl_toks))
                s2m = Sch.op("vector", lambda e, s, which=which: e.tensor_mul(
                    H["ubb_sb"][:, 0:512], qv,
                    wstr[0 if which == "q" else 1][:, 1536:2048]),
                    waits=(m1,) + tuple(tbl_toks) + (H.get("_p2_ubb_war", ("V", 0)),))
                rts = []
                for qq in range(4):
                    pb_ = qq * 32
                    sp_ = pb_ + 32 if (pb_ % 64) < 32 else pb_ - 32
                    rts.append(Sch.op("vector", lambda e, s, pb_=pb_, sp_=sp_: e.tensor_copy(
                        H["uta_sb"][pb_:pb_ + 32, 512:1024],
                        H["ubb_sb"][sp_:sp_ + 32, 0:512]), waits=(s2m,)))
                H["_p2_ubb_war"] = rts[-1]
                ad = Sch.op("vector", lambda e, s: e.tensor_add(t2b, t1b, t2b),
                            waits=tuple(rts) + (r1,))
                bc = Sch.op("tensor", lambda e, s, e_=e_: e.matmul(
                    ps[5][:, 0:512], sel16[:, e_ * 128:(e_ + 1) * 128], nrmsc[0:16, 0:512],
                    start=True, stop=True), waits=(rc_, bank_war[5]))
                fq = Sch.op("vector", lambda e, s, tile_ap=tile_ap: e.tensor_mul(
                    tile_ap, t2b, ps[5][:, 0:512]), waits=(ad, bc))
                qv_war = fq
                bank_war[5] = fq
                stq = H["St"].dma(lambda e, s, e_=e_, j=j, scr_d=scr_d: e.dma_start(
                    scr_d[:, e_ * S + j * 512: e_ * S + (j + 1) * 512],
                    stage[:, e_ * 512:(e_ + 1) * 512]), waits=(fq,))
                st_map[(j, e_)] = stq
                H[f"_p2_stgw_{e_}"] = stq
                sq2 = Sch.op("scalar", lambda e, s: e.activation(
                    sqb[:, :], qv, AF.Square), waits=(fq, sq_war))
                H["_p2_qv_sc_war"] = sq2
                fin_mm = Sch.op("tensor", lambda e, s, e_=e_: e.matmul(
                    ps[4][32:48, 0:512], ind16[:, e_ * 16:(e_ + 1) * 16], sqb[:, :],
                    start=(e_ == 0), stop=(e_ == CC - 1), skip_group_check=True),
                    waits=(sq2,) + ((bank_war[4],) if e_ == 0 else ()))
                sq_war = fin_mm
            ev2 = Sch.op("scalar", lambda e, s: e.copy(nffsc[0:16, 0:512], ps[4][32:48, 0:512]),
                         waits=(fin_mm, H["_p2_nffsc_war"]))
            bank_war[4] = ev2
            m2a = Sch.op("vector", lambda e, s: e.tensor_mul(
                nffsc[0:16, 0:512], nffsc[0:16, 0:512], nrmsc[0:16, 0:512]),
                waits=(ev2, rc_))
            m2b = Sch.op("vector", lambda e, s: e.tensor_mul(
                nffsc[0:16, 0:512], nffsc[0:16, 0:512], nrmsc[0:16, 0:512]),
                waits=(m2a,))
            gcol = (0 if which == "q" else 8) + 2 * j
            gm = Sch.op("vector", lambda e, s, gcol=gcol: e.tensor_reduce(
                gridB[0:16, gcol:gcol + 2],
                nffsc[0:16, 0:512].rearrange("p (w c) -> p w c", c=W0),
                AX.X, ALU.max), waits=(m2b, H["_p2_grid_war"]))
            H["_p2_nffsc_war"] = gm
            H["_p2_grid_war"] = gm
        for ti in range(4):
            H[f"_p2_tbl_war_{ti}"] = ("V", Sch.count.get("V", 0))
    H["_p2_nrmsc_war"] = ("T", Sch.count.get("T", 0))

    gw_toks = (H["_p2_grid_war"],)
    km = []
    for w in range(NW):
        if w == 0:
            km.append(Sch.op("vector", lambda e, s: e.tensor_copy(
                gridB[0:16, 16:17], gridB[0:16, 8:9]), waits=gw_toks))
        else:
            km.append(Sch.op("vector", lambda e, s, w=w: e.tensor_max(
                gridB[0:16, 16 + w:17 + w], gridB[0:16, 7 + w:8 + w],
                gridB[0:16, 8 + w:9 + w]), waits=gw_toks + (km[-1],)))
    b2_ = Sch.op("vector", lambda e, s: e.tensor_mul(
        gridB[0:16, 24:32], gridB[0:16, 0:8], gridB[0:16, 16:24]), waits=(km[-1],))
    b3_ = Sch.op("scalar", lambda e, s: e.activation(
        gridB[0:16, 32:40], gridB[0:16, 24:32], AF.Sqrt), waits=(b2_,))
    b4_ = Sch.op("vector", lambda e, s: e.tensor_scalar_mul(
        gridB[0:16, 40:48], gridB[0:16, 32:40], -8.0), waits=(b3_,))
    H["_p2_grid_war"] = b4_
    st_g = H["St"].dma(lambda e, s: e.dma_start(H["bgrid_d"], gridB[0:16, 40:48]),
                       waits=(b4_,))
    ld_g = H["Wt"].dma(lambda e, s: e.dma_start(negrow[0:1, 0:128], H["bgrid_flat_d"]),
                       waits=(st_g, H.get("_p2_negrow_war", ("T", 0))))
    bcg = Sch.op("tensor", lambda e, s: e.matmul(
        ps[5][:, 0:128], onerb[0:1, :], negrow[0:1, 0:128], start=True, stop=True),
        waits=(ld_g, H["_onerb_tok"], bank_war[5]))
    H["_p2_negrow_war"] = bcg
    nbc = Sch.op("scalar", lambda e, s: e.copy(negbc[:, :], ps[5][:, 0:128]),
                 waits=(bcg, H.get("_p2_negbc_war", ("Sc", 0))))
    bank_war[5] = nbc

    eb_war = [("G", 0)] * 4
    ebm_war = [("T", 0)] * 4
    ob_war = {2: bank_war[2], 3: bank_war[3]}
    den_war = ("V", 0)
    win_ld = {}
    win_war = {0: ("T", 0), 1: ("T", 0)}

    def load_window(w):
        i = w % 2
        kt0 = max(0, w * 2 - 2)
        nkt = (w + 1) * 2 - kt0
        wq = H["LdA"].dma(lambda e, s, w=w, i=i: e.dma_start(
            qwin[i][:, :].rearrange("p (c m) -> p c m", m=W0),
            qT_d.rearrange("p (c m) -> p c m", m=S)[:, :, w * W0:(w + 1) * W0]),
            waits=(win_war[i],) + tuple(q_st[(w // 2, e_)] for e_ in range(CC)))
        wk = H["LdB"].dma(lambda e, s, w=w, i=i, kt0=kt0, nkt=nkt: e.dma_start(
            kwin[i][:, 0:CC * nkt * 128].rearrange("p (c m) -> p c m", m=nkt * 128),
            kT_d.rearrange("p (c m) -> p c m", m=S)[:, :, kt0 * 128: kt0 * 128 + nkt * 128]),
            waits=(win_war[i],) + tuple(k_st[(jj, e_)] for jj in range(max(0, w // 2 - 1), w // 2 + 1) for e_ in range(CC)))
        wv = H["LdIn"].dma(lambda e, s, w=w, i=i, kt0=kt0, nkt=nkt: e.dma_start(
            vwin[i][:, 0:nkt * 1024],
            v_d[:, kt0 * 1024: kt0 * 1024 + nkt * 1024]),
            waits=(win_war[i],) + tuple(v_st[(tabs, h2)] for tabs in range(kt0, kt0 + nkt) for h2 in range(2)))
        win_ld[w] = (wq, wk, wv, kt0, nkt)

    scrm4 = [H["uba_sb"][:, k * W0:(k + 1) * W0] for k in range(4)]

    def exp_slot(k):
        return (expb if k < 2 else expm)[:, (k % 2) * W0:(k % 2 + 1) * W0]
    load_window(0)
    load_window(1)
    outT_rdy = None
    ebx_ctr = 0
    for w in range(NW):
        i = w % 2
        wq, wk, wv, kt0, nkt = win_ld[w]
        for h in range(NHD):
            cc_t = h // 2
            prow = (h % 2) * 64
            ob = 2 + w % 2
            av_done = None
            den_last = None
            first_kt = True
            for kn in range(nkt):
                ktile = kt0 + kn
                sb_ = kn % 2
                eb = ebx_ctr % 4
                ebx_ctr += 1
                kt_rel = ktile - (w * 2 - 2)
                smm = Sch.op("tensor", lambda e, s, kn=kn, sb_=sb_, cc_t=cc_t, prow=prow, i=i, nkt=nkt: e.matmul(
                    ps[sb_][:, 0:W0],
                    kwin[i][prow:prow + 64, cc_t * nkt * 128 + kn * 128: cc_t * nkt * 128 + (kn + 1) * 128],
                    qwin[i][prow:prow + 64, cc_t * W0:(cc_t + 1) * W0],
                    start=True, stop=True),
                    waits=(wq, wk, bank_war[sb_]))
                ma = Sch.op("vector", lambda e, s, sb_=sb_, kt_rel=kt_rel, eb=eb: e.tensor_add(
                    scrm4[eb], ps[sb_][:, 0:W0], masks[:, kt_rel * W0:(kt_rel + 1) * W0]),
                    waits=(smm, H["_masks_tok"], eb_war[eb]))
                bank_war[sb_] = ma
                ex = Sch.op("scalar", lambda e, s, h=h, w=w, eb=eb: e.activation(
                    exp_slot(eb), scrm4[eb], AF.Exp,
                    bias=negbc[:, h * 8 + w: h * 8 + w + 1], scale=8.0),
                    waits=(ma, nbc, ebm_war[eb]))
                eb_war[eb] = ex
                av_done = Sch.op("tensor", lambda e, s, h=h, kn=kn, ob=ob, first_kt=first_kt, nkt=nkt, eb=eb, i=i: e.matmul(
                    ps[ob][0:64, 0:W0],
                    vwin[i][:, kn * 1024 + h * 64: kn * 1024 + (h + 1) * 64],
                    exp_slot(eb),
                    start=first_kt, stop=(kn == nkt - 1)),
                    waits=(ex, wv) + ((ob_war[ob],) if first_kt else ()))
                den_last = Sch.op("tensor", lambda e, s, kn=kn, first_kt=first_kt, nkt=nkt, eb=eb: e.matmul(
                    ps[4][0:1, 0:W0], onec[:, 0:1],
                    exp_slot(eb),
                    start=first_kt, stop=(kn == nkt - 1), skip_group_check=True),
                    waits=(ex, H["_onec_tok"]) + ((den_war, bank_war[4]) if first_kt else ()),
                    inc=(kn == nkt - 1))
                ebm_war[eb] = den_last
                first_kt = False
            lnd = Sch.op("scalar", lambda e, s: e.activation(
                rec64[0:1, 0:W0], ps[4][0:1, 0:W0], AF.Ln),
                waits=(den_last, H.get("_p2_lnb_war", ("T", 0))))
            den_war = lnd
            bank_war[4] = lnd
            rex = Sch.op("scalar", lambda e, s: e.activation(
                recb[0:1, 0:W0], rec64[0:1, 0:W0], AF.Exp, scale=-1.0),
                waits=(lnd, H.get("_p2_recb_war", ("T", 0))))
            H["_p2_lnb_war"] = rex
            bcd = Sch.op("tensor", lambda e, s: e.matmul(
                ps[5][0:64, 0:W0], onerb[0:1, 0:64], recb[0:1, 0:W0],
                start=True, stop=True),
                waits=(rex, H["_onerb_tok"], bank_war[5]))
            H["_p2_recb_war"] = bcd
            evo = Sch.op("scalar", lambda e, s, ob=ob: e.copy(oun[:, 0:W0], ps[ob][0:64, 0:W0]),
                         waits=(av_done, H.get("_p2_oun_war", ("V", 0))))
            mlo = Sch.op("vector", lambda e, s, w=w, cc_t=cc_t, prow=prow: e.tensor_mul(
                RA[prow:prow + 64, cc_t * GW + 1 + w * W0: cc_t * GW + 1 + (w + 1) * W0],
                oun[:, 0:W0], ps[5][0:64, 0:W0]),
                waits=(bcd, evo, H.get("_p2_outT_war", ("T", 0))))
            bank_war[5] = mlo
            H["_p2_oun_war"] = mlo
            ob_war[ob] = mlo
            outT_rdy = mlo
        win_war[i] = ("T", Sch.count.get("T", 0))
        if w + 2 < NW:
            load_window(w + 2)
    zg = []
    for cc in range(CC):
        zg.append(Sch.op("vector", lambda e, s, cc=cc: e.memset(
            RB[:, cc * GW: cc * GW + 1], 0.0), waits=(H.get("_p2_rb_war", ("T", 0)),)))
    ld_u = {}
    fin_v = {}
    fin_st = {}

    def u_tile_load(t):
        a = t % 2
        wts = (fin_v[t - 2], fin_st[t - 2]) if t >= 2 else (qv_war, ("V", Sch.count.get("V", 0)))
        ld_u[t] = H["LdA"].dma(lambda e, s, t=t, a=a: e.dma_start(
            ut[a][:, :], u_in[t * 128:(t + 1) * 128, :]), waits=wts)

    u_tile_load(0)
    u_tile_load(1)
    wo_guard = outT_rdy
    tr_ps = [psb_war[0], psb_war[1]]
    for t in range(TT):
        a = t % 2
        for n2 in range(2):
            b = (t * 2 + n2) % 2
            mmx = None
            for vd in range(CC):
                mmx = Sch.op("tensor", lambda e, s, t=t, n2=n2, vd=vd, b=b: e.matmul(
                    ps[b][:, :], RA[:, vd * GW + 1 + t * 128: vd * GW + 1 + (t + 1) * 128],
                    RF[:, vd * 1024 + n2 * 512: vd * 1024 + (n2 + 1) * 512],
                    start=(vd == 0), stop=(vd == CC - 1)),
                    waits=(ld_wo, wo_guard, bank_war[b]), inc=(vd == CC - 1))
            c2 = Sch.op("vector", lambda e, s, n2=n2, a=a, b=b: e.tensor_add(
                ub[a][:, n2 * 512:(n2 + 1) * 512], ut[a][:, n2 * 512:(n2 + 1) * 512], ps[b][:, :]),
                waits=(mmx, ld_u[t], fin_v.get(t - 2, ("V", 0))))
            bank_war[b] = c2
            fin_v[t] = c2
        st = H["St"].dma(lambda e, s, t=t, a=a: e.dma_start(
            u_mid[t * 128:(t + 1) * 128, :], ub[a][:, :]), waits=(fin_v[t],))
        fin_st[t] = st
        cb1 = Sch.op("vector", lambda e, s, a=a: e.tensor_copy(expb[:, :], ub[a][:, 0:512]),
                     waits=(fin_v[t], ("Sc", Sch.count.get("Sc", 0)) if t == 0 else ("V", 0),
                            ("G", Sch.count.get("G", 0)) if t == 0 else ("V", 0)))
        cb = Sch.op("vector", lambda e, s, a=a: e.tensor_copy(expm[:, :], ub[a][:, 512:1024]),
                    waits=(cb1,))
        fin_v[t] = cb
        for ccj in range(CC):
            ebx = ccj % 2
            trk = Sch.op("tensor", lambda e, s, ccj=ccj, ebx=ebx: e.transpose(
                psb[ebx], ubf_r(ccj), ident[:, :]),
                waits=(cb, tr_ps[ebx], H["_ident_tok"]))
            ev = Sch.op("scalar", lambda e, s, t=t, ccj=ccj, ebx=ebx: e.copy(
                RB[:, ccj * GW + 1 + t * 128: ccj * GW + 1 + (t + 1) * 128], psb[ebx]),
                waits=(trk,) + tuple(zg))
            tr_ps[ebx] = ev
        if t + 2 < TT:
            u_tile_load(t + 2)
    psb_war[0], psb_war[1] = tr_ps
    xs2T_rdy = ("Sc", Sch.count.get("Sc", 0))

    def xs2_r(cc, rng):
        off = 0 if cc >= 4 else 1
        return RB[:, cc * GW + off + rng.start: cc * GW + off + rng.stop]

    eps_ap, eps_tok = Sch.const_col(1e-5)
    xn2_done = {}
    qv_war2 = ("V", Sch.count.get("V", 0))
    rd_war = ("T", Sch.count.get("T", 0))
    sq_war2 = sq_war
    for n in range(4):
        mm_mu = None
        mm_ss = None
        for cc in range(CC):
            mm_mu = Sch.op("tensor", lambda e, s, cc=cc, n=n: e.matmul(
                ps[4][0:1, 0:512], onec[:, 0:1],
                xs2_r(cc, slice(n * 512, (n + 1) * 512)),
                start=(cc == 0), stop=(cc == CC - 1)),
                waits=(xs2T_rdy, H["_onec_tok"]) + ((bank_war[4],) if cc == 0 else ()),
                inc=(cc == CC - 1))
            sq1 = Sch.op("scalar", lambda e, s, cc=cc, n=n: e.activation(
                sqb[:, :], xs2_r(cc, slice(n * 512, (n + 1) * 512)), AF.Square),
                waits=(xs2T_rdy, sq_war2))
            mm_ss = Sch.op("tensor", lambda e, s, cc=cc, n=n: e.matmul(
                ps[5][0:1, 0:512], onec[:, 0:1], sqb[:, :],
                start=(cc == 0), stop=(cc == CC - 1)),
                waits=(sq1,) + ((bank_war[5],) if cc == 0 else ()))
            sq_war2 = mm_ss
        e1 = Sch.op("scalar", lambda e, s: e.copy(statrow[0:1, 0:512], ps[4][0:1, 0:512]),
                    waits=(mm_mu, H.get("_p2_statrow_war", ("V", 0))))
        e2 = Sch.op("scalar", lambda e, s: e.copy(statrow[0:1, 512:1024], ps[5][0:1, 0:512]),
                    waits=(mm_ss,))
        bank_war[4] = e1
        bank_war[5] = e2
        t1_ = Sch.op("vector", lambda e, s: e.tensor_scalar_mul(
            statrow[0:1, 1024:1536], statrow[0:1, 0:512], 1.0 / D), waits=(e1, e2))
        t2_ = Sch.op("vector", lambda e, s: e.tensor_mul(
            statrow[0:1, 0:512], statrow[0:1, 1024:1536], statrow[0:1, 1024:1536]),
            waits=(t1_,))
        t3a = Sch.op("vector", lambda e, s: e.tensor_scalar_mul(
            statrow[0:1, 1536:2048], statrow[0:1, 512:1024], 1.0 / D), waits=(t2_,))
        t3_ = Sch.op("vector", lambda e, s: e.tensor_sub(
            statrow[0:1, 512:1024], statrow[0:1, 1536:2048], statrow[0:1, 0:512]),
            waits=(t3a,))
        t4_ = Sch.op("scalar", lambda e, s: e.activation(
            statrow[0:1, 1536:2048], statrow[0:1, 512:1024], AF.Sqrt,
            bias=eps_ap[0:1, 0:1]), waits=(t3_, eps_tok))
        t5_ = Sch.op("vector", lambda e, s: e.reciprocal(
            statrow[0:1, 0:512], statrow[0:1, 1536:2048]), waits=(t4_,))
        bmu = Sch.op("tensor", lambda e, s: e.matmul(
            ps[4][:, 0:512], onerf[0:1, :], statrow[0:1, 1024:1536],
            start=True, stop=True), waits=(t5_, H["_onerf_tok"], bank_war[4]))
        brs = Sch.op("tensor", lambda e, s: e.matmul(
            ps[5][:, 0:512], onerf[0:1, :], statrow[0:1, 0:512],
            start=True, stop=True), waits=(t5_, bank_war[5]))
        H["_p2_statrow_war"] = brs
        for cc in range(CC):
            v1x = Sch.op("vector", lambda e, s, cc=cc, n=n: e.tensor_sub(
                t1b, xs2_r(cc, slice(n * 512, (n + 1) * 512)), ps[4][:, 0:512]),
                waits=(bmu, qv_war2))
            v2x = Sch.op("vector", lambda e, s: e.tensor_mul(
                t2b, t1b, ps[5][:, 0:512]), waits=(brs, v1x))
            v3x = Sch.op("vector", lambda e, s, cc=cc, n=n: e.tensor_scalar(
                RD[:, cc * 2048 + n * 512: cc * 2048 + (n + 1) * 512],
                t2b, lng[:, cc:cc + 1], lnb[:, cc:cc + 1], ALU.mult, ALU.add),
                waits=(v2x, ld_cols, rd_war if (n == 0 and cc == 0) else ("V", 0)))
            qv_war2 = v3x
            xn2_done[(n, cc)] = v3x
        bank_war[4] = xn2_done[(n, CC - 1)]
        bank_war[5] = xn2_done[(n, CC - 1)]
    xn2_rdy = ("V", Sch.count.get("V", 0))

    def agt(j2):
        if j2 < 8:
            return RC[:, j2 * 2048: (j2 + 1) * 2048]
        if j2 < 16:
            return RB[:, (j2 - 8) * 2048: (j2 - 7) * 2048]
        if j2 < 20:
            return RF[:, (j2 - 16) * 2048: (j2 - 15) * 2048]
        return H["AG2_sb"][:, (j2 - 20) * 2048: (j2 - 19) * 2048]

    agt_done = {}
    wstr_war2 = [("T", Sch.count.get("T", 0)), ("T", Sch.count.get("T", 0))]
    gb_war = ("V", Sch.count.get("V", 0))
    agt_guard = ("T", Sch.count.get("T", 0))
    agt_sc_guard = ("Sc", Sch.count.get("Sc", 0))
    col_off = 0
    for j2 in range(NPAIR):
        szj = 128 if j2 < 21 else 42
        wbuf = j2 % 2
        ldw1 = H["Wt"].dma(lambda e, s, wbuf=wbuf, col_off=col_off, szj=szj: e.dma_start(
            wstr[wbuf][:, 0:2 * szj * CC].rearrange("p (c m) -> p c m", m=2 * szj),
            H["W1r_d"][li].rearrange("(c p) m -> p c m", p=128)[:, :, col_off: col_off + 2 * szj]),
            waits=(wstr_war2[wbuf],))
        for n in range(4):
            ba = 4 if n % 2 == 0 else 0
            bg = 5 if n % 2 == 0 else 1
            mma = None
            mmg = None
            for cc in range(CC):
                mma = Sch.op("tensor", lambda e, s, n=n, cc=cc, ba=ba, szj=szj, wbuf=wbuf: e.matmul(
                    ps[ba][0:szj, 0:512], wstr[wbuf][:, cc * 2 * szj: cc * 2 * szj + szj],
                    RD[:, cc * 2048 + n * 512: cc * 2048 + (n + 1) * 512],
                    start=(cc == 0), stop=(cc == CC - 1)),
                    waits=(ldw1, xn2_rdy) + ((bank_war[ba],) if cc == 0 else ()),
                    inc=(cc == CC - 1))
                mmg = Sch.op("tensor", lambda e, s, n=n, cc=cc, bg=bg, szj=szj, wbuf=wbuf: e.matmul(
                    ps[bg][0:szj, 0:512], wstr[wbuf][:, cc * 2 * szj + szj: cc * 2 * szj + 2 * szj],
                    RD[:, cc * 2048 + n * 512: cc * 2048 + (n + 1) * 512],
                    start=(cc == 0), stop=(cc == CC - 1)),
                    waits=((bank_war[bg],) if cc == 0 else ()), inc=(cc == CC - 1))
            gl = Sch.op("scalar", lambda e, s, j2=j2, bg=bg, szj=szj: e.activation(
                sqb[0:szj, 0:512], ps[bg][0:szj, 0:512], AF.Gelu,
                bias=b1c[0:szj, 2 * j2 + 1: 2 * j2 + 2]),
                waits=(mmg, gb_war, ld_cols, agt_sc_guard))
            agv = Sch.op("vector", lambda e, s, j2=j2, n=n, ba=ba, szj=szj: e.scalar_tensor_tensor(
                agt(j2)[0:szj, n * 512:(n + 1) * 512], ps[ba][0:szj, 0:512],
                b1c[0:szj, 2 * j2: 2 * j2 + 1], sqb[0:szj, 0:512], ALU.add, ALU.mult),
                waits=(gl, mma, agt_guard))
            gb_war = agv
            bank_war[ba] = agv
            bank_war[bg] = agv
            agt_done[(j2, n)] = agv
        wstr_war2[wbuf] = ("T", Sch.count.get("T", 0))
        col_off += 2 * szj

    def w2t(j2):
        if j2 < 16:
            return RD[:, j2 * 1024: (j2 + 1) * 1024]
        if j2 < 20:
            return RG[:, (j2 - 16) * 1024: (j2 - 15) * 1024]
        return RE[:, (j2 - 20) * 1024: (j2 - 19) * 1024]

    w2_guard = ("T", Sch.count.get("T", 0))
    ld_w2a = H["Wt"].dma(lambda e, s: e.dma_start(
        RD[:, 0:16 * 1024].rearrange("p (j m) -> p j m", m=1024),
        H["W2_d"][li][0:2048, :].rearrange("(j p) m -> p j m", p=128)),
        waits=(w2_guard,))
    ld_w2b = H["Wt"].dma(lambda e, s: e.dma_start(
        RG[:, 0:4096].rearrange("p (j m) -> p j m", m=1024),
        H["W2_d"][li][2048:2560, :].rearrange("(j p) m -> p j m", p=128)),
        waits=(w2_guard,))
    ld_w2c = H["Wt"].dma(lambda e, s: e.dma_start(
        RE[:, 0:1024], H["W2_d"][li][2560:2688, :]), waits=(w2_guard,))
    ld_w2d = H["Wt"].dma(lambda e, s: e.dma_start(
        RE[0:42, 1024:2048], H["W2_d"][li][2688:2730, :]), waits=(w2_guard,))
    w2_lds = (ld_w2a, ld_w2b, ld_w2c, ld_w2d)
    ld_u2 = {}
    fin2_v = {}
    fin2_st = {}

    def u2_tile_load(t):
        a = t % 2
        wts = (fin2_v[t - 2], fin2_st[t - 2]) if t >= 2 else (
            fin_st[TT - 2 + t], ("V", Sch.count.get("V", 0)))
        ld_u2[t] = H["LdB"].dma(lambda e, s, t=t, a=a: e.dma_start(
            ut[a][:, :], u_mid[t * 128:(t + 1) * 128, :]), waits=wts)

    u2_tile_load(0)
    u2_tile_load(1)
    zg1 = []
    for cc in range(CC):
        zg1.append(Sch.op("vector", lambda e, s, cc=cc: e.memset(
            RA[:, cc * GW: cc * GW + 1], 0.0), waits=(("T", Sch.count.get("T", 0)),)))
    tr2_ps = [psb_war[0], psb_war[1]]
    for t in range(TT):
        a = t % 2
        for n2 in range(2):
            b = (t * 2 + n2) % 2
            mmx = None
            for j2 in range(NPAIR):
                szj = 128 if j2 < 21 else 42
                mmx = Sch.op("tensor", lambda e, s, t=t, n2=n2, j2=j2, b=b, szj=szj: e.matmul(
                    ps[b][:, :], agt(j2)[0:szj, t * 128:(t + 1) * 128],
                    w2t(j2)[0:szj, n2 * 512:(n2 + 1) * 512],
                    start=(j2 == 0), stop=False, skip_group_check=True),
                    waits=w2_lds + (agt_done[(j2, t // 4)],) + ((bank_war[b],) if j2 == 0 else ()),
                    inc=False)
            mmb = Sch.op("tensor", lambda e, s, n2=n2, b=b: e.matmul(
                ps[b][:, :], onerb[0:1, 0:128], b2row[0:1, n2 * 512:(n2 + 1) * 512],
                start=False, stop=True, skip_group_check=True),
                waits=(ld_b2, H["_onerb_tok"]), inc=True)
            c2 = Sch.op("vector", lambda e, s, n2=n2, a=a, b=b: e.tensor_add(
                ub[a][:, n2 * 512:(n2 + 1) * 512], ut[a][:, n2 * 512:(n2 + 1) * 512],
                ps[b][:, :]), waits=(mmb, ld_u2[t], fin2_v.get(t - 2, ("V", 0))))
            bank_war[b] = c2
            fin2_v[t] = c2
        if last:
            sl = Sch.op("scalar", lambda e, s, a=a: e.activation(
                ut[a][:, 0:512], ub[a][:, 0:512], AF.Silu), waits=(fin2_v[t],))
            sl2 = Sch.op("scalar", lambda e, s, a=a: e.activation(
                ut[a][:, 512:1024], ub[a][:, 512:1024], AF.Silu), waits=(sl,))
            st = H["St"].dma(lambda e, s, t=t, a=a: e.dma_start(
                u_out[t * 128:(t + 1) * 128, :], ut[a][:, :]), waits=(sl2,))
            fin2_v[t] = sl2
        else:
            st = H["St"].dma(lambda e, s, t=t, a=a: e.dma_start(
                u_out[t * 128:(t + 1) * 128, :], ub[a][:, :]), waits=(fin2_v[t],))
        fin2_st[t] = st
        if not last:
            cb1 = Sch.op("vector", lambda e, s, a=a: e.tensor_copy(expb[:, :], ub[a][:, 0:512]),
                         waits=(fin2_v[t], ("Sc", Sch.count.get("Sc", 0)) if t == 0 else ("V", 0)))
            cb = Sch.op("vector", lambda e, s, a=a: e.tensor_copy(expm[:, :], ub[a][:, 512:1024]),
                        waits=(cb1,))
            fin2_v[t] = cb
            for ccj in range(CC):
                ebx = ccj % 2
                trk = Sch.op("tensor", lambda e, s, ccj=ccj, ebx=ebx: e.transpose(
                    psb[ebx], ubf_r(ccj), ident[:, :]),
                    waits=(cb, tr2_ps[ebx], H["_ident_tok"]))
                ev = Sch.op("scalar", lambda e, s, t=t, ccj=ccj, ebx=ebx: e.copy(
                    RA[:, ccj * GW + 1 + t * 128: ccj * GW + 1 + (t + 1) * 128], psb[ebx]),
                    waits=(trk,) + tuple(zg1))
                tr2_ps[ebx] = ev
        if t + 2 < TT:
            u2_tile_load(t + 2)
    psb_war[0], psb_war[1] = tr2_ps
    H["_p2_rf_war"] = ("T", Sch.count.get("T", 0))
    H["_p2_rb_war"] = ("T", Sch.count.get("T", 0))
    H["_p2_cols_war"] = ("V", Sch.count.get("V", 0))
    H["_p2_b2_war"] = ("T", Sch.count.get("T", 0))
    H["_p2_uTg_rdy"] = ("Sc", Sch.count.get("Sc", 0))
    H["_p2_negbc_war"] = ("Sc", Sch.count.get("Sc", 0))
    H["_p2_outT_war"] = ("T", Sch.count.get("T", 0))
    H["_p2_nrmsc_war"] = ("Sc", Sch.count.get("Sc", 0))
    H["_p2_sq_war"] = sq_war2
    H["_p2_qv_war"] = ("V", Sch.count.get("V", 0))
    H["_p2_stage_war"] = ("T", Sch.count.get("T", 0))
    H["_p2_grid_war"] = ("Sc", Sch.count.get("Sc", 0))


def alloc_p2(nc, e, H):
    H["RA_sb"] = e(nc.sbuf_tensor("RA_sb", [128, CC * GW], BF16))
    H["RB_sb"] = e(nc.sbuf_tensor("RB_sb", [128, CC * GW], BF16))
    H["RC_sb"] = e(nc.sbuf_tensor("RC_sb", [128, CC * 2048], BF16))
    H["RD_sb"] = e(nc.sbuf_tensor("RD_sb", [128, CC * 2048], BF16))
    H["RE_sb"] = e(nc.sbuf_tensor("RE_sb", [128, 2048], BF16))
    H["RF_sb"] = e(nc.sbuf_tensor("RF_sb", [128, 8192], BF16))
    H["RG_sb"] = e(nc.sbuf_tensor("RG_sb", [128, 4096], BF16))
    H["AG2_sb"] = e(nc.sbuf_tensor("AG2_sb", [128, 4096], BF16))
    H["masks_sb"] = e(nc.sbuf_tensor("masks_sb", [128, 4 * W0], BF16))
    H["expb_sb"] = e(nc.sbuf_tensor("expb_sb", [128, 2 * W0], BF16))
    H["expm_sb"] = e(nc.sbuf_tensor("expm_sb", [128, 2 * W0], BF16))
    H["sqb_sb"] = e(nc.sbuf_tensor("sqb_sb", [128, 512], BF16))
    H["nfsc_sb"] = e(nc.sbuf_tensor("nfsc_sb", [16, 512], F32))
    H["nffsc_sb"] = e(nc.sbuf_tensor("nffsc_sb", [16, 512], F32))
    H["nrmsc_sb"] = e(nc.sbuf_tensor("nrmsc_sb", [16, 512], BF16))
    H["gridB_sb"] = e(nc.sbuf_tensor("gridB_sb", [16, 256], F32))
    H["negrow_sb"] = e(nc.sbuf_tensor("negrow_sb", [1, 128], BF16))
    H["negbc_sb"] = e(nc.sbuf_tensor("negbc_sb", [128, 128], F32))
    H["oun_sb"] = e(nc.sbuf_tensor("oun_sb", [64, W0], BF16))
    H["recb_sb"] = e(nc.sbuf_tensor("recb_sb", [1, W0], BF16))
    H["rec64_sb"] = e(nc.sbuf_tensor("rec64_sb", [1, W0], F32))
    H["cols_sb"] = e(nc.sbuf_tensor("cols_sb", [128, 62], F32))
    H["b2row_sb"] = e(nc.sbuf_tensor("b2row_sb", [1, 1024], BF16))
    H["ones_col_sb"] = e(nc.sbuf_tensor("ones_col_sb", [128, 1], BF16))
    H["ones_rowf_sb"] = e(nc.sbuf_tensor("ones_rowf_sb", [1, 128], F32))
    H["ones_row_sb"] = e(nc.sbuf_tensor("ones_row_sb", [1, 128], BF16))
    H["ind16_sb"] = e(nc.sbuf_tensor("ind16_sb", [128, 128], BF16))
    H["sel16_sb"] = e(nc.sbuf_tensor("sel16_sb", [16, 8 * 128], BF16))
    H["uta_sb"] = e(nc.sbuf_tensor("uta_sb", [128, D], F32))
    H["utb_sb"] = e(nc.sbuf_tensor("utb_sb", [128, D], F32))
    H["uba_sb"] = e(nc.sbuf_tensor("uba_sb", [128, D], F32))
    H["ubb_sb"] = e(nc.sbuf_tensor("ubb_sb", [128, D], F32))
    H["wsa_sb"] = e(nc.sbuf_tensor("wsa_sb", [128, 2048], BF16))
    H["wsb_sb"] = e(nc.sbuf_tensor("wsb_sb", [128, 2048], BF16))
    H["ident_sb"] = e(nc.sbuf_tensor("ident_sb", [128, 128], BF16))


def p2_common_setup(nc, Sch, H):
    H["_onec_tok"] = Sch.op("vector", lambda e, s: e.memset(H["ones_col_sb"][:, :], 1.0))
    H["_onerb_tok"] = Sch.op("vector", lambda e, s: e.memset(H["ones_row_sb"][:, :], 1.0))
    H["_onerf_tok"] = Sch.op("vector", lambda e, s: e.memset(H["ones_rowf_sb"][:, :], 1.0))
    H["_ident_tok"] = H["LdIn"].dma(lambda e, s: e.dma_start(H["ident_sb"][:, :], H["ident_d"]))
    H["_masks_tok"] = H["LdIn"].dma(lambda e, s: e.dma_start(
        H["masks_sb"][:, :].rearrange("p (i n) -> p i n", n=W0),
        H["masks_d"].rearrange("i p n -> p i n")))
    ld_i16 = H["LdIn"].dma(lambda e, s: e.dma_start(H["ind16_sb"][:, :], H["ind16_d"]))
    ld_s16 = H["LdIn"].dma(lambda e, s: e.dma_start(H["sel16_sb"][0:16, :], H["sel16_d"]))
    Sch.op("tensor", lambda e, s: None, waits=(ld_i16, ld_s16, H["_ident_tok"]), inc=False)


def build_p2_program(li_list, debug_umid=False):
    nc = bass.Bass("TRN2", target_bir_lowering=False)
    H = {}
    first_li = li_list[0]
    last_li = li_list[-1]
    NL = len(li_list)
    H["Wqkv_full_d"] = nc.dram_tensor("Wqkv", [NL, D, 3072], F32, kind="ExternalInput").ap()
    H["Wo6_full_d"] = nc.dram_tensor("Wo6", [NL, D, D], F32, kind="ExternalInput").ap()
    H["W1r_full_d"] = nc.dram_tensor("W1r", [NL, D, 5460], F32, kind="ExternalInput").ap()
    H["W2_full_d"] = nc.dram_tensor("W2", [NL, DI2, D], F32, kind="ExternalInput").ap()
    H["cols_full_d"] = nc.dram_tensor("colsP2", [NL, 128, 62], F32, kind="ExternalInput").ap()
    H["b2_full_d"] = nc.dram_tensor("b2rows", [NL, 1024], BF16, kind="ExternalInput").ap()
    H["tbl_d"] = nc.dram_tensor("tblP2", [128, 4, S], BF16, kind="ExternalInput").ap()
    H["masks_d"] = nc.dram_tensor("masksP2", [4, 128, W0], BF16, kind="ExternalInput").ap()
    H["ident_d"] = nc.dram_tensor("ident", [128, 128], BF16, kind="ExternalInput").ap()
    H["ind16_d"] = nc.dram_tensor("ind16", [128, 128], BF16, kind="ExternalInput").ap()
    H["sel16_d"] = nc.dram_tensor("sel16", [16, 8 * 128], BF16, kind="ExternalInput").ap()
    H["uTg_in_d"] = nc.dram_tensor("uTg_in", [128, CC * GW], BF16, kind="ExternalInput").ap()
    H["bgrid_d"] = nc.dram_tensor("bgrid", [16, 8], F32).ap()
    H["qT_scr_d"] = nc.dram_tensor("qT_scr", [128, CC * S], BF16).ap()
    H["kT_scr_d"] = nc.dram_tensor("kT_scr", [128, CC * S], BF16).ap()
    H["v_scr_d"] = nc.dram_tensor("v_scr", [128, TT * 1024], BF16).ap()
    H["bgrid_flat_d"] = H["bgrid_d"].rearrange("a b -> (a b)").rearrange("(x y) -> x y", x=1)
    def sl3(apfull, li):
        r = li - li_list[0]
        return apfull[r:r + 1, :, :].rearrange("l a b -> (l a) b")
    H["Wqkv_d"] = {li: sl3(H["Wqkv_full_d"], li) for li in li_list}
    H["Wo6_d"] = {li: sl3(H["Wo6_full_d"], li) for li in li_list}
    H["W1r_d"] = {li: sl3(H["W1r_full_d"], li) for li in li_list}
    H["W2_d"] = {li: sl3(H["W2_full_d"], li) for li in li_list}
    H["cols_d"] = {li: sl3(H["cols_full_d"], li) for li in li_list}
    H["b2_d"] = {li: H["b2_full_d"][li - li_list[0]: li - li_list[0] + 1, :] for li in li_list}
    n_sub = len(li_list)
    H[f"u_in_{first_li}"] = nc.dram_tensor("u_in", [S, D], F32, kind="ExternalInput").ap()
    for i, li in enumerate(li_list):
        H[f"u_mid_{li}"] = nc.dram_tensor(
            f"u_mid_{li}", [S, D], F32,
            kind="ExternalOutput" if debug_umid else "Internal").ap()
        if li == last_li:
            H[f"u_out_{li}"] = nc.dram_tensor("u_out", [S, D], F32, kind="ExternalOutput").ap()
        else:
            H[f"u_out_{li}"] = nc.dram_tensor(f"u_pp_{li}", [S, D], F32).ap()
            H[f"u_in_{li + 1}"] = H[f"u_out_{li}"]
    H["uTg_out_d"] = nc.dram_tensor("uTg_out", [128, CC * GW], BF16, kind="ExternalOutput").ap()
    with ExitStack() as ctx:
        e = ctx.enter_context
        alloc_p2(nc, e, H)
        H["cpool"] = e(nc.sbuf_tensor("cpool", [128, 16], F32))
        H["ps_f"] = [e(nc.psum_tensor(f"psf{i}", [128, 512], F32))[:, :] for i in range(6)]
        H["psb0"] = e(nc.psum_tensor("psb0", [128, 128], BF16))[:, :]
        H["psb1"] = e(nc.psum_tensor("psb1", [128, 128], BF16))[:, :]
        Sch = Sched()
        Sch._const_pool = H["cpool"]
        mk_streams(Sch, H)
        p2_common_setup(nc, Sch, H)
        ld_uTg = H["LdIn"].dma(lambda e2, s: e2.dma_start(H["RA_sb"][:, :], H["uTg_in_d"]))
        H["_p2_uTg_rdy"] = ld_uTg
        for li in li_list:
            emit_P2_sublayer(nc, Sch, H, li, first=(li == first_li), last=(li == 5))
        H["St"].dma(lambda e2, s: e2.dma_start(H["uTg_out_d"], H["RA_sb"][:, :]),
                    waits=(H["_p2_uTg_rdy"],))
        finish_program(nc, Sch, H)
    return nc



def emit_P4a(nc, Sch, H):
    xn1T = H["xn1T_sb"]
    xrk = H["xrk_sb"]
    Wr = H["l1f_w_sb"]
    r_tm = H["r_sb"]
    kact = H["kact_sb"]
    mu = H["mu_sb"]
    om = H["om_sb"]
    tmp = [H["tmpa_sb"], H["tmpb_sb"]]
    gt = [H["gta_sb"], H["gtb_sb"]]
    ps = H["ps_f"]

    def wslice(which, cc, lo, hi):
        base = which * CC * 1024
        return Wr[:, base + cc * 1024 + lo: base + cc * 1024 + hi]

    ld_x = H["LdIn"].dma(lambda e, s: e.dma_start(xn1T[:, :], H["xn1T_d"]))
    ld_mu = H["LdIn"].dma(lambda e, s: e.dma_start(mu[:, 0:8], H["mu_r_d"]))
    ld_mk = H["LdIn"].dma(lambda e, s: e.dma_start(mu[:, 8:16], H["mu_k_d"]))
    lw = []
    for wi, wd in enumerate(("WrT_d", "WkT2_d", "WvT2_d")):
        lw.append(H["Wt"].dma(lambda e, s, wi=wi, wd=wd: e.dma_start(
            Wr[:, wi * CC * 1024:(wi + 1) * CC * 1024].rearrange("p (c n) -> p c n", n=1024),
            H[wd].rearrange("(c p) n -> p c n", p=128))))
    omt = Sch.op("vector", lambda e, s: e.tensor_scalar(
        om[:, :], mu[:, :], -1.0, 1.0, ALU.mult, ALU.add), waits=(ld_mu, ld_mk))

    def guard(cc, shifted):
        off = 0 if shifted else 1
        return xn1T[:, cc * GW + off: cc * GW + off + S]

    def build_xlerp(mu_off, out_tok_extra=()):
        toks = []
        for cc in range(CC):
            a = cc % 2
            t1 = Sch.op("vector", lambda e, s, cc=cc, a=a: e.tensor_scalar_mul(
                tmp[a][:, :], guard(cc, False), mu[:, mu_off + cc: mu_off + cc + 1]),
                waits=(ld_x, omt) + tuple(out_tok_extra))
            t2 = Sch.op("vector", lambda e, s, cc=cc, a=a: e.scalar_tensor_tensor(
                xrk[:, cc * S:(cc + 1) * S], guard(cc, True),
                om[:, mu_off + cc: mu_off + cc + 1], tmp[a][:, :],
                ALU.mult, ALU.add), waits=(t1,))
            toks.append(t2)
        return toks

    xr_toks = build_xlerp(0)
    r_toks = []
    for t in range(TT):
        for n in range(2):
            b = (t * 2 + n) % 2
            war = r_toks[-2] if len(r_toks) >= 2 else ("Sc", 0)
            mm = None
            for cc in range(CC):
                mm = Sch.op("tensor", lambda e, s, t=t, n=n, cc=cc, b=b: e.matmul(
                    ps[b][:, :], xrk[:, cc * S + t * 128: cc * S + (t + 1) * 128],
                    wslice(0, cc, n * 512, (n + 1) * 512),
                    start=(cc == 0), stop=(cc == CC - 1)),
                    waits=(xr_toks[cc], lw[0], war), inc=(cc == CC - 1))
            ev = Sch.op("scalar", lambda e, s, t=t, n=n, b=b: e.activation(
                r_tm[:, t * D + n * 512: t * D + (n + 1) * 512], ps[b][:, :], AF.Sigmoid),
                waits=(mm,))
            r_toks.append(ev)
    xk_toks = build_xlerp(8, out_tok_extra=(("T", Sch.count.get("T", 0)),))
    ka_toks = []
    relu_t = [H["tmpa_sb"], H["tmpb_sb"]]
    for kc in range(CC):
        for n in range(4):
            b = (kc * 4 + n) % 2 + 2
            war = ka_toks[-2] if len(ka_toks) >= 2 else ("Sc", 0)
            mm = None
            for cc in range(CC):
                mm = Sch.op("tensor", lambda e, s, kc=kc, n=n, cc=cc, b=b: e.matmul(
                    ps[b][:, :], wslice(1, cc, kc * 128, (kc + 1) * 128),
                    xrk[:, cc * S + n * 512: cc * S + (n + 1) * 512],
                    start=(cc == 0), stop=(cc == CC - 1)),
                    waits=(xk_toks[cc], lw[1], war), inc=(cc == CC - 1))
            a = (kc * 4 + n) % 2
            rl = Sch.op("vector", lambda e, s, a=a, b=b: e.tensor_scalar_max(
                relu_t[a][:, 0:512], ps[b][:, :], 0.0), waits=(mm,))
            sq = Sch.op("scalar", lambda e, s, kc=kc, n=n, a=a: e.activation(
                kact[:, kc * S + n * 512: kc * S + (n + 1) * 512],
                relu_t[a][:, 0:512], AF.Square), waits=(rl,))
            ka_toks.append(sq)
    out_toks = []
    st_toks = []
    for t in range(TT):
        a = t % 2
        for n in range(2):
            b = (t * 2 + n) % 2 + 4
            war = out_toks[-2] if len(out_toks) >= 2 else ("Sc", 0)
            mm = None
            for kc in range(CC):
                mm = Sch.op("tensor", lambda e, s, t=t, n=n, kc=kc, b=b: e.matmul(
                    ps[b][:, :], kact[:, kc * S + t * 128: kc * S + (t + 1) * 128],
                    wslice(2, kc, n * 512, (n + 1) * 512),
                    start=(kc == 0), stop=(kc == CC - 1)),
                    waits=(ka_toks[kc * 4 + t // 4], lw[2], war),
                    inc=(kc == CC - 1))
            wr_dma = st_toks[-2] if len(st_toks) >= 2 else ("V", 0)
            mu1 = Sch.op("vector", lambda e, s, t=t, n=n, a=a, b=b: e.tensor_mul(
                tmp[a][:, n * 512:(n + 1) * 512],
                r_tm[:, t * D + n * 512: t * D + (n + 1) * 512], ps[b][:, :]),
                waits=(mm, r_toks[t * 2 + n], ka_toks[-1],
                       out_toks[-3] if len(out_toks) >= 3 else ("Sc", 0)))
            si = Sch.op("scalar", lambda e, s, t=t, n=n, a=a: e.activation(
                gt[a][:, n * 512:(n + 1) * 512], tmp[a][:, n * 512:(n + 1) * 512], AF.Silu),
                waits=(mu1, wr_dma))
            out_toks.append(si)
        st = H["St"].dma(lambda e, s, t=t, a=a: e.dma_start(
            H["g1_d"][t * 128:(t + 1) * 128, :], gt[a][:, :]),
            waits=(out_toks[-1], out_toks[-2]))
        st_toks.append(st)
    return st_toks


def alloc_p4a(nc, e, H):
    H["xn1T_sb"] = e(nc.sbuf_tensor("xn1T_sb", [128, CC * GW], BF16))
    H["xrk_sb"] = e(nc.sbuf_tensor("xrk_sb", [128, CC * S], BF16))
    H["l1f_w_sb"] = e(nc.sbuf_tensor("l1f_w", [128, 3 * CC * 1024], BF16))
    H["r_sb"] = e(nc.sbuf_tensor("r_sb", [128, TT * D], BF16))
    H["kact_sb"] = e(nc.sbuf_tensor("kact_sb", [128, CC * S], BF16))
    H["mu_sb"] = e(nc.sbuf_tensor("mu_sb", [128, 16], F32))
    H["om_sb"] = e(nc.sbuf_tensor("om_sb", [128, 16], F32))
    H["tmpa_sb"] = e(nc.sbuf_tensor("tmpa", [128, S], F32))
    H["tmpb_sb"] = e(nc.sbuf_tensor("tmpb", [128, S], F32))
    H["gta_sb"] = e(nc.sbuf_tensor("gta", [128, D], F32))
    H["gtb_sb"] = e(nc.sbuf_tensor("gtb", [128, D], F32))


def build_p4a_program():
    nc = bass.Bass("TRN2", target_bir_lowering=False)
    H = {}
    H["xn1T_d"] = nc.dram_tensor("xn1T", [128, CC * GW], BF16, kind="ExternalInput").ap()
    H["WrT_d"] = nc.dram_tensor("WrT", [D, D], F32, kind="ExternalInput").ap()
    H["WkT2_d"] = nc.dram_tensor("WkT2", [D, D], F32, kind="ExternalInput").ap()
    H["WvT2_d"] = nc.dram_tensor("WvT2", [D, D], F32, kind="ExternalInput").ap()
    H["mu_r_d"] = nc.dram_tensor("mu_r_cols", [128, 8], F32, kind="ExternalInput").ap()
    H["mu_k_d"] = nc.dram_tensor("mu_k_cols", [128, 8], F32, kind="ExternalInput").ap()
    H["g1_d"] = nc.dram_tensor("g1", [S, D], F32, kind="ExternalOutput").ap()
    with ExitStack() as ctx:
        e = ctx.enter_context
        alloc_p4a(nc, e, H)
        H["cpool"] = e(nc.sbuf_tensor("cpool", [128, 16], F32))
        H["ps_f"] = [e(nc.psum_tensor(f"psf{i}", [128, 512], F32))[:, :] for i in range(6)]
        Sch = Sched()
        Sch._const_pool = H["cpool"]
        mk_streams(Sch, H)
        emit_P4a(nc, Sch, H)
        finish_program(nc, Sch, H)
    return nc



def emit_P3(nc, Sch, H):
    xnT = H["xnT_sb"]
    B = H["B_sb"]
    qt_ot = H["C_sb"]
    kT = H["D_sb"]
    v_tm = H["E_sb"]
    F = H["F_sb"]
    Wo = H["D_sb"]
    dmask = H["dmask_sb"]
    scr = H["scr_sb"]
    mx4 = H["mx4_sb"]
    nmx = H["nmx_sb"]
    den4 = H["den4_sb"]
    rec = H["rec_sb"]
    x0t = [H["x0a_sb"], H["x0b_sb"]]
    g0t = [H["g0a_sb"], H["g0b_sb"]]
    x1t = g0t
    xnbf = H["xnbf_sb"]
    mv = H["mv_sb"]; st6 = H["st6_sb"]; rstd = H["rstd_sb"]
    ps = H["ps_f"]
    psb = [H["psb0"], H["psb1"]]

    ld_x = H["LdIn"].dma(lambda e, s: e.dma_start(xnT[:, :], H["xnT_d"]))
    ld_dm = H["LdIn"].dma(lambda e, s: e.dma_start(
        dmask[:, :].rearrange("p (r n) -> p r n", n=512),
        H["dmask_d"].rearrange("r p n -> p r n")))
    ld_id = H["LdIn"].dma(lambda e, s: e.dma_start(H["ident_sb"][:, :], H["ident_d"]))
    def wload(dst, hbm, waits=()):
        return H["Wt"].dma(lambda e, s: e.dma_start(
            dst.rearrange("p (c n) -> p c n", n=1024),
            hbm.rearrange("(c p) n -> p c n", p=128)), waits=waits)
    ld_wq = wload(B[:, 0:8192], H["Wq_d"])
    ld_wk = wload(B[:, 8192:16384], H["Wk_d"])
    ld_wv = wload(F[:, :], H["Wv_d"])

    bank_war = {i: ("Sc", 0) for i in range(6)}
    proj_toks = {"q": [], "k": []}
    for which, wbase, ldw, dst in (("q", 0, ld_wq, qt_ot), ("k", 8192, ld_wk, kT)):
        for qc in range(CC):
            for n in range(4):
                b = (qc * 4 + n) % 4
                mm = None
                for dc in range(CC):
                    mm = Sch.op("tensor", lambda e, s, qc=qc, n=n, dc=dc, b=b, wbase=wbase: e.matmul(
                        ps[b][:, :], B[:, wbase + dc * 1024 + qc * 128: wbase + dc * 1024 + (qc + 1) * 128],
                        xnT[:, dc * GW + 1 + n * 512: dc * GW + 1 + (n + 1) * 512],
                        start=(dc == 0), stop=(dc == CC - 1)),
                        waits=(ld_x, ldw, bank_war[b]), inc=(dc == CC - 1))
                ev = Sch.op("scalar", lambda e, s, qc=qc, n=n, b=b, dst=dst: e.copy(
                    dst[:, qc * S + n * 512: qc * S + (n + 1) * 512], ps[b][:, :]), waits=(mm,))
                bank_war[b] = ev
                proj_toks[which].append(ev)
    v_toks = []
    for t in range(TT):
        for n in range(2):
            b = (t * 2 + n) % 4
            mm = None
            for dc in range(CC):
                mm = Sch.op("tensor", lambda e, s, t=t, n=n, dc=dc, b=b: e.matmul(
                    ps[b][:, :], xnT[:, dc * GW + 1 + t * 128: dc * GW + 1 + (t + 1) * 128],
                    F[:, dc * 1024 + n * 512: dc * 1024 + (n + 1) * 512],
                    start=(dc == 0), stop=(dc == CC - 1)),
                    waits=(ld_wv, bank_war[b]), inc=(dc == CC - 1))
            ev = Sch.op("scalar", lambda e, s, t=t, n=n, b=b: e.copy(
                v_tm[:, t * D + n * 512: t * D + (n + 1) * 512], ps[b][:, :]), waits=(mm,))
            bank_war[b] = ev
            v_toks.append(ev)
    v_done = v_toks[-1]

    SCL = 1.0 / 32.0
    raw_off = 0
    norm_off = [2048, 4096]
    et_off = [6144, 6272]
    exp_war = [v_done, v_done]
    et_war = [("T", 0), ("T", 0)]
    psb_war = [("Sc", 0), ("Sc", 0)]
    oT_evac_toks = []
    prev_nm = ("V", 0)
    for qt in range(TT):
        nch = qt // 4 + 1
        eo = norm_off[qt % 2]
        sc_mm = {}
        for dc in range(CC):
            for ch in range(nch):
                sc_mm[ch] = Sch.op("tensor", lambda e, s, qt=qt, dc=dc, ch=ch: e.matmul(
                    ps[ch][:, :], qt_ot[:, dc * S + qt * 128: dc * S + (qt + 1) * 128],
                    kT[:, dc * S + ch * 512: dc * S + (ch + 1) * 512],
                    start=(dc == 0), stop=(dc == CC - 1)),
                    waits=(proj_toks["q"][-1], proj_toks["k"][-1], prev_nm,
                           bank_war[ch],
                           oT_evac_toks[-1] if oT_evac_toks else ("Sc", 0)),
                    inc=(dc == CC - 1))
        dm = Sch.op("vector", lambda e, s, qt=qt, nch=nch: e.tensor_add(
            scr[:, :], ps[nch - 1][:, :], dmask[:, (qt % 4) * 512:(qt % 4 + 1) * 512]),
            waits=(sc_mm[nch - 1], ld_dm))
        mx_toks = []
        for ch in range(nch):
            src = scr[:, :] if ch == nch - 1 else ps[ch][:, :]
            mx_toks.append(Sch.op("vector", lambda e, s, ch=ch, src=src: e.reduce_max(
                mx4[:, ch:ch + 1], src, axis=AX.X), waits=(sc_mm[ch], dm)))
        nm = Sch.op("vector", lambda e, s, nch=nch: e.reduce_max(
            nmx[:, :], mx4[:, 0:nch], axis=AX.X, negate=True), waits=tuple(mx_toks))
        nm2 = Sch.op("vector", lambda e, s: e.tensor_scalar_mul(nmx[:, :], nmx[:, :], SCL),
                     waits=(nm,))
        exp_toks = []
        for ch in range(nch):
            src = scr[:, :] if ch == nch - 1 else ps[ch][:, :]
            exp_toks.append(Sch.op("scalar", lambda e, s, ch=ch, src=src: e.activation(
                F[:, raw_off + ch * 512: raw_off + (ch + 1) * 512], src, AF.Exp,
                bias=nmx[:, 0:1], scale=SCL, accum_out=den4[:, ch:ch + 1]),
                waits=(nm2, prev_nm)))
        dn = Sch.op("vector", lambda e, s, nch=nch: e.reduce_sum(
            rec[:, 0:1], den4[:, 0:nch], axis=AX.X), waits=tuple(exp_toks))
        dn2 = Sch.op("vector", lambda e, s: e.reciprocal(rec[:, 1:2], rec[:, 0:1]), waits=(dn,))
        nrm = Sch.op("vector", lambda e, s, eo=eo, nch=nch: e.tensor_scalar_mul(
            F[:, eo: eo + nch * 512], F[:, raw_off: raw_off + nch * 512], rec[:, 1:2]),
            waits=(dn2, exp_war[qt % 2]) + tuple(exp_toks))
        for ch in range(nch):
            bank_war[ch] = exp_toks[ch]
        last_av = {}
        for kt in range(qt + 1):
            eb = kt % 2
            tr = Sch.op("tensor", lambda e, s, eo=eo, kt=kt, eb=eb: e.transpose(
                psb[eb], F[:, eo + kt * 128: eo + (kt + 1) * 128], H["ident_sb"][:, :]),
                waits=(nrm, ld_id, psb_war[eb]))
            te = Sch.op("scalar", lambda e, s, kt=kt, eb=eb: e.copy(
                F[:, et_off[eb]: et_off[eb] + 128], psb[eb]), waits=(tr, et_war[eb]))
            psb_war[eb] = te
            for vd in range(CC):
                bank = 4 + vd // 4
                col = (vd % 4) * 128
                last_av[vd] = Sch.op("tensor", lambda e, s, kt=kt, vd=vd, eb=eb, bank=bank, col=col, qt=qt: e.matmul(
                    ps[bank][:, col:col + 128],
                    v_tm[:, kt * D + vd * 128: kt * D + (vd + 1) * 128],
                    F[:, et_off[eb]: et_off[eb] + 128],
                    start=(kt == 0 and vd % 4 == 0), stop=(kt == qt and vd % 4 == 3)),
                    waits=(te, v_toks[kt * 2 + vd // 4],
                           bank_war[bank] if kt == 0 else ("Sc", 0)),
                    inc=(kt == qt and vd % 4 == 3))
            et_war[eb] = last_av[7]
        exp_war[qt % 2] = ("T", Sch.count.get("T", 0))
        prev_nm = nrm
        gend = {4: last_av[3], 5: last_av[7]}
        for vd in range(CC):
            bank = 4 + vd // 4
            col = (vd % 4) * 128
            tok_ev = Sch.op("scalar", lambda e, s, vd=vd, qt=qt, bank=bank, col=col: e.copy(
                qt_ot[:, vd * S + qt * 128: vd * S + (qt + 1) * 128],
                ps[bank][:, col:col + 128]), waits=(gend[bank],))
            oT_evac_toks.append(tok_ev)
            bank_war[bank] = tok_ev

    attn_T_done = ("T", Sch.count.get("T", 0))
    ld_wo = H["Wt"].dma(lambda e, s: e.dma_start(
        Wo[:, 0:8192].rearrange("p (c n) -> p c n", n=1024),
        H["Wo_d"].rearrange("(c p) n -> p c n", p=128)), waits=(attn_T_done,))
    z_toks = []
    for cc in range(CC):
        z_toks.append(Sch.op("vector", lambda e, s, cc=cc: e.memset(
            B[:, cc * GW: cc * GW + 1], 0.0),
            waits=(proj_toks["q"][-1], proj_toks["k"][-1])))
    comb_toks = []
    ld_toks = {}
    v_guard = ("V", Sch.count.get("V", 0))
    sc_guard = ("Sc", Sch.count.get("Sc", 0))

    def emit_tile_loads(t):
        a = t % 2
        if t >= 2:
            waits = (ln_v_tok[t - 2], ln_st_tok[t - 2])
        else:
            waits = ()
        ld_toks[("x", t)] = H["LdA"].dma(lambda e, s, t=t, a=a: e.dma_start(
            x0t[a][:, :], H["x_d"][t * 128:(t + 1) * 128, :]), waits=waits)
        ld_toks[("g", t)] = H["LdB"].dma(lambda e, s, t=t, a=a: e.dma_start(
            g0t[a][:, :], H["g0_d"][t * 128:(t + 1) * 128, :]), waits=waits)

    st_x1 = []
    ln_toks = []
    ln_v_tok = {}
    ln_st_tok = {}
    ln_ps_tok = []
    emit_tile_loads(0)
    emit_tile_loads(1)
    for t in range(TT):
        a = t % 2
        for n in range(2):
            b = (t * 2 + n) % 2
            war = comb_toks[-2] if len(comb_toks) >= 2 else ("Sc", 0)
            mm = None
            for vd in range(CC):
                mm = Sch.op("tensor", lambda e, s, t=t, n=n, vd=vd, b=b: e.matmul(
                    ps[b][:, :], qt_ot[:, vd * S + t * 128: vd * S + (t + 1) * 128],
                    Wo[:, vd * 1024 + n * 512: vd * 1024 + (n + 1) * 512],
                    start=(vd == 0), stop=(vd == CC - 1)),
                    waits=(ld_wo, oT_evac_toks[-1], war, v_guard, bank_war[b]),
                    inc=(vd == CC - 1))
            c1 = Sch.op("vector", lambda e, s, t=t, n=n, a=a, b=b: e.tensor_mul(
                scr[:, :], g0t[a][:, n * 512:(n + 1) * 512], ps[b][:, :]),
                waits=(mm, ld_toks[("g", t)], ld_toks[("x", t)],
                       comb_toks[-1] if comb_toks else ("V", 0)))
            c2 = Sch.op("vector", lambda e, s, t=t, n=n, a=a: e.tensor_add(
                x1t[a][:, n * 512:(n + 1) * 512], x0t[a][:, n * 512:(n + 1) * 512], scr[:, :]),
                waits=(c1,))
            bank_war[b] = c1
            comb_toks.append(c2)
        st = H["St"].dma(lambda e, s, t=t, a=a: e.dma_start(
            H["x1_d"][t * 128:(t + 1) * 128, :], x1t[a][:, :]),
            waits=(comb_toks[-1], comb_toks[-2]))
        st_x1.append(st)
        s1a = Sch.op("vector", lambda e, s, a=a: e.bn_stats(st6[:, 0:6], x1t[a][:, 0:512]),
                     waits=(comb_toks[-1], comb_toks[-2]))
        s1b = Sch.op("vector", lambda e, s, a=a: e.bn_stats(st6[:, 6:12], x1t[a][:, 512:1024]),
                     waits=(s1a,))
        s2 = Sch.op("vector", lambda e, s: e.bn_aggr(mv[:, :], st6[:, :]), waits=(s1a, s1b))
        eps_ap, eps_tok = Sch.const_col(1e-5)
        s3 = Sch.op("scalar", lambda e, s, eps_ap=eps_ap: e.activation(
            rstd[:, 0:1], mv[:, 1:2], AF.Sqrt, bias=eps_ap), waits=(s2, eps_tok))
        s4 = Sch.op("vector", lambda e, s: e.reciprocal(rstd[:, 1:2], rstd[:, 0:1]), waits=(s3,))
        prev_tr = ln_toks[-1] if ln_toks else ("T", 0)
        s5 = Sch.op("vector", lambda e, s, a=a: e.tensor_scalar(
            xnbf[:, :], x1t[a][:, :], mv[:, 0:1], rstd[:, 1:2], ALU.subtract, ALU.mult),
            waits=(s4, prev_tr))
        pst = ln_ps_tok[0] if ln_ps_tok else [sc_guard, sc_guard]
        lastev, pst = transpose_tiles(
            Sch, H,
            lambda i: xnbf[:, i * 128:(i + 1) * 128],
            lambda i, t=t: B[:, i * GW + 1 + t * 128: i * GW + 1 + (t + 1) * 128],
            CC, psb, pst, waits=(s5,) + tuple(z_toks))
        ln_ps_tok[:] = [pst]
        ln_toks.append(lastev)
        ln_v_tok[t] = s5
        ln_st_tok[t] = st
        if t + 2 < TT:
            emit_tile_loads(t + 2)
    st_xn1 = H["St"].dma(lambda e, s: e.dma_start(H["xn1T_d"], B[:, 0:CC * GW]),
                    waits=(ln_toks[-1], ("Sc", Sch.count.get("Sc", 0))))
    return st_x1 + [st_xn1]


def alloc_p3(nc, e, H):
    H["scr_sb"] = e(nc.sbuf_tensor("scr", [128, 512], F32))
    H["xnT_sb"] = e(nc.sbuf_tensor("xnT_sb", [128, CC * GW], BF16))
    H["B_sb"] = e(nc.sbuf_tensor("B_sb", [128, CC * GW], BF16))
    H["C_sb"] = e(nc.sbuf_tensor("C_sb", [128, CC * S], BF16))
    H["D_sb"] = e(nc.sbuf_tensor("D_sb", [128, CC * S], BF16))
    H["E_sb"] = e(nc.sbuf_tensor("E_sb", [128, TT * D], BF16))
    H["F_sb"] = e(nc.sbuf_tensor("F_sb", [128, 8192], BF16))
    H["dmask_sb"] = e(nc.sbuf_tensor("dmask_sb", [128, 4 * 512], BF16))
    H["mx4_sb"] = e(nc.sbuf_tensor("mx4", [128, 4], F32))
    H["nmx_sb"] = e(nc.sbuf_tensor("nmx", [128, 1], F32))
    H["den4_sb"] = e(nc.sbuf_tensor("den4", [128, 4], F32))
    H["rec_sb"] = e(nc.sbuf_tensor("rec", [128, 2], F32))
    H["x0a_sb"] = e(nc.sbuf_tensor("x0a", [128, D], F32))
    H["x0b_sb"] = e(nc.sbuf_tensor("x0b", [128, D], F32))
    H["g0a_sb"] = e(nc.sbuf_tensor("g0a", [128, D], F32))
    H["g0b_sb"] = e(nc.sbuf_tensor("g0b", [128, D], F32))
    H["xnbf_sb"] = e(nc.sbuf_tensor("xnbf", [128, D], BF16))
    H["mv_sb"] = e(nc.sbuf_tensor("mv", [128, 2], F32))
    H["st6_sb"] = e(nc.sbuf_tensor("st6", [128, 12], F32))
    H["rstd_sb"] = e(nc.sbuf_tensor("rstd", [128, 2], F32))
    H["ident_sb"] = e(nc.sbuf_tensor("ident_sb", [128, 128], BF16))


def build_p3_program():
    nc = bass.Bass("TRN2", target_bir_lowering=False)
    H = {}
    H["xnT_d"] = nc.dram_tensor("xnT", [128, CC * GW], BF16, kind="ExternalInput").ap()
    H["x_d"] = nc.dram_tensor("x", [S, D], F32, kind="ExternalInput").ap()
    H["g0_d"] = nc.dram_tensor("g0", [S, D], F32, kind="ExternalInput").ap()
    H["Wq_d"] = nc.dram_tensor("Wq", [D, D], F32, kind="ExternalInput").ap()
    H["Wk_d"] = nc.dram_tensor("Wk", [D, D], F32, kind="ExternalInput").ap()
    H["Wv_d"] = nc.dram_tensor("Wv", [D, D], F32, kind="ExternalInput").ap()
    H["Wo_d"] = nc.dram_tensor("Wo", [D, D], F32, kind="ExternalInput").ap()
    H["dmask_d"] = nc.dram_tensor("dmask", [4, 128, 512], BF16, kind="ExternalInput").ap()
    H["ident_d"] = nc.dram_tensor("ident", [128, 128], BF16, kind="ExternalInput").ap()
    H["x1_d"] = nc.dram_tensor("x1", [S, D], F32, kind="ExternalOutput").ap()
    H["xn1T_d"] = nc.dram_tensor("xn1T", [128, CC * GW], BF16, kind="ExternalOutput").ap()
    with ExitStack() as ctx:
        e = ctx.enter_context
        alloc_p3(nc, e, H)
        H["cpool"] = e(nc.sbuf_tensor("cpool", [128, 16], F32))
        H["ps_f"] = [e(nc.psum_tensor(f"psf{i}", [128, 512], F32))[:, :] for i in range(6)]
        ps0 = e(nc.psum_tensor("psb0", [128, 128], BF16))
        ps1 = e(nc.psum_tensor("psb1", [128, 128], BF16))
        H["psb0"], H["psb1"] = ps0[:, :], ps1[:, :]
        Sch = Sched()
        Sch._const_pool = H["cpool"]
        mk_streams(Sch, H)
        emit_P3(nc, Sch, H)
        finish_program(nc, Sch, H)
    return nc



def emit_P4b(nc, Sch, H):
    xn1T = H["xn1T_sb"]
    WA = H["wa_sb"]
    WB = H["wb_sb"]
    WC = WB
    tm = H["tm_sb"]
    qT = H["qT_sb"]
    kT = H["kT_sb"]
    expb = H["expb_sb"]
    stage = H["stage_sb"]
    ropeC = H["ropeC_sb"]
    ropeS = H["ropeS_sb"]
    tri = H["tri_sb"]
    ssq = H["ssq_sb"]
    nB = H["nB_sb"]
    nBbf = H["nBbf_sb"]
    nrow = H["nrow_sb"]
    negbc = H["negbc_sb"]
    ones_row = H["ones_row_sb"]
    rec1 = H["rec1_sb"]
    oun = H["oun_sb"]
    rec64 = H["rec64_sb"]
    xf = xn1T.bitcast(F32)
    x1t = [xf[:, 0:D], xf[:, D:2 * D]]
    g1t = [xf[:, 2 * D:3 * D], xf[:, 3 * D:4 * D]]
    scr = xf[:, 4 * D:4 * D + 512]
    sqs = H["str4_sb"][:, 0:512]
    str4 = H["str4_sb"]
    ps = H["ps_f"]
    psb = [H["psb0"], H["psb1"]]
    bank_war = {i: ("Sc", 0) for i in range(5)}

    ld_x = H["LdIn"].dma(lambda e, s: e.dma_start(xn1T[:, :], H["xn1T_d"]))
    ld_tc = H["LdIn"].dma(lambda e, s: e.dma_start(ropeC[:, :], H["ropeC_d"]))
    ld_ts = H["LdIn"].dma(lambda e, s: e.dma_start(ropeS[:, :], H["ropeS_d"]))
    ld_tri = H["LdIn"].dma(lambda e, s: e.dma_start(tri[:, :], H["tri_d"]))
    ld_id = H["LdIn"].dma(lambda e, s: e.dma_start(H["ident_sb"][:, :], H["ident_d"]))

    def wload(dst, hbm, waits=()):
        return H["Wt"].dma(lambda e, s, dst=dst, hbm=hbm: e.dma_start(
            dst.rearrange("p (c n) -> p c n", n=1024),
            H[hbm].rearrange("(c p) n -> p c n", p=128)), waits=waits)
    ld_wq = wload(WA[:, 0:CC * 1024], "WqT_d")
    ld_wk = wload(WA[:, CC * 1024:2 * CC * 1024], "WkT_d")
    ld_wv = wload(WB[:, :], "WvT_d")

    one_tok = Sch.op("vector", lambda e, s: e.memset(ones_row[:, :], 1.0))
    ones_col = H["ones_col_sb"]
    onec_tok = Sch.op("vector", lambda e, s: e.memset(ones_col[:, :], 1.0))
    ones_rowf = H["ones_rowf_sb"]
    onef_tok = Sch.op("vector", lambda e, s: e.memset(ones_rowf[:, :], 1.0))
    zstage = Sch.op("vector", lambda e, s: e.memset(nBbf[:, :], 0.0))

    def proj_rope(which, wbase, ldw, dstT, ssq_off):
        toks = [("V", Sch.count.get("V", 0))]
        tr_last = None
        for t in range(TT):
            mm = {}
            for n in range(2):
                b = (t * 2 + n) % 2
                mmx = None
                for cc in range(CC):
                    mmx = Sch.op("tensor", lambda e, s, t=t, n=n, cc=cc, b=b, wbase=wbase: e.matmul(
                        ps[b][:, :], xn1T[:, cc * GW + 1 + t * 128: cc * GW + 1 + (t + 1) * 128],
                        WA[:, wbase + cc * 1024 + n * 512: wbase + cc * 1024 + (n + 1) * 512],
                        start=(cc == 0), stop=(cc == CC - 1)),
                        waits=(ld_x, ldw, bank_war[b]), inc=(cc == CC - 1))
                mm[n] = mmx
            rope_toks = []
            for n in range(2):
                b = (t * 2 + n) % 2
                pe = ps[b]
                Ct = ropeC[:, t * 32:(t + 1) * 32]
                St = ropeS[:, t * 32:(t + 1) * 32]
                def ap3(base_ap, off, st0, n0, st1, n1):
                    t_ = base_ap.tensor
                    return bass.AP(t_, base_ap.offset + off, [[base_ap.ap[0][0], 128], [st0, n0], [st1, n1]])
                teap = ap3(pe, 0, 64, 8, 2, 32)
                toap = ap3(pe, 1, 64, 8, 2, 32)
                Cb = ap3(Ct, 0, 0, 8, 1, 32)
                Sb = ap3(St, 0, 0, 8, 1, 32)
                sA = ap3(str4[:, 0:256], 0, 32, 8, 1, 32)
                sB = ap3(str4[:, 256:512], 0, 32, 8, 1, 32)
                sC = ap3(str4[:, 512:768], 0, 32, 8, 1, 32)
                sD = ap3(str4[:, 768:1024], 0, 32, 8, 1, 32)
                qe = ap3(tm[:, t * D + n * 512: t * D + (n + 1) * 512], 0, 64, 8, 2, 32)
                qo = ap3(tm[:, t * D + n * 512: t * D + (n + 1) * 512], 1, 64, 8, 2, 32)
                war_s = rope_toks[-1] if rope_toks else toks[-1]
                r1 = Sch.op("vector", lambda e, s, teap=teap, Cb=Cb, sA=sA: e.tensor_mul(sA, teap, Cb),
                            waits=(mm[n], ld_tc, war_s))
                r2 = Sch.op("vector", lambda e, s, toap=toap, Sb=Sb, sB=sB: e.tensor_mul(sB, toap, Sb),
                            waits=(mm[n], ld_ts, war_s))
                r3 = Sch.op("vector", lambda e, s, teap=teap, Sb=Sb, sC=sC: e.tensor_mul(sC, teap, Sb),
                            waits=(mm[n], war_s))
                r4 = Sch.op("vector", lambda e, s, toap=toap, Cb=Cb, sD=sD: e.tensor_mul(sD, toap, Cb),
                            waits=(mm[n], war_s))
                r5 = Sch.op("vector", lambda e, s, sA=sA, sB=sB, qe=qe: e.scalar_tensor_tensor(
                    qe, sB, -1.0, sA, ALU.mult, ALU.add), waits=(r1, r2))
                r6 = Sch.op("vector", lambda e, s, sC=sC, sD=sD, qo=qo: e.tensor_add(qo, sC, sD),
                            waits=(r3, r4))
                rope_toks += [r5, r6]
                bank_war[b] = r4
            s2 = None
            for half in range(2):
                s1 = Sch.op("vector", lambda e, s, t=t, half=half: e.tensor_mul(
                    sqs, tm[:, t * D + half * 512: t * D + half * 512 + 512],
                    tm[:, t * D + half * 512: t * D + half * 512 + 512]),
                    waits=tuple(rope_toks) + ((s2,) if s2 else ()))
                def red_fn(e, s, t=t, half=half, ssq_off=ssq_off):
                    with nc.allow_low_precision(reason="softmax bound only"):
                        return e.reduce_sum(
                            ssq[:, ssq_off + t * 16 + half * 8: ssq_off + t * 16 + half * 8 + 8],
                            sqs.rearrange("p (h d) -> p h d", d=64), axis=AX.X)
                s2 = Sch.op("vector", red_fn, waits=(s1,))
            toks.append(s2)
            for ccj in range(CC):
                eb = ccj % 2
                trw = psb_war[eb]
                trk = Sch.op("tensor", lambda e, s, t=t, ccj=ccj, eb=eb: e.transpose(
                    psb[eb], tm[:, t * D + ccj * 128: t * D + (ccj + 1) * 128], H["ident_sb"][:, :]),
                    waits=(rope_toks[0], rope_toks[1], rope_toks[2], rope_toks[3], ld_id, trw))
                ev = Sch.op("scalar", lambda e, s, t=t, ccj=ccj, eb=eb, dstT=dstT: e.copy(
                    dstT[:, ccj * S + t * 128: ccj * S + (t + 1) * 128], psb[eb]), waits=(trk,))
                psb_war[eb] = ev
                tr_last = ev
        return toks, tr_last

    psb_war = [("Sc", 0), ("Sc", 0)]
    q_toks, q_tr = proj_rope("q", 0, ld_wq, qT, 0)
    k_toks, k_tr = proj_rope("k", CC * 1024, ld_wk, kT, TT * 16)
    qk_done = ("Sc", Sch.count.get("Sc", 0))

    v_toks = []
    for t in range(TT):
        for n in range(2):
            b = (t * 2 + n) % 2
            mmx = None
            for cc in range(CC):
                mmx = Sch.op("tensor", lambda e, s, t=t, n=n, cc=cc, b=b: e.matmul(
                    ps[b][:, :], xn1T[:, cc * GW + 1 + t * 128: cc * GW + 1 + (t + 1) * 128],
                    WB[:, cc * 1024 + n * 512: cc * 1024 + (n + 1) * 512],
                    start=(cc == 0), stop=(cc == CC - 1)),
                    waits=(ld_wv, bank_war[b],
                           ld_wk if (t == 0 and n == 0) else ("V", 0)), inc=(cc == CC - 1))
            ev = Sch.op("scalar", lambda e, s, t=t, n=n, b=b: e.copy(
                WA[:, t * D + n * 512: t * D + (n + 1) * 512], ps[b][:, :]), waits=(mmx, qk_done))
            bank_war[b] = ev
            v_toks.append(ev)
    v_tm = WA
    ld_wo = wload(WC[:, :], "WoT_d", waits=(("T", Sch.count.get("T", 0)),))

    z0 = Sch.op("vector", lambda e, s: e.memset(nB[0:16, 0:1], 0.0))
    z1 = Sch.op("vector", lambda e, s: e.memset(nB[0:16, 2:3], 0.0))
    tr_war = ("V", 0)
    mq = mk = None
    for which, off, toks, base in (("q", 0, q_toks, 0), ("k", TT * 16, k_toks, 2)):
        run = z0 if which == "q" else z1
        for t in range(TT):
            trk = Sch.op("tensor", lambda e, s, t=t, off=off: e.transpose(
                psb[1][0:16, 0:128], ssq[:, off + t * 16: off + (t + 1) * 16],
                H["ident_sb"][:, :]), waits=(toks[t + 1], tr_war, psb_war[1]))
            tmx = Sch.op("vector", lambda e, s: e.reduce_max(
                nB[0:16, 4:5], psb[1][0:16, 0:128], axis=AX.X), waits=(trk,))
            psb_war[1] = tmx
            run = Sch.op("vector", lambda e, s, t=t, base=base: e.tensor_max(
                nB[0:16, base + (t + 1) % 2: base + (t + 1) % 2 + 1],
                nB[0:16, base + t % 2: base + t % 2 + 1], nB[0:16, 4:5]),
                waits=(tmx, run))
            tr_war = run
        if which == "q":
            mq = run
        else:
            mk = run
    qcol = TT % 2
    pr = Sch.op("vector", lambda e, s, qcol=qcol: e.tensor_mul(
        nB[0:16, 5:6], nB[0:16, qcol:qcol + 1], nB[0:16, 2 + qcol:3 + qcol]),
        waits=(mq, mk))
    sqr = Sch.op("scalar", lambda e, s: e.activation(nB[0:16, 6:7], nB[0:16, 5:6], AF.Sqrt), waits=(pr,))
    ngb = Sch.op("vector", lambda e, s: e.tensor_scalar_mul(nBbf[0:16, 0:1], nB[0:16, 6:7], -0.125),
                 waits=(sqr, zstage))
    trb = Sch.op("tensor", lambda e, s: e.transpose(psb[0][:, 0:16], nBbf[0:16, :],
                 H["ident_sb"][0:16, 0:16]), waits=(ngb, psb_war[0]))
    evb = Sch.op("scalar", lambda e, s: e.copy(nrow[:, :], psb[0][0:1, 0:16]), waits=(trb,))
    bcb = Sch.op("tensor", lambda e, s: e.matmul(
        H["ps_n"][:, 0:16], ones_row[0:1, :], nrow[0:1, :], start=True, stop=True),
        waits=(evb, one_tok, ("Sc", Sch.count.get("Sc", 0))))
    nbc = Sch.op("scalar", lambda e, s: e.copy(negbc[:, :], H["ps_n"][:, 0:16]), waits=(bcb,))
    psb_war[0] = evb

    SCL = 0.125
    outT = tm
    out_evacs = []
    eb_war = [("G", 0)] * 4
    stage_war = ("G", 0)
    den_war = ("V", 0)
    ps_n_war = nbc
    for h in range(16):
        cc_t = h // 2
        prow = (h % 2) * 64
        for qc in range(4):
            ob = 3 + qc % 2
            nkt = 4 * qc + 4
            av_last = None
            den_last = None
            for kt in range(nkt):
                off = max(0, kt * 128 - qc * 512)
                N = 512 - off
                sb_ = kt % 2
                diag = kt >= 4 * qc
                smm = Sch.op("tensor", lambda e, s, h=h, qc=qc, kt=kt, sb_=sb_, off=off, N=N, cc_t=cc_t, prow=prow: e.matmul(
                    ps[sb_][:, 0:N],
                    kT[prow:prow + 64, cc_t * S + kt * 128: cc_t * S + (kt + 1) * 128],
                    qT[prow:prow + 64, cc_t * S + qc * 512 + off: cc_t * S + (qc + 1) * 512],
                    start=True, stop=True),
                    waits=(q_tr, k_tr, bank_war[sb_]))
                eb = kt % 4
                if diag:
                    ex1 = Sch.op("scalar", lambda e, s, sb_=sb_, h=h: e.activation(
                        stage[:, 0:128], ps[sb_][:, 0:128], AF.Exp,
                        bias=negbc[:, h:h + 1], scale=SCL),
                        waits=(smm, nbc, stage_war))
                    toks_e = [ex1]
                    if N > 128:
                        ex2 = Sch.op("scalar", lambda e, s, sb_=sb_, eb=eb, N=N, h=h: e.activation(
                            expb[:, eb * 512 + 128: eb * 512 + N], ps[sb_][:, 128:N], AF.Exp,
                            bias=negbc[:, h:h + 1], scale=SCL),
                            waits=(smm, eb_war[eb]))
                        toks_e.append(ex2)
                    msk = Sch.op("gpsimd", lambda e, s, eb=eb: e.tensor_mul(
                        expb[:, eb * 512: eb * 512 + 128], stage[:, 0:128], tri[:, :]),
                        waits=(ex1, ld_tri, eb_war[eb]))
                    stage_war = msk
                    e_tok = (msk,) + tuple(toks_e)
                else:
                    ex = Sch.op("scalar", lambda e, s, sb_=sb_, eb=eb, N=N, h=h: e.activation(
                        expb[:, eb * 512: eb * 512 + N], ps[sb_][:, 0:N], AF.Exp,
                        bias=negbc[:, h:h + 1], scale=SCL),
                        waits=(smm, eb_war[eb], nbc))
                    e_tok = (ex,)
                bank_war[sb_] = e_tok[0]
                av_last = Sch.op("tensor", lambda e, s, h=h, kt=kt, eb=eb, ob=ob, off=off, N=N: e.matmul(
                    ps[ob][0:64, off:512], v_tm[:, kt * D + h * 64: kt * D + (h + 1) * 64],
                    expb[:, eb * 512: eb * 512 + N],
                    start=(kt == 0), stop=(kt == nkt - 1), skip_group_check=True),
                    waits=e_tok + ((v_toks[-1], bank_war[ob]) if kt == 0 else ()))
                den_last = Sch.op("tensor", lambda e, s, kt=kt, eb=eb, off=off, N=N: e.matmul(
                    ps[2][0:1, off:512], ones_col[:, 0:1],
                    expb[:, eb * 512: eb * 512 + N],
                    start=(kt == 0), stop=(kt == nkt - 1), skip_group_check=True),
                    waits=e_tok + (onec_tok,) + ((den_war,) if kt == 0 else ()),
                    inc=(kt == nkt - 1))
                eb_war[eb] = den_last
            lnd = Sch.op("scalar", lambda e, s: e.activation(
                rec1[0:1, 0:512], ps[2][0:1, 0:512], AF.Ln),
                waits=(den_last, H.get("_rec1_war", ("T", 0))))
            den_war = lnd
            rex = Sch.op("scalar", lambda e, s: e.activation(
                rec64[0:1, 0:512], rec1[0:1, 0:512], AF.Exp, scale=-1.0),
                waits=(lnd, H.get("_rec64_war", ("T", 0))))
            H["_rec1_war"] = rex
            bcd = Sch.op("tensor", lambda e, s: e.matmul(
                H["ps_n"][0:64, 0:512], ones_rowf[0:1, 0:64], rec64[0:1, 0:512],
                start=True, stop=True), waits=(rex, ps_n_war, onef_tok))
            H["_rec64_war"] = bcd
            evo = Sch.op("scalar", lambda e, s, ob=ob: e.copy(oun[:, :], ps[ob][0:64, 0:512]),
                         waits=(den_last, H.get("_oun_war", ("V", 0))))
            mlo = Sch.op("vector", lambda e, s, h=h, qc=qc, cc_t=cc_t, prow=prow: e.tensor_mul(
                outT[prow:prow + 64, cc_t * S + qc * 512: cc_t * S + (qc + 1) * 512],
                oun[:, :], H["ps_n"][0:64, 0:512]),
                waits=(bcd, evo, (v_toks[-1] if (h == 0 and qc == 0) else ("V", 0))))
            ps_n_war = mlo
            H["_oun_war"] = mlo
            bank_war[ob] = mlo
            out_evacs.append(mlo)

    comb_toks = []
    ld_toks = {}
    fin_v = {}
    fin_st = {}

    proj_T_done = ("T", Sch.count.get("T", 0))

    def tile_loads(t):
        a = t % 2
        if t >= 2:
            waits = (fin_v[t - 2], fin_st[t - 2])
        else:
            waits = (proj_T_done,)
        ld_toks[("x", t)] = H["LdA"].dma(lambda e, s, t=t, a=a: e.dma_start(
            x1t[a][:, :], H["x1_d"][t * 128:(t + 1) * 128, :]), waits=waits)
        ld_toks[("g", t)] = H["LdB"].dma(lambda e, s, t=t, a=a: e.dma_start(
            g1t[a][:, :], H["g1_d"][t * 128:(t + 1) * 128, :]), waits=waits)

    tile_loads(0)
    tile_loads(1)
    v_guard = ("V", Sch.count.get("V", 0))
    for t in range(TT):
        a = t % 2
        for n in range(2):
            b = (t * 2 + n) % 2
            mmx = None
            for vd in range(CC):
                mmx = Sch.op("tensor", lambda e, s, t=t, n=n, vd=vd, b=b: e.matmul(
                    ps[b][:, :], outT[:, vd * S + t * 128: vd * S + (t + 1) * 128],
                    WC[:, vd * 1024 + n * 512: vd * 1024 + (n + 1) * 512],
                    start=(vd == 0), stop=(vd == CC - 1)),
                    waits=(ld_wo, out_evacs[-1], bank_war[b], v_guard),
                    inc=(vd == CC - 1))
            c1 = Sch.op("vector", lambda e, s, t=t, n=n, a=a, b=b: e.tensor_mul(
                scr[:, :], g1t[a][:, n * 512:(n + 1) * 512], ps[b][:, :]),
                waits=(mmx, ld_toks[("g", t)], ld_toks[("x", t)],
                       comb_toks[-1] if comb_toks else ("V", 0)))
            c2 = Sch.op("vector", lambda e, s, t=t, n=n, a=a: e.tensor_add(
                g1t[a][:, n * 512:(n + 1) * 512], x1t[a][:, n * 512:(n + 1) * 512], scr[:, :]),
                waits=(c1,))
            bank_war[b] = c1
            comb_toks.append(c2)
        st = H["St"].dma(lambda e, s, t=t, a=a: e.dma_start(
            H["out_d"][t * 128:(t + 1) * 128, :], g1t[a][:, :]),
            waits=(comb_toks[-1], comb_toks[-2]))
        fin_v[t] = comb_toks[-1]
        fin_st[t] = st
        if t + 2 < TT:
            tile_loads(t + 2)


def alloc_p4b(nc, e, H):
    H["xn1T_sb"] = e(nc.sbuf_tensor("xn1T_sb", [128, CC * GW], BF16))
    H["wa_sb"] = e(nc.sbuf_tensor("wa_sb", [128, 2 * CC * 1024], BF16))
    H["wb_sb"] = e(nc.sbuf_tensor("wb_sb", [128, CC * 1024], BF16))
    H["tm_sb"] = e(nc.sbuf_tensor("tm_sb", [128, TT * D], BF16))
    H["qT_sb"] = e(nc.sbuf_tensor("qT_sb", [128, CC * S], BF16))
    H["kT_sb"] = e(nc.sbuf_tensor("kT_sb", [128, CC * S], BF16))
    H["expb_sb"] = e(nc.sbuf_tensor("expb_sb", [128, 4 * 512], BF16))
    H["stage_sb"] = e(nc.sbuf_tensor("stage_sb", [128, 128], BF16))
    H["ropeC_sb"] = e(nc.sbuf_tensor("ropeC_sb", [128, TT * 32], F32))
    H["ropeS_sb"] = e(nc.sbuf_tensor("ropeS_sb", [128, TT * 32], F32))
    H["tri_sb"] = e(nc.sbuf_tensor("tri_sb", [128, 128], BF16))
    H["ssq_sb"] = e(nc.sbuf_tensor("ssq_sb", [128, 2 * TT * 16], BF16))
    H["nB_sb"] = e(nc.sbuf_tensor("nB_sb", [16, 12], F32))
    H["nBbf_sb"] = e(nc.sbuf_tensor("nBbf_sb", [16, 128], BF16))
    H["nrow_sb"] = e(nc.sbuf_tensor("nrow_sb", [1, 16], BF16))
    H["negbc_sb"] = e(nc.sbuf_tensor("negbc_sb", [128, 16], F32))
    H["ones_row_sb"] = e(nc.sbuf_tensor("ones_row_sb", [1, 128], BF16))
    H["ones_col_sb"] = e(nc.sbuf_tensor("ones_col_sb", [128, 1], BF16))
    H["ones_rowf_sb"] = e(nc.sbuf_tensor("ones_rowf_sb", [1, 128], F32))
    H["rec1_sb"] = e(nc.sbuf_tensor("rec1_sb", [1, 512], F32))
    H["oun_sb"] = e(nc.sbuf_tensor("oun_sb", [64, 512], BF16))
    H["rec64_sb"] = e(nc.sbuf_tensor("rec64_sb", [1, 512], F32))
    H["str4_sb"] = e(nc.sbuf_tensor("str4", [128, 4 * 256], F32))
    H["ident_sb"] = e(nc.sbuf_tensor("ident_sb", [128, 128], BF16))


def build_p4b_program():
    nc = bass.Bass("TRN2", target_bir_lowering=False)
    H = {}
    H["xn1T_d"] = nc.dram_tensor("xn1T", [128, CC * GW], BF16, kind="ExternalInput").ap()
    H["WqT_d"] = nc.dram_tensor("WqT", [D, D], F32, kind="ExternalInput").ap()
    H["WkT_d"] = nc.dram_tensor("WkT", [D, D], F32, kind="ExternalInput").ap()
    H["WvT_d"] = nc.dram_tensor("WvT", [D, D], F32, kind="ExternalInput").ap()
    H["WoT_d"] = nc.dram_tensor("WoT", [D, D], F32, kind="ExternalInput").ap()
    H["ropeC_d"] = nc.dram_tensor("ropeC", [128, TT * 32], F32, kind="ExternalInput").ap()
    H["ropeS_d"] = nc.dram_tensor("ropeS", [128, TT * 32], F32, kind="ExternalInput").ap()
    H["tri_d"] = nc.dram_tensor("tri01", [128, 128], BF16, kind="ExternalInput").ap()
    H["ident_d"] = nc.dram_tensor("ident", [128, 128], BF16, kind="ExternalInput").ap()
    H["x1_d"] = nc.dram_tensor("x1", [S, D], F32, kind="ExternalInput").ap()
    H["g1_d"] = nc.dram_tensor("g1", [S, D], F32, kind="ExternalInput").ap()
    H["out_d"] = nc.dram_tensor("out", [S, D], F32, kind="ExternalOutput").ap()
    with ExitStack() as ctx:
        e = ctx.enter_context
        alloc_p4b(nc, e, H)
        H["cpool"] = e(nc.sbuf_tensor("cpool", [128, 16], F32))
        H["ps_f"] = [e(nc.psum_tensor(f"psf{i}", [128, 512], F32))[:, :] for i in range(5)]
        H["ps_n"] = e(nc.psum_tensor("ps_n", [128, 512], F32))[:, :]
        ps0 = e(nc.psum_tensor("psb0", [128, 128], BF16))
        H["psb0"] = ps0[:, :]
        H["psb1"] = e(nc.psum_tensor("psb1", [128, 128], BF16))[:, :]
        Sch = Sched()
        Sch._const_pool = H["cpool"]
        mk_streams(Sch, H)
        emit_P4b(nc, Sch, H)
        finish_program(nc, Sch, H)
    return nc



def build_p1_program():
    nc = bass.Bass("TRN2", target_bir_lowering=False)
    H = {}
    H["x_d"] = nc.dram_tensor("x", [S, D], F32, kind="ExternalInput").ap()
    H["ident_d"] = nc.dram_tensor("ident", [128, 128], BF16, kind="ExternalInput").ap()
    H["u0_d"] = nc.dram_tensor("u0", [S, D], F32, kind="ExternalOutput").ap()
    H["xnT_d"] = nc.dram_tensor("xnT", [128, CC * GW], BF16, kind="ExternalOutput").ap()
    with ExitStack() as ctx:
        e = ctx.enter_context
        H["x_sb"] = e(nc.sbuf_tensor("x_sb", [128, TT * D], F32))
        H["u_sb"] = e(nc.sbuf_tensor("u_sb", [128, TT * D], F32))
        H["xnbf_sb"] = e(nc.sbuf_tensor("xnbf", [128, D], BF16))
        H["xnT_sb"] = e(nc.sbuf_tensor("xnT_sb", [128, CC * GW], BF16))
        H["mv_sb"] = e(nc.sbuf_tensor("mv", [128, 2], F32))
        H["st6_sb"] = e(nc.sbuf_tensor("st6", [128, 12], F32))
        H["rstd_sb"] = e(nc.sbuf_tensor("rstd", [128, 2], F32))
        H["ident_sb"] = e(nc.sbuf_tensor("ident_sb", [128, 128], BF16))
        ps0 = e(nc.psum_tensor("psb0", [128, 128], BF16))
        ps1 = e(nc.psum_tensor("psb1", [128, 128], BF16))
        H["psb0"], H["psb1"] = ps0[:, :], ps1[:, :]
        Sch = Sched()
        H["cpool"] = e(nc.sbuf_tensor("cpool", [128, 16], F32))
        Sch._const_pool = H["cpool"]
        mk_streams(Sch, H)
        ld_id = H["Wt"].dma(lambda e2, s: e2.dma_start(H["ident_sb"][:, :], H["ident_d"]))
        Sch.op("tensor", lambda e2, s: None, waits=(ld_id,), inc=False)
        emit_P1(nc, Sch, H)
        finish_program(nc, Sch, H)
    return nc





class _NCProxy:

    def __init__(self, nc, prefix):
        self._nc = nc
        self._prefix = prefix

    def sbuf_tensor(self, name, *a, **k):
        return self._nc.sbuf_tensor(self._prefix + name, *a, **k)

    def __getattr__(self, k):
        return getattr(self._nc, k)


def build_fused_program(debug=False, phases=("P1", "P2", "P3", "P4a", "P4b"),
                        p2_lis=tuple(range(6))):
    nc = bass.Bass("TRN2", target_bir_lowering=False)
    H = {}
    EI = "ExternalInput"
    IK = "ExternalOutput" if debug else "Internal"
    H["x_d"] = nc.dram_tensor("x", [S, D], F32, kind=EI).ap()
    H["ident_d"] = nc.dram_tensor("ident", [128, 128], BF16, kind=EI).ap()
    H["Wq_d"] = nc.dram_tensor("Wq", [D, D], F32, kind=EI).ap()
    H["Wk_d"] = nc.dram_tensor("Wk", [D, D], F32, kind=EI).ap()
    H["Wv_d"] = nc.dram_tensor("Wv", [D, D], F32, kind=EI).ap()
    H["Wo_d"] = nc.dram_tensor("Wo", [D, D], F32, kind=EI).ap()
    H["dmask_d"] = nc.dram_tensor("dmask", [4, 128, 512], BF16, kind=EI).ap()
    H["Wqkv_full_d"] = nc.dram_tensor("Wqkv", [6, D, 3072], F32, kind=EI).ap()
    H["Wo6_full_d"] = nc.dram_tensor("Wo6", [6, D, D], F32, kind=EI).ap()
    H["W1r_full_d"] = nc.dram_tensor("W1r", [6, D, 5460], F32, kind=EI).ap()
    H["W2_full_d"] = nc.dram_tensor("W2", [6, DI2, D], F32, kind=EI).ap()
    H["cols_full_d"] = nc.dram_tensor("colsP2", [6, 128, 62], F32, kind=EI).ap()
    H["b2_full_d"] = nc.dram_tensor("b2rows", [6, 1024], BF16, kind=EI).ap()
    H["tbl_d"] = nc.dram_tensor("tblP2", [128, 4, S], BF16, kind=EI).ap()
    H["masks_d"] = nc.dram_tensor("masksP2", [4, 128, W0], BF16, kind=EI).ap()
    H["ind16_d"] = nc.dram_tensor("ind16", [128, 128], BF16, kind=EI).ap()
    H["sel16_d"] = nc.dram_tensor("sel16", [16, 8 * 128], BF16, kind=EI).ap()
    H["WrT_d"] = nc.dram_tensor("WrT", [D, D], F32, kind=EI).ap()
    H["WkT2_d"] = nc.dram_tensor("WkT2", [D, D], F32, kind=EI).ap()
    H["WvT2_d"] = nc.dram_tensor("WvT2", [D, D], F32, kind=EI).ap()
    H["mu_r_d"] = nc.dram_tensor("mu_r_cols", [128, 8], F32, kind=EI).ap()
    H["mu_k_d"] = nc.dram_tensor("mu_k_cols", [128, 8], F32, kind=EI).ap()
    H["WqT_d"] = nc.dram_tensor("WqT", [D, D], F32, kind=EI).ap()
    H["WkT_d"] = nc.dram_tensor("WkT", [D, D], F32, kind=EI).ap()
    H["WvT_d"] = nc.dram_tensor("WvT", [D, D], F32, kind=EI).ap()
    H["WoT_d"] = nc.dram_tensor("WoT", [D, D], F32, kind=EI).ap()
    H["ropeC_d"] = nc.dram_tensor("ropeC", [128, TT * 32], F32, kind=EI).ap()
    H["ropeS_d"] = nc.dram_tensor("ropeS", [128, TT * 32], F32, kind=EI).ap()
    H["tri_d"] = nc.dram_tensor("tri01", [128, 128], BF16, kind=EI).ap()
    H["u0_d"] = nc.dram_tensor("u0", [S, D], F32, kind=IK).ap()
    H["xnT_d"] = nc.dram_tensor("xnT", [128, CC * GW], BF16, kind=IK).ap()
    H["bgrid_d"] = nc.dram_tensor("bgrid", [16, 8], F32).ap()
    H["bgrid_flat_d"] = H["bgrid_d"].rearrange("a b -> (a b)").rearrange("(x y) -> x y", x=1)
    H["qT_scr_d"] = nc.dram_tensor("qT_scr", [128, CC * S], BF16).ap()
    H["kT_scr_d"] = nc.dram_tensor("kT_scr", [128, CC * S], BF16).ap()
    H["v_scr_d"] = nc.dram_tensor("v_scr", [128, TT * 1024], BF16).ap()
    H["u_in_0"] = H["u0_d"]
    for li in range(6):
        H[f"u_mid_{li}"] = nc.dram_tensor(f"u_mid_{li}", [S, D], F32, kind=IK).ap()
        H[f"u_out_{li}"] = nc.dram_tensor(f"u_pp_{li}", [S, D], F32, kind=IK).ap()
        if li < 5:
            H[f"u_in_{li + 1}"] = H[f"u_out_{li}"]
    H["g0_d"] = H["u_out_5"]
    H["x1_d"] = nc.dram_tensor("x1", [S, D], F32, kind=IK).ap()
    H["xn1T_d"] = nc.dram_tensor("xn1T", [128, CC * GW], BF16, kind=IK).ap()
    H["g1_d"] = nc.dram_tensor("g1", [S, D], F32, kind=IK).ap()
    H["out_d"] = nc.dram_tensor("out", [S, D], F32, kind="ExternalOutput").ap()
    def sl3(apfull, li):
        return apfull[li:li + 1, :, :].rearrange("l a b -> (l a) b")
    H["Wqkv_d"] = {li: sl3(H["Wqkv_full_d"], li) for li in range(6)}
    H["Wo6_d"] = {li: sl3(H["Wo6_full_d"], li) for li in range(6)}
    H["W1r_d"] = {li: sl3(H["W1r_full_d"], li) for li in range(6)}
    H["W2_d"] = {li: sl3(H["W2_full_d"], li) for li in range(6)}
    H["cols_d"] = {li: sl3(H["cols_full_d"], li) for li in range(6)}
    H["b2_d"] = {li: H["b2_full_d"][li:li + 1, :] for li in range(6)}

    with ExitStack() as ctx:
        e = ctx.enter_context
        H["cpool"] = e(nc.sbuf_tensor("cpool", [128, 16], F32))
        ps_f = [e(nc.psum_tensor(f"psf{i}", [128, 512], F32))[:, :] for i in range(6)]
        H["ps_f"] = ps_f
        H["psb0"] = e(nc.psum_tensor("psb0", [128, 128], BF16))[:, :]
        H["psb1"] = e(nc.psum_tensor("psb1", [128, 128], BF16))[:, :]
        Sch = Sched()
        Sch._const_pool = H["cpool"]
        mk_streams(Sch, H)

        if "P1" in phases:
            with ExitStack() as c1:
                e1 = c1.enter_context
                p1 = _NCProxy(nc, "p1_")
                H["x_sb"] = e1(p1.sbuf_tensor("x_sb", [128, TT * D], F32))
                H["u_sb"] = e1(p1.sbuf_tensor("u_sb", [128, TT * D], F32))
                H["xnbf_sb"] = e1(p1.sbuf_tensor("xnbf", [128, D], BF16))
                H["xnT_sb"] = e1(p1.sbuf_tensor("xnT_sb", [128, CC * GW], BF16))
                H["mv_sb"] = e1(p1.sbuf_tensor("mv", [128, 2], F32))
                H["st6_sb"] = e1(p1.sbuf_tensor("st6", [128, 12], F32))
                H["rstd_sb"] = e1(p1.sbuf_tensor("rstd", [128, 2], F32))
                H["ident_sb"] = e1(p1.sbuf_tensor("ident_sb", [128, 128], BF16))
                Hp1 = dict(H)
                ld_id = Hp1["Wt"].dma(
                    lambda e2, s: e2.dma_start(Hp1["ident_sb"][:, :], Hp1["ident_d"]))
                Sch.op("tensor", lambda e2, s: None, waits=(ld_id,), inc=False)
                emit_P1(nc, Sch, Hp1)
                Sch.barrier()

        if "P2" in phases:
            with ExitStack() as c2:
                e2_ = c2.enter_context
                p2 = _NCProxy(nc, "p2_")
                alloc_p2(p2, e2_, H)
                Hp2 = dict(H)
                p2_common_setup(nc, Sch, Hp2)
                ld_uTg = Hp2["LdIn"].dma(
                    lambda e2, s: e2.dma_start(Hp2["RA_sb"][:, :], Hp2["xnT_d"]))
                Hp2["_p2_uTg_rdy"] = ld_uTg
                for li in p2_lis:
                    emit_P2_sublayer(nc, Sch, Hp2, li, first=(li == p2_lis[0]), last=(li == 5))
                Sch.barrier()

        if any(p.startswith("P3stub") for p in phases):
            mode = [p for p in phases if p.startswith("P3stub")][0]
            with ExitStack() as c3:
                e3 = c3.enter_context
                p3 = _NCProxy(nc, "p3_")
                alloc_p3(p3, e3, H)
                B = H["B_sb"]
                F = H["F_sb"]
                bar_toks = tuple(Sch._tok(s) for s in sorted(Sch.sem_names))
                if mode == "P3stubX":
                    Sch.last_waited["sync"] = {}
                if mode == "P3stubH":
                    scrh = nc.dram_tensor("stub_scr", [128, CC * GW], BF16).ap()
                    H["LdIn"].dma(lambda e2, s: e2.dma_start(scrh, H["xnT_d"]))
                    H["LdIn"].dma(lambda e2, s: e2.dma_start(
                        scrh[:, 0:128], H["ident_d"]))
                    H["LdIn"].dma(lambda e2, s: e2.dma_start(
                        scrh[:, 128:256], H["ident_d"]))
                if mode in ("P3stub", "P3stubL", "P3stubX"):
                    w0 = bar_toks if mode == "P3stubX" else ()
                    H["LdIn"].dma(lambda e2, s: e2.dma_start(H["xnT_sb"][:, :], H["xnT_d"]),
                                  waits=w0)
                    H["LdIn"].dma(lambda e2, s: e2.dma_start(
                        H["dmask_sb"][:, :].rearrange("p (r n) -> p r n", n=512),
                        H["dmask_d"].rearrange("r p n -> p r n")), waits=w0)
                    H["LdIn"].dma(lambda e2, s: e2.dma_start(H["ident_sb"][:, :], H["ident_d"]),
                                  waits=w0)
                if mode in ("P3stub", "P3stubW"):
                    for dst, hbm in ((B[:, 0:8192], H["Wq_d"]), (B[:, 8192:16384], H["Wk_d"]),
                                     (F[:, :], H["Wv_d"])):
                        H["Wt"].dma(lambda e2, s, dst=dst, hbm=hbm: e2.dma_start(
                            dst.rearrange("p (c n) -> p c n", n=1024),
                            hbm.rearrange("(c p) n -> p c n", p=128)))
                Sch.barrier()
        if "P3" in phases:
            with ExitStack() as c3:
                e3 = c3.enter_context
                p3 = _NCProxy(nc, "p3_")
                alloc_p3(p3, e3, H)
                Hp3 = dict(H)
                emit_P3(nc, Sch, Hp3)
                Sch.barrier()

        if "P4a" in phases:
            with ExitStack() as c4:
                e4 = c4.enter_context
                p4 = _NCProxy(nc, "p4a_")
                alloc_p4a(p4, e4, H)
                Hp4 = dict(H)
                emit_P4a(nc, Sch, Hp4)
                Sch.barrier()

        if "P4b" in phases:
            with ExitStack() as c5:
                e5 = c5.enter_context
                p5 = _NCProxy(nc, "p4b_")
                alloc_p4b(p5, e5, H)
                Hp5 = dict(H)
                Hp5["ps_f"] = ps_f[0:5]
                Hp5["ps_n"] = ps_f[5]
                emit_P4b(nc, Sch, Hp5)
        finish_program(nc, Sch, H)
    return nc


FUSED = True


def kernel(**inputs):
    d = host_prep(inputs)
    hp2 = host_prep_p2(inputs)
    x = d["x"]
    n = NCORES

    if FUSED:
        nc = build_fused_program()
        shared = {
            "ident": d["ident"],
            "Wq": d["L0_attn_Wq"], "Wk": d["L0_attn_Wk"],
            "Wv": d["L0_attn_Wv"], "Wo": d["L0_attn_Wo"],
            "dmask": d["diagmask"].astype(ml_dtypes.bfloat16),
            "Wqkv": hp2["Wqkv"], "Wo6": hp2["Wo6"], "W1r": hp2["W1r"],
            "W2": hp2["W2"], "colsP2": hp2["colsP2"], "b2rows": hp2["b2rows"],
            "tblP2": hp2["tblP2"], "masksP2": hp2["masksP2"],
            "ind16": hp2["ind16"], "sel16": hp2["sel16"],
            "WrT": d["WrT"], "WkT2": d["WkT2"], "WvT2": d["WvT2"],
            "mu_r_cols": d["mu_r_cols"], "mu_k_cols": d["mu_k_cols"],
            "WqT": d["L1_WqT"], "WkT": d["L1_WkT"], "WvT": d["L1_WvT"],
            "WoT": d["L1_WoT"], "ropeC": d["ropeC"], "ropeS": d["ropeS"],
            "tri01": d["tri01"],
        }
        in_maps = []
        for b in range(n):
            m = dict(shared)
            m["x"] = np.ascontiguousarray(x[b])
            in_maps.append(m)
        res = run_bass_kernel_spmd(nc, in_maps, core_ids=list(range(n))).results
        return np.stack([np.asarray(r["out"], np.float32) for r in res], axis=0)

    def run(nc, per_core, shared):
        in_maps = []
        for b in range(n):
            m = dict(shared)
            for k, v in per_core.items():
                m[k] = v[b]
            in_maps.append(m)
        return run_bass_kernel_spmd(nc, in_maps, core_ids=list(range(n))).results

    ident = d["ident"]
    nc1 = build_p1_program()
    r1 = run(nc1, {"x": [np.ascontiguousarray(x[b]) for b in range(n)]},
             {"ident": ident})
    u0 = [np.asarray(r["u0"]) for r in r1]
    xnT = [np.asarray(r["xnT"]) for r in r1]
    u_cur = u0
    uTg_cur = xnT
    for li in range(6):
        nc2 = build_p2_program([li])
        shared = {
            "Wqkv": np.ascontiguousarray(hp2["Wqkv"][li:li + 1]),
            "Wo6": np.ascontiguousarray(hp2["Wo6"][li:li + 1]),
            "W1r": np.ascontiguousarray(hp2["W1r"][li:li + 1]),
            "W2": np.ascontiguousarray(hp2["W2"][li:li + 1]),
            "colsP2": np.ascontiguousarray(hp2["colsP2"][li:li + 1]),
            "b2rows": np.ascontiguousarray(hp2["b2rows"][li:li + 1]),
            "tblP2": hp2["tblP2"], "masksP2": hp2["masksP2"],
            "ident": ident, "ind16": hp2["ind16"], "sel16": hp2["sel16"],
        }
        r2 = run(nc2, {"u_in": u_cur, "uTg_in": uTg_cur}, shared)
        u_cur = [np.asarray(r["u_out"]) for r in r2]
        uTg_cur = [np.asarray(r["uTg_out"]) for r in r2]
    g0 = u_cur
    nc3 = build_p3_program()
    r3 = run(nc3, {"xnT": xnT,
                   "x": [np.ascontiguousarray(x[b]) for b in range(n)],
                   "g0": g0},
             {"Wq": d["L0_attn_Wq"], "Wk": d["L0_attn_Wk"],
              "Wv": d["L0_attn_Wv"], "Wo": d["L0_attn_Wo"],
              "dmask": d["diagmask"].astype(ml_dtypes.bfloat16), "ident": ident})
    x1 = [np.asarray(r["x1"]) for r in r3]
    xn1T = [np.asarray(r["xn1T"]) for r in r3]
    nc4a = build_p4a_program()
    r4a = run(nc4a, {"xn1T": xn1T},
              {"WrT": d["WrT"], "WkT2": d["WkT2"], "WvT2": d["WvT2"],
               "mu_r_cols": d["mu_r_cols"], "mu_k_cols": d["mu_k_cols"]})
    g1 = [np.asarray(r["g1"]) for r in r4a]
    nc4b = build_p4b_program()
    r4b = run(nc4b, {"xn1T": xn1T, "x1": x1, "g1": g1},
              {"WqT": d["L1_WqT"], "WkT": d["L1_WkT"], "WvT": d["L1_WvT"],
               "WoT": d["L1_WoT"], "ropeC": d["ropeC"], "ropeS": d["ropeS"],
               "tri01": d["tri01"], "ident": ident})
    out = np.stack([np.asarray(r["out"], np.float32) for r in r4b], axis=0)
    return out


# version 09 (final)
# speedup vs baseline: 1.0541x; 1.0038x over previous
import numpy as np
import ml_dtypes
import concourse.bass as bass
import concourse.mybir as mybir
from concourse.bass_utils import run_bass_kernel_spmd
from contextlib import ExitStack

F32 = mybir.dt.float32
BF16 = mybir.dt.bfloat16
AF = mybir.ActivationFunctionType
ALU = mybir.AluOpType
AX = mybir.AxisListType

S = 2048
D = 1024
TT = S // 128
CC = D // 128
GW = S + 1

NCORES = 8


ENGINES = ("tensor", "vector", "scalar", "gpsimd", "sync")
ENG_SEM = {"tensor": "T", "vector": "V", "scalar": "Sc", "gpsimd": "G", "sync": "Sy"}


class Sched:
    def __init__(self):
        self.ops = {e: [] for e in ENGINES}
        self.count = {}
        self.last_waited = {e: {} for e in ENGINES}
        self.sem_names = set(ENG_SEM.values())
        self._consts = {}
        self._const_pool = None

    def _tok(self, sem):
        return (sem, self.count.get(sem, 0))

    def op(self, eng, fn, waits=(), inc=True, dma_sem=None):
        sem = dma_sem if dma_sem else ENG_SEM[eng]
        amount = 16 if dma_sem else 1
        tgt = None
        if inc:
            self.count[sem] = self.count.get(sem, 0) + amount
            tgt = self.count[sem]
        lw = self.last_waited[eng]
        eff_waits = []
        for (ws, wc) in waits:
            if wc <= 0:
                continue
            if lw.get(ws, 0) >= wc:
                continue
            eff_waits.append((ws, wc))
            lw[ws] = wc
        self.ops[eng].append((tuple(eff_waits), fn, sem if inc else None, amount))
        return (sem, tgt if tgt is not None else self.count.get(sem, 0))

    def barrier(self):
        toks = [self._tok(s) for s in self.sem_names]
        for e in ENGINES:
            def mk(fn_eng=e):
                def fn(eng, sems):
                    return None
                return fn
            self.op(e, mk(), waits=toks, inc=False)
        return toks

    def const_col(self, name_val):
        val = name_val
        key = float(val)
        if key not in self._consts:
            idx = len(self._consts)
            assert idx < self._const_pool.shape[1] if hasattr(self._const_pool, "shape") else True
            tok = self.op("vector", lambda e, s, idx=idx, val=val: e.memset(
                self._const_pool[:, idx:idx + 1], val))
            self._consts[key] = (idx, tok)
        idx, tok = self._consts[key]
        return self._const_pool[:, idx:idx + 1], tok

    def run(self, nc, sems):
        with nc.Block() as block:
            for eng_name in ENGINES:
                deco = getattr(block, eng_name)
                ops = self.ops[eng_name]

                def make_body(ops=ops):
                    def body(eng):
                        for eff_waits, fn, sem, amount in ops:
                            for (ws, wc) in eff_waits:
                                eng.wait_ge(sems[ws], wc)
                            ins = fn(eng, sems)
                            if sem is not None:
                                assert ins is not None, "op with inc must return instruction"
                                ins.then_inc(sems[sem], amount)
                    return body

                deco(make_body())


class Stream:

    def __init__(self, sched, eng, name, depth=2):
        self.S = sched
        self.eng = eng
        self.depth = depth
        self.semn = [f"{name}{i}" for i in range(depth)]
        for s in self.semn:
            sched.sem_names.add(s)
        self.j = 0

    def dma(self, fn, waits=()):
        i = self.j % self.depth
        sem = self.semn[i]
        w = list(waits)
        if self.j >= self.depth:
            w.append((sem, self.S.count.get(sem, 0)))
        tok = self.S.op(self.eng, fn, waits=w, dma_sem=sem)
        self.j += 1
        return tok

    def total_toks(self):
        return [(s, self.S.count.get(s, 0)) for s in self.semn]


def mk_streams(Sch, H):
    H["LdIn"] = Stream(Sch, "sync", "LdIn", 4)
    H["LdA"] = Stream(Sch, "sync", "LdA", 2)
    H["LdB"] = Stream(Sch, "sync", "LdB", 2)
    H["St"] = Stream(Sch, "sync", "St", 2)
    H["Wt"] = Stream(Sch, "gpsimd", "Wt", 4)
    H["_streams"] = [H["LdIn"], H["LdA"], H["LdB"], H["St"], H["Wt"]]


def finish_program(nc, Sch, H):
    waits = []
    for st in H["_streams"]:
        waits += st.total_toks()
    Sch.op("sync", lambda e2, s: None, waits=waits, inc=False)
    semstack = ExitStack()
    sems = {}
    for name in sorted(Sch.sem_names):
        sems[name] = semstack.enter_context(nc.semaphore(name))
    Sch.run(nc, sems)
    semstack.close()



def _rotary_tables_l0ffn():
    DIM_HEAD = 64
    WINDOW = 256
    half = np.arange(0, DIM_HEAD, 2, dtype=np.float64)
    inv_freq = 1.0 / (10000.0 ** (half / DIM_HEAD))
    t = np.arange(S, dtype=np.float64)
    freqs = t[:, None] * inv_freq[None, :]
    freqs = np.concatenate([freqs, freqs], -1)
    base_scale = (half + 0.4 * DIM_HEAD) / (1.4 * DIM_HEAD)
    power = (t - S // 2) / (WINDOW // 2)
    scale = base_scale[None, :] ** power[:, None]
    scale = np.concatenate([scale, scale], -1)
    cos, sin = np.cos(freqs), np.sin(freqs)
    inv = 1.0 / scale
    cs_q = (cos * scale).T
    sn_q_t = (sin * scale).T
    cs_k = (cos * inv).T
    sn_k_t = (sin * inv).T
    def signed(sn):
        s = sn.copy()
        s[:32, :] *= -1.0
        return s
    sn_q = signed(sn_q_t)
    sn_k = signed(sn_k_t)
    rot = np.array([(p + 32) % 64 if (p % 64) < 32 else p - 32 for p in range(64)])
    sn_q = sn_q[rot]
    sn_k = sn_k[rot]
    def dup(a):
        return np.concatenate([a, a], axis=0).astype(np.float32)
    return dup(cs_q), dup(sn_q), dup(cs_k), dup(sn_k)


def _rope_tables_l1():
    HEAD_DIM = 64
    pos = np.arange(S, dtype=np.float64)
    inv_freq = 10000.0 ** (-np.arange(0, HEAD_DIM, 2, dtype=np.float64) / HEAD_DIM)
    ang = pos[:, None] * inv_freq[None, :]
    C = np.cos(ang).astype(np.float32)
    Sn = np.sin(ang).astype(np.float32)
    Ct = np.concatenate([C[t * 128:(t + 1) * 128] for t in range(TT)], axis=1)
    St = np.concatenate([Sn[t * 128:(t + 1) * 128] for t in range(TT)], axis=1)
    return Ct.astype(np.float32), St.astype(np.float32)


def _l0ffn_window_masks():
    masks = np.zeros((4, 128, 256), dtype=np.float32)
    i = np.arange(128)[:, None]
    j = np.arange(256)[None, :]
    for kt in range(4):
        k = kt * 128 + i
        masks[kt] = np.where((j >= k - 256) & (j <= k), 0.0, -1.0e30)
    return masks.astype(ml_dtypes.bfloat16)


def _l0attn_diag_masks():
    m = np.zeros((4, 128, 512), dtype=np.float32)
    i = np.arange(128)[:, None]
    j = np.arange(512)[None, :]
    for r in range(4):
        m[r] = np.where(j > r * 128 + i, -1.0e30, 0.0)
    return m


def _l1_tri_mask():
    i = np.arange(128)[:, None]
    j = np.arange(128)[None, :]
    return (i <= j).astype(ml_dtypes.bfloat16)


def _reorder_w1(W1, b1, W2):
    DI = W2.shape[-2]
    nb = (DI + 127) // 128
    colblocks = []
    sizes = []
    for t in range(nb):
        lo, hi = t * 128, min((t + 1) * 128, DI)
        colblocks.append((np.arange(lo, hi), np.arange(lo, hi) + DI))
        sizes.append(hi - lo)
    order_a = []
    order = []
    for (ai, gi) in colblocks:
        order.append(("a", ai))
        order.append(("g", gi))
    col_idx = np.concatenate([idx for _, idx in order])
    W1r = W1[..., col_idx]
    b1r = b1[..., col_idx]
    W2r = W2
    return np.ascontiguousarray(W1r), np.ascontiguousarray(b1r), W2r, sizes




def host_prep_p2(inputs):
    f32 = np.float32
    W1 = np.asarray(inputs["L0_ffn_W1"], f32)
    b1 = np.asarray(inputs["L0_ffn_b1"], f32)
    W2 = np.asarray(inputs["L0_ffn_W2"], f32)
    W1r, b1r, W2r, _ = _reorder_w1(W1, b1, W2)
    qs = np.asarray(inputs["L0_ffn_q_scale"], f32)
    ks = np.asarray(inputs["L0_ffn_k_scale"], f32)
    lng = np.asarray(inputs["L0_ffn_ln_g"], f32).reshape(6, 8, 128)
    lnb = np.asarray(inputs["L0_ffn_ln_b"], f32).reshape(6, 8, 128)
    cols = np.zeros((6, 128, 62), f32)
    cols[:, :, 0] = np.concatenate([qs, qs], -1)
    cols[:, :, 1] = np.concatenate([ks, ks], -1)
    cols[:, :, 2:10] = lng.transpose(0, 2, 1)
    cols[:, :, 10:18] = lnb.transpose(0, 2, 1)
    off = 0
    ci = 18
    for t in range(21):
        for _ in range(2):
            cols[:, :, ci] = b1r[:, off:off + 128]
            off += 128
            ci += 1
    a42 = np.zeros((6, 128), f32); a42[:, :42] = b1r[:, off:off + 42]; off += 42
    g42 = np.zeros((6, 128), f32); g42[:, :42] = b1r[:, off:off + 42]; off += 42
    cols[:, :, 60] = a42
    cols[:, :, 61] = g42
    cs_q, sn_q, cs_k, sn_k = _rotary_tables_l0ffn()
    tbl = np.stack([cs_q, sn_q, cs_k, sn_k], axis=1).astype(ml_dtypes.bfloat16)
    ind = np.zeros((128, 8, 16), f32)
    for qc in range(8):
        ind[:64, qc, 2 * qc] = 1.0
        ind[64:, qc, 2 * qc + 1] = 1.0
    sel = np.zeros((16, 8, 128), f32)
    for e_ in range(8):
        sel[2 * e_, e_, 0:64] = 1.0
        sel[2 * e_ + 1, e_, 64:128] = 1.0
    return {
        "Wqkv": np.ascontiguousarray(np.asarray(inputs["L0_ffn_Wqkv"], f32)),
        "Wo6": np.ascontiguousarray(np.asarray(inputs["L0_ffn_Wo"], f32)),
        "W1r": np.ascontiguousarray(W1r),
        "W2": np.ascontiguousarray(W2r),
        "colsP2": np.ascontiguousarray(cols),
        "b2rows": np.asarray(inputs["L0_ffn_b2"], f32).astype(ml_dtypes.bfloat16),
        "tblP2": np.ascontiguousarray(tbl),
        "masksP2": np.ascontiguousarray(_l0ffn_window_masks()),
        "ind16": ind.reshape(128, 128).astype(ml_dtypes.bfloat16),
        "sel16": np.ascontiguousarray(sel.reshape(16, 8 * 128).astype(ml_dtypes.bfloat16)),
    }


def host_prep(inputs):
    f32 = np.float32
    d = {}
    x = np.asarray(inputs["L0_attn_x"], f32)
    d["x"] = x
    for k in ("L0_attn_Wq", "L0_attn_Wk", "L0_attn_Wv", "L0_attn_Wo"):
        d[k] = np.ascontiguousarray(np.asarray(inputs[k], f32))
    d["Wqkv"] = np.ascontiguousarray(np.asarray(inputs["L0_ffn_Wqkv"], f32))
    d["Wo6"] = np.ascontiguousarray(np.asarray(inputs["L0_ffn_Wo"], f32))
    W1 = np.asarray(inputs["L0_ffn_W1"], f32)
    b1 = np.asarray(inputs["L0_ffn_b1"], f32)
    W2 = np.asarray(inputs["L0_ffn_W2"], f32)
    W1r, b1r, W2r, blk = _reorder_w1(W1, b1, W2)
    d["W1r"] = W1r
    d["W2"] = np.ascontiguousarray(W2r)
    d["b2"] = np.ascontiguousarray(np.asarray(inputs["L0_ffn_b2"], f32))
    nb2 = b1r.shape[-1] // 0 if False else None
    cols = []
    off = 0
    for t in range(21):
        for part in range(2):
            cols.append(b1r[:, off:off + 128])
            off += 128
    a42 = np.zeros((6, 128), f32); a42[:, :42] = b1r[:, off:off + 42]; off += 42
    g42 = np.zeros((6, 128), f32); g42[:, :42] = b1r[:, off:off + 42]; off += 42
    cols.append(a42); cols.append(g42)
    assert off == b1r.shape[1]
    d["b1cols"] = np.ascontiguousarray(np.stack(cols, axis=1))
    qs = np.asarray(inputs["L0_ffn_q_scale"], f32)
    ks = np.asarray(inputs["L0_ffn_k_scale"], f32)
    d["qs_cols"] = np.ascontiguousarray(np.concatenate([qs, qs], 1))
    d["ks_cols"] = np.ascontiguousarray(np.concatenate([ks, ks], 1))
    g = np.asarray(inputs["L0_ffn_ln_g"], f32).reshape(6, 8, 128)
    b = np.asarray(inputs["L0_ffn_ln_b"], f32).reshape(6, 8, 128)
    d["lng"] = np.ascontiguousarray(g)
    d["lnb"] = np.ascontiguousarray(b)
    d["L1_WqT"] = np.ascontiguousarray(np.asarray(inputs["L1_attn_Wq"], f32).T)
    d["L1_WkT"] = np.ascontiguousarray(np.asarray(inputs["L1_attn_Wk"], f32).T)
    d["L1_WvT"] = np.ascontiguousarray(np.asarray(inputs["L1_attn_Wv"], f32).T)
    d["L1_WoT"] = np.ascontiguousarray(np.asarray(inputs["L1_attn_Wo"], f32).T)
    d["WrT"] = np.ascontiguousarray(np.asarray(inputs["L1_ffn_W_r"], f32).T)
    d["WkT2"] = np.ascontiguousarray(np.asarray(inputs["L1_ffn_W_k"], f32).T)
    d["WvT2"] = np.ascontiguousarray(np.asarray(inputs["L1_ffn_W_v"], f32).T)
    mu_r = np.asarray(inputs["L1_ffn_mu_r"], f32).reshape(CC, 128).T
    mu_k = np.asarray(inputs["L1_ffn_mu_k"], f32).reshape(CC, 128).T
    d["mu_r_cols"] = np.ascontiguousarray(mu_r)
    d["mu_k_cols"] = np.ascontiguousarray(mu_k)
    d["ident"] = np.eye(128, dtype=ml_dtypes.bfloat16)
    cs_q, sn_q, cs_k, sn_k = _rotary_tables_l0ffn()
    d["cs_q"], d["sn_q"], d["cs_k"], d["sn_k"] = cs_q, sn_q, cs_k, sn_k
    C1, S1 = _rope_tables_l1()
    d["ropeC"], d["ropeS"] = C1, S1
    d["wmask01"] = _l0ffn_window_masks()
    d["diagmask"] = _l0attn_diag_masks()
    d["tri01"] = _l1_tri_mask()
    ind = np.zeros((128, 8, 16), np.float32)
    for qc in range(8):
        ind[:64, qc, 2 * qc] = 1.0
        ind[64:, qc, 2 * qc + 1] = 1.0
    d["ind16"] = ind.reshape(128, 128).astype(ml_dtypes.bfloat16)
    ones_col = np.ones((128, 1), np.float32)
    d["ones_col_bf"] = ones_col.astype(ml_dtypes.bfloat16)
    d["b2_rows"] = np.asarray(inputs["L0_ffn_b2"], f32).astype(ml_dtypes.bfloat16)
    return d



def transpose_tiles(Sch, H, src_fn, dst_fn, ntiles, psb, ps_tok, waits=()):
    ev = None
    for i in range(ntiles):
        b = i % 2
        t_tok = Sch.op("tensor", lambda e, s, i=i, b=b: e.transpose(
            psb[b], src_fn(i), H["ident_sb"][:, :]),
            waits=tuple(waits) + (ps_tok[b],))
        ev = Sch.op("scalar", lambda e, s, i=i, b=b: e.copy(dst_fn(i), psb[b]),
                    waits=(t_tok,))
        ps_tok[b] = ev
    return ev, ps_tok



def emit_P1(nc, Sch, H):
    x_sb = H["x_sb"]
    u_sb = H["u_sb"]
    xnbf = H["xnbf_sb"]
    xnT = H["xnT_sb"]
    mv = H["mv_sb"]
    st6 = H["st6_sb"]
    rstd = H["rstd_sb"]
    psb = [H["psb0"], H["psb1"]]

    for cc in range(CC):
        z_tok = Sch.op("vector", lambda e, s, cc=cc: e.memset(
            xnT[:, cc * GW: cc * GW + 1], 0.0))
    ld = H["LdIn"].dma(lambda e, s: e.dma_start(
        x_sb[:, :].rearrange("p (t d) -> p t d", d=D),
        H["x_d"].rearrange("(t p) d -> p t d", p=128)))

    ps_tok = [("T", 0), ("T", 0)]
    last_ev = None
    for t in range(TT):
        xt = x_sb[:, t * D:(t + 1) * D]
        ut = u_sb[:, t * D:(t + 1) * D]
        s1a = Sch.op("vector", lambda e, s, xt=xt: e.bn_stats(st6[:, 0:6], xt[:, 0:512]), waits=(ld,))
        s1b = Sch.op("vector", lambda e, s, xt=xt: e.bn_stats(st6[:, 6:12], xt[:, 512:1024]), waits=(ld,))
        s2 = Sch.op("vector", lambda e, s: e.bn_aggr(mv[:, :], st6[:, :]), waits=(s1a, s1b))
        eps_ap, eps_tok = Sch.const_col(1e-5)
        s3 = Sch.op("scalar", lambda e, s, eps_ap=eps_ap: e.activation(
            rstd[:, 0:1], mv[:, 1:2], AF.Sqrt, bias=eps_ap), waits=(s2, eps_tok))
        s4 = Sch.op("vector", lambda e, s: e.reciprocal(rstd[:, 1:2], rstd[:, 0:1]), waits=(s3,))
        s5 = Sch.op("vector", lambda e, s, xt=xt, ut=ut: e.tensor_scalar(
            ut, xt, mv[:, 0:1], rstd[:, 1:2], ALU.subtract, ALU.mult), waits=(s4,))
        s6 = Sch.op("vector", lambda e, s, ut=ut: e.tensor_copy(xnbf[:, :], ut),
                    waits=(s5, last_ev if last_ev else ("T", 0)))
        last_ev, ps_tok = transpose_tiles(
            Sch, H,
            lambda i: xnbf[:, i * 128:(i + 1) * 128],
            lambda i, t=t: xnT[:, i * GW + 1 + t * 128: i * GW + 1 + (t + 1) * 128],
            CC, psb, ps_tok, waits=(s6, z_tok))
    st1 = H["St"].dma(lambda e, s: e.dma_start(
        H["u0_d"].rearrange("(t p) d -> p t d", p=128),
        u_sb[:, :].rearrange("p (t d) -> p t d", d=D)),
        waits=(("V", Sch.count.get("V", 0)),))
    st2 = H["St"].dma(lambda e, s: e.dma_start(H["xnT_d"], xnT[:, :]),
                 waits=(("Sc", Sch.count.get("Sc", 0)), z_tok))
    return [st1, st2]



W0 = 256
NW = S // W0
NHD = 16
DI2 = 2730
NPAIR = 22


def emit_P2_sublayer(nc, Sch, H, li, first, last):
    RA = H["RA_sb"]; RB = H["RB_sb"]; RC = H["RC_sb"]; RD = H["RD_sb"]
    RE = H["RE_sb"]; RF = H["RF_sb"]; RG = H["RG_sb"]
    ps = H["ps_f"]; psb = [H["psb0"], H["psb1"]]
    ident = H["ident_sb"]
    masks = H["masks_sb"]; expb = H["expb_sb"]; expm = H["expm_sb"]
    sqb = H["sqb_sb"]
    nfsc = H["nfsc_sb"]; nffsc = H["nffsc_sb"]; nrmsc = H["nrmsc_sb"]
    gridB = H["gridB_sb"]
    negrow = H["negrow_sb"]; negbc = H["negbc_sb"]
    oun = H["oun_sb"]; recb = H["recb_sb"]
    rec64 = H["rec64_sb"]
    statrow = RG.bitcast(F32)[0:1, 0:2048]
    cols = H["cols_sb"]; b2row = H["b2row_sb"]
    onec = H["ones_col_sb"]; onerf = H["ones_rowf_sb"]; onerb = H["ones_row_sb"]
    ind16 = H["ind16_sb"]; sel16 = H["sel16_sb"]
    ut = [H["uta_sb"], H["utb_sb"]]
    ub = [H["uba_sb"], H["ubb_sb"]]
    wstr = [H["wsa_sb"], H["wsb_sb"]]

    def ubf_r(ccj):
        return (expb if ccj < 4 else expm)[:, (ccj % 4) * 128:(ccj % 4) * 128 + 128]
    qv = H["uba_sb"][:, 0:512]
    t1b = H["uta_sb"][:, 0:512]
    t2b = H["uta_sb"][:, 512:1024]
    u_in = H[f"u_in_{li}"]; u_mid = H[f"u_mid_{li}"]; u_out = H[f"u_out_{li}"]
    qT_d = H["qT_scr_d"]; kT_d = H["kT_scr_d"]; v_d = H["v_scr_d"]
    bank_war = H.setdefault("_p2_bank_war", {i: ("Sc", 0) for i in range(6)})
    psb_war = H.setdefault("_p2_psb_war", [("Sc", 0), ("Sc", 0)])
    uTg_rdy = H["_p2_uTg_rdy"]

    def uTg_r(cc, rng):
        off = 0 if cc >= 4 else 1
        return RA[:, cc * GW + off + rng.start: cc * GW + off + rng.stop]

    qwin = [RC[:, 0:2048], RC[:, 10240:12288]]
    kwin = [RC[:, 2048:6144], RC[:, 12288:16384]]
    vwin = [RC[:, 6144:10240], RD[:, 12288:16384]]
    stage = RD[:, 0:4096]
    vstage = [RD[:, 4096:4608], RD[:, 4608:5120]]

    ld_wo = H["Wt"].dma(lambda e, s: e.dma_start(
        RF[:, 0:CC * 1024].rearrange("p (c n) -> p c n", n=1024),
        H["Wo6_d"][li].rearrange("(c p) n -> p c n", p=128)),
        waits=(H.get("_p2_rf_war", ("T", 0)),))
    ld_cols = H["LdIn"].dma(lambda e, s: e.dma_start(
        cols[:, 0:62], H["cols_d"][li]), waits=(H.get("_p2_cols_war", ("V", 0)),))
    ld_b2 = H["LdIn"].dma(lambda e, s: e.dma_start(
        b2row[0:1, :], H["b2_d"][li]), waits=(H.get("_p2_b2_war", ("T", 0)),))
    qs_col = cols[:, 0:1]; ks_col = cols[:, 1:2]
    lng = cols[:, 2:10]; lnb = cols[:, 10:18]; b1c = cols[:, 18:62]
    CSQ = wstr[0][:, 1024:1536]; SNQ = wstr[0][:, 1536:2048]
    CSK = wstr[1][:, 1024:1536]; SNK = wstr[1][:, 1536:2048]

    zgrid = Sch.op("vector", lambda e, s: e.memset(gridB[0:16, :], 0.0),
                   waits=(H.get("_p2_grid_war", ("V", 0)),))
    H["_p2_grid_war"] = zgrid

    sq_war = H.get("_p2_sq_war", ("T", 0))
    qv_war = H.get("_p2_qv_war", ("V", 0))
    v_st = {}
    q_st = {}
    k_st = {}
    for j in range(4):
        tbl_toks = []
        for ti, dst in enumerate((CSQ, SNQ, CSK, SNK)):
            tbl_toks.append(H["LdIn"].dma(lambda e, s, ti=ti, dst=dst, j=j: e.dma_start(
                dst, H["tbl_d"][:, ti, j * 512:(j + 1) * 512]),
                waits=(H.get(f"_p2_tbl_war_{ti}", ("V", 0)),)))
        for h2 in range(2):
            ldv = H["Wt"].dma(lambda e, s, h2=h2: e.dma_start(
                RG[:, 0:4096].rearrange("p (c n) -> p c n", n=512),
                H["Wqkv_d"][li].rearrange("(c p) n -> p c n", p=128)[:, :, 2048 + h2 * 512: 2048 + (h2 + 1) * 512]),
                waits=(("T", Sch.count.get("T", 0)),))
            for tt in range(4):
                t_abs = 4 * j + tt
                b = tt % 2
                vb = (t_abs * 2 + h2) % 2
                mmx = None
                for ccx in range(CC):
                    mmx = Sch.op("tensor", lambda e, s, t_abs=t_abs, ccx=ccx, b=b: e.matmul(
                        ps[b][:, :], uTg_r(ccx, slice(t_abs * 128, (t_abs + 1) * 128)),
                        RG[:, ccx * 512:(ccx + 1) * 512],
                        start=(ccx == 0), stop=(ccx == CC - 1)),
                        waits=(ldv, uTg_rdy, bank_war[b]), inc=(ccx == CC - 1))
                ev = Sch.op("scalar", lambda e, s, vb=vb, b=b: e.copy(
                    vstage[vb][:, :], ps[b][:, :]),
                    waits=(mmx, v_st.get((t_abs - 1, h2), ("Sc", 0)),
                           H.get("_p2_vstage_war_%d" % vb, ("V", 0))))
                bank_war[b] = ev
                stv = H["St"].dma(lambda e, s, t_abs=t_abs, h2=h2, vb=vb: e.dma_start(
                    v_d[:, t_abs * 1024 + h2 * 512: t_abs * 1024 + (h2 + 1) * 512],
                    vstage[vb][:, :]), waits=(ev,))
                H["_p2_vstage_war_%d" % vb] = stv
                v_st[(t_abs, h2)] = stv
        for which, wq, scr_d, st_map in (("q", 0, qT_d, q_st), ("k", 1024, kT_d, k_st)):
            raw_mm = None
            for e_ in range(CC):
                b = e_ % 2
                ldw = H["Wt"].dma(lambda e, s, e_=e_, wq=wq: e.dma_start(
                    wstr[e_ % 2][:, 0:1024].rearrange("p (c m) -> p c m", m=128),
                    H["Wqkv_d"][li].rearrange("(c p) n -> p c n", p=128)[:, :, wq + e_ * 128: wq + (e_ + 1) * 128]),
                    waits=(H.get(f"_p2_wstr_war_{e_ % 2}", ("T", 0)),))
                mmx = None
                for ccx in range(CC):
                    mmx = Sch.op("tensor", lambda e, s, e_=e_, ccx=ccx, b=b, j=j: e.matmul(
                        ps[b][:, :], wstr[e_ % 2][:, ccx * 128:(ccx + 1) * 128],
                        uTg_r(ccx, slice(j * 512, (j + 1) * 512)),
                        start=(ccx == 0), stop=(ccx == CC - 1)),
                        waits=(ldw, uTg_rdy, bank_war[b]), inc=(ccx == CC - 1))
                H[f"_p2_wstr_war_{e_ % 2}"] = mmx
                ev = Sch.op("scalar", lambda e, s, e_=e_, b=b: e.copy(
                    stage[:, e_ * 512:(e_ + 1) * 512], ps[b][:, :]),
                    waits=(mmx, H.get(f"_p2_stgw_{e_}", ("V", 0))))
                sq1 = Sch.op("scalar", lambda e, s, b=b: e.activation(
                    sqb[:, :], ps[b][:, :], AF.Square), waits=(mmx, sq_war))
                raw_mm = Sch.op("tensor", lambda e, s, e_=e_: e.matmul(
                    ps[4][0:16, 0:512], ind16[:, e_ * 16:(e_ + 1) * 16], sqb[:, :],
                    start=(e_ == 0), stop=(e_ == CC - 1), skip_group_check=True),
                    waits=(sq1,) + ((bank_war[4],) if e_ == 0 else ()))
                sq_war = raw_mm
                bank_war[b] = sq1
            raw_done = ("Sc", Sch.count.get("Sc", 0))
            ev1 = Sch.op("scalar", lambda e, s: e.copy(nfsc[0:16, 0:512], ps[4][0:16, 0:512]),
                         waits=(raw_mm, H.get("_p2_nfsc_war", ("V", 0))))
            bank_war[4] = ev1
            sr = Sch.op("scalar", lambda e, s: e.activation(
                nffsc[0:16, 0:512], nfsc[0:16, 0:512], AF.Sqrt),
                waits=(ev1, H.get("_p2_nffsc_war", ("V", 0))))
            mx = Sch.op("vector", lambda e, s: e.tensor_scalar_max(
                nfsc[0:16, 0:512], nffsc[0:16, 0:512], 1e-12), waits=(sr,))
            H["_p2_nffsc_war"] = mx

            def rec_fn(e, s):
                with nc.allow_low_precision(reason="l2norm scale bf16 operand"):
                    return e.reciprocal(nrmsc[0:16, 0:512], nfsc[0:16, 0:512])
            rc_ = Sch.op("vector", rec_fn, waits=(mx, H.get("_p2_nrmsc_war", ("Sc", 0))))
            H["_p2_nfsc_war"] = rc_
            fin_mm = None
            for e_ in range(CC):
                tile_ap = stage[:, e_ * 512:(e_ + 1) * 512]
                col_sc = qs_col if which == "q" else ks_col
                m1 = Sch.op("vector", lambda e, s, tile_ap=tile_ap, col_sc=col_sc: e.tensor_scalar_mul(
                    qv, tile_ap, col_sc),
                    waits=(ld_cols, qv_war, raw_done, H.get("_p2_qv_sc_war", ("Sc", 0))))
                r1 = Sch.op("vector", lambda e, s, which=which: e.tensor_mul(
                    t1b, qv, CSQ if which == "q" else CSK), waits=(m1,) + tuple(tbl_toks))
                s2m = Sch.op("vector", lambda e, s, which=which: e.tensor_mul(
                    H["ubb_sb"][:, 0:512], qv,
                    wstr[0 if which == "q" else 1][:, 1536:2048]),
                    waits=(m1,) + tuple(tbl_toks) + (H.get("_p2_ubb_war", ("V", 0)),))
                rts = []
                for qq in range(4):
                    pb_ = qq * 32
                    sp_ = pb_ + 32 if (pb_ % 64) < 32 else pb_ - 32
                    rts.append(Sch.op("vector", lambda e, s, pb_=pb_, sp_=sp_: e.tensor_copy(
                        H["uta_sb"][pb_:pb_ + 32, 512:1024],
                        H["ubb_sb"][sp_:sp_ + 32, 0:512]), waits=(s2m,)))
                H["_p2_ubb_war"] = rts[-1]
                ad = Sch.op("vector", lambda e, s: e.tensor_add(t2b, t1b, t2b),
                            waits=tuple(rts) + (r1,))
                bc = Sch.op("tensor", lambda e, s, e_=e_: e.matmul(
                    ps[5][:, 0:512], sel16[:, e_ * 128:(e_ + 1) * 128], nrmsc[0:16, 0:512],
                    start=True, stop=True), waits=(rc_, bank_war[5]))
                fq = Sch.op("vector", lambda e, s, tile_ap=tile_ap: e.tensor_mul(
                    tile_ap, t2b, ps[5][:, 0:512]), waits=(ad, bc))
                qv_war = fq
                bank_war[5] = fq
                stq = H["St"].dma(lambda e, s, e_=e_, j=j, scr_d=scr_d: e.dma_start(
                    scr_d[:, e_ * S + j * 512: e_ * S + (j + 1) * 512],
                    stage[:, e_ * 512:(e_ + 1) * 512]), waits=(fq,))
                st_map[(j, e_)] = stq
                H[f"_p2_stgw_{e_}"] = stq
                sq2 = Sch.op("scalar", lambda e, s: e.activation(
                    sqb[:, :], qv, AF.Square), waits=(fq, sq_war))
                H["_p2_qv_sc_war"] = sq2
                fin_mm = Sch.op("tensor", lambda e, s, e_=e_: e.matmul(
                    ps[4][32:48, 0:512], ind16[:, e_ * 16:(e_ + 1) * 16], sqb[:, :],
                    start=(e_ == 0), stop=(e_ == CC - 1), skip_group_check=True),
                    waits=(sq2,) + ((bank_war[4],) if e_ == 0 else ()))
                sq_war = fin_mm
            ev2 = Sch.op("scalar", lambda e, s: e.copy(nffsc[0:16, 0:512], ps[4][32:48, 0:512]),
                         waits=(fin_mm, H["_p2_nffsc_war"]))
            bank_war[4] = ev2
            m2a = Sch.op("vector", lambda e, s: e.tensor_mul(
                nffsc[0:16, 0:512], nffsc[0:16, 0:512], nrmsc[0:16, 0:512]),
                waits=(ev2, rc_))
            m2b = Sch.op("vector", lambda e, s: e.tensor_mul(
                nffsc[0:16, 0:512], nffsc[0:16, 0:512], nrmsc[0:16, 0:512]),
                waits=(m2a,))
            gcol = (0 if which == "q" else 8) + 2 * j
            gm = Sch.op("vector", lambda e, s, gcol=gcol: e.tensor_reduce(
                gridB[0:16, gcol:gcol + 2],
                nffsc[0:16, 0:512].rearrange("p (w c) -> p w c", c=W0),
                AX.X, ALU.max), waits=(m2b, H["_p2_grid_war"]))
            H["_p2_nffsc_war"] = gm
            H["_p2_grid_war"] = gm
        for ti in range(4):
            H[f"_p2_tbl_war_{ti}"] = ("V", Sch.count.get("V", 0))
    H["_p2_nrmsc_war"] = ("T", Sch.count.get("T", 0))

    gw_toks = (H["_p2_grid_war"],)
    km = []
    for w in range(NW):
        if w == 0:
            km.append(Sch.op("vector", lambda e, s: e.tensor_copy(
                gridB[0:16, 16:17], gridB[0:16, 8:9]), waits=gw_toks))
        else:
            km.append(Sch.op("vector", lambda e, s, w=w: e.tensor_max(
                gridB[0:16, 16 + w:17 + w], gridB[0:16, 7 + w:8 + w],
                gridB[0:16, 8 + w:9 + w]), waits=gw_toks + (km[-1],)))
    b2_ = Sch.op("vector", lambda e, s: e.tensor_mul(
        gridB[0:16, 24:32], gridB[0:16, 0:8], gridB[0:16, 16:24]), waits=(km[-1],))
    b3_ = Sch.op("scalar", lambda e, s: e.activation(
        gridB[0:16, 32:40], gridB[0:16, 24:32], AF.Sqrt), waits=(b2_,))
    b4_ = Sch.op("vector", lambda e, s: e.tensor_scalar_mul(
        gridB[0:16, 40:48], gridB[0:16, 32:40], -8.0), waits=(b3_,))
    H["_p2_grid_war"] = b4_
    st_g = H["St"].dma(lambda e, s: e.dma_start(H["bgrid_d"], gridB[0:16, 40:48]),
                       waits=(b4_,))
    ld_g = H["Wt"].dma(lambda e, s: e.dma_start(negrow[0:1, 0:128], H["bgrid_flat_d"]),
                       waits=(st_g, H.get("_p2_negrow_war", ("T", 0))))
    bcg = Sch.op("tensor", lambda e, s: e.matmul(
        ps[5][:, 0:128], onerb[0:1, :], negrow[0:1, 0:128], start=True, stop=True),
        waits=(ld_g, H["_onerb_tok"], bank_war[5]))
    H["_p2_negrow_war"] = bcg
    nbc = Sch.op("scalar", lambda e, s: e.copy(negbc[:, :], ps[5][:, 0:128]),
                 waits=(bcg, H.get("_p2_negbc_war", ("Sc", 0))))
    bank_war[5] = nbc

    eb_war = [("G", 0)] * 4
    ebm_war = [("T", 0)] * 4
    ob_war = {2: bank_war[2], 3: bank_war[3]}
    den_war = ("V", 0)
    win_ld = {}
    win_war = {0: ("T", 0), 1: ("T", 0)}

    def load_window(w):
        i = w % 2
        kt0 = max(0, w * 2 - 2)
        nkt = (w + 1) * 2 - kt0
        wq = H["LdA"].dma(lambda e, s, w=w, i=i: e.dma_start(
            qwin[i][:, :].rearrange("p (c m) -> p c m", m=W0),
            qT_d.rearrange("p (c m) -> p c m", m=S)[:, :, w * W0:(w + 1) * W0]),
            waits=(win_war[i],) + tuple(q_st[(w // 2, e_)] for e_ in range(CC)))
        wk = H["LdB"].dma(lambda e, s, w=w, i=i, kt0=kt0, nkt=nkt: e.dma_start(
            kwin[i][:, 0:CC * nkt * 128].rearrange("p (c m) -> p c m", m=nkt * 128),
            kT_d.rearrange("p (c m) -> p c m", m=S)[:, :, kt0 * 128: kt0 * 128 + nkt * 128]),
            waits=(win_war[i],) + tuple(k_st[(jj, e_)] for jj in range(max(0, w // 2 - 1), w // 2 + 1) for e_ in range(CC)))
        wv = H["LdIn"].dma(lambda e, s, w=w, i=i, kt0=kt0, nkt=nkt: e.dma_start(
            vwin[i][:, 0:nkt * 1024],
            v_d[:, kt0 * 1024: kt0 * 1024 + nkt * 1024]),
            waits=(win_war[i],) + tuple(v_st[(tabs, h2)] for tabs in range(kt0, kt0 + nkt) for h2 in range(2)))
        win_ld[w] = (wq, wk, wv, kt0, nkt)

    scrm4 = [H["uba_sb"][:, k * W0:(k + 1) * W0] for k in range(4)]

    def exp_slot(k):
        return (expb if k < 2 else expm)[:, (k % 2) * W0:(k % 2 + 1) * W0]
    load_window(0)
    load_window(1)
    outT_rdy = None
    ebx_ctr = 0
    for w in range(NW):
        i = w % 2
        wq, wk, wv, kt0, nkt = win_ld[w]
        for h in range(NHD):
            cc_t = h // 2
            prow = (h % 2) * 64
            ob = 2 + h % 2
            av_done = None
            den_last = None
            first_kt = True
            for kn in range(nkt):
                ktile = kt0 + kn
                sb_ = kn % 2
                eb = ebx_ctr % 4
                ebx_ctr += 1
                kt_rel = ktile - (w * 2 - 2)
                smm = Sch.op("tensor", lambda e, s, kn=kn, sb_=sb_, cc_t=cc_t, prow=prow, i=i, nkt=nkt: e.matmul(
                    ps[sb_][:, 0:W0],
                    kwin[i][prow:prow + 64, cc_t * nkt * 128 + kn * 128: cc_t * nkt * 128 + (kn + 1) * 128],
                    qwin[i][prow:prow + 64, cc_t * W0:(cc_t + 1) * W0],
                    start=True, stop=True),
                    waits=(wq, wk, bank_war[sb_]))
                ma = Sch.op("vector", lambda e, s, sb_=sb_, kt_rel=kt_rel, eb=eb: e.tensor_add(
                    scrm4[eb], ps[sb_][:, 0:W0], masks[:, kt_rel * W0:(kt_rel + 1) * W0]),
                    waits=(smm, H["_masks_tok"], eb_war[eb]))
                bank_war[sb_] = ma
                ex = Sch.op("scalar", lambda e, s, h=h, w=w, eb=eb: e.activation(
                    exp_slot(eb), scrm4[eb], AF.Exp,
                    bias=negbc[:, h * 8 + w: h * 8 + w + 1], scale=8.0),
                    waits=(ma, nbc, ebm_war[eb]))
                eb_war[eb] = ex
                av_done = Sch.op("tensor", lambda e, s, h=h, kn=kn, ob=ob, first_kt=first_kt, nkt=nkt, eb=eb, i=i: e.matmul(
                    ps[ob][0:64, 0:W0],
                    vwin[i][:, kn * 1024 + h * 64: kn * 1024 + (h + 1) * 64],
                    exp_slot(eb),
                    start=first_kt, stop=(kn == nkt - 1)),
                    waits=(ex, wv) + ((ob_war[ob],) if first_kt else ()))
                den_last = Sch.op("tensor", lambda e, s, kn=kn, first_kt=first_kt, nkt=nkt, eb=eb: e.matmul(
                    ps[4][0:1, 0:W0], onec[:, 0:1],
                    exp_slot(eb),
                    start=first_kt, stop=(kn == nkt - 1), skip_group_check=True),
                    waits=(ex, H["_onec_tok"]) + ((den_war, bank_war[4]) if first_kt else ()),
                    inc=(kn == nkt - 1))
                ebm_war[eb] = den_last
                first_kt = False
            lnd = Sch.op("scalar", lambda e, s: e.activation(
                rec64[0:1, 0:W0], ps[4][0:1, 0:W0], AF.Ln),
                waits=(den_last, H.get("_p2_lnb_war", ("T", 0))))
            den_war = lnd
            bank_war[4] = lnd
            rex = Sch.op("scalar", lambda e, s: e.activation(
                recb[0:1, 0:W0], rec64[0:1, 0:W0], AF.Exp, scale=-1.0),
                waits=(lnd, H.get("_p2_recb_war", ("T", 0))))
            H["_p2_lnb_war"] = rex
            bcd = Sch.op("tensor", lambda e, s: e.matmul(
                ps[5][0:64, 0:W0], onerb[0:1, 0:64], recb[0:1, 0:W0],
                start=True, stop=True),
                waits=(rex, H["_onerb_tok"], bank_war[5]))
            H["_p2_recb_war"] = bcd
            evo = Sch.op("scalar", lambda e, s, ob=ob: e.copy(oun[:, 0:W0], ps[ob][0:64, 0:W0]),
                         waits=(av_done, H.get("_p2_oun_war", ("V", 0))))
            mlo = Sch.op("vector", lambda e, s, w=w, cc_t=cc_t, prow=prow: e.tensor_mul(
                RA[prow:prow + 64, cc_t * GW + 1 + w * W0: cc_t * GW + 1 + (w + 1) * W0],
                oun[:, 0:W0], ps[5][0:64, 0:W0]),
                waits=(bcd, evo, H.get("_p2_outT_war", ("T", 0))))
            bank_war[5] = mlo
            H["_p2_oun_war"] = mlo
            ob_war[ob] = mlo
            outT_rdy = mlo
        win_war[i] = ("T", Sch.count.get("T", 0))
        if w + 2 < NW:
            load_window(w + 2)
    zg = []
    for cc in range(CC):
        zg.append(Sch.op("vector", lambda e, s, cc=cc: e.memset(
            RB[:, cc * GW: cc * GW + 1], 0.0), waits=(H.get("_p2_rb_war", ("T", 0)),)))
    ld_u = {}
    fin_v = {}
    fin_st = {}

    def u_tile_load(t):
        a = t % 2
        wts = (fin_v[t - 2], fin_st[t - 2]) if t >= 2 else (qv_war, ("V", Sch.count.get("V", 0)))
        ld_u[t] = H["LdA"].dma(lambda e, s, t=t, a=a: e.dma_start(
            ut[a][:, :], u_in[t * 128:(t + 1) * 128, :]), waits=wts)

    u_tile_load(0)
    u_tile_load(1)
    wo_guard = outT_rdy
    tr_ps = [psb_war[0], psb_war[1]]
    for t in range(TT):
        a = t % 2
        for n2 in range(2):
            b = (t * 2 + n2) % 2
            mmx = None
            for vd in range(CC):
                mmx = Sch.op("tensor", lambda e, s, t=t, n2=n2, vd=vd, b=b: e.matmul(
                    ps[b][:, :], RA[:, vd * GW + 1 + t * 128: vd * GW + 1 + (t + 1) * 128],
                    RF[:, vd * 1024 + n2 * 512: vd * 1024 + (n2 + 1) * 512],
                    start=(vd == 0), stop=(vd == CC - 1)),
                    waits=(ld_wo, wo_guard, bank_war[b]), inc=(vd == CC - 1))
            c2 = Sch.op("vector", lambda e, s, n2=n2, a=a, b=b: e.tensor_add(
                ub[a][:, n2 * 512:(n2 + 1) * 512], ut[a][:, n2 * 512:(n2 + 1) * 512], ps[b][:, :]),
                waits=(mmx, ld_u[t], fin_v.get(t - 2, ("V", 0))))
            bank_war[b] = c2
            fin_v[t] = c2
        st = H["St"].dma(lambda e, s, t=t, a=a: e.dma_start(
            u_mid[t * 128:(t + 1) * 128, :], ub[a][:, :]), waits=(fin_v[t],))
        fin_st[t] = st
        cb1 = Sch.op("vector", lambda e, s, a=a: e.tensor_copy(expb[:, :], ub[a][:, 0:512]),
                     waits=(fin_v[t], ("Sc", Sch.count.get("Sc", 0)) if t == 0 else ("V", 0),
                            ("G", Sch.count.get("G", 0)) if t == 0 else ("V", 0)))
        cb = Sch.op("vector", lambda e, s, a=a: e.tensor_copy(expm[:, :], ub[a][:, 512:1024]),
                    waits=(cb1,))
        fin_v[t] = cb
        for ccj in range(CC):
            ebx = ccj % 2
            trk = Sch.op("tensor", lambda e, s, ccj=ccj, ebx=ebx: e.transpose(
                psb[ebx], ubf_r(ccj), ident[:, :]),
                waits=(cb, tr_ps[ebx], H["_ident_tok"]))
            ev = Sch.op("scalar", lambda e, s, t=t, ccj=ccj, ebx=ebx: e.copy(
                RB[:, ccj * GW + 1 + t * 128: ccj * GW + 1 + (t + 1) * 128], psb[ebx]),
                waits=(trk,) + tuple(zg))
            tr_ps[ebx] = ev
        if t + 2 < TT:
            u_tile_load(t + 2)
    psb_war[0], psb_war[1] = tr_ps
    xs2T_rdy = ("Sc", Sch.count.get("Sc", 0))

    def xs2_r(cc, rng):
        off = 0 if cc >= 4 else 1
        return RB[:, cc * GW + off + rng.start: cc * GW + off + rng.stop]

    eps_ap, eps_tok = Sch.const_col(1e-5)
    xn2_done = {}
    qv_war2 = ("V", Sch.count.get("V", 0))
    rd_war = ("T", Sch.count.get("T", 0))
    sq_war2 = sq_war
    for n in range(4):
        mm_mu = None
        mm_ss = None
        for cc in range(CC):
            mm_mu = Sch.op("tensor", lambda e, s, cc=cc, n=n: e.matmul(
                ps[4][0:1, 0:512], onec[:, 0:1],
                xs2_r(cc, slice(n * 512, (n + 1) * 512)),
                start=(cc == 0), stop=(cc == CC - 1)),
                waits=(xs2T_rdy, H["_onec_tok"]) + ((bank_war[4],) if cc == 0 else ()),
                inc=(cc == CC - 1))
            sq1 = Sch.op("scalar", lambda e, s, cc=cc, n=n: e.activation(
                sqb[:, :], xs2_r(cc, slice(n * 512, (n + 1) * 512)), AF.Square),
                waits=(xs2T_rdy, sq_war2))
            mm_ss = Sch.op("tensor", lambda e, s, cc=cc, n=n: e.matmul(
                ps[5][0:1, 0:512], onec[:, 0:1], sqb[:, :],
                start=(cc == 0), stop=(cc == CC - 1)),
                waits=(sq1,) + ((bank_war[5],) if cc == 0 else ()))
            sq_war2 = mm_ss
        e1 = Sch.op("scalar", lambda e, s: e.copy(statrow[0:1, 0:512], ps[4][0:1, 0:512]),
                    waits=(mm_mu, H.get("_p2_statrow_war", ("V", 0))))
        e2 = Sch.op("scalar", lambda e, s: e.copy(statrow[0:1, 512:1024], ps[5][0:1, 0:512]),
                    waits=(mm_ss,))
        bank_war[4] = e1
        bank_war[5] = e2
        t1_ = Sch.op("vector", lambda e, s: e.tensor_scalar_mul(
            statrow[0:1, 1024:1536], statrow[0:1, 0:512], 1.0 / D), waits=(e1, e2))
        t2_ = Sch.op("vector", lambda e, s: e.tensor_mul(
            statrow[0:1, 0:512], statrow[0:1, 1024:1536], statrow[0:1, 1024:1536]),
            waits=(t1_,))
        t3a = Sch.op("vector", lambda e, s: e.tensor_scalar_mul(
            statrow[0:1, 1536:2048], statrow[0:1, 512:1024], 1.0 / D), waits=(t2_,))
        t3_ = Sch.op("vector", lambda e, s: e.tensor_sub(
            statrow[0:1, 512:1024], statrow[0:1, 1536:2048], statrow[0:1, 0:512]),
            waits=(t3a,))
        t4_ = Sch.op("scalar", lambda e, s: e.activation(
            statrow[0:1, 1536:2048], statrow[0:1, 512:1024], AF.Sqrt,
            bias=eps_ap[0:1, 0:1]), waits=(t3_, eps_tok))
        t5_ = Sch.op("vector", lambda e, s: e.reciprocal(
            statrow[0:1, 0:512], statrow[0:1, 1536:2048]), waits=(t4_,))
        bmu = Sch.op("tensor", lambda e, s: e.matmul(
            ps[4][:, 0:512], onerf[0:1, :], statrow[0:1, 1024:1536],
            start=True, stop=True), waits=(t5_, H["_onerf_tok"], bank_war[4]))
        brs = Sch.op("tensor", lambda e, s: e.matmul(
            ps[5][:, 0:512], onerf[0:1, :], statrow[0:1, 0:512],
            start=True, stop=True), waits=(t5_, bank_war[5]))
        H["_p2_statrow_war"] = brs
        for cc in range(CC):
            v1x = Sch.op("vector", lambda e, s, cc=cc, n=n: e.tensor_sub(
                t1b, xs2_r(cc, slice(n * 512, (n + 1) * 512)), ps[4][:, 0:512]),
                waits=(bmu, qv_war2))
            v2x = Sch.op("vector", lambda e, s: e.tensor_mul(
                t2b, t1b, ps[5][:, 0:512]), waits=(brs, v1x))
            v3x = Sch.op("vector", lambda e, s, cc=cc, n=n: e.tensor_scalar(
                RD[:, cc * 2048 + n * 512: cc * 2048 + (n + 1) * 512],
                t2b, lng[:, cc:cc + 1], lnb[:, cc:cc + 1], ALU.mult, ALU.add),
                waits=(v2x, ld_cols, rd_war if (n == 0 and cc == 0) else ("V", 0)))
            qv_war2 = v3x
            xn2_done[(n, cc)] = v3x
        bank_war[4] = xn2_done[(n, CC - 1)]
        bank_war[5] = xn2_done[(n, CC - 1)]
    xn2_rdy = ("V", Sch.count.get("V", 0))

    def agt(j2):
        if j2 < 8:
            return RC[:, j2 * 2048: (j2 + 1) * 2048]
        if j2 < 16:
            return RB[:, (j2 - 8) * 2048: (j2 - 7) * 2048]
        if j2 < 20:
            return RF[:, (j2 - 16) * 2048: (j2 - 15) * 2048]
        return H["AG2_sb"][:, (j2 - 20) * 2048: (j2 - 19) * 2048]

    agt_done = {}
    wstr_war2 = [("T", Sch.count.get("T", 0)), ("T", Sch.count.get("T", 0))]
    gb_war = ("V", Sch.count.get("V", 0))
    agt_guard = ("T", Sch.count.get("T", 0))
    agt_sc_guard = ("Sc", Sch.count.get("Sc", 0))
    col_off = 0
    for j2 in range(NPAIR):
        szj = 128 if j2 < 21 else 42
        wbuf = j2 % 2
        ldw1 = H["Wt"].dma(lambda e, s, wbuf=wbuf, col_off=col_off, szj=szj: e.dma_start(
            wstr[wbuf][:, 0:2 * szj * CC].rearrange("p (c m) -> p c m", m=2 * szj),
            H["W1r_d"][li].rearrange("(c p) m -> p c m", p=128)[:, :, col_off: col_off + 2 * szj]),
            waits=(wstr_war2[wbuf],))
        for n in range(4):
            ba = 4 if n % 2 == 0 else 0
            bg = 5 if n % 2 == 0 else 1
            mma = None
            mmg = None
            for cc in range(CC):
                mma = Sch.op("tensor", lambda e, s, n=n, cc=cc, ba=ba, szj=szj, wbuf=wbuf: e.matmul(
                    ps[ba][0:szj, 0:512], wstr[wbuf][:, cc * 2 * szj: cc * 2 * szj + szj],
                    RD[:, cc * 2048 + n * 512: cc * 2048 + (n + 1) * 512],
                    start=(cc == 0), stop=(cc == CC - 1)),
                    waits=(ldw1, xn2_rdy) + ((bank_war[ba],) if cc == 0 else ()),
                    inc=(cc == CC - 1))
                mmg = Sch.op("tensor", lambda e, s, n=n, cc=cc, bg=bg, szj=szj, wbuf=wbuf: e.matmul(
                    ps[bg][0:szj, 0:512], wstr[wbuf][:, cc * 2 * szj + szj: cc * 2 * szj + 2 * szj],
                    RD[:, cc * 2048 + n * 512: cc * 2048 + (n + 1) * 512],
                    start=(cc == 0), stop=(cc == CC - 1)),
                    waits=((bank_war[bg],) if cc == 0 else ()), inc=(cc == CC - 1))
            gl = Sch.op("scalar", lambda e, s, j2=j2, bg=bg, szj=szj: e.activation(
                sqb[0:szj, 0:512], ps[bg][0:szj, 0:512], AF.Gelu,
                bias=b1c[0:szj, 2 * j2 + 1: 2 * j2 + 2]),
                waits=(mmg, gb_war, ld_cols, agt_sc_guard))
            agv = Sch.op("vector", lambda e, s, j2=j2, n=n, ba=ba, szj=szj: e.scalar_tensor_tensor(
                agt(j2)[0:szj, n * 512:(n + 1) * 512], ps[ba][0:szj, 0:512],
                b1c[0:szj, 2 * j2: 2 * j2 + 1], sqb[0:szj, 0:512], ALU.add, ALU.mult),
                waits=(gl, mma, agt_guard))
            gb_war = agv
            bank_war[ba] = agv
            bank_war[bg] = agv
            agt_done[(j2, n)] = agv
        wstr_war2[wbuf] = ("T", Sch.count.get("T", 0))
        col_off += 2 * szj

    def w2t(j2):
        if j2 < 16:
            return RD[:, j2 * 1024: (j2 + 1) * 1024]
        if j2 < 20:
            return RG[:, (j2 - 16) * 1024: (j2 - 15) * 1024]
        return RE[:, (j2 - 20) * 1024: (j2 - 19) * 1024]

    w2_guard = ("T", Sch.count.get("T", 0))
    ld_w2a = H["Wt"].dma(lambda e, s: e.dma_start(
        RD[:, 0:16 * 1024].rearrange("p (j m) -> p j m", m=1024),
        H["W2_d"][li][0:2048, :].rearrange("(j p) m -> p j m", p=128)),
        waits=(w2_guard,))
    ld_w2b = H["Wt"].dma(lambda e, s: e.dma_start(
        RG[:, 0:4096].rearrange("p (j m) -> p j m", m=1024),
        H["W2_d"][li][2048:2560, :].rearrange("(j p) m -> p j m", p=128)),
        waits=(w2_guard,))
    ld_w2c = H["Wt"].dma(lambda e, s: e.dma_start(
        RE[:, 0:1024], H["W2_d"][li][2560:2688, :]), waits=(w2_guard,))
    ld_w2d = H["Wt"].dma(lambda e, s: e.dma_start(
        RE[0:42, 1024:2048], H["W2_d"][li][2688:2730, :]), waits=(w2_guard,))
    w2_lds = (ld_w2a, ld_w2b, ld_w2c, ld_w2d)
    ld_u2 = {}
    fin2_v = {}
    fin2_st = {}

    def u2_tile_load(t):
        a = t % 2
        wts = (fin2_v[t - 2], fin2_st[t - 2]) if t >= 2 else (
            fin_st[TT - 2 + t], ("V", Sch.count.get("V", 0)))
        ld_u2[t] = H["LdB"].dma(lambda e, s, t=t, a=a: e.dma_start(
            ut[a][:, :], u_mid[t * 128:(t + 1) * 128, :]), waits=wts)

    u2_tile_load(0)
    u2_tile_load(1)
    zg1 = []
    for cc in range(CC):
        zg1.append(Sch.op("vector", lambda e, s, cc=cc: e.memset(
            RA[:, cc * GW: cc * GW + 1], 0.0), waits=(("T", Sch.count.get("T", 0)),)))
    tr2_ps = [psb_war[0], psb_war[1]]
    for t in range(TT):
        a = t % 2
        for n2 in range(2):
            b = (t * 2 + n2) % 2
            mmx = None
            for j2 in range(NPAIR):
                szj = 128 if j2 < 21 else 42
                mmx = Sch.op("tensor", lambda e, s, t=t, n2=n2, j2=j2, b=b, szj=szj: e.matmul(
                    ps[b][:, :], agt(j2)[0:szj, t * 128:(t + 1) * 128],
                    w2t(j2)[0:szj, n2 * 512:(n2 + 1) * 512],
                    start=(j2 == 0), stop=False, skip_group_check=True),
                    waits=w2_lds + (agt_done[(j2, t // 4)],) + ((bank_war[b],) if j2 == 0 else ()),
                    inc=False)
            mmb = Sch.op("tensor", lambda e, s, n2=n2, b=b: e.matmul(
                ps[b][:, :], onerb[0:1, 0:128], b2row[0:1, n2 * 512:(n2 + 1) * 512],
                start=False, stop=True, skip_group_check=True),
                waits=(ld_b2, H["_onerb_tok"]), inc=True)
            c2 = Sch.op("vector", lambda e, s, n2=n2, a=a, b=b: e.tensor_add(
                ub[a][:, n2 * 512:(n2 + 1) * 512], ut[a][:, n2 * 512:(n2 + 1) * 512],
                ps[b][:, :]), waits=(mmb, ld_u2[t], fin2_v.get(t - 2, ("V", 0))))
            bank_war[b] = c2
            fin2_v[t] = c2
        if last:
            sl = Sch.op("scalar", lambda e, s, a=a: e.activation(
                ut[a][:, 0:512], ub[a][:, 0:512], AF.Silu), waits=(fin2_v[t],))
            sl2 = Sch.op("scalar", lambda e, s, a=a: e.activation(
                ut[a][:, 512:1024], ub[a][:, 512:1024], AF.Silu), waits=(sl,))
            st = H["St"].dma(lambda e, s, t=t, a=a: e.dma_start(
                u_out[t * 128:(t + 1) * 128, :], ut[a][:, :]), waits=(sl2,))
            fin2_v[t] = sl2
        else:
            st = H["St"].dma(lambda e, s, t=t, a=a: e.dma_start(
                u_out[t * 128:(t + 1) * 128, :], ub[a][:, :]), waits=(fin2_v[t],))
        fin2_st[t] = st
        if not last:
            cb1 = Sch.op("vector", lambda e, s, a=a: e.tensor_copy(expb[:, :], ub[a][:, 0:512]),
                         waits=(fin2_v[t], ("Sc", Sch.count.get("Sc", 0)) if t == 0 else ("V", 0)))
            cb = Sch.op("vector", lambda e, s, a=a: e.tensor_copy(expm[:, :], ub[a][:, 512:1024]),
                        waits=(cb1,))
            fin2_v[t] = cb
            for ccj in range(CC):
                ebx = ccj % 2
                trk = Sch.op("tensor", lambda e, s, ccj=ccj, ebx=ebx: e.transpose(
                    psb[ebx], ubf_r(ccj), ident[:, :]),
                    waits=(cb, tr2_ps[ebx], H["_ident_tok"]))
                ev = Sch.op("scalar", lambda e, s, t=t, ccj=ccj, ebx=ebx: e.copy(
                    RA[:, ccj * GW + 1 + t * 128: ccj * GW + 1 + (t + 1) * 128], psb[ebx]),
                    waits=(trk,) + tuple(zg1))
                tr2_ps[ebx] = ev
        if t + 2 < TT:
            u2_tile_load(t + 2)
    psb_war[0], psb_war[1] = tr2_ps
    H["_p2_rf_war"] = ("T", Sch.count.get("T", 0))
    H["_p2_rb_war"] = ("T", Sch.count.get("T", 0))
    H["_p2_cols_war"] = ("V", Sch.count.get("V", 0))
    H["_p2_b2_war"] = ("T", Sch.count.get("T", 0))
    H["_p2_uTg_rdy"] = ("Sc", Sch.count.get("Sc", 0))
    H["_p2_negbc_war"] = ("Sc", Sch.count.get("Sc", 0))
    H["_p2_outT_war"] = ("T", Sch.count.get("T", 0))
    H["_p2_nrmsc_war"] = ("Sc", Sch.count.get("Sc", 0))
    H["_p2_sq_war"] = sq_war2
    H["_p2_qv_war"] = ("V", Sch.count.get("V", 0))
    H["_p2_stage_war"] = ("T", Sch.count.get("T", 0))
    H["_p2_grid_war"] = ("Sc", Sch.count.get("Sc", 0))


def alloc_p2(nc, e, H):
    H["RA_sb"] = e(nc.sbuf_tensor("RA_sb", [128, CC * GW], BF16))
    H["RB_sb"] = e(nc.sbuf_tensor("RB_sb", [128, CC * GW], BF16))
    H["RC_sb"] = e(nc.sbuf_tensor("RC_sb", [128, CC * 2048], BF16))
    H["RD_sb"] = e(nc.sbuf_tensor("RD_sb", [128, CC * 2048], BF16))
    H["RE_sb"] = e(nc.sbuf_tensor("RE_sb", [128, 2048], BF16))
    H["RF_sb"] = e(nc.sbuf_tensor("RF_sb", [128, 8192], BF16))
    H["RG_sb"] = e(nc.sbuf_tensor("RG_sb", [128, 4096], BF16))
    H["AG2_sb"] = e(nc.sbuf_tensor("AG2_sb", [128, 4096], BF16))
    H["masks_sb"] = e(nc.sbuf_tensor("masks_sb", [128, 4 * W0], BF16))
    H["expb_sb"] = e(nc.sbuf_tensor("expb_sb", [128, 2 * W0], BF16))
    H["expm_sb"] = e(nc.sbuf_tensor("expm_sb", [128, 2 * W0], BF16))
    H["sqb_sb"] = e(nc.sbuf_tensor("sqb_sb", [128, 512], BF16))
    H["nfsc_sb"] = e(nc.sbuf_tensor("nfsc_sb", [16, 512], F32))
    H["nffsc_sb"] = e(nc.sbuf_tensor("nffsc_sb", [16, 512], F32))
    H["nrmsc_sb"] = e(nc.sbuf_tensor("nrmsc_sb", [16, 512], BF16))
    H["gridB_sb"] = e(nc.sbuf_tensor("gridB_sb", [16, 256], F32))
    H["negrow_sb"] = e(nc.sbuf_tensor("negrow_sb", [1, 128], BF16))
    H["negbc_sb"] = e(nc.sbuf_tensor("negbc_sb", [128, 128], F32))
    H["oun_sb"] = e(nc.sbuf_tensor("oun_sb", [64, W0], BF16))
    H["recb_sb"] = e(nc.sbuf_tensor("recb_sb", [1, W0], BF16))
    H["rec64_sb"] = e(nc.sbuf_tensor("rec64_sb", [1, W0], F32))
    H["cols_sb"] = e(nc.sbuf_tensor("cols_sb", [128, 62], F32))
    H["b2row_sb"] = e(nc.sbuf_tensor("b2row_sb", [1, 1024], BF16))
    H["ones_col_sb"] = e(nc.sbuf_tensor("ones_col_sb", [128, 1], BF16))
    H["ones_rowf_sb"] = e(nc.sbuf_tensor("ones_rowf_sb", [1, 128], F32))
    H["ones_row_sb"] = e(nc.sbuf_tensor("ones_row_sb", [1, 128], BF16))
    H["ind16_sb"] = e(nc.sbuf_tensor("ind16_sb", [128, 128], BF16))
    H["sel16_sb"] = e(nc.sbuf_tensor("sel16_sb", [16, 8 * 128], BF16))
    H["uta_sb"] = e(nc.sbuf_tensor("uta_sb", [128, D], F32))
    H["utb_sb"] = e(nc.sbuf_tensor("utb_sb", [128, D], F32))
    H["uba_sb"] = e(nc.sbuf_tensor("uba_sb", [128, D], F32))
    H["ubb_sb"] = e(nc.sbuf_tensor("ubb_sb", [128, D], F32))
    H["wsa_sb"] = e(nc.sbuf_tensor("wsa_sb", [128, 2048], BF16))
    H["wsb_sb"] = e(nc.sbuf_tensor("wsb_sb", [128, 2048], BF16))
    H["ident_sb"] = e(nc.sbuf_tensor("ident_sb", [128, 128], BF16))


def p2_common_setup(nc, Sch, H):
    H["_onec_tok"] = Sch.op("vector", lambda e, s: e.memset(H["ones_col_sb"][:, :], 1.0))
    H["_onerb_tok"] = Sch.op("vector", lambda e, s: e.memset(H["ones_row_sb"][:, :], 1.0))
    H["_onerf_tok"] = Sch.op("vector", lambda e, s: e.memset(H["ones_rowf_sb"][:, :], 1.0))
    H["_ident_tok"] = H["LdIn"].dma(lambda e, s: e.dma_start(H["ident_sb"][:, :], H["ident_d"]))
    H["_masks_tok"] = H["LdIn"].dma(lambda e, s: e.dma_start(
        H["masks_sb"][:, :].rearrange("p (i n) -> p i n", n=W0),
        H["masks_d"].rearrange("i p n -> p i n")))
    ld_i16 = H["LdIn"].dma(lambda e, s: e.dma_start(H["ind16_sb"][:, :], H["ind16_d"]))
    ld_s16 = H["LdIn"].dma(lambda e, s: e.dma_start(H["sel16_sb"][0:16, :], H["sel16_d"]))
    Sch.op("tensor", lambda e, s: None, waits=(ld_i16, ld_s16, H["_ident_tok"]), inc=False)


def build_p2_program(li_list, debug_umid=False):
    nc = bass.Bass("TRN2", target_bir_lowering=False)
    H = {}
    first_li = li_list[0]
    last_li = li_list[-1]
    NL = len(li_list)
    H["Wqkv_full_d"] = nc.dram_tensor("Wqkv", [NL, D, 3072], F32, kind="ExternalInput").ap()
    H["Wo6_full_d"] = nc.dram_tensor("Wo6", [NL, D, D], F32, kind="ExternalInput").ap()
    H["W1r_full_d"] = nc.dram_tensor("W1r", [NL, D, 5460], F32, kind="ExternalInput").ap()
    H["W2_full_d"] = nc.dram_tensor("W2", [NL, DI2, D], F32, kind="ExternalInput").ap()
    H["cols_full_d"] = nc.dram_tensor("colsP2", [NL, 128, 62], F32, kind="ExternalInput").ap()
    H["b2_full_d"] = nc.dram_tensor("b2rows", [NL, 1024], BF16, kind="ExternalInput").ap()
    H["tbl_d"] = nc.dram_tensor("tblP2", [128, 4, S], BF16, kind="ExternalInput").ap()
    H["masks_d"] = nc.dram_tensor("masksP2", [4, 128, W0], BF16, kind="ExternalInput").ap()
    H["ident_d"] = nc.dram_tensor("ident", [128, 128], BF16, kind="ExternalInput").ap()
    H["ind16_d"] = nc.dram_tensor("ind16", [128, 128], BF16, kind="ExternalInput").ap()
    H["sel16_d"] = nc.dram_tensor("sel16", [16, 8 * 128], BF16, kind="ExternalInput").ap()
    H["uTg_in_d"] = nc.dram_tensor("uTg_in", [128, CC * GW], BF16, kind="ExternalInput").ap()
    H["bgrid_d"] = nc.dram_tensor("bgrid", [16, 8], F32).ap()
    H["qT_scr_d"] = nc.dram_tensor("qT_scr", [128, CC * S], BF16).ap()
    H["kT_scr_d"] = nc.dram_tensor("kT_scr", [128, CC * S], BF16).ap()
    H["v_scr_d"] = nc.dram_tensor("v_scr", [128, TT * 1024], BF16).ap()
    H["bgrid_flat_d"] = H["bgrid_d"].rearrange("a b -> (a b)").rearrange("(x y) -> x y", x=1)
    def sl3(apfull, li):
        r = li - li_list[0]
        return apfull[r:r + 1, :, :].rearrange("l a b -> (l a) b")
    H["Wqkv_d"] = {li: sl3(H["Wqkv_full_d"], li) for li in li_list}
    H["Wo6_d"] = {li: sl3(H["Wo6_full_d"], li) for li in li_list}
    H["W1r_d"] = {li: sl3(H["W1r_full_d"], li) for li in li_list}
    H["W2_d"] = {li: sl3(H["W2_full_d"], li) for li in li_list}
    H["cols_d"] = {li: sl3(H["cols_full_d"], li) for li in li_list}
    H["b2_d"] = {li: H["b2_full_d"][li - li_list[0]: li - li_list[0] + 1, :] for li in li_list}
    n_sub = len(li_list)
    H[f"u_in_{first_li}"] = nc.dram_tensor("u_in", [S, D], F32, kind="ExternalInput").ap()
    for i, li in enumerate(li_list):
        H[f"u_mid_{li}"] = nc.dram_tensor(
            f"u_mid_{li}", [S, D], F32,
            kind="ExternalOutput" if debug_umid else "Internal").ap()
        if li == last_li:
            H[f"u_out_{li}"] = nc.dram_tensor("u_out", [S, D], F32, kind="ExternalOutput").ap()
        else:
            H[f"u_out_{li}"] = nc.dram_tensor(f"u_pp_{li}", [S, D], F32).ap()
            H[f"u_in_{li + 1}"] = H[f"u_out_{li}"]
    H["uTg_out_d"] = nc.dram_tensor("uTg_out", [128, CC * GW], BF16, kind="ExternalOutput").ap()
    with ExitStack() as ctx:
        e = ctx.enter_context
        alloc_p2(nc, e, H)
        H["cpool"] = e(nc.sbuf_tensor("cpool", [128, 16], F32))
        H["ps_f"] = [e(nc.psum_tensor(f"psf{i}", [128, 512], F32))[:, :] for i in range(6)]
        H["psb0"] = e(nc.psum_tensor("psb0", [128, 128], BF16))[:, :]
        H["psb1"] = e(nc.psum_tensor("psb1", [128, 128], BF16))[:, :]
        Sch = Sched()
        Sch._const_pool = H["cpool"]
        mk_streams(Sch, H)
        p2_common_setup(nc, Sch, H)
        ld_uTg = H["LdIn"].dma(lambda e2, s: e2.dma_start(H["RA_sb"][:, :], H["uTg_in_d"]))
        H["_p2_uTg_rdy"] = ld_uTg
        for li in li_list:
            emit_P2_sublayer(nc, Sch, H, li, first=(li == first_li), last=(li == 5))
        H["St"].dma(lambda e2, s: e2.dma_start(H["uTg_out_d"], H["RA_sb"][:, :]),
                    waits=(H["_p2_uTg_rdy"],))
        finish_program(nc, Sch, H)
    return nc



def emit_P4a(nc, Sch, H):
    xn1T = H["xn1T_sb"]
    xrk = H["xrk_sb"]
    Wr = H["l1f_w_sb"]
    r_tm = H["r_sb"]
    kact = H["kact_sb"]
    mu = H["mu_sb"]
    om = H["om_sb"]
    tmp = [H["tmpa_sb"], H["tmpb_sb"]]
    gt = [H["gta_sb"], H["gtb_sb"]]
    ps = H["ps_f"]

    def wslice(which, cc, lo, hi):
        base = which * CC * 1024
        return Wr[:, base + cc * 1024 + lo: base + cc * 1024 + hi]

    ld_x = H["LdIn"].dma(lambda e, s: e.dma_start(xn1T[:, :], H["xn1T_d"]))
    ld_mu = H["LdIn"].dma(lambda e, s: e.dma_start(mu[:, 0:8], H["mu_r_d"]))
    ld_mk = H["LdIn"].dma(lambda e, s: e.dma_start(mu[:, 8:16], H["mu_k_d"]))
    lw = []
    for wi, wd in enumerate(("WrT_d", "WkT2_d", "WvT2_d")):
        lw.append(H["Wt"].dma(lambda e, s, wi=wi, wd=wd: e.dma_start(
            Wr[:, wi * CC * 1024:(wi + 1) * CC * 1024].rearrange("p (c n) -> p c n", n=1024),
            H[wd].rearrange("(c p) n -> p c n", p=128))))
    omt = Sch.op("vector", lambda e, s: e.tensor_scalar(
        om[:, :], mu[:, :], -1.0, 1.0, ALU.mult, ALU.add), waits=(ld_mu, ld_mk))

    def guard(cc, shifted):
        off = 0 if shifted else 1
        return xn1T[:, cc * GW + off: cc * GW + off + S]

    def build_xlerp(mu_off, out_tok_extra=()):
        toks = []
        for cc in range(CC):
            a = cc % 2
            t1 = Sch.op("vector", lambda e, s, cc=cc, a=a: e.tensor_scalar_mul(
                tmp[a][:, :], guard(cc, False), mu[:, mu_off + cc: mu_off + cc + 1]),
                waits=(ld_x, omt) + tuple(out_tok_extra))
            t2 = Sch.op("vector", lambda e, s, cc=cc, a=a: e.scalar_tensor_tensor(
                xrk[:, cc * S:(cc + 1) * S], guard(cc, True),
                om[:, mu_off + cc: mu_off + cc + 1], tmp[a][:, :],
                ALU.mult, ALU.add), waits=(t1,))
            toks.append(t2)
        return toks

    xr_toks = build_xlerp(0)
    r_toks = []
    for t in range(TT):
        for n in range(2):
            b = (t * 2 + n) % 2
            war = r_toks[-2] if len(r_toks) >= 2 else ("Sc", 0)
            mm = None
            for cc in range(CC):
                mm = Sch.op("tensor", lambda e, s, t=t, n=n, cc=cc, b=b: e.matmul(
                    ps[b][:, :], xrk[:, cc * S + t * 128: cc * S + (t + 1) * 128],
                    wslice(0, cc, n * 512, (n + 1) * 512),
                    start=(cc == 0), stop=(cc == CC - 1)),
                    waits=(xr_toks[cc], lw[0], war), inc=(cc == CC - 1))
            ev = Sch.op("scalar", lambda e, s, t=t, n=n, b=b: e.activation(
                r_tm[:, t * D + n * 512: t * D + (n + 1) * 512], ps[b][:, :], AF.Sigmoid),
                waits=(mm,))
            r_toks.append(ev)
    xk_toks = build_xlerp(8, out_tok_extra=(("T", Sch.count.get("T", 0)),))
    ka_toks = []
    relu_t = [H["tmpa_sb"], H["tmpb_sb"]]
    for kc in range(CC):
        for n in range(4):
            b = (kc * 4 + n) % 2 + 2
            war = ka_toks[-2] if len(ka_toks) >= 2 else ("Sc", 0)
            mm = None
            for cc in range(CC):
                mm = Sch.op("tensor", lambda e, s, kc=kc, n=n, cc=cc, b=b: e.matmul(
                    ps[b][:, :], wslice(1, cc, kc * 128, (kc + 1) * 128),
                    xrk[:, cc * S + n * 512: cc * S + (n + 1) * 512],
                    start=(cc == 0), stop=(cc == CC - 1)),
                    waits=(xk_toks[cc], lw[1], war), inc=(cc == CC - 1))
            a = (kc * 4 + n) % 2
            rl = Sch.op("vector", lambda e, s, a=a, b=b: e.tensor_scalar_max(
                relu_t[a][:, 0:512], ps[b][:, :], 0.0), waits=(mm,))
            sq = Sch.op("scalar", lambda e, s, kc=kc, n=n, a=a: e.activation(
                kact[:, kc * S + n * 512: kc * S + (n + 1) * 512],
                relu_t[a][:, 0:512], AF.Square), waits=(rl,))
            ka_toks.append(sq)
    out_toks = []
    st_toks = []
    for t in range(TT):
        a = t % 2
        for n in range(2):
            b = (t * 2 + n) % 2 + 4
            war = out_toks[-2] if len(out_toks) >= 2 else ("Sc", 0)
            mm = None
            for kc in range(CC):
                mm = Sch.op("tensor", lambda e, s, t=t, n=n, kc=kc, b=b: e.matmul(
                    ps[b][:, :], kact[:, kc * S + t * 128: kc * S + (t + 1) * 128],
                    wslice(2, kc, n * 512, (n + 1) * 512),
                    start=(kc == 0), stop=(kc == CC - 1)),
                    waits=(ka_toks[kc * 4 + t // 4], lw[2], war),
                    inc=(kc == CC - 1))
            wr_dma = st_toks[-2] if len(st_toks) >= 2 else ("V", 0)
            mu1 = Sch.op("vector", lambda e, s, t=t, n=n, a=a, b=b: e.tensor_mul(
                tmp[a][:, n * 512:(n + 1) * 512],
                r_tm[:, t * D + n * 512: t * D + (n + 1) * 512], ps[b][:, :]),
                waits=(mm, r_toks[t * 2 + n], ka_toks[-1],
                       out_toks[-3] if len(out_toks) >= 3 else ("Sc", 0)))
            si = Sch.op("scalar", lambda e, s, t=t, n=n, a=a: e.activation(
                gt[a][:, n * 512:(n + 1) * 512], tmp[a][:, n * 512:(n + 1) * 512], AF.Silu),
                waits=(mu1, wr_dma))
            out_toks.append(si)
        st = H["St"].dma(lambda e, s, t=t, a=a: e.dma_start(
            H["g1_d"][t * 128:(t + 1) * 128, :], gt[a][:, :]),
            waits=(out_toks[-1], out_toks[-2]))
        st_toks.append(st)
    return st_toks


def alloc_p4a(nc, e, H):
    H["xn1T_sb"] = e(nc.sbuf_tensor("xn1T_sb", [128, CC * GW], BF16))
    H["xrk_sb"] = e(nc.sbuf_tensor("xrk_sb", [128, CC * S], BF16))
    H["l1f_w_sb"] = e(nc.sbuf_tensor("l1f_w", [128, 3 * CC * 1024], BF16))
    H["r_sb"] = e(nc.sbuf_tensor("r_sb", [128, TT * D], BF16))
    H["kact_sb"] = e(nc.sbuf_tensor("kact_sb", [128, CC * S], BF16))
    H["mu_sb"] = e(nc.sbuf_tensor("mu_sb", [128, 16], F32))
    H["om_sb"] = e(nc.sbuf_tensor("om_sb", [128, 16], F32))
    H["tmpa_sb"] = e(nc.sbuf_tensor("tmpa", [128, S], F32))
    H["tmpb_sb"] = e(nc.sbuf_tensor("tmpb", [128, S], F32))
    H["gta_sb"] = e(nc.sbuf_tensor("gta", [128, D], F32))
    H["gtb_sb"] = e(nc.sbuf_tensor("gtb", [128, D], F32))


def build_p4a_program():
    nc = bass.Bass("TRN2", target_bir_lowering=False)
    H = {}
    H["xn1T_d"] = nc.dram_tensor("xn1T", [128, CC * GW], BF16, kind="ExternalInput").ap()
    H["WrT_d"] = nc.dram_tensor("WrT", [D, D], F32, kind="ExternalInput").ap()
    H["WkT2_d"] = nc.dram_tensor("WkT2", [D, D], F32, kind="ExternalInput").ap()
    H["WvT2_d"] = nc.dram_tensor("WvT2", [D, D], F32, kind="ExternalInput").ap()
    H["mu_r_d"] = nc.dram_tensor("mu_r_cols", [128, 8], F32, kind="ExternalInput").ap()
    H["mu_k_d"] = nc.dram_tensor("mu_k_cols", [128, 8], F32, kind="ExternalInput").ap()
    H["g1_d"] = nc.dram_tensor("g1", [S, D], F32, kind="ExternalOutput").ap()
    with ExitStack() as ctx:
        e = ctx.enter_context
        alloc_p4a(nc, e, H)
        H["cpool"] = e(nc.sbuf_tensor("cpool", [128, 16], F32))
        H["ps_f"] = [e(nc.psum_tensor(f"psf{i}", [128, 512], F32))[:, :] for i in range(6)]
        Sch = Sched()
        Sch._const_pool = H["cpool"]
        mk_streams(Sch, H)
        emit_P4a(nc, Sch, H)
        finish_program(nc, Sch, H)
    return nc



def emit_P3(nc, Sch, H):
    xnT = H["xnT_sb"]
    B = H["B_sb"]
    qt_ot = H["C_sb"]
    kT = H["D_sb"]
    v_tm = H["E_sb"]
    F = H["F_sb"]
    Wo = H["D_sb"]
    dmask = H["dmask_sb"]
    scr = H["scr_sb"]
    mx4 = H["mx4_sb"]
    nmx = H["nmx_sb"]
    den4 = H["den4_sb"]
    rec = H["rec_sb"]
    x0t = [H["x0a_sb"], H["x0b_sb"]]
    g0t = [H["g0a_sb"], H["g0b_sb"]]
    x1t = g0t
    xnbf = H["xnbf_sb"]
    mv = H["mv_sb"]; st6 = H["st6_sb"]; rstd = H["rstd_sb"]
    ps = H["ps_f"]
    psb = [H["psb0"], H["psb1"]]

    ld_x = H["LdIn"].dma(lambda e, s: e.dma_start(xnT[:, :], H["xnT_d"]))
    ld_dm = H["LdIn"].dma(lambda e, s: e.dma_start(
        dmask[:, :].rearrange("p (r n) -> p r n", n=512),
        H["dmask_d"].rearrange("r p n -> p r n")))
    ld_id = H["LdIn"].dma(lambda e, s: e.dma_start(H["ident_sb"][:, :], H["ident_d"]))
    def wload(dst, hbm, waits=()):
        return H["Wt"].dma(lambda e, s: e.dma_start(
            dst.rearrange("p (c n) -> p c n", n=1024),
            hbm.rearrange("(c p) n -> p c n", p=128)), waits=waits)
    ld_wq = wload(B[:, 0:8192], H["Wq_d"])
    ld_wk = wload(B[:, 8192:16384], H["Wk_d"])
    ld_wv = wload(F[:, :], H["Wv_d"])

    bank_war = {i: ("Sc", 0) for i in range(6)}
    proj_toks = {"q": [], "k": []}
    for which, wbase, ldw, dst in (("q", 0, ld_wq, qt_ot), ("k", 8192, ld_wk, kT)):
        for qc in range(CC):
            for n in range(4):
                b = (qc * 4 + n) % 4
                mm = None
                for dc in range(CC):
                    mm = Sch.op("tensor", lambda e, s, qc=qc, n=n, dc=dc, b=b, wbase=wbase: e.matmul(
                        ps[b][:, :], B[:, wbase + dc * 1024 + qc * 128: wbase + dc * 1024 + (qc + 1) * 128],
                        xnT[:, dc * GW + 1 + n * 512: dc * GW + 1 + (n + 1) * 512],
                        start=(dc == 0), stop=(dc == CC - 1)),
                        waits=(ld_x, ldw, bank_war[b]), inc=(dc == CC - 1))
                ev = Sch.op("scalar", lambda e, s, qc=qc, n=n, b=b, dst=dst: e.copy(
                    dst[:, qc * S + n * 512: qc * S + (n + 1) * 512], ps[b][:, :]), waits=(mm,))
                bank_war[b] = ev
                proj_toks[which].append(ev)
    v_toks = []
    for t in range(TT):
        for n in range(2):
            b = (t * 2 + n) % 4
            mm = None
            for dc in range(CC):
                mm = Sch.op("tensor", lambda e, s, t=t, n=n, dc=dc, b=b: e.matmul(
                    ps[b][:, :], xnT[:, dc * GW + 1 + t * 128: dc * GW + 1 + (t + 1) * 128],
                    F[:, dc * 1024 + n * 512: dc * 1024 + (n + 1) * 512],
                    start=(dc == 0), stop=(dc == CC - 1)),
                    waits=(ld_wv, bank_war[b]), inc=(dc == CC - 1))
            ev = Sch.op("scalar", lambda e, s, t=t, n=n, b=b: e.copy(
                v_tm[:, t * D + n * 512: t * D + (n + 1) * 512], ps[b][:, :]), waits=(mm,))
            bank_war[b] = ev
            v_toks.append(ev)
    v_done = v_toks[-1]

    SCL = 1.0 / 32.0
    raw_off = 0
    norm_off = [2048, 4096]
    et_off = [6144, 6272]
    exp_war = [v_done, v_done]
    et_war = [("T", 0), ("T", 0)]
    psb_war = [("Sc", 0), ("Sc", 0)]
    oT_evac_toks = []
    prev_nm = ("V", 0)
    for qt in range(TT):
        nch = qt // 4 + 1
        eo = norm_off[qt % 2]
        sc_mm = {}
        for dc in range(CC):
            for ch in range(nch):
                sc_mm[ch] = Sch.op("tensor", lambda e, s, qt=qt, dc=dc, ch=ch: e.matmul(
                    ps[ch][:, :], qt_ot[:, dc * S + qt * 128: dc * S + (qt + 1) * 128],
                    kT[:, dc * S + ch * 512: dc * S + (ch + 1) * 512],
                    start=(dc == 0), stop=(dc == CC - 1)),
                    waits=(proj_toks["q"][-1], proj_toks["k"][-1], prev_nm,
                           bank_war[ch],
                           oT_evac_toks[-1] if oT_evac_toks else ("Sc", 0)),
                    inc=(dc == CC - 1))
        dm = Sch.op("vector", lambda e, s, qt=qt, nch=nch: e.tensor_add(
            scr[:, :], ps[nch - 1][:, :], dmask[:, (qt % 4) * 512:(qt % 4 + 1) * 512]),
            waits=(sc_mm[nch - 1], ld_dm))
        mx_toks = []
        for ch in range(nch):
            src = scr[:, :] if ch == nch - 1 else ps[ch][:, :]
            mx_toks.append(Sch.op("vector", lambda e, s, ch=ch, src=src: e.reduce_max(
                mx4[:, ch:ch + 1], src, axis=AX.X), waits=(sc_mm[ch], dm)))
        nm = Sch.op("vector", lambda e, s, nch=nch: e.reduce_max(
            nmx[:, :], mx4[:, 0:nch], axis=AX.X, negate=True), waits=tuple(mx_toks))
        nm2 = Sch.op("vector", lambda e, s: e.tensor_scalar_mul(nmx[:, :], nmx[:, :], SCL),
                     waits=(nm,))
        exp_toks = []
        for ch in range(nch):
            src = scr[:, :] if ch == nch - 1 else ps[ch][:, :]
            exp_toks.append(Sch.op("scalar", lambda e, s, ch=ch, src=src: e.activation(
                F[:, raw_off + ch * 512: raw_off + (ch + 1) * 512], src, AF.Exp,
                bias=nmx[:, 0:1], scale=SCL, accum_out=den4[:, ch:ch + 1]),
                waits=(nm2, prev_nm)))
        dn = Sch.op("vector", lambda e, s, nch=nch: e.reduce_sum(
            rec[:, 0:1], den4[:, 0:nch], axis=AX.X), waits=tuple(exp_toks))
        dn2 = Sch.op("vector", lambda e, s: e.reciprocal(rec[:, 1:2], rec[:, 0:1]), waits=(dn,))
        nrm = Sch.op("vector", lambda e, s, eo=eo, nch=nch: e.tensor_scalar_mul(
            F[:, eo: eo + nch * 512], F[:, raw_off: raw_off + nch * 512], rec[:, 1:2]),
            waits=(dn2, exp_war[qt % 2]) + tuple(exp_toks))
        for ch in range(nch):
            bank_war[ch] = exp_toks[ch]
        last_av = {}
        for kt in range(qt + 1):
            eb = kt % 2
            tr = Sch.op("tensor", lambda e, s, eo=eo, kt=kt, eb=eb: e.transpose(
                psb[eb], F[:, eo + kt * 128: eo + (kt + 1) * 128], H["ident_sb"][:, :]),
                waits=(nrm, ld_id, psb_war[eb]))
            te = Sch.op("scalar", lambda e, s, kt=kt, eb=eb: e.copy(
                F[:, et_off[eb]: et_off[eb] + 128], psb[eb]), waits=(tr, et_war[eb]))
            psb_war[eb] = te
            for vd in range(CC):
                bank = 4 + vd // 4
                col = (vd % 4) * 128
                last_av[vd] = Sch.op("tensor", lambda e, s, kt=kt, vd=vd, eb=eb, bank=bank, col=col, qt=qt: e.matmul(
                    ps[bank][:, col:col + 128],
                    v_tm[:, kt * D + vd * 128: kt * D + (vd + 1) * 128],
                    F[:, et_off[eb]: et_off[eb] + 128],
                    start=(kt == 0 and vd % 4 == 0), stop=(kt == qt and vd % 4 == 3)),
                    waits=(te, v_toks[kt * 2 + vd // 4],
                           bank_war[bank] if kt == 0 else ("Sc", 0)),
                    inc=(kt == qt and vd % 4 == 3))
            et_war[eb] = last_av[7]
        exp_war[qt % 2] = ("T", Sch.count.get("T", 0))
        prev_nm = nrm
        gend = {4: last_av[3], 5: last_av[7]}
        for vd in range(CC):
            bank = 4 + vd // 4
            col = (vd % 4) * 128
            tok_ev = Sch.op("scalar", lambda e, s, vd=vd, qt=qt, bank=bank, col=col: e.copy(
                qt_ot[:, vd * S + qt * 128: vd * S + (qt + 1) * 128],
                ps[bank][:, col:col + 128]), waits=(gend[bank],))
            oT_evac_toks.append(tok_ev)
            bank_war[bank] = tok_ev

    attn_T_done = ("T", Sch.count.get("T", 0))
    ld_wo = H["Wt"].dma(lambda e, s: e.dma_start(
        Wo[:, 0:8192].rearrange("p (c n) -> p c n", n=1024),
        H["Wo_d"].rearrange("(c p) n -> p c n", p=128)), waits=(attn_T_done,))
    z_toks = []
    for cc in range(CC):
        z_toks.append(Sch.op("vector", lambda e, s, cc=cc: e.memset(
            B[:, cc * GW: cc * GW + 1], 0.0),
            waits=(proj_toks["q"][-1], proj_toks["k"][-1])))
    comb_toks = []
    ld_toks = {}
    v_guard = ("V", Sch.count.get("V", 0))
    sc_guard = ("Sc", Sch.count.get("Sc", 0))

    def emit_tile_loads(t):
        a = t % 2
        if t >= 2:
            waits = (ln_v_tok[t - 2], ln_st_tok[t - 2])
        else:
            waits = ()
        ld_toks[("x", t)] = H["LdA"].dma(lambda e, s, t=t, a=a: e.dma_start(
            x0t[a][:, :], H["x_d"][t * 128:(t + 1) * 128, :]), waits=waits)
        ld_toks[("g", t)] = H["LdB"].dma(lambda e, s, t=t, a=a: e.dma_start(
            g0t[a][:, :], H["g0_d"][t * 128:(t + 1) * 128, :]), waits=waits)

    st_x1 = []
    ln_toks = []
    ln_v_tok = {}
    ln_st_tok = {}
    ln_ps_tok = []
    emit_tile_loads(0)
    emit_tile_loads(1)
    for t in range(TT):
        a = t % 2
        for n in range(2):
            b = (t * 2 + n) % 2
            war = comb_toks[-2] if len(comb_toks) >= 2 else ("Sc", 0)
            mm = None
            for vd in range(CC):
                mm = Sch.op("tensor", lambda e, s, t=t, n=n, vd=vd, b=b: e.matmul(
                    ps[b][:, :], qt_ot[:, vd * S + t * 128: vd * S + (t + 1) * 128],
                    Wo[:, vd * 1024 + n * 512: vd * 1024 + (n + 1) * 512],
                    start=(vd == 0), stop=(vd == CC - 1)),
                    waits=(ld_wo, oT_evac_toks[-1], war, v_guard, bank_war[b]),
                    inc=(vd == CC - 1))
            c1 = Sch.op("vector", lambda e, s, t=t, n=n, a=a, b=b: e.tensor_mul(
                scr[:, :], g0t[a][:, n * 512:(n + 1) * 512], ps[b][:, :]),
                waits=(mm, ld_toks[("g", t)], ld_toks[("x", t)],
                       comb_toks[-1] if comb_toks else ("V", 0)))
            c2 = Sch.op("vector", lambda e, s, t=t, n=n, a=a: e.tensor_add(
                x1t[a][:, n * 512:(n + 1) * 512], x0t[a][:, n * 512:(n + 1) * 512], scr[:, :]),
                waits=(c1,))
            bank_war[b] = c1
            comb_toks.append(c2)
        st = H["St"].dma(lambda e, s, t=t, a=a: e.dma_start(
            H["x1_d"][t * 128:(t + 1) * 128, :], x1t[a][:, :]),
            waits=(comb_toks[-1], comb_toks[-2]))
        st_x1.append(st)
        s1a = Sch.op("vector", lambda e, s, a=a: e.bn_stats(st6[:, 0:6], x1t[a][:, 0:512]),
                     waits=(comb_toks[-1], comb_toks[-2]))
        s1b = Sch.op("vector", lambda e, s, a=a: e.bn_stats(st6[:, 6:12], x1t[a][:, 512:1024]),
                     waits=(s1a,))
        s2 = Sch.op("vector", lambda e, s: e.bn_aggr(mv[:, :], st6[:, :]), waits=(s1a, s1b))
        eps_ap, eps_tok = Sch.const_col(1e-5)
        s3 = Sch.op("scalar", lambda e, s, eps_ap=eps_ap: e.activation(
            rstd[:, 0:1], mv[:, 1:2], AF.Sqrt, bias=eps_ap), waits=(s2, eps_tok))
        s4 = Sch.op("vector", lambda e, s: e.reciprocal(rstd[:, 1:2], rstd[:, 0:1]), waits=(s3,))
        prev_tr = ln_toks[-1] if ln_toks else ("T", 0)
        s5 = Sch.op("vector", lambda e, s, a=a: e.tensor_scalar(
            xnbf[:, :], x1t[a][:, :], mv[:, 0:1], rstd[:, 1:2], ALU.subtract, ALU.mult),
            waits=(s4, prev_tr))
        pst = ln_ps_tok[0] if ln_ps_tok else [sc_guard, sc_guard]
        lastev, pst = transpose_tiles(
            Sch, H,
            lambda i: xnbf[:, i * 128:(i + 1) * 128],
            lambda i, t=t: B[:, i * GW + 1 + t * 128: i * GW + 1 + (t + 1) * 128],
            CC, psb, pst, waits=(s5,) + tuple(z_toks))
        ln_ps_tok[:] = [pst]
        ln_toks.append(lastev)
        ln_v_tok[t] = s5
        ln_st_tok[t] = st
        if t + 2 < TT:
            emit_tile_loads(t + 2)
    st_xn1 = H["St"].dma(lambda e, s: e.dma_start(H["xn1T_d"], B[:, 0:CC * GW]),
                    waits=(ln_toks[-1], ("Sc", Sch.count.get("Sc", 0))))
    return st_x1 + [st_xn1]


def alloc_p3(nc, e, H):
    H["scr_sb"] = e(nc.sbuf_tensor("scr", [128, 512], F32))
    H["xnT_sb"] = e(nc.sbuf_tensor("xnT_sb", [128, CC * GW], BF16))
    H["B_sb"] = e(nc.sbuf_tensor("B_sb", [128, CC * GW], BF16))
    H["C_sb"] = e(nc.sbuf_tensor("C_sb", [128, CC * S], BF16))
    H["D_sb"] = e(nc.sbuf_tensor("D_sb", [128, CC * S], BF16))
    H["E_sb"] = e(nc.sbuf_tensor("E_sb", [128, TT * D], BF16))
    H["F_sb"] = e(nc.sbuf_tensor("F_sb", [128, 8192], BF16))
    H["dmask_sb"] = e(nc.sbuf_tensor("dmask_sb", [128, 4 * 512], BF16))
    H["mx4_sb"] = e(nc.sbuf_tensor("mx4", [128, 4], F32))
    H["nmx_sb"] = e(nc.sbuf_tensor("nmx", [128, 1], F32))
    H["den4_sb"] = e(nc.sbuf_tensor("den4", [128, 4], F32))
    H["rec_sb"] = e(nc.sbuf_tensor("rec", [128, 2], F32))
    H["x0a_sb"] = e(nc.sbuf_tensor("x0a", [128, D], F32))
    H["x0b_sb"] = e(nc.sbuf_tensor("x0b", [128, D], F32))
    H["g0a_sb"] = e(nc.sbuf_tensor("g0a", [128, D], F32))
    H["g0b_sb"] = e(nc.sbuf_tensor("g0b", [128, D], F32))
    H["xnbf_sb"] = e(nc.sbuf_tensor("xnbf", [128, D], BF16))
    H["mv_sb"] = e(nc.sbuf_tensor("mv", [128, 2], F32))
    H["st6_sb"] = e(nc.sbuf_tensor("st6", [128, 12], F32))
    H["rstd_sb"] = e(nc.sbuf_tensor("rstd", [128, 2], F32))
    H["ident_sb"] = e(nc.sbuf_tensor("ident_sb", [128, 128], BF16))


def build_p3_program():
    nc = bass.Bass("TRN2", target_bir_lowering=False)
    H = {}
    H["xnT_d"] = nc.dram_tensor("xnT", [128, CC * GW], BF16, kind="ExternalInput").ap()
    H["x_d"] = nc.dram_tensor("x", [S, D], F32, kind="ExternalInput").ap()
    H["g0_d"] = nc.dram_tensor("g0", [S, D], F32, kind="ExternalInput").ap()
    H["Wq_d"] = nc.dram_tensor("Wq", [D, D], F32, kind="ExternalInput").ap()
    H["Wk_d"] = nc.dram_tensor("Wk", [D, D], F32, kind="ExternalInput").ap()
    H["Wv_d"] = nc.dram_tensor("Wv", [D, D], F32, kind="ExternalInput").ap()
    H["Wo_d"] = nc.dram_tensor("Wo", [D, D], F32, kind="ExternalInput").ap()
    H["dmask_d"] = nc.dram_tensor("dmask", [4, 128, 512], BF16, kind="ExternalInput").ap()
    H["ident_d"] = nc.dram_tensor("ident", [128, 128], BF16, kind="ExternalInput").ap()
    H["x1_d"] = nc.dram_tensor("x1", [S, D], F32, kind="ExternalOutput").ap()
    H["xn1T_d"] = nc.dram_tensor("xn1T", [128, CC * GW], BF16, kind="ExternalOutput").ap()
    with ExitStack() as ctx:
        e = ctx.enter_context
        alloc_p3(nc, e, H)
        H["cpool"] = e(nc.sbuf_tensor("cpool", [128, 16], F32))
        H["ps_f"] = [e(nc.psum_tensor(f"psf{i}", [128, 512], F32))[:, :] for i in range(6)]
        ps0 = e(nc.psum_tensor("psb0", [128, 128], BF16))
        ps1 = e(nc.psum_tensor("psb1", [128, 128], BF16))
        H["psb0"], H["psb1"] = ps0[:, :], ps1[:, :]
        Sch = Sched()
        Sch._const_pool = H["cpool"]
        mk_streams(Sch, H)
        emit_P3(nc, Sch, H)
        finish_program(nc, Sch, H)
    return nc



def emit_P4b(nc, Sch, H):
    xn1T = H["xn1T_sb"]
    WA = H["wa_sb"]
    WB = H["wb_sb"]
    WC = WB
    tm = H["tm_sb"]
    qT = H["qT_sb"]
    kT = H["kT_sb"]
    expb = H["expb_sb"]
    stage = H["stage_sb"]
    ropeC = H["ropeC_sb"]
    ropeS = H["ropeS_sb"]
    tri = H["tri_sb"]
    ssq = H["ssq_sb"]
    nB = H["nB_sb"]
    nBbf = H["nBbf_sb"]
    nrow = H["nrow_sb"]
    negbc = H["negbc_sb"]
    ones_row = H["ones_row_sb"]
    rec1 = H["rec1_sb"]
    oun = H["oun_sb"]
    rec64 = H["rec64_sb"]
    xf = xn1T.bitcast(F32)
    x1t = [xf[:, 0:D], xf[:, D:2 * D]]
    g1t = [xf[:, 2 * D:3 * D], xf[:, 3 * D:4 * D]]
    scr = xf[:, 4 * D:4 * D + 512]
    sqs = H["str4_sb"][:, 0:512]
    str4 = H["str4_sb"]
    ps = H["ps_f"]
    psb = [H["psb0"], H["psb1"]]
    bank_war = {i: ("Sc", 0) for i in range(5)}

    ld_x = H["LdIn"].dma(lambda e, s: e.dma_start(xn1T[:, :], H["xn1T_d"]))
    ld_tc = H["LdIn"].dma(lambda e, s: e.dma_start(ropeC[:, :], H["ropeC_d"]))
    ld_ts = H["LdIn"].dma(lambda e, s: e.dma_start(ropeS[:, :], H["ropeS_d"]))
    ld_tri = H["LdIn"].dma(lambda e, s: e.dma_start(tri[:, :], H["tri_d"]))
    ld_id = H["LdIn"].dma(lambda e, s: e.dma_start(H["ident_sb"][:, :], H["ident_d"]))

    def wload(dst, hbm, waits=()):
        return H["Wt"].dma(lambda e, s, dst=dst, hbm=hbm: e.dma_start(
            dst.rearrange("p (c n) -> p c n", n=1024),
            H[hbm].rearrange("(c p) n -> p c n", p=128)), waits=waits)
    ld_wq = wload(WA[:, 0:CC * 1024], "WqT_d")
    ld_wk = wload(WA[:, CC * 1024:2 * CC * 1024], "WkT_d")
    ld_wv = wload(WB[:, :], "WvT_d")

    one_tok = Sch.op("vector", lambda e, s: e.memset(ones_row[:, :], 1.0))
    ones_col = H["ones_col_sb"]
    onec_tok = Sch.op("vector", lambda e, s: e.memset(ones_col[:, :], 1.0))
    ones_rowf = H["ones_rowf_sb"]
    onef_tok = Sch.op("vector", lambda e, s: e.memset(ones_rowf[:, :], 1.0))
    zstage = Sch.op("vector", lambda e, s: e.memset(nBbf[:, :], 0.0))

    def proj_rope(which, wbase, ldw, dstT, ssq_off):
        toks = [("V", Sch.count.get("V", 0))]
        tr_last = None
        for t in range(TT):
            mm = {}
            for n in range(2):
                b = (t * 2 + n) % 2
                mmx = None
                for cc in range(CC):
                    mmx = Sch.op("tensor", lambda e, s, t=t, n=n, cc=cc, b=b, wbase=wbase: e.matmul(
                        ps[b][:, :], xn1T[:, cc * GW + 1 + t * 128: cc * GW + 1 + (t + 1) * 128],
                        WA[:, wbase + cc * 1024 + n * 512: wbase + cc * 1024 + (n + 1) * 512],
                        start=(cc == 0), stop=(cc == CC - 1)),
                        waits=(ld_x, ldw, bank_war[b]), inc=(cc == CC - 1))
                mm[n] = mmx
            rope_toks = []
            for n in range(2):
                b = (t * 2 + n) % 2
                pe = ps[b]
                Ct = ropeC[:, t * 32:(t + 1) * 32]
                St = ropeS[:, t * 32:(t + 1) * 32]
                def ap3(base_ap, off, st0, n0, st1, n1):
                    t_ = base_ap.tensor
                    return bass.AP(t_, base_ap.offset + off, [[base_ap.ap[0][0], 128], [st0, n0], [st1, n1]])
                teap = ap3(pe, 0, 64, 8, 2, 32)
                toap = ap3(pe, 1, 64, 8, 2, 32)
                Cb = ap3(Ct, 0, 0, 8, 1, 32)
                Sb = ap3(St, 0, 0, 8, 1, 32)
                sA = ap3(str4[:, 0:256], 0, 32, 8, 1, 32)
                sB = ap3(str4[:, 256:512], 0, 32, 8, 1, 32)
                sC = ap3(str4[:, 512:768], 0, 32, 8, 1, 32)
                sD = ap3(str4[:, 768:1024], 0, 32, 8, 1, 32)
                qe = ap3(tm[:, t * D + n * 512: t * D + (n + 1) * 512], 0, 64, 8, 2, 32)
                qo = ap3(tm[:, t * D + n * 512: t * D + (n + 1) * 512], 1, 64, 8, 2, 32)
                war_s = rope_toks[-1] if rope_toks else toks[-1]
                r1 = Sch.op("vector", lambda e, s, teap=teap, Cb=Cb, sA=sA: e.tensor_mul(sA, teap, Cb),
                            waits=(mm[n], ld_tc, war_s))
                r2 = Sch.op("vector", lambda e, s, toap=toap, Sb=Sb, sB=sB: e.tensor_mul(sB, toap, Sb),
                            waits=(mm[n], ld_ts, war_s))
                r3 = Sch.op("vector", lambda e, s, teap=teap, Sb=Sb, sC=sC: e.tensor_mul(sC, teap, Sb),
                            waits=(mm[n], war_s))
                r4 = Sch.op("vector", lambda e, s, toap=toap, Cb=Cb, sD=sD: e.tensor_mul(sD, toap, Cb),
                            waits=(mm[n], war_s))
                r5 = Sch.op("vector", lambda e, s, sA=sA, sB=sB, qe=qe: e.scalar_tensor_tensor(
                    qe, sB, -1.0, sA, ALU.mult, ALU.add), waits=(r1, r2))
                r6 = Sch.op("vector", lambda e, s, sC=sC, sD=sD, qo=qo: e.tensor_add(qo, sC, sD),
                            waits=(r3, r4))
                rope_toks += [r5, r6]
                bank_war[b] = r4
            s2 = None
            for half in range(2):
                s1 = Sch.op("vector", lambda e, s, t=t, half=half: e.tensor_mul(
                    sqs, tm[:, t * D + half * 512: t * D + half * 512 + 512],
                    tm[:, t * D + half * 512: t * D + half * 512 + 512]),
                    waits=tuple(rope_toks) + ((s2,) if s2 else ()))
                def red_fn(e, s, t=t, half=half, ssq_off=ssq_off):
                    with nc.allow_low_precision(reason="softmax bound only"):
                        return e.reduce_sum(
                            ssq[:, ssq_off + t * 16 + half * 8: ssq_off + t * 16 + half * 8 + 8],
                            sqs.rearrange("p (h d) -> p h d", d=64), axis=AX.X)
                s2 = Sch.op("vector", red_fn, waits=(s1,))
            toks.append(s2)
            for ccj in range(CC):
                eb = ccj % 2
                trw = psb_war[eb]
                trk = Sch.op("tensor", lambda e, s, t=t, ccj=ccj, eb=eb: e.transpose(
                    psb[eb], tm[:, t * D + ccj * 128: t * D + (ccj + 1) * 128], H["ident_sb"][:, :]),
                    waits=(rope_toks[0], rope_toks[1], rope_toks[2], rope_toks[3], ld_id, trw))
                ev = Sch.op("scalar", lambda e, s, t=t, ccj=ccj, eb=eb, dstT=dstT: e.copy(
                    dstT[:, ccj * S + t * 128: ccj * S + (t + 1) * 128], psb[eb]), waits=(trk,))
                psb_war[eb] = ev
                tr_last = ev
        return toks, tr_last

    psb_war = [("Sc", 0), ("Sc", 0)]
    q_toks, q_tr = proj_rope("q", 0, ld_wq, qT, 0)
    k_toks, k_tr = proj_rope("k", CC * 1024, ld_wk, kT, TT * 16)
    qk_done = ("Sc", Sch.count.get("Sc", 0))

    v_toks = []
    for t in range(TT):
        for n in range(2):
            b = (t * 2 + n) % 2
            mmx = None
            for cc in range(CC):
                mmx = Sch.op("tensor", lambda e, s, t=t, n=n, cc=cc, b=b: e.matmul(
                    ps[b][:, :], xn1T[:, cc * GW + 1 + t * 128: cc * GW + 1 + (t + 1) * 128],
                    WB[:, cc * 1024 + n * 512: cc * 1024 + (n + 1) * 512],
                    start=(cc == 0), stop=(cc == CC - 1)),
                    waits=(ld_wv, bank_war[b],
                           ld_wk if (t == 0 and n == 0) else ("V", 0)), inc=(cc == CC - 1))
            ev = Sch.op("scalar", lambda e, s, t=t, n=n, b=b: e.copy(
                WA[:, t * D + n * 512: t * D + (n + 1) * 512], ps[b][:, :]), waits=(mmx, qk_done))
            bank_war[b] = ev
            v_toks.append(ev)
    v_tm = WA
    ld_wo = wload(WC[:, :], "WoT_d", waits=(("T", Sch.count.get("T", 0)),))

    z0 = Sch.op("vector", lambda e, s: e.memset(nB[0:16, 0:1], 0.0))
    z1 = Sch.op("vector", lambda e, s: e.memset(nB[0:16, 2:3], 0.0))
    tr_war = ("V", 0)
    mq = mk = None
    for which, off, toks, base in (("q", 0, q_toks, 0), ("k", TT * 16, k_toks, 2)):
        run = z0 if which == "q" else z1
        for t in range(TT):
            trk = Sch.op("tensor", lambda e, s, t=t, off=off: e.transpose(
                psb[1][0:16, 0:128], ssq[:, off + t * 16: off + (t + 1) * 16],
                H["ident_sb"][:, :]), waits=(toks[t + 1], tr_war, psb_war[1]))
            tmx = Sch.op("vector", lambda e, s: e.reduce_max(
                nB[0:16, 4:5], psb[1][0:16, 0:128], axis=AX.X), waits=(trk,))
            psb_war[1] = tmx
            run = Sch.op("vector", lambda e, s, t=t, base=base: e.tensor_max(
                nB[0:16, base + (t + 1) % 2: base + (t + 1) % 2 + 1],
                nB[0:16, base + t % 2: base + t % 2 + 1], nB[0:16, 4:5]),
                waits=(tmx, run))
            tr_war = run
        if which == "q":
            mq = run
        else:
            mk = run
    qcol = TT % 2
    pr = Sch.op("vector", lambda e, s, qcol=qcol: e.tensor_mul(
        nB[0:16, 5:6], nB[0:16, qcol:qcol + 1], nB[0:16, 2 + qcol:3 + qcol]),
        waits=(mq, mk))
    sqr = Sch.op("scalar", lambda e, s: e.activation(nB[0:16, 6:7], nB[0:16, 5:6], AF.Sqrt), waits=(pr,))
    ngb = Sch.op("vector", lambda e, s: e.tensor_scalar_mul(nBbf[0:16, 0:1], nB[0:16, 6:7], -0.125),
                 waits=(sqr, zstage))
    trb = Sch.op("tensor", lambda e, s: e.transpose(psb[0][:, 0:16], nBbf[0:16, :],
                 H["ident_sb"][0:16, 0:16]), waits=(ngb, psb_war[0]))
    evb = Sch.op("scalar", lambda e, s: e.copy(nrow[:, :], psb[0][0:1, 0:16]), waits=(trb,))
    bcb = Sch.op("tensor", lambda e, s: e.matmul(
        H["ps_n"][:, 0:16], ones_row[0:1, :], nrow[0:1, :], start=True, stop=True),
        waits=(evb, one_tok, ("Sc", Sch.count.get("Sc", 0))))
    nbc = Sch.op("scalar", lambda e, s: e.copy(negbc[:, :], H["ps_n"][:, 0:16]), waits=(bcb,))
    psb_war[0] = evb

    SCL = 0.125
    outT = tm
    out_evacs = []
    eb_war = [("G", 0)] * 4
    stage_war = ("G", 0)
    den_war = ("V", 0)
    ps_n_war = nbc
    for h in range(16):
        cc_t = h // 2
        prow = (h % 2) * 64
        for qc in range(4):
            ob = 3 + qc % 2
            nkt = 4 * qc + 4
            av_last = None
            den_last = None
            for kt in range(nkt):
                off = max(0, kt * 128 - qc * 512)
                N = 512 - off
                sb_ = kt % 2
                diag = kt >= 4 * qc
                smm = Sch.op("tensor", lambda e, s, h=h, qc=qc, kt=kt, sb_=sb_, off=off, N=N, cc_t=cc_t, prow=prow: e.matmul(
                    ps[sb_][:, 0:N],
                    kT[prow:prow + 64, cc_t * S + kt * 128: cc_t * S + (kt + 1) * 128],
                    qT[prow:prow + 64, cc_t * S + qc * 512 + off: cc_t * S + (qc + 1) * 512],
                    start=True, stop=True),
                    waits=(q_tr, k_tr, bank_war[sb_]))
                eb = kt % 4
                if diag:
                    ex1 = Sch.op("scalar", lambda e, s, sb_=sb_, h=h: e.activation(
                        stage[:, 0:128], ps[sb_][:, 0:128], AF.Exp,
                        bias=negbc[:, h:h + 1], scale=SCL),
                        waits=(smm, nbc, stage_war))
                    toks_e = [ex1]
                    if N > 128:
                        ex2 = Sch.op("scalar", lambda e, s, sb_=sb_, eb=eb, N=N, h=h: e.activation(
                            expb[:, eb * 512 + 128: eb * 512 + N], ps[sb_][:, 128:N], AF.Exp,
                            bias=negbc[:, h:h + 1], scale=SCL),
                            waits=(smm, eb_war[eb]))
                        toks_e.append(ex2)
                    msk = Sch.op("gpsimd", lambda e, s, eb=eb: e.tensor_mul(
                        expb[:, eb * 512: eb * 512 + 128], stage[:, 0:128], tri[:, :]),
                        waits=(ex1, ld_tri, eb_war[eb]))
                    stage_war = msk
                    e_tok = (msk,) + tuple(toks_e)
                else:
                    ex = Sch.op("scalar", lambda e, s, sb_=sb_, eb=eb, N=N, h=h: e.activation(
                        expb[:, eb * 512: eb * 512 + N], ps[sb_][:, 0:N], AF.Exp,
                        bias=negbc[:, h:h + 1], scale=SCL),
                        waits=(smm, eb_war[eb], nbc))
                    e_tok = (ex,)
                bank_war[sb_] = e_tok[0]
                av_last = Sch.op("tensor", lambda e, s, h=h, kt=kt, eb=eb, ob=ob, off=off, N=N: e.matmul(
                    ps[ob][0:64, off:512], v_tm[:, kt * D + h * 64: kt * D + (h + 1) * 64],
                    expb[:, eb * 512: eb * 512 + N],
                    start=(kt == 0), stop=(kt == nkt - 1), skip_group_check=True),
                    waits=e_tok + ((v_toks[-1], bank_war[ob]) if kt == 0 else ()))
                den_last = Sch.op("tensor", lambda e, s, kt=kt, eb=eb, off=off, N=N: e.matmul(
                    ps[2][0:1, off:512], ones_col[:, 0:1],
                    expb[:, eb * 512: eb * 512 + N],
                    start=(kt == 0), stop=(kt == nkt - 1), skip_group_check=True),
                    waits=e_tok + (onec_tok,) + ((den_war,) if kt == 0 else ()),
                    inc=(kt == nkt - 1))
                eb_war[eb] = den_last
            lnd = Sch.op("scalar", lambda e, s: e.activation(
                rec1[0:1, 0:512], ps[2][0:1, 0:512], AF.Ln),
                waits=(den_last, H.get("_rec1_war", ("T", 0))))
            den_war = lnd
            rex = Sch.op("scalar", lambda e, s: e.activation(
                rec64[0:1, 0:512], rec1[0:1, 0:512], AF.Exp, scale=-1.0),
                waits=(lnd, H.get("_rec64_war", ("T", 0))))
            H["_rec1_war"] = rex
            bcd = Sch.op("tensor", lambda e, s: e.matmul(
                H["ps_n"][0:64, 0:512], ones_rowf[0:1, 0:64], rec64[0:1, 0:512],
                start=True, stop=True), waits=(rex, ps_n_war, onef_tok))
            H["_rec64_war"] = bcd
            evo = Sch.op("scalar", lambda e, s, ob=ob: e.copy(oun[:, :], ps[ob][0:64, 0:512]),
                         waits=(den_last, H.get("_oun_war", ("V", 0))))
            mlo = Sch.op("vector", lambda e, s, h=h, qc=qc, cc_t=cc_t, prow=prow: e.tensor_mul(
                outT[prow:prow + 64, cc_t * S + qc * 512: cc_t * S + (qc + 1) * 512],
                oun[:, :], H["ps_n"][0:64, 0:512]),
                waits=(bcd, evo, (v_toks[-1] if (h == 0 and qc == 0) else ("V", 0))))
            ps_n_war = mlo
            H["_oun_war"] = mlo
            bank_war[ob] = mlo
            out_evacs.append(mlo)

    comb_toks = []
    ld_toks = {}
    fin_v = {}
    fin_st = {}

    proj_T_done = ("T", Sch.count.get("T", 0))

    def tile_loads(t):
        a = t % 2
        if t >= 2:
            waits = (fin_v[t - 2], fin_st[t - 2])
        else:
            waits = (proj_T_done,)
        ld_toks[("x", t)] = H["LdA"].dma(lambda e, s, t=t, a=a: e.dma_start(
            x1t[a][:, :], H["x1_d"][t * 128:(t + 1) * 128, :]), waits=waits)
        ld_toks[("g", t)] = H["LdB"].dma(lambda e, s, t=t, a=a: e.dma_start(
            g1t[a][:, :], H["g1_d"][t * 128:(t + 1) * 128, :]), waits=waits)

    tile_loads(0)
    tile_loads(1)
    v_guard = ("V", Sch.count.get("V", 0))
    for t in range(TT):
        a = t % 2
        for n in range(2):
            b = (t * 2 + n) % 2
            mmx = None
            for vd in range(CC):
                mmx = Sch.op("tensor", lambda e, s, t=t, n=n, vd=vd, b=b: e.matmul(
                    ps[b][:, :], outT[:, vd * S + t * 128: vd * S + (t + 1) * 128],
                    WC[:, vd * 1024 + n * 512: vd * 1024 + (n + 1) * 512],
                    start=(vd == 0), stop=(vd == CC - 1)),
                    waits=(ld_wo, out_evacs[-1], bank_war[b], v_guard),
                    inc=(vd == CC - 1))
            c1 = Sch.op("vector", lambda e, s, t=t, n=n, a=a, b=b: e.tensor_mul(
                scr[:, :], g1t[a][:, n * 512:(n + 1) * 512], ps[b][:, :]),
                waits=(mmx, ld_toks[("g", t)], ld_toks[("x", t)],
                       comb_toks[-1] if comb_toks else ("V", 0)))
            c2 = Sch.op("vector", lambda e, s, t=t, n=n, a=a: e.tensor_add(
                g1t[a][:, n * 512:(n + 1) * 512], x1t[a][:, n * 512:(n + 1) * 512], scr[:, :]),
                waits=(c1,))
            bank_war[b] = c1
            comb_toks.append(c2)
        st = H["St"].dma(lambda e, s, t=t, a=a: e.dma_start(
            H["out_d"][t * 128:(t + 1) * 128, :], g1t[a][:, :]),
            waits=(comb_toks[-1], comb_toks[-2]))
        fin_v[t] = comb_toks[-1]
        fin_st[t] = st
        if t + 2 < TT:
            tile_loads(t + 2)


def alloc_p4b(nc, e, H):
    H["xn1T_sb"] = e(nc.sbuf_tensor("xn1T_sb", [128, CC * GW], BF16))
    H["wa_sb"] = e(nc.sbuf_tensor("wa_sb", [128, 2 * CC * 1024], BF16))
    H["wb_sb"] = e(nc.sbuf_tensor("wb_sb", [128, CC * 1024], BF16))
    H["tm_sb"] = e(nc.sbuf_tensor("tm_sb", [128, TT * D], BF16))
    H["qT_sb"] = e(nc.sbuf_tensor("qT_sb", [128, CC * S], BF16))
    H["kT_sb"] = e(nc.sbuf_tensor("kT_sb", [128, CC * S], BF16))
    H["expb_sb"] = e(nc.sbuf_tensor("expb_sb", [128, 4 * 512], BF16))
    H["stage_sb"] = e(nc.sbuf_tensor("stage_sb", [128, 128], BF16))
    H["ropeC_sb"] = e(nc.sbuf_tensor("ropeC_sb", [128, TT * 32], F32))
    H["ropeS_sb"] = e(nc.sbuf_tensor("ropeS_sb", [128, TT * 32], F32))
    H["tri_sb"] = e(nc.sbuf_tensor("tri_sb", [128, 128], BF16))
    H["ssq_sb"] = e(nc.sbuf_tensor("ssq_sb", [128, 2 * TT * 16], BF16))
    H["nB_sb"] = e(nc.sbuf_tensor("nB_sb", [16, 12], F32))
    H["nBbf_sb"] = e(nc.sbuf_tensor("nBbf_sb", [16, 128], BF16))
    H["nrow_sb"] = e(nc.sbuf_tensor("nrow_sb", [1, 16], BF16))
    H["negbc_sb"] = e(nc.sbuf_tensor("negbc_sb", [128, 16], F32))
    H["ones_row_sb"] = e(nc.sbuf_tensor("ones_row_sb", [1, 128], BF16))
    H["ones_col_sb"] = e(nc.sbuf_tensor("ones_col_sb", [128, 1], BF16))
    H["ones_rowf_sb"] = e(nc.sbuf_tensor("ones_rowf_sb", [1, 128], F32))
    H["rec1_sb"] = e(nc.sbuf_tensor("rec1_sb", [1, 512], F32))
    H["oun_sb"] = e(nc.sbuf_tensor("oun_sb", [64, 512], BF16))
    H["rec64_sb"] = e(nc.sbuf_tensor("rec64_sb", [1, 512], F32))
    H["str4_sb"] = e(nc.sbuf_tensor("str4", [128, 4 * 256], F32))
    H["ident_sb"] = e(nc.sbuf_tensor("ident_sb", [128, 128], BF16))


def build_p4b_program():
    nc = bass.Bass("TRN2", target_bir_lowering=False)
    H = {}
    H["xn1T_d"] = nc.dram_tensor("xn1T", [128, CC * GW], BF16, kind="ExternalInput").ap()
    H["WqT_d"] = nc.dram_tensor("WqT", [D, D], F32, kind="ExternalInput").ap()
    H["WkT_d"] = nc.dram_tensor("WkT", [D, D], F32, kind="ExternalInput").ap()
    H["WvT_d"] = nc.dram_tensor("WvT", [D, D], F32, kind="ExternalInput").ap()
    H["WoT_d"] = nc.dram_tensor("WoT", [D, D], F32, kind="ExternalInput").ap()
    H["ropeC_d"] = nc.dram_tensor("ropeC", [128, TT * 32], F32, kind="ExternalInput").ap()
    H["ropeS_d"] = nc.dram_tensor("ropeS", [128, TT * 32], F32, kind="ExternalInput").ap()
    H["tri_d"] = nc.dram_tensor("tri01", [128, 128], BF16, kind="ExternalInput").ap()
    H["ident_d"] = nc.dram_tensor("ident", [128, 128], BF16, kind="ExternalInput").ap()
    H["x1_d"] = nc.dram_tensor("x1", [S, D], F32, kind="ExternalInput").ap()
    H["g1_d"] = nc.dram_tensor("g1", [S, D], F32, kind="ExternalInput").ap()
    H["out_d"] = nc.dram_tensor("out", [S, D], F32, kind="ExternalOutput").ap()
    with ExitStack() as ctx:
        e = ctx.enter_context
        alloc_p4b(nc, e, H)
        H["cpool"] = e(nc.sbuf_tensor("cpool", [128, 16], F32))
        H["ps_f"] = [e(nc.psum_tensor(f"psf{i}", [128, 512], F32))[:, :] for i in range(5)]
        H["ps_n"] = e(nc.psum_tensor("ps_n", [128, 512], F32))[:, :]
        ps0 = e(nc.psum_tensor("psb0", [128, 128], BF16))
        H["psb0"] = ps0[:, :]
        H["psb1"] = e(nc.psum_tensor("psb1", [128, 128], BF16))[:, :]
        Sch = Sched()
        Sch._const_pool = H["cpool"]
        mk_streams(Sch, H)
        emit_P4b(nc, Sch, H)
        finish_program(nc, Sch, H)
    return nc



def build_p1_program():
    nc = bass.Bass("TRN2", target_bir_lowering=False)
    H = {}
    H["x_d"] = nc.dram_tensor("x", [S, D], F32, kind="ExternalInput").ap()
    H["ident_d"] = nc.dram_tensor("ident", [128, 128], BF16, kind="ExternalInput").ap()
    H["u0_d"] = nc.dram_tensor("u0", [S, D], F32, kind="ExternalOutput").ap()
    H["xnT_d"] = nc.dram_tensor("xnT", [128, CC * GW], BF16, kind="ExternalOutput").ap()
    with ExitStack() as ctx:
        e = ctx.enter_context
        H["x_sb"] = e(nc.sbuf_tensor("x_sb", [128, TT * D], F32))
        H["u_sb"] = e(nc.sbuf_tensor("u_sb", [128, TT * D], F32))
        H["xnbf_sb"] = e(nc.sbuf_tensor("xnbf", [128, D], BF16))
        H["xnT_sb"] = e(nc.sbuf_tensor("xnT_sb", [128, CC * GW], BF16))
        H["mv_sb"] = e(nc.sbuf_tensor("mv", [128, 2], F32))
        H["st6_sb"] = e(nc.sbuf_tensor("st6", [128, 12], F32))
        H["rstd_sb"] = e(nc.sbuf_tensor("rstd", [128, 2], F32))
        H["ident_sb"] = e(nc.sbuf_tensor("ident_sb", [128, 128], BF16))
        ps0 = e(nc.psum_tensor("psb0", [128, 128], BF16))
        ps1 = e(nc.psum_tensor("psb1", [128, 128], BF16))
        H["psb0"], H["psb1"] = ps0[:, :], ps1[:, :]
        Sch = Sched()
        H["cpool"] = e(nc.sbuf_tensor("cpool", [128, 16], F32))
        Sch._const_pool = H["cpool"]
        mk_streams(Sch, H)
        ld_id = H["Wt"].dma(lambda e2, s: e2.dma_start(H["ident_sb"][:, :], H["ident_d"]))
        Sch.op("tensor", lambda e2, s: None, waits=(ld_id,), inc=False)
        emit_P1(nc, Sch, H)
        finish_program(nc, Sch, H)
    return nc





class _NCProxy:

    def __init__(self, nc, prefix):
        self._nc = nc
        self._prefix = prefix

    def sbuf_tensor(self, name, *a, **k):
        return self._nc.sbuf_tensor(self._prefix + name, *a, **k)

    def __getattr__(self, k):
        return getattr(self._nc, k)


def build_fused_program(debug=False, phases=("P1", "P2", "P3", "P4a", "P4b"),
                        p2_lis=tuple(range(6))):
    nc = bass.Bass("TRN2", target_bir_lowering=False)
    H = {}
    EI = "ExternalInput"
    IK = "ExternalOutput" if debug else "Internal"
    H["x_d"] = nc.dram_tensor("x", [S, D], F32, kind=EI).ap()
    H["ident_d"] = nc.dram_tensor("ident", [128, 128], BF16, kind=EI).ap()
    H["Wq_d"] = nc.dram_tensor("Wq", [D, D], F32, kind=EI).ap()
    H["Wk_d"] = nc.dram_tensor("Wk", [D, D], F32, kind=EI).ap()
    H["Wv_d"] = nc.dram_tensor("Wv", [D, D], F32, kind=EI).ap()
    H["Wo_d"] = nc.dram_tensor("Wo", [D, D], F32, kind=EI).ap()
    H["dmask_d"] = nc.dram_tensor("dmask", [4, 128, 512], BF16, kind=EI).ap()
    H["Wqkv_full_d"] = nc.dram_tensor("Wqkv", [6, D, 3072], F32, kind=EI).ap()
    H["Wo6_full_d"] = nc.dram_tensor("Wo6", [6, D, D], F32, kind=EI).ap()
    H["W1r_full_d"] = nc.dram_tensor("W1r", [6, D, 5460], F32, kind=EI).ap()
    H["W2_full_d"] = nc.dram_tensor("W2", [6, DI2, D], F32, kind=EI).ap()
    H["cols_full_d"] = nc.dram_tensor("colsP2", [6, 128, 62], F32, kind=EI).ap()
    H["b2_full_d"] = nc.dram_tensor("b2rows", [6, 1024], BF16, kind=EI).ap()
    H["tbl_d"] = nc.dram_tensor("tblP2", [128, 4, S], BF16, kind=EI).ap()
    H["masks_d"] = nc.dram_tensor("masksP2", [4, 128, W0], BF16, kind=EI).ap()
    H["ind16_d"] = nc.dram_tensor("ind16", [128, 128], BF16, kind=EI).ap()
    H["sel16_d"] = nc.dram_tensor("sel16", [16, 8 * 128], BF16, kind=EI).ap()
    H["WrT_d"] = nc.dram_tensor("WrT", [D, D], F32, kind=EI).ap()
    H["WkT2_d"] = nc.dram_tensor("WkT2", [D, D], F32, kind=EI).ap()
    H["WvT2_d"] = nc.dram_tensor("WvT2", [D, D], F32, kind=EI).ap()
    H["mu_r_d"] = nc.dram_tensor("mu_r_cols", [128, 8], F32, kind=EI).ap()
    H["mu_k_d"] = nc.dram_tensor("mu_k_cols", [128, 8], F32, kind=EI).ap()
    H["WqT_d"] = nc.dram_tensor("WqT", [D, D], F32, kind=EI).ap()
    H["WkT_d"] = nc.dram_tensor("WkT", [D, D], F32, kind=EI).ap()
    H["WvT_d"] = nc.dram_tensor("WvT", [D, D], F32, kind=EI).ap()
    H["WoT_d"] = nc.dram_tensor("WoT", [D, D], F32, kind=EI).ap()
    H["ropeC_d"] = nc.dram_tensor("ropeC", [128, TT * 32], F32, kind=EI).ap()
    H["ropeS_d"] = nc.dram_tensor("ropeS", [128, TT * 32], F32, kind=EI).ap()
    H["tri_d"] = nc.dram_tensor("tri01", [128, 128], BF16, kind=EI).ap()
    H["u0_d"] = nc.dram_tensor("u0", [S, D], F32, kind=IK).ap()
    H["xnT_d"] = nc.dram_tensor("xnT", [128, CC * GW], BF16, kind=IK).ap()
    H["bgrid_d"] = nc.dram_tensor("bgrid", [16, 8], F32).ap()
    H["bgrid_flat_d"] = H["bgrid_d"].rearrange("a b -> (a b)").rearrange("(x y) -> x y", x=1)
    H["qT_scr_d"] = nc.dram_tensor("qT_scr", [128, CC * S], BF16).ap()
    H["kT_scr_d"] = nc.dram_tensor("kT_scr", [128, CC * S], BF16).ap()
    H["v_scr_d"] = nc.dram_tensor("v_scr", [128, TT * 1024], BF16).ap()
    H["u_in_0"] = H["u0_d"]
    for li in range(6):
        H[f"u_mid_{li}"] = nc.dram_tensor(f"u_mid_{li}", [S, D], F32, kind=IK).ap()
        H[f"u_out_{li}"] = nc.dram_tensor(f"u_pp_{li}", [S, D], F32, kind=IK).ap()
        if li < 5:
            H[f"u_in_{li + 1}"] = H[f"u_out_{li}"]
    H["g0_d"] = H["u_out_5"]
    H["x1_d"] = nc.dram_tensor("x1", [S, D], F32, kind=IK).ap()
    H["xn1T_d"] = nc.dram_tensor("xn1T", [128, CC * GW], BF16, kind=IK).ap()
    H["g1_d"] = nc.dram_tensor("g1", [S, D], F32, kind=IK).ap()
    H["out_d"] = nc.dram_tensor("out", [S, D], F32, kind="ExternalOutput").ap()
    def sl3(apfull, li):
        return apfull[li:li + 1, :, :].rearrange("l a b -> (l a) b")
    H["Wqkv_d"] = {li: sl3(H["Wqkv_full_d"], li) for li in range(6)}
    H["Wo6_d"] = {li: sl3(H["Wo6_full_d"], li) for li in range(6)}
    H["W1r_d"] = {li: sl3(H["W1r_full_d"], li) for li in range(6)}
    H["W2_d"] = {li: sl3(H["W2_full_d"], li) for li in range(6)}
    H["cols_d"] = {li: sl3(H["cols_full_d"], li) for li in range(6)}
    H["b2_d"] = {li: H["b2_full_d"][li:li + 1, :] for li in range(6)}

    with ExitStack() as ctx:
        e = ctx.enter_context
        H["cpool"] = e(nc.sbuf_tensor("cpool", [128, 16], F32))
        ps_f = [e(nc.psum_tensor(f"psf{i}", [128, 512], F32))[:, :] for i in range(6)]
        H["ps_f"] = ps_f
        H["psb0"] = e(nc.psum_tensor("psb0", [128, 128], BF16))[:, :]
        H["psb1"] = e(nc.psum_tensor("psb1", [128, 128], BF16))[:, :]
        Sch = Sched()
        Sch._const_pool = H["cpool"]
        mk_streams(Sch, H)

        if "P1" in phases:
            with ExitStack() as c1:
                e1 = c1.enter_context
                p1 = _NCProxy(nc, "p1_")
                H["x_sb"] = e1(p1.sbuf_tensor("x_sb", [128, TT * D], F32))
                H["u_sb"] = e1(p1.sbuf_tensor("u_sb", [128, TT * D], F32))
                H["xnbf_sb"] = e1(p1.sbuf_tensor("xnbf", [128, D], BF16))
                H["xnT_sb"] = e1(p1.sbuf_tensor("xnT_sb", [128, CC * GW], BF16))
                H["mv_sb"] = e1(p1.sbuf_tensor("mv", [128, 2], F32))
                H["st6_sb"] = e1(p1.sbuf_tensor("st6", [128, 12], F32))
                H["rstd_sb"] = e1(p1.sbuf_tensor("rstd", [128, 2], F32))
                H["ident_sb"] = e1(p1.sbuf_tensor("ident_sb", [128, 128], BF16))
                Hp1 = dict(H)
                ld_id = Hp1["Wt"].dma(
                    lambda e2, s: e2.dma_start(Hp1["ident_sb"][:, :], Hp1["ident_d"]))
                Sch.op("tensor", lambda e2, s: None, waits=(ld_id,), inc=False)
                emit_P1(nc, Sch, Hp1)
                Sch.barrier()

        if "P2" in phases:
            with ExitStack() as c2:
                e2_ = c2.enter_context
                p2 = _NCProxy(nc, "p2_")
                alloc_p2(p2, e2_, H)
                Hp2 = dict(H)
                p2_common_setup(nc, Sch, Hp2)
                ld_uTg = Hp2["LdIn"].dma(
                    lambda e2, s: e2.dma_start(Hp2["RA_sb"][:, :], Hp2["xnT_d"]))
                Hp2["_p2_uTg_rdy"] = ld_uTg
                for li in p2_lis:
                    emit_P2_sublayer(nc, Sch, Hp2, li, first=(li == p2_lis[0]), last=(li == 5))
                Sch.barrier()

        if any(p.startswith("P3stub") for p in phases):
            mode = [p for p in phases if p.startswith("P3stub")][0]
            with ExitStack() as c3:
                e3 = c3.enter_context
                p3 = _NCProxy(nc, "p3_")
                alloc_p3(p3, e3, H)
                B = H["B_sb"]
                F = H["F_sb"]
                bar_toks = tuple(Sch._tok(s) for s in sorted(Sch.sem_names))
                if mode == "P3stubX":
                    Sch.last_waited["sync"] = {}
                if mode == "P3stubH":
                    scrh = nc.dram_tensor("stub_scr", [128, CC * GW], BF16).ap()
                    H["LdIn"].dma(lambda e2, s: e2.dma_start(scrh, H["xnT_d"]))
                    H["LdIn"].dma(lambda e2, s: e2.dma_start(
                        scrh[:, 0:128], H["ident_d"]))
                    H["LdIn"].dma(lambda e2, s: e2.dma_start(
                        scrh[:, 128:256], H["ident_d"]))
                if mode in ("P3stub", "P3stubL", "P3stubX"):
                    w0 = bar_toks if mode == "P3stubX" else ()
                    H["LdIn"].dma(lambda e2, s: e2.dma_start(H["xnT_sb"][:, :], H["xnT_d"]),
                                  waits=w0)
                    H["LdIn"].dma(lambda e2, s: e2.dma_start(
                        H["dmask_sb"][:, :].rearrange("p (r n) -> p r n", n=512),
                        H["dmask_d"].rearrange("r p n -> p r n")), waits=w0)
                    H["LdIn"].dma(lambda e2, s: e2.dma_start(H["ident_sb"][:, :], H["ident_d"]),
                                  waits=w0)
                if mode in ("P3stub", "P3stubW"):
                    for dst, hbm in ((B[:, 0:8192], H["Wq_d"]), (B[:, 8192:16384], H["Wk_d"]),
                                     (F[:, :], H["Wv_d"])):
                        H["Wt"].dma(lambda e2, s, dst=dst, hbm=hbm: e2.dma_start(
                            dst.rearrange("p (c n) -> p c n", n=1024),
                            hbm.rearrange("(c p) n -> p c n", p=128)))
                Sch.barrier()
        if "P3" in phases:
            with ExitStack() as c3:
                e3 = c3.enter_context
                p3 = _NCProxy(nc, "p3_")
                alloc_p3(p3, e3, H)
                Hp3 = dict(H)
                emit_P3(nc, Sch, Hp3)
                Sch.barrier()

        if "P4a" in phases:
            with ExitStack() as c4:
                e4 = c4.enter_context
                p4 = _NCProxy(nc, "p4a_")
                alloc_p4a(p4, e4, H)
                Hp4 = dict(H)
                emit_P4a(nc, Sch, Hp4)
                Sch.barrier()

        if "P4b" in phases:
            with ExitStack() as c5:
                e5 = c5.enter_context
                p5 = _NCProxy(nc, "p4b_")
                alloc_p4b(p5, e5, H)
                Hp5 = dict(H)
                Hp5["ps_f"] = ps_f[0:5]
                Hp5["ps_n"] = ps_f[5]
                emit_P4b(nc, Sch, Hp5)
        finish_program(nc, Sch, H)
    return nc


FUSED = True


def kernel(**inputs):
    d = host_prep(inputs)
    hp2 = host_prep_p2(inputs)
    x = d["x"]
    n = NCORES

    if FUSED:
        nc = build_fused_program()
        shared = {
            "ident": d["ident"],
            "Wq": d["L0_attn_Wq"], "Wk": d["L0_attn_Wk"],
            "Wv": d["L0_attn_Wv"], "Wo": d["L0_attn_Wo"],
            "dmask": d["diagmask"].astype(ml_dtypes.bfloat16),
            "Wqkv": hp2["Wqkv"], "Wo6": hp2["Wo6"], "W1r": hp2["W1r"],
            "W2": hp2["W2"], "colsP2": hp2["colsP2"], "b2rows": hp2["b2rows"],
            "tblP2": hp2["tblP2"], "masksP2": hp2["masksP2"],
            "ind16": hp2["ind16"], "sel16": hp2["sel16"],
            "WrT": d["WrT"], "WkT2": d["WkT2"], "WvT2": d["WvT2"],
            "mu_r_cols": d["mu_r_cols"], "mu_k_cols": d["mu_k_cols"],
            "WqT": d["L1_WqT"], "WkT": d["L1_WkT"], "WvT": d["L1_WvT"],
            "WoT": d["L1_WoT"], "ropeC": d["ropeC"], "ropeS": d["ropeS"],
            "tri01": d["tri01"],
        }
        in_maps = []
        for b in range(n):
            m = dict(shared)
            m["x"] = np.ascontiguousarray(x[b])
            in_maps.append(m)
        res = run_bass_kernel_spmd(nc, in_maps, core_ids=list(range(n))).results
        return np.stack([np.asarray(r["out"], np.float32) for r in res], axis=0)

    def run(nc, per_core, shared):
        in_maps = []
        for b in range(n):
            m = dict(shared)
            for k, v in per_core.items():
                m[k] = v[b]
            in_maps.append(m)
        return run_bass_kernel_spmd(nc, in_maps, core_ids=list(range(n))).results

    ident = d["ident"]
    nc1 = build_p1_program()
    r1 = run(nc1, {"x": [np.ascontiguousarray(x[b]) for b in range(n)]},
             {"ident": ident})
    u0 = [np.asarray(r["u0"]) for r in r1]
    xnT = [np.asarray(r["xnT"]) for r in r1]
    u_cur = u0
    uTg_cur = xnT
    for li in range(6):
        nc2 = build_p2_program([li])
        shared = {
            "Wqkv": np.ascontiguousarray(hp2["Wqkv"][li:li + 1]),
            "Wo6": np.ascontiguousarray(hp2["Wo6"][li:li + 1]),
            "W1r": np.ascontiguousarray(hp2["W1r"][li:li + 1]),
            "W2": np.ascontiguousarray(hp2["W2"][li:li + 1]),
            "colsP2": np.ascontiguousarray(hp2["colsP2"][li:li + 1]),
            "b2rows": np.ascontiguousarray(hp2["b2rows"][li:li + 1]),
            "tblP2": hp2["tblP2"], "masksP2": hp2["masksP2"],
            "ident": ident, "ind16": hp2["ind16"], "sel16": hp2["sel16"],
        }
        r2 = run(nc2, {"u_in": u_cur, "uTg_in": uTg_cur}, shared)
        u_cur = [np.asarray(r["u_out"]) for r in r2]
        uTg_cur = [np.asarray(r["uTg_out"]) for r in r2]
    g0 = u_cur
    nc3 = build_p3_program()
    r3 = run(nc3, {"xnT": xnT,
                   "x": [np.ascontiguousarray(x[b]) for b in range(n)],
                   "g0": g0},
             {"Wq": d["L0_attn_Wq"], "Wk": d["L0_attn_Wk"],
              "Wv": d["L0_attn_Wv"], "Wo": d["L0_attn_Wo"],
              "dmask": d["diagmask"].astype(ml_dtypes.bfloat16), "ident": ident})
    x1 = [np.asarray(r["x1"]) for r in r3]
    xn1T = [np.asarray(r["xn1T"]) for r in r3]
    nc4a = build_p4a_program()
    r4a = run(nc4a, {"xn1T": xn1T},
              {"WrT": d["WrT"], "WkT2": d["WkT2"], "WvT2": d["WvT2"],
               "mu_r_cols": d["mu_r_cols"], "mu_k_cols": d["mu_k_cols"]})
    g1 = [np.asarray(r["g1"]) for r in r4a]
    nc4b = build_p4b_program()
    r4b = run(nc4b, {"xn1T": xn1T, "x1": x1, "g1": g1},
              {"WqT": d["L1_WqT"], "WkT": d["L1_WkT"], "WvT": d["L1_WvT"],
               "WoT": d["L1_WoT"], "ropeC": d["ropeC"], "ropeS": d["ropeS"],
               "tri01": d["tri01"], "ident": ident})
    out = np.stack([np.asarray(r["out"], np.float32) for r in r4b], axis=0)
    return out
